# version 07
# baseline (speedup 1.0000x reference)
import numpy as np
"""Auto-sync layer over raw Bass: records ops with read/write sets, inserts
semaphore edges (vector clocks for dedup), plays back per engine in one Block.

Rules encoded:
- every compute op incs its engine sem by 1 (value = per-engine op index)
- every dma incs a per-destination-tensor sem by 16 (value = 16 * count)
- RAW/WAW/WAR edges all get waits; same-engine edges too (except TensorE,
  whose queue is FIFO with pc-ordered completion)
- waits deduped via per-engine vector clocks with transitive closure through
  producer snapshots
- finalize(): sync engine waits every dma sem to its final total (timing contract)
"""
import concourse.bass as bass
import concourse.mybir as mybir
from concourse.alu_op_type import AluOpType as OP

F32 = mybir.dt.float32
BF16 = mybir.dt.bfloat16
AF = mybir.ActivationFunctionType
AX = mybir.AxisListType

ENGINES = ("tensor", "vector", "scalar", "gpsimd", "sync")


class Prog:
    def __init__(self, nc, same_engine_sync=True):
        self.nc = nc
        self.same_engine_sync = same_engine_sync
        self.seq = {e: [] for e in ENGINES}
        self.eng_count = {e: 0 for e in ENGINES}
        self.clock = {e: {} for e in ENGINES}
        self.bufs = {}
        self.alias = {}
        self.dma_count = {}
        self.snapshots = {}
        self.n_waits = 0

    def _buf(self, ap):
        name = ap.tensor.name
        name = self.alias.get(name, name)
        b = self.bufs.get(name)
        if b is None:
            b = {"w": None, "r": []}
            self.bufs[name] = b
        return b

    def register_view(self, name, region, lo, hi):
        self.alias.setdefault("__regions__", {})
        regs = self.alias["__regions__"]
        regs.setdefault(region, {})[name] = (lo, hi)

    def _targets(self, ap):
        name = ap.tensor.name
        out = [self._buf(ap)]
        regs = self.alias.get("__regions__", {})
        for region, members in regs.items():
            if name in members:
                lo, hi = members[name]
                for other, (lo2, hi2) in members.items():
                    if other != name and lo < hi2 and lo2 < hi:
                        b = self.bufs.get(other)
                        if b is None:
                            b = {"w": None, "r": []}
                            self.bufs[other] = b
                        out.append(b)
        return out

    def _wait(self, eng, sem_key, val):
        c = self.clock[eng]
        if c.get(sem_key, 0) >= val:
            return
        self.seq[eng].append(("wait", sem_key, val))
        self.n_waits += 1
        if c.get(sem_key, 0) < val:
            c[sem_key] = val
        snap = self.snapshots.get((sem_key, val))
        if snap:
            for k, v in snap.items():
                if c.get(k, 0) < v:
                    c[k] = v

    def _skip_same(self, eng, sem_key):
        if sem_key != "E:" + eng:
            return False
        if eng == "tensor":
            return True
        return not self.same_engine_sync

    def _read_dep(self, eng, ap):
        for b in self._targets(ap):
            w = b["w"]
            if w is None:
                continue
            sem, val = w
            if self._skip_same(eng, sem):
                continue
            self._wait(eng, sem, val)

    def _write_dep(self, eng, ap):
        for b in self._targets(ap):
            if b["w"] is not None:
                sem, val = b["w"]
                if not self._skip_same(eng, sem):
                    self._wait(eng, sem, val)
            for sem, val in b["r"]:
                if not self._skip_same(eng, sem):
                    self._wait(eng, sem, val)

    def _record_effects(self, ev, outs, ins):
        for ap in ins:
            self._buf(ap)["r"].append(ev)
        for ap in outs:
            b = self._buf(ap)
            b["w"] = ev
            b["r"] = []

    def op(self, eng, emit, outs=(), ins=()):
        for ap in ins:
            self._read_dep(eng, ap)
        for ap in outs:
            self._write_dep(eng, ap)
        self.eng_count[eng] += 1
        v = self.eng_count[eng]
        key = "E:" + eng
        self.seq[eng].append(("op", emit, key, v))
        ev = (key, v)
        self._record_effects(ev, outs, ins)
        snap = dict(self.clock[eng])
        snap[key] = v
        self.snapshots[ev] = snap
        return ev

    def dma(self, queue, emit, outs=(), ins=(), sem_key=None):
        assert queue in ("sync", "gpsimd", "scalar")
        for ap in ins:
            self._read_dep(queue, ap)
        for ap in outs:
            self._write_dep(queue, ap)
        sem_key = "D:" + outs[0].tensor.name
        cnt = self.dma_count.get(sem_key, 0) + 1
        self.dma_count[sem_key] = cnt
        val = 16 * cnt
        self.seq[queue].append(("dma", emit, sem_key, val))
        ev = (sem_key, val)
        self._record_effects(ev, outs, ins)
        snap = dict(self.clock[queue])
        snap[sem_key] = val
        self.snapshots[ev] = snap
        return ev

    def barrier_read(self, eng, ap):
        self._read_dep(eng, ap)

    def build(self):
        for key, cnt in self.dma_count.items():
            self._wait("sync", key, 16 * cnt)
        nc = self.nc
        sems = {}
        for name in ["E:" + e for e in ENGINES] + list(self.dma_count.keys()):
            sems[name] = nc.alloc_semaphore(name=name.replace(":", "_").replace("/", "_"))
        nsems = len(sems)
        assert nsems <= 100, f"too many semaphores: {nsems}"
        with nc.Block() as block:
            for e in ENGINES:
                seq = self.seq[e]
                if not seq:
                    continue

                def mk_body(seq=seq):
                    def body(engobj):
                        for item in seq:
                            if item[0] == "wait":
                                engobj.wait_ge(sems[item[1]], item[2])
                            else:
                                kind, emit, key, val = item
                                inst = emit(engobj)
                                inst.then_inc(sems[key], 16 if kind == "dma" else 1)
                    return body

                getattr(block, e)(mk_body())
        return sems


"""Layer builders for mk_recurrent_d4. Conventions:
- token-major tile: [128 tokens, feat]; feature-major (fm): [128 feat-chunk, tokens]
- biases / -rowmax / per-token shifts folded in via K=1 ones-matmul PSUM accumulation
- softmax: two-pass (pass1 rowmax token-major; pass2 transposed scores, exp->P^T bf16,
  PV fm accumulation + ones-col Z row; normalize via K=1 broadcast of 1/Z)
- ALL lambdas must bind loop vars as default args (deferred emission).
"""

TILE = 128


class Ctx:
    def __init__(self, nc, p, B=2, T=2048, D=1024, arena_bytes=116000):
        self.nc, self.p = nc, p
        self.B, self.T, self.D = B, T, D
        self.NT = T // TILE
        self.DC = D // TILE
        self.sb = {}
        self.pnb = 2
        self.banks = [nc.alloc_psum_tensor(f"bank{i}", [128, 512], F32).ap() for i in range(8)]
        self.arena_h = nc.alloc_sbuf_tensor("arena", [128, arena_bytes], mybir.dt.uint8)
        self.arena_base = nc.sbuf_base - arena_bytes
        self.arena_sz = arena_bytes
        self._mkcommon()

    def bk(self, i):
        return self.banks[i]

    def sbuf(self, name, shape, dtype):
        if name not in self.sb:
            self.sb[name] = self.nc.alloc_sbuf_tensor(name, shape, dtype).ap()
        return self.sb[name]

    def view(self, name, shape, dtype, off):
        if name not in self.sb:
            n = int(np.prod(shape[1:]))
            esz = {F32: 4, BF16: 2}[dtype]
            assert off + n * esz <= self.arena_sz, (name, off, n * esz, self.arena_sz)
            assert (self.arena_base + off) % 32 == 0, (name, off)
            h = self.nc.alloc_sbuf_tensor_at(name, shape, dtype, offset=self.arena_base + off)
            self.sb[name] = h.ap()
            self.p.register_view(self.sb[name].tensor.name, "arena", off, off + n * esz)
        return self.sb[name]

    def _mkcommon(self):
        p = self.p
        self.idn = self.sbuf("c_idn", [128, 128], F32)
        self.pidx = self.sbuf("c_pidx", [128, 1], F32)
        self.ones_row = self.sbuf("c_ones_row", [1, 512], BF16)
        self.neg_row = self.sbuf("c_neg_row", [1, 128], BF16)
        self.ones_col = self.sbuf("c_ones_col", [128, 1], BF16)
        self.tri = self.sbuf("c_tri", [128, 128], BF16)
        p.op("gpsimd", lambda g: g.iota(self.idn, base=0, pattern=[[1, 128]], channel_multiplier=0,
                                        allow_small_or_imprecise_dtypes=True), outs=[self.idn])
        p.op("gpsimd", lambda g: g.iota(self.pidx, base=0, pattern=[[0, 1]], channel_multiplier=1,
                                        allow_small_or_imprecise_dtypes=True), outs=[self.pidx])
        p.op("vector", lambda v: v.tensor_scalar(self.tri, self.idn, self.pidx, None,
                                                 op0=OP.is_ge, op1=OP.bypass),
             outs=[self.tri], ins=[self.idn, self.pidx])
        p.op("vector", lambda v: v.tensor_scalar(self.idn, self.idn, self.pidx, None,
                                                 op0=OP.is_equal, op1=OP.bypass),
             outs=[self.idn], ins=[self.idn, self.pidx])
        p.op("gpsimd", lambda g: g.memset(self.ones_row, 1.0), outs=[self.ones_row])
        p.op("gpsimd", lambda g: g.memset(self.neg_row, -1.0), outs=[self.neg_row])
        p.op("gpsimd", lambda g: g.memset(self.ones_col, 1.0), outs=[self.ones_col])

    def load_cast(self, name, dram, kchunks, mcols, queue="gpsimd", view_off=None, sem_key=None):
        if view_off is None:
            t = self.sbuf(name, [128, kchunks, mcols], BF16)
        else:
            t = self.view(name, [128, kchunks, mcols], BF16, view_off)
        self.p.dma(queue, lambda e, t=t, dram=dram: e.dma_start(
            out=t, in_=dram.rearrange("(a p) m -> p a m", p=128)),
            outs=[t], ins=[dram], sem_key=sem_key)
        return t

    def load_transposed(self, name, dram, dout, din, stage_off=0, bank=1, out_off=None):
        nc, p = self.nc, self.p
        rc, cc = (dout + 127) // 128, din // 128
        rpart = min(dout, 128)
        stage = self.view(name + "_stg", [128, rc, din], F32, stage_off)
        p.dma("gpsimd", lambda e, stage=stage, dram=dram: e.dma_start(
            out=stage[0:rpart, :, :] if rc == 1 else stage,
            in_=dram.rearrange("(a p) m -> p a m", p=rpart)),
            outs=[stage], ins=[dram], sem_key="D:wstg")
        if out_off is None:
            t = self.sbuf(name, [128, cc, dout], BF16)
        else:
            t = self.view(name, [128, cc, dout], BF16, out_off)
        pb = self.bk(bank)
        for r in range(rc):
            for c in range(cc):
                p.op("tensor", lambda tt, r=r, c=c, stage=stage, t=t: tt.transpose(
                    self.bk(bank)[:, 0:rpart], stage[0:rpart, r, c * 128:(c + 1) * 128],
                    self.idn[0:rpart, 0:rpart]), outs=[pb], ins=[stage, self.idn])
                p.op("scalar", lambda s, r=r, c=c, t=t: s.copy(
                    t[:, c, r * 128:r * 128 + rpart], self.bk(bank)[:, 0:rpart]),
                    outs=[t], ins=[pb])
        return t

    def ln_tile(self, x_tile, xn_tile, eps=1e-5):
        p, D = self.p, self.D
        epsc = self.sbuf("ln_eps", [128, 1], F32)
        if not getattr(self, "_eps_init", False):
            self._eps_init = True
            p.op("gpsimd", lambda g: g.memset(epsc, float(eps)), outs=[epsc])
        s = self.sbuf("ln_s", [128, 1], F32)
        nm = self.sbuf("ln_nm", [128, 1], F32)
        sq = self.sbuf("ln_sq", [128, D], F32)
        ssq = self.sbuf("ln_ssq", [128, 1], F32)
        var = self.sbuf("ln_var", [128, 1], F32)
        rstd = self.sbuf("ln_rstd", [128, 1], F32)
        p.op("vector", lambda v: v.tensor_reduce(s, x_tile, axis=AX.X, op=OP.add), outs=[s], ins=[x_tile])
        p.op("vector", lambda v: v.tensor_scalar(nm, s, -1.0 / D, None, op0=OP.mult, op1=OP.bypass),
             outs=[nm], ins=[s])
        p.op("scalar", lambda sc: sc.activation(sq, x_tile, AF.Square, accum_out=ssq),
             outs=[sq, ssq], ins=[x_tile])
        p.op("vector", lambda v: v.tensor_tensor(var, nm, nm, op=OP.mult), outs=[var], ins=[nm])
        p.op("vector", lambda v: v.scalar_tensor_tensor(var, ssq, 1.0 / D, var, op0=OP.mult, op1=OP.subtract),
             outs=[var], ins=[ssq, var])
        p.op("scalar", lambda sc: sc.activation(rstd, var, AF.Sqrt, bias=epsc), outs=[rstd], ins=[var, epsc])
        p.op("vector", lambda v: v.reciprocal(rstd, rstd), outs=[rstd], ins=[rstd])
        p.op("vector", lambda v: v.tensor_scalar(xn_tile, x_tile, nm, rstd, op0=OP.add, op1=OP.mult),
             outs=[xn_tile], ins=[x_tile, nm, rstd])

    def phase_norm(self, x_in, xnb, b, xnb_lo=None, per_tile=None, xnf=None):
        p = self.p
        for i in range(self.NT):
            xt = self.sbuf(f"pn_x{i % self.pnb}", [128, self.D], F32)
            xn = self.sbuf(f"pn_xn{i % self.pnb}", [128, self.D], F32)
            p.dma("sync", lambda e, b=b, i=i, xt=xt: e.dma_start(
                out=xt, in_=x_in[b, i * 128:(i + 1) * 128, :]), outs=[xt], ins=[x_in])
            self.ln_tile(xt, xn)
            p.dma("gpsimd", lambda e, b=b, i=i, xn=xn, xnb=xnb: e.dma_start(
                out=xnb[b, i * 128:(i + 1) * 128, :], in_=xn), outs=[xnb], ins=[xn], sem_key="D:xnb")
            if xnf is not None:
                p.dma("gpsimd", lambda e, b=b, i=i, xn=xn, xnf=xnf: e.dma_start(
                    out=xnf[b, i * 128:(i + 1) * 128, :], in_=xn), outs=[xnf], ins=[xn], sem_key="D:xnf")
            if xnb_lo is not None:
                hi = self.sbuf("pn_hi", [128, self.D], BF16)
                lo = self.sbuf("pn_lo", [128, self.D], F32)
                p.op("vector", lambda v, hi=hi, xn=xn: v.tensor_copy(hi, xn), outs=[hi], ins=[xn])
                p.op("vector", lambda v, lo=lo, xn=xn, hi=hi: v.tensor_tensor(lo, xn, hi, op=OP.subtract),
                     outs=[lo], ins=[xn, hi])
                p.dma("gpsimd", lambda e, b=b, i=i, lo=lo, xnb_lo=xnb_lo: e.dma_start(
                    out=xnb_lo[b, i * 128:(i + 1) * 128, :], in_=lo), outs=[xnb_lo], ins=[lo],
                    sem_key="D:xnblo")
            if per_tile is not None:
                per_tile(b, i, xn)

    def load_xn_fm_b(self, xnb, b, name="xn_fm"):
        p = self.p
        t = self.sbuf(name, [128, self.DC, self.T], BF16)
        for c in range(self.DC):
            p.dma("sync", lambda e, b=b, c=c, t=t, xnb=xnb: e.dma_start_transpose(
                t[:, c, :], xnb[b, :, c * 128:(c + 1) * 128]),
                outs=[t], ins=[xnb], sem_key="D:" + name)
        return t

    def hilo_split(self, name, dram, kchunks, mcols, queue="gpsimd"):
        p = self.p
        stg = self.sbuf(name + "_stg", [128, kchunks, mcols], F32)
        p.dma(queue, lambda e, stg=stg, dram=dram: e.dma_start(
            out=stg, in_=dram.rearrange("(a p) m -> p a m", p=128)), outs=[stg], ins=[dram])
        hi = self.sbuf(name + "_hi", [128, kchunks, mcols], BF16)
        lo = self.sbuf(name + "_lo", [128, kchunks, mcols], BF16)
        tmp = self.sbuf(name + "_tmp", [128, kchunks, mcols], F32)
        p.op("vector", lambda v: v.tensor_copy(hi, stg), outs=[hi], ins=[stg])
        p.op("vector", lambda v: v.tensor_tensor(tmp, stg, hi, op=OP.subtract), outs=[tmp], ins=[stg, hi])
        p.op("vector", lambda v: v.tensor_copy(lo, tmp), outs=[lo], ins=[tmp])
        return hi, lo

    def mrow_of(self, mcols, mrow, n, bank=1):
        p = self.p
        pb = self.bk(bank)
        for i in range(n):
            p.op("tensor", lambda t, i=i: t.transpose(pb[0:1, 0:128], mcols[:, i:i + 1], self.idn),
                 outs=[pb], ins=[mcols, self.idn])
            p.op("scalar", lambda s, i=i: s.copy(mrow[0:1, i * 128:(i + 1) * 128], pb[0:1, 0:128]),
                 outs=[mrow], ins=[pb])

    def col_load(self, name, dram, nchunks, bank=1):
        p = self.p
        row = self.sbuf("c_rowbuf", [1, 512], F32)
        t = self.sbuf(name, [128, nchunks], F32)
        pb = self.bk(bank)
        for c0 in range(0, nchunks, 4):
            cn = min(4, nchunks - c0)
            p.dma("sync", lambda e, c0=c0, cn=cn, row=row, dram=dram: e.dma_start(
                out=row[0:1, 0:cn * 128], in_=dram[None, c0 * 128:(c0 + cn) * 128]),
                outs=[row], ins=[dram], sem_key="D:colload")
            for c in range(cn):
                p.op("tensor", lambda tt, c=c: tt.transpose(pb[0:128, 0:1], row[0:1, c * 128:(c + 1) * 128],
                                                            self.idn[0:1, 0:1]), outs=[pb], ins=[row, self.idn])
                p.op("scalar", lambda sc, c=c, c0=c0: sc.copy(t[:, c0 + c:c0 + c + 1], pb[0:128, 0:1]),
                     outs=[t], ins=[pb])
        return t

    def neg_scalar_max(self, mcols, n, negm, bank=1):
        p = self.p
        pb = self.bk(bank)
        mm = self.sbuf("nsm_mm", [128, 1], F32)
        mr = self.sbuf("nsm_mr", [1, 1], F32)
        mrb = self.sbuf("nsm_mrb", [1, 1], BF16)
        p.op("vector", lambda v: v.tensor_reduce(mm, mcols[:, 0:n], axis=AX.X, op=OP.max),
             outs=[mm], ins=[mcols])
        p.op("tensor", lambda t: t.transpose(pb[0:1, 0:128], mm, self.idn), outs=[pb], ins=[mm, self.idn])
        p.op("vector", lambda v: v.tensor_reduce(mr, pb[0:1, 0:128], axis=AX.X, op=OP.max, negate=True),
             outs=[mr], ins=[pb])
        p.op("vector", lambda v: v.tensor_copy(mrb, mr), outs=[mrb], ins=[mr])
        p.op("tensor", lambda t: t.matmul(pb[:, 0:1], self.ones_row[0:1, 0:128], mrb, start=True, stop=True),
             outs=[pb], ins=[self.ones_row, mrb])
        p.op("scalar", lambda sc: sc.copy(negm, pb[:, 0:1]), outs=[negm], ins=[pb])

    def bcast_row(self, row_f32, out_sbuf, n, bank=4, row_bf=None):
        p = self.p
        if row_bf is None:
            row_bf = self.sbuf("bc_rowbf", [1, 512], BF16)
        p.op("vector", lambda v: v.tensor_copy(row_bf[:, 0:n], row_f32[:, 0:n]),
             outs=[row_bf], ins=[row_f32])
        pb = self.bk(bank)
        p.op("tensor", lambda t: t.matmul(pb[:, 0:n], self.ones_row[0:1, 0:128], row_bf[:, 0:n],
                                          start=True, stop=True), outs=[pb], ins=[self.ones_row, row_bf])
        p.op("scalar", lambda s: s.copy(out_sbuf[:, 0:n], pb[:, 0:n]), outs=[out_sbuf], ins=[pb])



def build_l0(ctx, x_in, y_out, W, C, scr):
    p, nc = ctx.p, ctx.nc
    B, T, D, NT, DC = ctx.B, ctx.T, ctx.D, ctx.NT, ctx.DC
    HID = 2816
    HC = HID // 128
    NKV = 2

    WkT = ctx.load_transposed("l0_WkT", W["Wk"], 128, D, stage_off=0)
    WvT = ctx.load_transposed("l0_WvT", W["Wv"], 128, D, stage_off=0)
    WqT = [ctx.load_transposed(f"l0_WqT{h}", W["Wq"][h], 128, D, stage_off=0) for h in range(NKV)]
    WoT = ctx.load_transposed("l0_WoT", W["Wo"], D, NKV * 128, stage_off=0)
    Wr_hi, Wr_lo = ctx.hilo_split("l0_Wr", W["Wr"], DC, 7)
    rbias = ctx.sbuf("l0_rbias", [1, 7], F32)
    p.dma("sync", lambda e: e.dma_start(out=rbias, in_=W["rb"][None, :]), outs=[rbias], ins=[W["rb"]])
    rb_bc = ctx.sbuf("l0_rb_bc", [128, 7], F32)
    ctx.bcast_row(rbias, rb_bc, 7, bank=0)

    ctab = {}
    for nm in ("cosk", "sink", "cosq", "sinq"):
        t = ctx.sbuf("l0_" + nm, [128, NT, 64], BF16)
        p.dma("gpsimd", lambda e, t=t, nm=nm: e.dma_start(
            out=t, in_=C[nm].rearrange("(n p) i -> p n i", p=128)), outs=[t], ins=[C[nm]], sem_key="D:ctab")
        ctab[nm] = t

    def rope_evac(ps, out_t, i, cos, sin):
        E = ctx.sbuf("rp_e", [128, 64], F32)
        O = ctx.sbuf("rp_o", [128, 64], F32)
        t1 = ctx.sbuf("rp_t1", [128, 64], F32)
        p.op("vector", lambda v, i=i: v.tensor_copy(E, ps[:, 0:128:2]), outs=[E], ins=[ps])
        p.op("vector", lambda v, i=i: v.tensor_copy(O, ps[:, 1:128:2]), outs=[O], ins=[ps])
        p.op("vector", lambda v, i=i: v.tensor_tensor(t1, O, sin[:, i, :], op=OP.mult), outs=[t1], ins=[O, sin])
        p.op("vector", lambda v, i=i: v.tensor_tensor(out_t[:, 0:128:2], E, cos[:, i, :], op=OP.mult),
             outs=[out_t], ins=[E, cos])
        p.op("vector", lambda v: v.tensor_tensor(out_t[:, 0:128:2], out_t[:, 0:128:2], t1, op=OP.subtract),
             outs=[out_t], ins=[out_t, t1])
        p.op("vector", lambda v, i=i: v.tensor_tensor(t1, E, sin[:, i, :], op=OP.mult), outs=[t1], ins=[E, sin])
        p.op("vector", lambda v, i=i: v.tensor_tensor(out_t[:, 1:128:2], O, cos[:, i, :], op=OP.mult),
             outs=[out_t], ins=[O, cos])
        p.op("vector", lambda v: v.tensor_tensor(out_t[:, 1:128:2], out_t[:, 1:128:2], t1, op=OP.add),
             outs=[out_t], ins=[out_t, t1])

    for b in range(B):
        ctx.phase_norm(x_in, scr["xnb"], b, xnb_lo=scr["xnb_lo"])
        xn_fm = ctx.load_xn_fm_b(scr["xnb"], b)

        K_fm = ctx.view("l0_Kfm", [128, T], BF16, 0)
        Q_fm = [ctx.view(f"l0_Qfm{h}", [128, T], BF16, 4096 + h * 4096) for h in range(NKV)]
        V_tm = ctx.view("l0_Vtm", [128, NT, 128], BF16, 12288)
        attnout = ctx.view("l0_attnout", [128, NKV, T], BF16, 16384)
        rope_t = ctx.sbuf("l0_rope_t", [128, 128], F32)
        pA, pB = ctx.bk(0), ctx.bk(1)
        for i in range(NT):
            for c in range(DC):
                p.op("tensor", lambda t, i=i, c=c, xn_fm=xn_fm: t.matmul(
                    pA[:, 0:128], xn_fm[:, c, i * 128:(i + 1) * 128], WkT[:, c, :],
                    start=(c == 0), stop=(c == DC - 1)), outs=[pA], ins=[xn_fm, WkT])
            rope_evac(pA, rope_t, i, ctab["cosk"], ctab["sink"])
            p.op("tensor", lambda t, i=i: t.transpose(pB[:, 0:128], rope_t, ctx.idn),
                 outs=[pB], ins=[rope_t, ctx.idn])
            p.op("scalar", lambda s, i=i: s.copy(K_fm[:, i * 128:(i + 1) * 128], pB[:, 0:128]),
                 outs=[K_fm], ins=[pB])
            for c in range(DC):
                p.op("tensor", lambda t, i=i, c=c, xn_fm=xn_fm: t.matmul(
                    pA[:, 0:128], xn_fm[:, c, i * 128:(i + 1) * 128], WvT[:, c, :],
                    start=(c == 0), stop=(c == DC - 1)), outs=[pA], ins=[xn_fm, WvT])
            p.op("scalar", lambda s, i=i: s.copy(V_tm[:, i, :], pA[:, 0:128]), outs=[V_tm], ins=[pA])
            for h in range(NKV):
                for c in range(DC):
                    p.op("tensor", lambda t, i=i, c=c, h=h, xn_fm=xn_fm: t.matmul(
                        pA[:, 0:128], xn_fm[:, c, i * 128:(i + 1) * 128], WqT[h][:, c, :],
                        start=(c == 0), stop=(c == DC - 1)), outs=[pA], ins=[xn_fm, WqT[h]])
                rope_evac(pA, rope_t, i, ctab["cosq"], ctab["sinq"])
                p.op("tensor", lambda t, i=i, h=h: t.transpose(pB[:, 0:128], rope_t, ctx.idn),
                     outs=[pB], ins=[rope_t, ctx.idn])
                p.op("scalar", lambda s, i=i, h=h: s.copy(Q_fm[h][:, i * 128:(i + 1) * 128], pB[:, 0:128]),
                     outs=[Q_fm[h]], ins=[pB])

        for h in range(NKV):
            mcols = ctx.sbuf("l0_mcols", [128, NT], F32)
            mtmp = ctx.sbuf("l0_mtmp", [128, 4], F32)
            for i in range(NT):
                nk = (i + 1) * 128
                nchunk = (nk + 511) // 512
                for j in range(nchunk):
                    n = min(512, nk - j * 512)
                    p.op("tensor", lambda t, i=i, j=j, n=n, h=h: t.matmul(
                        pA[:, 0:n], Q_fm[h][:, i * 128:(i + 1) * 128], K_fm[:, j * 512:j * 512 + n],
                        start=True, stop=True), outs=[pA], ins=[Q_fm[h], K_fm])
                    p.op("vector", lambda v, j=j, n=n: v.tensor_reduce(
                        mtmp[:, j:j + 1], pA[:, 0:n], axis=AX.X, op=OP.max), outs=[mtmp], ins=[pA])
                p.op("vector", lambda v, i=i, nchunk=nchunk: v.tensor_reduce(
                    mcols[:, i:i + 1], mtmp[:, 0:nchunk], axis=AX.X, op=OP.max), outs=[mcols], ins=[mtmp])
            negm = ctx.sbuf("l0_negm", [128, 1], F32)
            ctx.neg_scalar_max(mcols, NT, negm, bank=1)

            pS, pO, pZ = ctx.bk(2), ctx.bk(3), ctx.bk(4)
            PT = ctx.sbuf("l0_PT", [128, 512], BF16)
            zrow = ctx.sbuf("l0_zrow", [1, 512], F32)
            zbc = ctx.sbuf("l0_zbc", [128, 512], F32)
            for qc in range(T // 512):
                tq0 = qc * 512
                ktiles = list(range(0, (tq0 + 512) // 128))
                for kk, k in enumerate(ktiles):
                    st, sp = (kk == 0), (kk == len(ktiles) - 1)
                    p.op("tensor", lambda t, k=k, tq0=tq0, h=h: t.matmul(
                        pS[:, 0:512], K_fm[:, k * 128:(k + 1) * 128], Q_fm[h][:, tq0:tq0 + 512],
                        start=True, stop=True), outs=[pS], ins=[K_fm, Q_fm[h]])
                    p.op("scalar", lambda s: s.activation(PT, pS[:, 0:512], AF.Exp, bias=negm),
                         outs=[PT], ins=[pS, negm])
                    dsub = k - tq0 // 128
                    if 0 <= dsub < 4:
                        p.op("vector", lambda v, dsub=dsub: v.tensor_tensor(
                            PT[:, dsub * 128:(dsub + 1) * 128], PT[:, dsub * 128:(dsub + 1) * 128],
                            ctx.tri, op=OP.mult), outs=[PT], ins=[PT, ctx.tri])
                        for z in range(dsub):
                            p.op("gpsimd", lambda g, z=z: g.memset(PT[:, z * 128:(z + 1) * 128], 0.0),
                                 outs=[PT])
                    p.op("tensor", lambda t, k=k, st=st, sp=sp: t.matmul(
                        pO[:, 0:512], V_tm[:, k, :], PT, start=st, stop=sp), outs=[pO], ins=[V_tm, PT])
                    p.op("tensor", lambda t, st=st, sp=sp: t.matmul(
                        pZ[0:1, 0:512], ctx.ones_col, PT, start=st, stop=sp),
                        outs=[pZ], ins=[ctx.ones_col, PT])
                p.op("vector", lambda v: v.reciprocal(zrow, pZ[0:1, 0:512]), outs=[zrow], ins=[pZ])
                ctx.bcast_row(zrow, zbc, 512, bank=4)
                p.op("vector", lambda v, h=h, tq0=tq0: v.tensor_tensor(
                    attnout[:, h, tq0:tq0 + 512], pO[:, 0:512], zbc, op=OP.mult),
                    outs=[attnout], ins=[pO, zbc])

        s_t = ctx.sbuf("ln_sq", [128, D], F32)
        for i in range(NT):
            for half in range(2):
                for hc in range(NKV):
                    p.op("tensor", lambda t, i=i, half=half, hc=hc: t.matmul(
                        pA[:, 0:512], attnout[:, hc, i * 128:(i + 1) * 128],
                        WoT[:, hc, half * 512:(half + 1) * 512],
                        start=(hc == 0), stop=(hc == NKV - 1)), outs=[pA], ins=[attnout, WoT])
                p.op("vector", lambda v, half=half: v.tensor_copy(
                    s_t[:, half * 512:(half + 1) * 512], pA[:, 0:512]), outs=[s_t], ins=[pA])
            p.dma("gpsimd", lambda e, i=i, b=b: e.dma_start(
                out=scr["s"][b, i * 128:(i + 1) * 128, :], in_=s_t), outs=[scr["s"]], ins=[s_t],
                sem_key="D:l0s")

        xn_lo_fm = ctx.view("xn_lo_fm", [128, DC, T], BF16, 24576)
        for c in range(DC):
            p.dma("sync", lambda e, b=b, c=c, t=xn_lo_fm: e.dma_start_transpose(
                t[:, c, :], scr["xnb_lo"][b, :, c * 128:(c + 1) * 128]),
                outs=[xn_lo_fm], ins=[scr["xnb_lo"]], sem_key="D:xnlofm")
        w_all = ctx.sbuf("l0_w_all", [128, NT, 7], F32)
        for i in range(NT):
            for ci, (xs, ws) in enumerate([(xn_fm, Wr_hi), (xn_lo_fm, Wr_hi), (xn_fm, Wr_lo)]):
                for c in range(DC):
                    p.op("tensor", lambda t, i=i, c=c, xs=xs, ws=ws, ci=ci: t.matmul(
                        pA[:, 0:7], xs[:, c, i * 128:(i + 1) * 128], ws[:, c, :],
                        start=(ci == 0 and c == 0), stop=(ci == 2 and c == DC - 1)),
                        outs=[pA], ins=[xs, ws])
            lg = ctx.sbuf("l0_lg", [128, 7], F32)
            pr = ctx.sbuf("l0_pr", [128, 7], F32)
            m1 = ctx.sbuf("l0_m1", [128, 1], F32)
            m2 = ctx.sbuf("l0_m2", [128, 1], F32)
            eq1 = ctx.sbuf("l0_eq1", [128, 7], F32)
            eq2 = ctx.sbuf("l0_eq2", [128, 7], F32)
            pr2 = ctx.sbuf("l0_pr2", [128, 7], F32)
            rs = ctx.sbuf("l0_rs", [128, 1], F32)
            p.op("vector", lambda v: v.tensor_tensor(lg, pA[:, 0:7], rb_bc, op=OP.mult),
                 outs=[lg], ins=[pA, rb_bc])
            p.op("scalar", lambda s: s.activation(pr, lg, AF.Sigmoid), outs=[pr], ins=[lg])
            p.op("vector", lambda v: v.tensor_reduce(m1, pr, axis=AX.X, op=OP.max), outs=[m1], ins=[pr])
            p.op("vector", lambda v: v.tensor_scalar(eq1, pr, m1, None, op0=OP.is_ge, op1=OP.bypass),
                 outs=[eq1], ins=[pr, m1])
            p.op("vector", lambda v: v.scalar_tensor_tensor(pr2, eq1, -4.0, pr, op0=OP.mult, op1=OP.add),
                 outs=[pr2], ins=[eq1, pr])
            p.op("vector", lambda v: v.tensor_reduce(m2, pr2, axis=AX.X, op=OP.max), outs=[m2], ins=[pr2])
            p.op("vector", lambda v: v.tensor_scalar(eq2, pr2, m2, None, op0=OP.is_ge, op1=OP.bypass),
                 outs=[eq2], ins=[pr2, m2])
            p.op("vector", lambda v: v.tensor_tensor(rs, m1, m2, op=OP.add), outs=[rs], ins=[m1, m2])
            p.op("vector", lambda v: v.reciprocal(rs, rs), outs=[rs], ins=[rs])
            p.op("vector", lambda v: v.tensor_tensor(eq1, eq1, eq2, op=OP.add), outs=[eq1], ins=[eq1, eq2])
            p.op("vector", lambda v: v.tensor_tensor(eq1, eq1, pr, op=OP.mult), outs=[eq1], ins=[eq1, pr])
            p.op("vector", lambda v, i=i: v.tensor_scalar(
                w_all[:, i, :], eq1, rs, None, op0=OP.mult, op1=OP.bypass), outs=[w_all], ins=[eq1, rs])

        UN = [3, 3, 3, 3, 3, 3, 3, 1]
        pg, pu, pd = ctx.bk(5), ctx.bk(6), ctx.bk(7)
        sil = ctx.sbuf("l0_sil", [128, 512], BF16)
        hdram = scr["h"]
        HT = T // 2
        NSUB = HT // 128
        for half_t in range(2):
            hb = half_t * HT
            comb = ctx.view("l0_comb", [128, NSUB, D], F32, 82048)
            for e in range(8):
                if e == 0:
                    gd, ud, dd = W["sg"], W["su"], W["sd"]
                else:
                    gd, ud, dd = W["rg"][e - 1], W["ru"][e - 1], W["rd"][e - 1]
                Wd = ctx.view("l0_Wd", [128, HC, D], BF16, 24576)
                p.dma("gpsimd", lambda ee, dd=dd, Wd=Wd: ee.dma_start(
                    out=Wd, in_=dd.rearrange("(a p) m -> p a m", p=128)),
                    outs=[Wd], ins=[dd], sem_key="D:l0wd")
                hc0 = 0
                for u, un in enumerate(UN):
                    ub = u % 2
                    Wg = ctx.view(f"l0_Wg{ub}", [128, DC, 3 * 128], BF16, 0 + ub * 6144)
                    Wu = ctx.view(f"l0_Wu{ub}", [128, DC, 3 * 128], BF16, 12288 + ub * 6144)
                    p.dma("gpsimd", lambda ee, gd=gd, Wg=Wg, hc0=hc0, un=un: ee.dma_start(
                        out=Wg[:, :, 0:un * 128],
                        in_=gd[:, hc0 * 128:(hc0 + un) * 128].rearrange("(a p) m -> p a m", p=128)),
                        outs=[Wg], ins=[gd], sem_key=f"D:l0wg{ub}")
                    p.dma("gpsimd", lambda ee, ud=ud, Wu=Wu, hc0=hc0, un=un: ee.dma_start(
                        out=Wu[:, :, 0:un * 128],
                        in_=ud[:, hc0 * 128:(hc0 + un) * 128].rearrange("(a p) m -> p a m", p=128)),
                        outs=[Wu], ins=[ud], sem_key=f"D:l0wu{ub}")
                    for uc in range(un):
                        hc = hc0 + uc
                        for tcc in range(HT // 512):
                            t0 = hb + tcc * 512
                            ht = ctx.sbuf(f"l0_ht{(hc * 2 + tcc) % 2}", [128, 512], BF16)
                            for c in range(DC):
                                p.op("tensor", lambda t, uc=uc, c=c, t0=t0, Wg=Wg, xn_fm=xn_fm: t.matmul(
                                    pg[:, 0:512], Wg[:, c, uc * 128:(uc + 1) * 128], xn_fm[:, c, t0:t0 + 512],
                                    start=(c == 0), stop=(c == DC - 1)), outs=[pg], ins=[Wg, xn_fm])
                            for c in range(DC):
                                p.op("tensor", lambda t, uc=uc, c=c, t0=t0, Wu=Wu, xn_fm=xn_fm: t.matmul(
                                    pu[:, 0:512], Wu[:, c, uc * 128:(uc + 1) * 128], xn_fm[:, c, t0:t0 + 512],
                                    start=(c == 0), stop=(c == DC - 1)), outs=[pu], ins=[Wu, xn_fm])
                            p.op("scalar", lambda s, ht=ht: s.activation(sil, pg, AF.Silu), outs=[sil], ins=[pg])
                            p.op("vector", lambda v, ht=ht: v.tensor_tensor(ht, pu, sil, op=OP.mult),
                                 outs=[ht], ins=[pu, sil])
                            p.dma("gpsimd", lambda ee, hc=hc, tcc=tcc, ht=ht: ee.dma_start(
                                out=hdram[:, hc, tcc * 512:(tcc + 1) * 512], in_=ht),
                                outs=[hdram], ins=[ht], sem_key="D:l0h")
                    hc0 += un
                for sub in range(NSUB):
                    i = half_t * NSUB + sub
                    hsub = ctx.view(f"l0_hs{sub % 2}", [128, HC, 128], BF16, 69664 + (sub % 2) * 5888)
                    for hh in range(2):
                        p.dma("sync", lambda ee, sub=sub, hsub=hsub, hh=hh: ee.dma_start(
                            out=hsub[:, hh * 11:(hh + 1) * 11, :],
                            in_=hdram[:, hh * 11:(hh + 1) * 11, sub * 128:(sub + 1) * 128]),
                            outs=[hsub], ins=[hdram], sem_key=f"D:l0hs{sub % 2}")
                    for halfd in range(2):
                        for hc in range(HC):
                            p.op("tensor", lambda t, sub=sub, halfd=halfd, Wd=Wd, hc=hc, hsub=hsub: t.matmul(
                                pd[:, 0:512], hsub[:, hc, :], Wd[:, hc, halfd * 512:(halfd + 1) * 512],
                                start=(hc == 0), stop=(hc == HC - 1)), outs=[pd], ins=[hsub, Wd])
                        if e == 0:
                            p.op("vector", lambda v, sub=sub, halfd=halfd: v.tensor_copy(
                                comb[:, sub, halfd * 512:(halfd + 1) * 512], pd[:, 0:512]),
                                outs=[comb], ins=[pd])
                        else:
                            p.op("vector", lambda v, i=i, sub=sub, halfd=halfd, e=e: v.scalar_tensor_tensor(
                                comb[:, sub, halfd * 512:(halfd + 1) * 512], pd[:, 0:512],
                                w_all[:, i, e - 1:e], comb[:, sub, halfd * 512:(halfd + 1) * 512],
                                op0=OP.mult, op1=OP.add), outs=[comb], ins=[pd, w_all, comb])
            for sub in range(NSUB):
                i = half_t * NSUB + sub
                xt = ctx.sbuf(f"pn_x{sub % ctx.pnb}", [128, D], F32)
                st = ctx.sbuf(f"pn_xn{sub % ctx.pnb}", [128, D], F32)
                gt = ctx.sbuf("ln_sq", [128, D], F32)
                p.dma("sync", lambda e, b=b, i=i, xt=xt: e.dma_start(
                    out=xt, in_=x_in[b, i * 128:(i + 1) * 128, :]), outs=[xt], ins=[x_in])
                p.dma("sync", lambda e, b=b, i=i, st=st: e.dma_start(
                    out=st, in_=scr["s"][b, i * 128:(i + 1) * 128, :]), outs=[st], ins=[scr["s"]],
                    sem_key="D:l0s_rd")
                p.op("scalar", lambda s, sub=sub, gt=gt: s.activation(gt, comb[:, sub, :], AF.Silu),
                     outs=[gt], ins=[comb])
                p.op("vector", lambda v, gt=gt, st=st: v.tensor_tensor(gt, gt, st, op=OP.mult),
                     outs=[gt], ins=[gt, st])
                p.op("vector", lambda v, gt=gt, xt=xt: v.tensor_tensor(gt, gt, xt, op=OP.add),
                     outs=[gt], ins=[gt, xt])
                p.dma("gpsimd", lambda e, b=b, i=i, gt=gt: e.dma_start(
                    out=y_out[b, i * 128:(i + 1) * 128, :], in_=gt), outs=[y_out], ins=[gt], sem_key="D:y")


def l0_consts(T=2048):
    pos = np.arange(0, 128, 2, dtype=np.float32)
    theta = (10000.0 ** (-2.0 * pos / 128.0)).astype(np.float32)
    ang = (np.arange(T, dtype=np.float32)[:, None] * theta[None, :]).astype(np.float32)
    cos = np.cos(ang.astype(np.float64)).astype(np.float32)
    sin = np.sin(ang.astype(np.float64)).astype(np.float32)
    s = np.float32(128.0 ** -0.5)
    return {"cosk": cos, "sink": sin, "cosq": (cos * s).astype(np.float32),
            "sinq": (sin * s).astype(np.float32)}



def build_l1(ctx, x_in, y_out, W, scr):
    ctx.pnb = 1
    p, nc = ctx.p, ctx.nc
    B, T, D, NT, DC = ctx.B, ctx.T, ctx.D, ctx.NT, ctx.DC
    NH, DH, WIN = 16, 64, 256
    NHP = NH // 2
    pA, pB = ctx.bk(0), ctx.bk(1)

    wout = ctx.load_cast("l1_wout", W["w_out"], DC, D)
    w1l = ctx.load_cast("l1_w1l", W["w1_light"], DC, 512)
    w2l = ctx.load_cast("l1_w2l", W["w2_light"], 4, D)
    b1l = ctx.col_load("l1_b1l", W["b1_light"], 4)
    b1h = ctx.col_load("l1_b1h", W["b1_heavy"], 32)
    b2l_row = ctx.sbuf("l1_b2l", [1, D], BF16)
    p.dma("gpsimd", lambda e: e.dma_start(out=b2l_row, in_=W["b2_light"][None, :]),
          outs=[b2l_row], ins=[W["b2_light"]])
    b2h_row = ctx.sbuf("l1_b2h", [1, D], BF16)
    p.dma("gpsimd", lambda e: e.dma_start(out=b2h_row, in_=W["b2_heavy"][None, :]),
          outs=[b2h_row], ins=[W["b2_heavy"]])
    gml = ctx.col_load("l1_gml", W["gamma_light"], DC)
    gmh = ctx.col_load("l1_gmh", W["gamma_heavy"], DC)
    rt_row = ctx.sbuf("c_rowbuf", [1, 512], F32)
    ones_row_f = ctx.sbuf("c_ones_row_f", [1, 128], F32)
    p.op("gpsimd", lambda g: g.memset(ones_row_f, 1.0), outs=[ones_row_f])
    rt_bc = ctx.sbuf("l1_rt_bc", [128, D], F32)
    for hf in range(2):
        p.dma("sync", lambda e, hf=hf: e.dma_start(
            out=rt_row, in_=W["routing_token"][None, hf * 512:(hf + 1) * 512]),
            outs=[rt_row], ins=[W["routing_token"]], sem_key="D:colload")
        p.op("tensor", lambda t, hf=hf: t.matmul(pA[:, 0:512], ones_row_f, rt_row,
                                                 start=True, stop=True), outs=[pA], ins=[ones_row_f, rt_row])
        p.op("vector", lambda v, hf=hf: v.tensor_copy(rt_bc[:, hf * 512:(hf + 1) * 512], pA[:, 0:512]),
             outs=[rt_bc], ins=[pA])

    msks = {}
    c2v = ctx.sbuf("l1_c2v", [128, 256], F32)
    for dt in (-128, 0, 128, 256):
        m = ctx.sbuf(f"l1_msk{dt}", [128, 256], BF16)
        p.op("gpsimd", lambda g, dt=dt: g.iota(c2v, base=dt, pattern=[[1, 256]], channel_multiplier=-1,
                                               allow_small_or_imprecise_dtypes=True), outs=[c2v])
        t1 = ctx.sbuf("l1_mt1", [128, 256], F32)
        p.op("vector", lambda v: v.tensor_scalar(t1, c2v, -0.5, None, op0=OP.is_gt, op1=OP.bypass),
             outs=[t1], ins=[c2v])
        p.op("vector", lambda v, m=m: v.tensor_scalar(m, c2v, 255.5, None, op0=OP.is_lt, op1=OP.bypass),
             outs=[m], ins=[c2v])
        p.op("vector", lambda v, m=m: v.tensor_tensor(m, m, t1, op=OP.mult), outs=[m], ins=[m, t1])
        msks[dt] = m

    HT = T // 2
    NSUB = HT // 128
    for b in range(B):
        s_cols = ctx.sbuf("l1_scols", [128, NT], F32)
        sdump = ctx.sbuf("ln_sq", [128, self.D] if False else [128, D], F32)

        def score_tile(bb, i, xn_tile):
            p.op("vector", lambda v, i=i, xn_tile=xn_tile: v.tensor_tensor(
                sdump, xn_tile, rt_bc, op=OP.mult), outs=[sdump], ins=[xn_tile, rt_bc])
            p.op("vector", lambda v, i=i: v.tensor_reduce(
                s_cols[:, i:i + 1], sdump, axis=AX.X, op=OP.add), outs=[s_cols], ins=[sdump])

        rms_cols = ctx.sbuf("l1_rmscols", [128, NT], F32)

        def rms_tile(bb, i, xn_tile):
            var, rstd = ctx.sb["ln_var"], ctx.sb["ln_rstd"]
            t1 = ctx.sbuf("l1_rt1", [128, 1], F32)
            p.op("scalar", lambda sc, i=i: sc.activation(t1, var, AF.Sqrt, scale=float(D)),
                 outs=[t1], ins=[var])
            p.op("vector", lambda v, i=i: v.tensor_tensor(t1, t1, rstd, op=OP.mult), outs=[t1], ins=[t1, rstd])
            p.op("vector", lambda v, i=i: v.tensor_scalar(t1, t1, 1e-12, None, op0=OP.max, op1=OP.bypass),
                 outs=[t1], ins=[t1])
            p.op("vector", lambda v, i=i: v.reciprocal(t1, t1), outs=[t1], ins=[t1])
            p.op("vector", lambda v, i=i: v.tensor_scalar(
                rms_cols[:, i:i + 1], t1, float(np.sqrt(D)), None, op0=OP.mult, op1=OP.bypass),
                outs=[rms_cols], ins=[t1])

        def per_tile(bb, i, xn_tile):
            score_tile(bb, i, xn_tile)
            rms_tile(bb, i, xn_tile)

        ctx.phase_norm(x_in, scr["xnb"], b, per_tile=per_tile)
        xn_fm = ctx.load_xn_fm_b(scr["xnb"], b)

        thr = ctx.sbuf("l1_thr", [128, 1], F32)
        cnt = ctx.sbuf("l1_cnt", [128, 1], F32)
        cntb = ctx.sbuf("l1_cntb", [128, 1], BF16)
        tot = ctx.sbuf("l1_tot", [128, 1], F32)
        d01 = ctx.sbuf("l1_d01", [128, 1], F32)
        bmask = ctx.sbuf("l1_bmask", [128, NT], F32)
        ones_mat = ctx.sbuf("c_ones_mat", [128, 128], BF16)
        p.op("gpsimd", lambda g: g.memset(ones_mat, 1.0), outs=[ones_mat])
        p.op("gpsimd", lambda g: g.memset(thr, 0.0), outs=[thr])
        step = 256.0
        for it in range(40):
            p.op("vector", lambda v: v.tensor_scalar(bmask, s_cols, thr, None, op0=OP.is_gt, op1=OP.bypass),
                 outs=[bmask], ins=[s_cols, thr])
            p.op("vector", lambda v: v.tensor_reduce(cnt, bmask, axis=AX.X, op=OP.add), outs=[cnt], ins=[bmask])
            p.op("vector", lambda v: v.tensor_copy(cntb, cnt), outs=[cntb], ins=[cnt])
            p.op("tensor", lambda t: t.matmul(pB[:, 0:1], ones_mat, cntb, start=True, stop=True),
                 outs=[pB], ins=[ones_mat, cntb])
            p.op("scalar", lambda s: s.copy(tot, pB[:, 0:1]), outs=[tot], ins=[pB])
            p.op("vector", lambda v: v.tensor_scalar(d01, tot, float(T // 2) + 0.5, None,
                                                     op0=OP.is_gt, op1=OP.bypass), outs=[d01], ins=[tot])
            p.op("vector", lambda v, step=step: v.scalar_tensor_tensor(
                thr, d01, 2.0 * step, thr, op0=OP.mult, op1=OP.add), outs=[thr], ins=[d01, thr])
            p.op("vector", lambda v, step=step: v.tensor_scalar(
                thr, thr, -step, None, op0=OP.add, op1=OP.bypass), outs=[thr], ins=[thr])
            step = step / 2.0
        p.op("vector", lambda v: v.tensor_scalar(bmask, s_cols, thr, None, op0=OP.is_gt, op1=OP.bypass),
             outs=[bmask], ins=[s_cols, thr])

        v_tm = ctx.view("l1_vtm", [128, NT, D], BF16, 0)
        attnout = ctx.view("l1_attnout", [128, NHP, T], BF16, 32768)
        wv = ctx.view("l1_wv", [128, DC, D], BF16, 32768)
        p.dma("gpsimd", lambda e: e.dma_start(
            out=wv, in_=W["w_qkv"][:, 2048:3072].rearrange("(a p) m -> p a m", p=128)),
            outs=[wv], ins=[W["w_qkv"]], sem_key="D:l1wv")
        for i in range(NT):
            for half in range(2):
                for c in range(DC):
                    p.op("tensor", lambda t, i=i, half=half, c=c, xn_fm=xn_fm: t.matmul(
                        pA[:, 0:512], xn_fm[:, c, i * 128:(i + 1) * 128], wv[:, c, half * 512:(half + 1) * 512],
                        start=(c == 0), stop=(c == DC - 1)), outs=[pA], ins=[xn_fm, wv])
                p.op("scalar", lambda s, i=i, half=half: s.copy(
                    v_tm[:, i, half * 512:(half + 1) * 512], pA[:, 0:512]), outs=[v_tm], ins=[pA])

        for hp in range(NHP):
            qw = ctx.view(f"l1_qw{hp % 2}", [128, DC, 128], BF16, 65536 + (hp % 2) * 2048)
            kw = ctx.view(f"l1_kw{hp % 2}", [128, DC, 128], BF16, 69632 + (hp % 2) * 2048)
            p.dma("gpsimd", lambda e, hp=hp, qw=qw: e.dma_start(
                out=qw, in_=W["w_qkv"][:, hp * 128:(hp + 1) * 128].rearrange("(a p) m -> p a m", p=128)),
                outs=[qw], ins=[W["w_qkv"]], sem_key=f"D:l1qw{hp % 2}")
            p.dma("gpsimd", lambda e, hp=hp, kw=kw: e.dma_start(
                out=kw, in_=W["w_qkv"][:, 1024 + hp * 128:1024 + (hp + 1) * 128].rearrange(
                    "(a p) m -> p a m", p=128)),
                outs=[kw], ins=[W["w_qkv"]], sem_key=f"D:l1kw{hp % 2}")
            q_fm = ctx.view("l1_qfm", [128, T], BF16, 73728)
            k_fm = ctx.view("l1_kfm", [128, T], BF16, 77824)
            for tc in range(T // 512):
                for c in range(DC):
                    p.op("tensor", lambda t, tc=tc, c=c, qw=qw, xn_fm=xn_fm: t.matmul(
                        pA[:, 0:512], qw[:, c, :], xn_fm[:, c, tc * 512:(tc + 1) * 512],
                        start=(c == 0), stop=(c == DC - 1)), outs=[pA], ins=[qw, xn_fm])
                p.op("scalar", lambda s, tc=tc: s.activation(
                    q_fm[:, tc * 512:(tc + 1) * 512], pA[:, 0:512], AF.Copy, scale=float(DH ** -0.5)),
                    outs=[q_fm], ins=[pA])
                for c in range(DC):
                    p.op("tensor", lambda t, tc=tc, c=c, kw=kw, xn_fm=xn_fm: t.matmul(
                        pA[:, 0:512], kw[:, c, :], xn_fm[:, c, tc * 512:(tc + 1) * 512],
                        start=(c == 0), stop=(c == DC - 1)), outs=[pA], ins=[kw, xn_fm])
                p.op("scalar", lambda s, tc=tc: s.copy(k_fm[:, tc * 512:(tc + 1) * 512], pA[:, 0:512]),
                     outs=[k_fm], ins=[pA])
            for hh in range(2):
                h = hp * 2 + hh
                hb64 = hh * 64
                mcols = ctx.sbuf("l1_mcols", [128, NT], F32)
                for i in range(NT):
                    w0 = max(0, (i // 2) * 256 - 256)
                    wn = (i // 2) * 256 + 256 - w0
                    p.op("tensor", lambda t, i=i, w0=w0, wn=wn, hb64=hb64: t.matmul(
                        pA[:, 0:wn], q_fm[hb64:hb64 + 64, i * 128:(i + 1) * 128],
                        k_fm[hb64:hb64 + 64, w0:w0 + wn], start=True, stop=True),
                        outs=[pA], ins=[q_fm, k_fm])
                    p.op("vector", lambda v, i=i, wn=wn: v.tensor_reduce(
                        mcols[:, i:i + 1], pA[:, 0:wn], axis=AX.X, op=OP.max), outs=[mcols], ins=[pA])
                negm = ctx.sbuf("l1_negm", [128, 1], F32)
                ctx.neg_scalar_max(mcols, NT, negm, bank=1)
                pS, pO, pZ = ctx.bk(2), ctx.bk(3), ctx.bk(4)
                PT = ctx.sbuf("l1_PT", [128, 256], BF16)
                zrow = ctx.sbuf("l1_zrow", [1, 256], F32)
                zbc = ctx.sbuf("l1_zbc", [128, 256], F32)
                for blk in range(T // 256):
                    tq0 = blk * 256
                    kts = [k for k in range(2 * blk - 2, 2 * blk + 2) if k >= 0]
                    for kk, kt in enumerate(kts):
                        st, sp = (kk == 0), (kk == len(kts) - 1)
                        p.op("tensor", lambda t, kt=kt, tq0=tq0, hb64=hb64: t.matmul(
                            pS[:, 0:256], k_fm[hb64:hb64 + 64, kt * 128:(kt + 1) * 128],
                            q_fm[hb64:hb64 + 64, tq0:tq0 + 256], start=True, stop=True),
                            outs=[pS], ins=[k_fm, q_fm])
                        p.op("scalar", lambda s: s.activation(PT, pS[:, 0:256], AF.Exp, bias=negm),
                             outs=[PT], ins=[pS, negm])
                        dt = tq0 - kt * 128
                        p.op("vector", lambda v, dt=dt: v.tensor_tensor(PT, PT, msks[dt], op=OP.mult),
                             outs=[PT], ins=[PT, msks[dt]])
                        p.op("tensor", lambda t, kt=kt, st=st, sp=sp, h=h: t.matmul(
                            pO[0:64, 0:256], v_tm[:, kt, h * 64:(h + 1) * 64], PT, start=st, stop=sp),
                            outs=[pO], ins=[v_tm, PT])
                        p.op("tensor", lambda t, st=st, sp=sp: t.matmul(
                            pZ[0:1, 0:256], ctx.ones_col, PT, start=st, stop=sp),
                            outs=[pZ], ins=[ctx.ones_col, PT])
                    p.op("vector", lambda v: v.reciprocal(zrow, pZ[0:1, 0:256]), outs=[zrow], ins=[pZ])
                    zrow_bf = ctx.sbuf("l1_zrow_bf", [1, 256], BF16)
                    p.op("vector", lambda v: v.tensor_copy(zrow_bf, zrow), outs=[zrow_bf], ins=[zrow])
                    p.op("tensor", lambda t: t.matmul(pZ[:, 0:256], ctx.ones_row[0:1, 0:128], zrow_bf,
                                                      start=True, stop=True), outs=[pZ], ins=[ctx.ones_row, zrow_bf])
                    p.op("scalar", lambda s: s.copy(zbc, pZ[:, 0:256]), outs=[zbc], ins=[pZ])
                    p.op("vector", lambda v, hp=hp, hb64=hb64, tq0=tq0: v.tensor_tensor(
                        attnout[hb64:hb64 + 64, hp, tq0:tq0 + 256], pO[0:64, 0:256], zbc[0:64, :], op=OP.mult),
                        outs=[attnout], ins=[pO, zbc])
        s_t = ctx.sbuf("ln_sq", [128, D], F32)
        for i in range(NT):
            for half in range(2):
                for hc in range(NHP):
                    p.op("tensor", lambda t, i=i, half=half, hc=hc: t.matmul(
                        pA[:, 0:512], attnout[:, hc, i * 128:(i + 1) * 128],
                        wout[:, hc, half * 512:(half + 1) * 512],
                        start=(hc == 0), stop=(hc == NHP - 1)), outs=[pA], ins=[attnout, wout])
                p.op("vector", lambda v, half=half: v.tensor_copy(
                    s_t[:, half * 512:(half + 1) * 512], pA[:, 0:512]), outs=[s_t], ins=[pA])
            p.dma("gpsimd", lambda e, i=i, b=b: e.dma_start(
                out=scr["s"][b, i * 128:(i + 1) * 128, :], in_=s_t), outs=[scr["s"]], ins=[s_t],
                sem_key="D:l1s")

        rms_row = ctx.sbuf("l1_rms_row", [1, T], BF16)
        for i in range(NT):
            p.op("tensor", lambda t, i=i: t.transpose(pB[0:1, 0:128], rms_cols[:, i:i + 1], ctx.idn),
                 outs=[pB], ins=[rms_cols, ctx.idn])
            p.op("scalar", lambda s, i=i: s.copy(rms_row[0:1, i * 128:(i + 1) * 128], pB[0:1, 0:128]),
                 outs=[rms_row], ins=[pB])
        rmsbc = ctx.sbuf("l1_rmsbc", [128, 512], BF16)
        pg = ctx.bk(5)
        for tc in range(T // 512):
            t0 = tc * 512
            i0 = tc * 4
            p.op("tensor", lambda t, t0=t0: t.matmul(pg[:, 0:512], ctx.ones_row[0:1, 0:128],
                                                     rms_row[0:1, t0:t0 + 512], start=True, stop=True),
                 outs=[pg], ins=[ctx.ones_row, rms_row])
            p.op("scalar", lambda s: s.copy(rmsbc, pg[:, 0:512]), outs=[rmsbc], ins=[pg])
            rnl = ctx.view("l1_rnl", [128, DC, 512], BF16, 0)
            rnh = ctx.view("l1_rnh", [128, DC, 512], BF16, 8192)
            for c in range(DC):
                p.op("vector", lambda v, c=c, t0=t0: v.tensor_tensor(
                    rnl[:, c, :], xn_fm[:, c, t0:t0 + 512], rmsbc, op=OP.mult), outs=[rnl], ins=[xn_fm, rmsbc])
                p.op("vector", lambda v, c=c: v.tensor_scalar(
                    rnh[:, c, :], rnl[:, c, :], gmh[:, c:c + 1], None, op0=OP.mult, op1=OP.bypass),
                    outs=[rnh], ins=[rnl, gmh])
                p.op("vector", lambda v, c=c: v.tensor_scalar(
                    rnl[:, c, :], rnl[:, c, :], gml[:, c:c + 1], None, op0=OP.mult, op1=OP.bypass),
                    outs=[rnl], ins=[rnl, gml])
            h1l = ctx.view("l1_h1l", [128, 4, 512], BF16, 16384)
            for hc in range(4):
                for c in range(DC):
                    p.op("tensor", lambda t, hc=hc, c=c: t.matmul(
                        pg[:, 0:512], w1l[:, c, hc * 128:(hc + 1) * 128], rnl[:, c, :],
                        start=(c == 0), stop=(c == DC - 1)), outs=[pg], ins=[w1l, rnl])
                p.op("scalar", lambda s, hc=hc: s.activation(
                    h1l[:, hc, :], pg[:, 0:512], AF.Gelu, bias=b1l[:, hc:hc + 1]),
                    outs=[h1l], ins=[pg, b1l])
            lt = ctx.view("l1_lt", [128, 4, D], F32, 20480)
            for sub in range(4):
                for half in range(2):
                    for hc in range(4):
                        p.op("tensor", lambda t, sub=sub, half=half, hc=hc: t.matmul(
                            pg[:, 0:512], h1l[:, hc, sub * 128:(sub + 1) * 128],
                            w2l[:, hc, half * 512:(half + 1) * 512],
                            start=(hc == 0), stop=False), outs=[pg], ins=[h1l, w2l])
                    p.op("tensor", lambda t, sub=sub, half=half: t.matmul(
                        pg[:, 0:512], ctx.ones_row[0:1, 0:128], b2l_row[0:1, half * 512:(half + 1) * 512],
                        start=False, stop=True), outs=[pg], ins=[ctx.ones_row, b2l_row])
                    p.op("vector", lambda v, sub=sub, half=half: v.tensor_copy(
                        lt[:, sub, half * 512:(half + 1) * 512], pg[:, 0:512]), outs=[lt], ins=[pg])
            h1h = ctx.view("l1_h1h", [128, 32, 512], BF16, 36864)
            for q in range(4):
                w1h = ctx.view("l1_w1h", [128, DC, 1024], BF16, 69632)
                p.dma("gpsimd", lambda e, q=q, w1h=w1h: e.dma_start(
                    out=w1h, in_=W["w1_heavy"][:, q * 1024:(q + 1) * 1024].rearrange("(a p) m -> p a m", p=128)),
                    outs=[w1h], ins=[W["w1_heavy"]], sem_key="D:l1w1h")
                for hcq in range(8):
                    hc = q * 8 + hcq
                    for c in range(DC):
                        p.op("tensor", lambda t, hcq=hcq, c=c, w1h=w1h: t.matmul(
                            pg[:, 0:512], w1h[:, c, hcq * 128:(hcq + 1) * 128], rnh[:, c, :],
                            start=(c == 0), stop=(c == DC - 1)), outs=[pg], ins=[w1h, rnh])
                    p.op("scalar", lambda s, hc=hc: s.activation(
                        h1h[:, hc, :], pg[:, 0:512], AF.Gelu, bias=b1h[:, hc:hc + 1]),
                        outs=[h1h], ins=[pg, b1h])
            for u in range(8):
                w2h = ctx.view(f"l1_w2h{u % 2}", [128, 4, D], BF16, 86016 + (u % 2) * 8192)
                p.dma("gpsimd", lambda e, u=u, w2h=w2h: e.dma_start(
                    out=w2h, in_=W["w2_heavy"][u * 512:(u + 1) * 512, :].rearrange("(a p) m -> p a m", p=128)),
                    outs=[w2h], ins=[W["w2_heavy"]], sem_key=f"D:l1w2h{u % 2}")
                for sub in range(4):
                    for half in range(2):
                        bidx = sub * 2 + half
                        for hcu in range(4):
                            hc = u * 4 + hcu
                            p.op("tensor", lambda t, sub=sub, half=half, hcu=hcu, hc=hc, w2h=w2h, u=u,
                                 bidx=bidx: t.matmul(
                                ctx.bk(bidx)[:, 0:512], h1h[:, hc, sub * 128:(sub + 1) * 128],
                                w2h[:, hcu, half * 512:(half + 1) * 512],
                                start=(u == 0 and hcu == 0), stop=False),
                                outs=[ctx.bk(bidx)], ins=[h1h, w2h])
            for sub in range(4):
                i = i0 + sub
                for half in range(2):
                    bidx = sub * 2 + half
                    p.op("tensor", lambda t, half=half, bidx=bidx: t.matmul(
                        ctx.bk(bidx)[:, 0:512], ctx.ones_row[0:1, 0:128],
                        b2h_row[0:1, half * 512:(half + 1) * 512], start=False, stop=True),
                        outs=[ctx.bk(bidx)], ins=[ctx.ones_row, b2h_row])
                xt = ctx.sbuf(f"pn_x{sub % ctx.pnb}", [128, D], F32)
                st = ctx.sbuf(f"pn_xn{sub % ctx.pnb}", [128, D], F32)
                gt = ctx.sbuf("l1_ep_g", [128, D], F32)
                p.dma("sync", lambda e, b=b, i=i, xt=xt: e.dma_start(
                    out=xt, in_=x_in[b, i * 128:(i + 1) * 128, :]), outs=[xt], ins=[x_in])
                p.dma("sync", lambda e, b=b, i=i, st=st: e.dma_start(
                    out=st, in_=scr["s"][b, i * 128:(i + 1) * 128, :]), outs=[st], ins=[scr["s"]],
                    sem_key="D:l1s_rd")
                for half in range(2):
                    bidx = sub * 2 + half
                    p.op("vector", lambda v, sub=sub, half=half, i=i, bidx=bidx: v.scalar_tensor_tensor(
                        gt[:, half * 512:(half + 1) * 512], ctx.bk(bidx)[:, 0:512], bmask[:, i:i + 1],
                        lt[:, sub, half * 512:(half + 1) * 512], op0=OP.mult, op1=OP.add),
                        outs=[gt], ins=[ctx.bk(bidx), bmask, lt])
                gs = ctx.sbuf("ln_sq", [128, D], F32)
                p.op("scalar", lambda s, gt=gt, gs=gs: s.activation(gs, gt, AF.Silu), outs=[gs], ins=[gt])
                p.op("vector", lambda v, gs=gs, st=st: v.tensor_tensor(gs, gs, st, op=OP.mult),
                     outs=[gs], ins=[gs, st])
                p.op("vector", lambda v, gs=gs, xt=xt: v.tensor_tensor(gs, gs, xt, op=OP.add),
                     outs=[gs], ins=[gs, xt])
                p.dma("gpsimd", lambda e, b=b, i=i, gs=gs: e.dma_start(
                    out=y_out[b, i * 128:(i + 1) * 128, :], in_=gs), outs=[y_out], ins=[gs], sem_key="D:y")



def build_l2(ctx, x_in, y_out, W, scr):
    ctx.pnb = 1
    p, nc = ctx.p, ctx.nc
    B, T, D, NT, DC = ctx.B, ctx.T, ctx.D, ctx.NT, ctx.DC
    NH, DH = 16, 64
    NHP = NH // 2
    pA, pB = ctx.bk(0), ctx.bk(1)

    sim_stg = ctx.sbuf("l2_sim_stg", [128, DC, 16], F32)
    p.dma("sync", lambda e: e.dma_start(out=sim_stg, in_=W["sim_matrix"].rearrange("(a p) m -> p a m", p=128)),
          outs=[sim_stg], ins=[W["sim_matrix"]])
    ones_col_f = ctx.sbuf("c_ones_col_f", [128, 1], F32)
    p.op("gpsimd", lambda g: g.memset(ones_col_f, 1.0), outs=[ones_col_f])
    simsq = ctx.sbuf("l2_simsq", [128, DC, 16], F32)
    p.op("scalar", lambda s: s.activation(simsq, sim_stg, AF.Square), outs=[simsq], ins=[sim_stg])
    for c in range(DC):
        p.op("tensor", lambda t, c=c: t.matmul(pB[0:16, 0:1], simsq[:, c, :], ones_col_f,
                                               start=(c == 0), stop=(c == DC - 1)),
             outs=[pB], ins=[simsq, ones_col_f])
    cn = ctx.sbuf("l2_cn", [16, 1], F32)
    p.op("scalar", lambda s: s.activation(cn, pB[0:16, 0:1], AF.Sqrt), outs=[cn], ins=[pB])
    p.op("vector", lambda v: v.tensor_scalar(cn, cn, 1e-12, None, op0=OP.max, op1=OP.bypass),
         outs=[cn], ins=[cn])
    p.op("vector", lambda v: v.reciprocal(cn, cn), outs=[cn], ins=[cn])
    cnr = ctx.sbuf("l2_cnr", [1, 16], F32)
    p.op("tensor", lambda t: t.transpose(pB[0:1, 0:16], cn, ctx.idn[0:16, 0:16]), outs=[pB], ins=[cn, ctx.idn])
    p.op("scalar", lambda s: s.copy(cnr, pB[0:1, 0:16]), outs=[cnr], ins=[pB])
    ones_row_f = ctx.sbuf("c_ones_row_f", [1, 128], F32)
    p.op("gpsimd", lambda g: g.memset(ones_row_f, 1.0), outs=[ones_row_f])
    cnbc = ctx.sbuf("l2_cnbc", [128, 16], F32)
    p.op("tensor", lambda t: t.matmul(pB[:, 0:16], ones_row_f, cnr, start=True, stop=True),
         outs=[pB], ins=[ones_row_f, cnr])
    p.op("scalar", lambda s: s.copy(cnbc, pB[:, 0:16]), outs=[cnbc], ins=[pB])
    simn = ctx.sbuf("l2_simn", [128, DC, 16], F32)
    for c in range(DC):
        p.op("vector", lambda v, c=c: v.tensor_tensor(simn[:, c, :], sim_stg[:, c, :], cnbc, op=OP.mult),
             outs=[simn], ins=[sim_stg, cnbc])
    sim_hi = ctx.sbuf("l2_sim_hi", [128, DC, 16], BF16)
    sim_lo = ctx.sbuf("l2_sim_lo", [128, DC, 16], BF16)
    p.op("vector", lambda v: v.tensor_copy(sim_hi, simn), outs=[sim_hi], ins=[simn])
    p.op("vector", lambda v: v.tensor_tensor(simsq, simn, sim_hi, op=OP.subtract), outs=[simsq], ins=[simn, sim_hi])
    p.op("vector", lambda v: v.tensor_copy(sim_lo, simsq), outs=[sim_lo], ins=[simsq])
    grow = ctx.sbuf("c_rowbuf", [1, 512], F32)
    p.dma("sync", lambda e: e.dma_start(out=grow[0:1, 0:16], in_=W["gates"][None, :]),
          outs=[grow], ins=[W["gates"]], sem_key="D:colload")
    gsig = ctx.sbuf("l2_gsig", [1, 16], F32)
    p.op("scalar", lambda s: s.activation(gsig, grow[0:1, 0:16], AF.Sigmoid), outs=[gsig], ins=[grow])
    sigbc = ctx.sbuf("l2_sigbc", [128, 16], F32)
    p.op("tensor", lambda t: t.matmul(pB[:, 0:16], ones_row_f, gsig, start=True, stop=True),
         outs=[pB], ins=[ones_row_f, gsig])
    p.op("scalar", lambda s: s.copy(sigbc, pB[:, 0:16]), outs=[sigbc], ins=[pB])

    bq = ctx.col_load("l2_bq", W["b_Q"].rearrange("e h -> (e h)"), 8)
    bk_ = ctx.col_load("l2_bk", W["b_K"].rearrange("e h -> (e h)"), 8)
    bv_row = ctx.sbuf("l2_bv_row", [1, NH * DH], BF16)
    p.dma("gpsimd", lambda e: e.dma_start(out=bv_row, in_=W["b_V"].rearrange("e h -> (e h)")[None, :]),
          outs=[bv_row], ins=[W["b_V"]])
    bo_row = ctx.sbuf("l2_bo_row", [1, D], BF16)
    p.dma("gpsimd", lambda e: e.dma_start(out=bo_row, in_=W["b_O"][None, :]),
          outs=[bo_row], ins=[W["b_O"]])

    HT = T // 2
    for b in range(B):
        rn_cols = ctx.sbuf("l2_rncols", [128, NT], F32)

        def per_tile(bb, i, xn_tile):
            var, rstd = ctx.sb["ln_var"], ctx.sb["ln_rstd"]
            t1 = ctx.sbuf("l2_rt1", [128, 1], F32)
            p.op("scalar", lambda sc, i=i: sc.activation(t1, var, AF.Sqrt, scale=float(D)),
                 outs=[t1], ins=[var])
            p.op("vector", lambda v, i=i: v.tensor_tensor(t1, t1, rstd, op=OP.mult), outs=[t1], ins=[t1, rstd])
            p.op("vector", lambda v, i=i: v.tensor_scalar(t1, t1, 1e-12, None, op0=OP.max, op1=OP.bypass),
                 outs=[t1], ins=[t1])
            p.op("vector", lambda v, i=i: v.reciprocal(rn_cols[:, i:i + 1], t1), outs=[rn_cols], ins=[t1])

        ctx.phase_norm(x_in, scr["xnb"], b, xnb_lo=scr["xnb_lo"], per_tile=per_tile)
        xn_fm = ctx.load_xn_fm_b(scr["xnb"], b)
        xn_lo_fm = ctx.view("xn_lo_fm", [128, DC, T], BF16, 0)
        for c in range(DC):
            p.dma("sync", lambda e, b=b, c=c, t=xn_lo_fm: e.dma_start_transpose(
                t[:, c, :], scr["xnb_lo"][b, :, c * 128:(c + 1) * 128]),
                outs=[xn_lo_fm], ins=[scr["xnb_lo"]], sem_key="D:xnlofm")

        rw_all = ctx.sbuf("l2_rw_all", [128, NT, 16], F32)
        for i in range(NT):
            for ci, (xs, ws) in enumerate([(xn_fm, sim_hi), (xn_lo_fm, sim_hi), (xn_fm, sim_lo)]):
                for c in range(DC):
                    p.op("tensor", lambda t, i=i, c=c, xs=xs, ws=ws, ci=ci: t.matmul(
                        pA[:, 0:16], xs[:, c, i * 128:(i + 1) * 128], ws[:, c, :],
                        start=(ci == 0 and c == 0), stop=(ci == 2 and c == DC - 1)),
                        outs=[pA], ins=[xs, ws])
            lg = ctx.sbuf("l2_lg", [128, 16], F32)
            gated = ctx.sbuf("l2_gated", [128, 16], F32)
            m1 = ctx.sbuf("l2_m1", [128, 1], F32)
            m2 = ctx.sbuf("l2_m2", [128, 1], F32)
            eq1 = ctx.sbuf("l2_eq1", [128, 16], F32)
            eq2 = ctx.sbuf("l2_eq2", [128, 16], F32)
            lg2 = ctx.sbuf("l2_lg2", [128, 16], F32)
            act01 = ctx.sbuf("l2_act01", [128, 1], F32)
            fm = ctx.sbuf("l2_fm", [128, 16], F32)
            ml = ctx.sbuf("l2_ml", [128, 16], F32)
            nm3 = ctx.sbuf("l2_nm3", [128, 1], F32)
            zs = ctx.sbuf("l2_zs", [128, 1], F32)
            p.op("vector", lambda v, i=i: v.tensor_scalar(lg, pA[:, 0:16], rn_cols[:, i:i + 1], None,
                                                          op0=OP.mult, op1=OP.bypass), outs=[lg], ins=[pA, rn_cols])
            p.op("vector", lambda v: v.tensor_tensor(lg, lg, sigbc, op=OP.subtract), outs=[lg], ins=[lg, sigbc])
            p.op("vector", lambda v: v.tensor_scalar(gated, lg, 0.0, None, op0=OP.max, op1=OP.bypass),
                 outs=[gated], ins=[lg])
            p.op("vector", lambda v: v.tensor_reduce(m1, lg, axis=AX.X, op=OP.max), outs=[m1], ins=[lg])
            p.op("vector", lambda v: v.tensor_scalar(eq1, lg, m1, None, op0=OP.is_ge, op1=OP.bypass),
                 outs=[eq1], ins=[lg, m1])
            p.op("vector", lambda v: v.scalar_tensor_tensor(lg2, eq1, -1e30, lg, op0=OP.mult, op1=OP.add),
                 outs=[lg2], ins=[eq1, lg])
            p.op("vector", lambda v: v.tensor_reduce(m2, lg2, axis=AX.X, op=OP.max), outs=[m2], ins=[lg2])
            p.op("vector", lambda v: v.tensor_scalar(eq2, lg2, m2, None, op0=OP.is_ge, op1=OP.bypass),
                 outs=[eq2], ins=[lg2, m2])
            p.op("vector", lambda v: v.tensor_scalar(act01, m1, 0.0, None, op0=OP.is_gt, op1=OP.bypass),
                 outs=[act01], ins=[m1])
            p.op("vector", lambda v: v.tensor_tensor(fm, eq1, eq2, op=OP.add), outs=[fm], ins=[eq1, eq2])
            p.op("vector", lambda v: v.tensor_scalar(ml, lg, 0.0, None, op0=OP.is_gt, op1=OP.bypass),
                 outs=[ml], ins=[lg])
            p.op("vector", lambda v: v.tensor_tensor(ml, ml, fm, op=OP.subtract), outs=[ml], ins=[ml, fm])
            p.op("vector", lambda v: v.tensor_scalar(ml, ml, act01, None, op0=OP.mult, op1=OP.bypass),
                 outs=[ml], ins=[ml, act01])
            p.op("vector", lambda v: v.tensor_tensor(fm, fm, ml, op=OP.add), outs=[fm], ins=[fm, ml])
            p.op("vector", lambda v: v.tensor_scalar(ml, fm, 1e30, -1e30, op0=OP.mult, op1=OP.add),
                 outs=[ml], ins=[fm])
            p.op("vector", lambda v: v.scalar_tensor_tensor(ml, gated, 1.0, ml, op0=OP.mult, op1=OP.add)
                 if False else v.tensor_tensor(lg2, gated, fm, op=OP.mult), outs=[lg2], ins=[gated, fm])
            p.op("vector", lambda v: v.tensor_tensor(ml, ml, lg2, op=OP.add), outs=[ml], ins=[ml, lg2])
            p.op("vector", lambda v: v.tensor_reduce(nm3, ml, axis=AX.X, op=OP.max, negate=True),
                 outs=[nm3], ins=[ml])
            p.op("scalar", lambda s, i=i: s.activation(rw_all[:, i, :], ml, AF.Exp, bias=nm3,
                                                       accum_out=zs), outs=[rw_all, zs], ins=[ml, nm3])
            p.op("vector", lambda v: v.reciprocal(zs, zs), outs=[zs], ins=[zs])
            p.op("vector", lambda v, i=i: v.tensor_scalar(rw_all[:, i, :], rw_all[:, i, :], zs, None,
                                                          op0=OP.mult, op1=OP.bypass),
                 outs=[rw_all], ins=[rw_all, zs])

        wv = ctx.view("l2_wv", [128, DC, NH, DH], BF16, 0)
        for c in range(DC):
            p.dma("gpsimd", lambda e, c=c: e.dma_start(
                out=wv[:, c, :, :], in_=W["W_V"].rearrange("e (a p) h -> a p e h", p=128)[c]),
                outs=[wv], ins=[W["W_V"]], sem_key="D:l2wv")
        v_tm = ctx.view("l2_vtm", [128, NT, NH * DH], BF16, 16384)
        for i in range(NT):
            for half in range(2):
                for c in range(DC):
                    p.op("tensor", lambda t, i=i, half=half, c=c, xn_fm=xn_fm: t.matmul(
                        pA[:, 0:512], xn_fm[:, c, i * 128:(i + 1) * 128],
                        wv[:, c, half * 8:(half + 1) * 8, :], start=(c == 0), stop=False),
                        outs=[pA], ins=[xn_fm, wv])
                p.op("tensor", lambda t, i=i, half=half: t.matmul(
                    pA[:, 0:512], ctx.ones_row[0:1, 0:128], bv_row[0:1, half * 512:(half + 1) * 512],
                    start=False, stop=True), outs=[pA], ins=[ctx.ones_row, bv_row])
                p.op("scalar", lambda s, i=i, half=half: s.copy(
                    v_tm[:, i, half * 512:(half + 1) * 512], pA[:, 0:512]), outs=[v_tm], ins=[pA])
        attnout = ctx.view("l2_attnout", [128, NHP, T], BF16, 49152)
        for hp in range(NHP):
            wqs = ctx.view(f"l2_wqs{hp % 2}", [128, DC, 128], BF16, 81920 + (hp % 2) * 2048)
            wks = ctx.view(f"l2_wks{hp % 2}", [128, DC, 128], BF16, 86016 + (hp % 2) * 2048)
            for c in range(DC):
                p.dma("gpsimd", lambda e, hp=hp, c=c, wqs=wqs: e.dma_start(
                    out=wqs[:, c, :].rearrange("p (e h) -> p e h", e=2),
                    in_=W["W_Q"][2 * hp:2 * hp + 2].rearrange("e (a p) h -> a p e h", p=128)[c]),
                    outs=[wqs], ins=[W["W_Q"]], sem_key=f"D:l2wq{hp % 2}")
                p.dma("gpsimd", lambda e, hp=hp, c=c, wks=wks: e.dma_start(
                    out=wks[:, c, :].rearrange("p (e h) -> p e h", e=2),
                    in_=W["W_K"][2 * hp:2 * hp + 2].rearrange("e (a p) h -> a p e h", p=128)[c]),
                    outs=[wks], ins=[W["W_K"]], sem_key=f"D:l2wk{hp % 2}")
            q_fm = ctx.view("l2_qfm", [128, T], BF16, 90112)
            k_fm = ctx.view("l2_kfm", [128, T], BF16, 94208)
            for tc in range(T // 512):
                for c in range(DC):
                    p.op("tensor", lambda t, tc=tc, c=c, wqs=wqs, xn_fm=xn_fm: t.matmul(
                        pA[:, 0:512], wqs[:, c, :], xn_fm[:, c, tc * 512:(tc + 1) * 512],
                        start=(c == 0), stop=(c == DC - 1)), outs=[pA], ins=[wqs, xn_fm])
                p.op("scalar", lambda s, tc=tc, hp=hp: s.activation(
                    q_fm[:, tc * 512:(tc + 1) * 512], pA[:, 0:512], AF.Copy, bias=0.0, scale=0.125),
                    outs=[q_fm], ins=[pA])
                for c in range(DC):
                    p.op("tensor", lambda t, tc=tc, c=c, wks=wks, xn_fm=xn_fm: t.matmul(
                        pA[:, 0:512], wks[:, c, :], xn_fm[:, c, tc * 512:(tc + 1) * 512],
                        start=(c == 0), stop=(c == DC - 1)), outs=[pA], ins=[wks, xn_fm])
                p.op("scalar", lambda s, tc=tc, hp=hp: s.copy(k_fm[:, tc * 512:(tc + 1) * 512], pA[:, 0:512]),
                     outs=[k_fm], ins=[pA])
            bqc = ctx.sbuf("l2_bqc", [128, 1], F32)
            p.op("vector", lambda v, hp=hp: v.tensor_scalar(bqc, bq[:, hp:hp + 1], 0.125, None,
                                                            op0=OP.mult, op1=OP.bypass), outs=[bqc], ins=[bq])
            for tc in range(T // 512):
                p.op("vector", lambda v, tc=tc: v.tensor_scalar(
                    q_fm[:, tc * 512:(tc + 1) * 512], q_fm[:, tc * 512:(tc + 1) * 512], bqc, None,
                    op0=OP.add, op1=OP.bypass), outs=[q_fm], ins=[q_fm, bqc])
                p.op("vector", lambda v, tc=tc, hp=hp: v.tensor_scalar(
                    k_fm[:, tc * 512:(tc + 1) * 512], k_fm[:, tc * 512:(tc + 1) * 512], bk_[:, hp:hp + 1],
                    None, op0=OP.add, op1=OP.bypass), outs=[k_fm], ins=[k_fm, bk_])
            for hh in range(2):
                h = 2 * hp + hh
                hb64 = hh * 64
                mcols = ctx.sbuf("l2_mcols", [128, NT], F32)
                mtmp = ctx.sbuf("l2_mtmp", [128, 4], F32)
                for i in range(NT):
                    nk = (i + 1) * 128
                    nchunk = (nk + 511) // 512
                    for j in range(nchunk):
                        n = min(512, nk - j * 512)
                        p.op("tensor", lambda t, i=i, j=j, n=n, hb64=hb64: t.matmul(
                            pA[:, 0:n], q_fm[hb64:hb64 + 64, i * 128:(i + 1) * 128],
                            k_fm[hb64:hb64 + 64, j * 512:j * 512 + n], start=True, stop=True),
                            outs=[pA], ins=[q_fm, k_fm])
                        p.op("vector", lambda v, j=j, n=n: v.tensor_reduce(
                            mtmp[:, j:j + 1], pA[:, 0:n], axis=AX.X, op=OP.max), outs=[mtmp], ins=[pA])
                    p.op("vector", lambda v, i=i, nchunk=nchunk: v.tensor_reduce(
                        mcols[:, i:i + 1], mtmp[:, 0:nchunk], axis=AX.X, op=OP.max), outs=[mcols], ins=[mtmp])
                negm = ctx.sbuf("l2_negm", [128, 1], F32)
                ctx.neg_scalar_max(mcols, NT, negm, bank=1)
                pS, pO, pZ = ctx.bk(2), ctx.bk(3), ctx.bk(4)
                PT = ctx.sbuf("l2_PT", [128, 512], BF16)
                zrow = ctx.sbuf("l2_zrow", [1, 512], F32)
                zbc = ctx.sbuf("l2_zbc", [128, 512], F32)
                for qc in range(T // 512):
                    tq0 = qc * 512
                    ktiles = list(range(0, (tq0 + 512) // 128))
                    for kk, k in enumerate(ktiles):
                        st, sp = (kk == 0), (kk == len(ktiles) - 1)
                        p.op("tensor", lambda t, k=k, tq0=tq0, hb64=hb64: t.matmul(
                            pS[:, 0:512], k_fm[hb64:hb64 + 64, k * 128:(k + 1) * 128],
                            q_fm[hb64:hb64 + 64, tq0:tq0 + 512], start=True, stop=True),
                            outs=[pS], ins=[k_fm, q_fm])
                        p.op("scalar", lambda s: s.activation(PT, pS[:, 0:512], AF.Exp, bias=negm),
                             outs=[PT], ins=[pS, negm])
                        dsub = k - tq0 // 128
                        if 0 <= dsub < 4:
                            p.op("vector", lambda v, dsub=dsub: v.tensor_tensor(
                                PT[:, dsub * 128:(dsub + 1) * 128], PT[:, dsub * 128:(dsub + 1) * 128],
                                ctx.tri, op=OP.mult), outs=[PT], ins=[PT, ctx.tri])
                            for z in range(dsub):
                                p.op("gpsimd", lambda g, z=z: g.memset(PT[:, z * 128:(z + 1) * 128], 0.0),
                                     outs=[PT])
                        p.op("tensor", lambda t, k=k, st=st, sp=sp, h=h: t.matmul(
                            pO[0:64, 0:512], v_tm[:, k, h * 64:(h + 1) * 64], PT, start=st, stop=sp),
                            outs=[pO], ins=[v_tm, PT])
                        p.op("tensor", lambda t, st=st, sp=sp: t.matmul(
                            pZ[0:1, 0:512], ctx.ones_col, PT, start=st, stop=sp),
                            outs=[pZ], ins=[ctx.ones_col, PT])
                    p.op("vector", lambda v: v.reciprocal(zrow, pZ[0:1, 0:512]), outs=[zrow], ins=[pZ])
                    zrow_bf = ctx.sbuf("l2_zrow_bf", [1, 512], BF16)
                    p.op("vector", lambda v: v.tensor_copy(zrow_bf, zrow), outs=[zrow_bf], ins=[zrow])
                    p.op("tensor", lambda t: t.matmul(pZ[:, 0:512], ctx.ones_row[0:1, 0:128], zrow_bf,
                                                      start=True, stop=True), outs=[pZ], ins=[ctx.ones_row, zrow_bf])
                    p.op("scalar", lambda s: s.copy(zbc, pZ[:, 0:512]), outs=[zbc], ins=[pZ])
                    p.op("vector", lambda v, hp=hp, hb64=hb64, tq0=tq0: v.tensor_tensor(
                        attnout[hb64:hb64 + 64, hp, tq0:tq0 + 512], pO[0:64, 0:512], zbc[0:64, :], op=OP.mult),
                        outs=[attnout], ins=[pO, zbc])
        wo = ctx.view("l2_wo", [128, NHP, D], BF16, 98304)
        p.dma("gpsimd", lambda e: e.dma_start(
            out=wo.rearrange("p a d -> p a d"),
            in_=W["W_O"].rearrange("(a e2) h d -> (e2 h) a d", e2=2)), outs=[wo], ins=[W["W_O"]],
            sem_key="D:l2wo")
        s_t = ctx.sbuf("ln_sq", [128, D], F32)
        for i in range(NT):
            for half in range(2):
                for hc in range(NHP):
                    p.op("tensor", lambda t, i=i, half=half, hc=hc: t.matmul(
                        pA[:, 0:512], attnout[:, hc, i * 128:(i + 1) * 128],
                        wo[:, hc, half * 512:(half + 1) * 512],
                        start=(hc == 0), stop=False), outs=[pA], ins=[attnout, wo])
                p.op("tensor", lambda t, half=half: t.matmul(
                    pA[:, 0:512], ctx.ones_row[0:1, 0:128], bo_row[0:1, half * 512:(half + 1) * 512],
                    start=False, stop=True), outs=[pA], ins=[ctx.ones_row, bo_row])
                p.op("vector", lambda v, half=half: v.tensor_copy(
                    s_t[:, half * 512:(half + 1) * 512], pA[:, 0:512]), outs=[s_t], ins=[pA])
            p.dma("gpsimd", lambda e, i=i, b=b: e.dma_start(
                out=scr["s"][b, i * 128:(i + 1) * 128, :], in_=s_t), outs=[scr["s"]], ins=[s_t],
                sem_key="D:l2s")

        wq3 = ctx.view("l2_wq3", [128, DC, NH, DH], BF16, 0)
        wk3 = ctx.view("l2_wk3", [128, DC, NH, DH], BF16, 16384)
        wv3 = ctx.view("l2_wv3", [128, DC, NH, DH], BF16, 32768)
        for nm, dst, src in (("q", wq3, "q_proj"), ("k", wk3, "k_proj"), ("v", wv3, "v_proj")):
            for c in range(DC):
                p.dma("gpsimd", lambda e, dst=dst, src=src, c=c: e.dma_start(
                    out=dst[:, c, :, :], in_=W[src].rearrange("e (a p) h -> a p e h", p=128)[c]),
                    outs=[dst], ins=[W[src]], sem_key=f"D:l2w3{nm}")
        q2f = ctx.view("l2_q2f", [64, T], BF16, 73728)
        k2f = ctx.view("l2_k2f", [64, T], BF16, 77824)
        v2a = ctx.view("l2_v2a", [128, NT, DH + 1], BF16, 81920)
        mixq = ctx.sbuf("l2_mixq", [128, DH], F32)
        mixk = ctx.sbuf("l2_mixk", [128, DH], F32)
        mixv = ctx.sbuf("l2_mixv", [128, DH], F32)
        pall = ctx.view("l2_pall", [128, 3, NH * DH], F32, 86016)
        for i in range(NT):
            for ti, (w3, mx) in enumerate([(wq3, mixq), (wk3, mixk), (wv3, mixv)]):
                for half in range(2):
                    for c in range(DC):
                        p.op("tensor", lambda t, i=i, ti=ti, half=half, c=c, w3=w3, xn_fm=xn_fm: t.matmul(
                            pA[:, 0:512], xn_fm[:, c, i * 128:(i + 1) * 128],
                            w3[:, c, half * 8:(half + 1) * 8, :], start=(c == 0), stop=(c == DC - 1)),
                            outs=[pA], ins=[xn_fm, w3])
                    p.op("scalar", lambda s, i=i, ti=ti, half=half: s.copy(
                        pall[:, ti, half * 512:(half + 1) * 512], pA[:, 0:512]), outs=[pall], ins=[pA])
                for e in range(NH):
                    if e == 0:
                        p.op("vector", lambda v, i=i, ti=ti, mx=mx: v.tensor_scalar(
                            mx, pall[:, ti, 0:DH], rw_all[:, i, 0:1], None, op0=OP.mult, op1=OP.bypass),
                            outs=[mx], ins=[pall, rw_all])
                    else:
                        p.op("vector", lambda v, i=i, ti=ti, e=e, mx=mx: v.scalar_tensor_tensor(
                            mx, pall[:, ti, e * DH:(e + 1) * DH], rw_all[:, i, e:e + 1], mx,
                            op0=OP.mult, op1=OP.add), outs=[mx], ins=[pall, rw_all, mx])
            p.op("vector", lambda v, i=i: v.tensor_scalar(mixq, mixq, 0.125, None, op0=OP.mult, op1=OP.bypass),
                 outs=[mixq], ins=[mixq])
            p.op("tensor", lambda t, i=i: t.transpose(pB[0:64, 0:128], mixq, ctx.idn), outs=[pB], ins=[mixq, ctx.idn])
            p.op("scalar", lambda s, i=i: s.copy(q2f[:, i * 128:(i + 1) * 128], pB[0:64, 0:128]),
                 outs=[q2f], ins=[pB])
            p.op("tensor", lambda t, i=i: t.transpose(pB[0:64, 0:128], mixk, ctx.idn), outs=[pB], ins=[mixk, ctx.idn])
            p.op("scalar", lambda s, i=i: s.copy(k2f[:, i * 128:(i + 1) * 128], pB[0:64, 0:128]),
                 outs=[k2f], ins=[pB])
            p.op("vector", lambda v, i=i: v.tensor_copy(v2a[:, i, 0:DH], mixv), outs=[v2a], ins=[mixv])
            p.op("gpsimd", lambda g, i=i: g.memset(v2a[:, i, DH:DH + 1], 1.0), outs=[v2a])
        mcols = ctx.sbuf("l2_mcols", [128, NT], F32)
        mtmp = ctx.sbuf("l2_mtmp", [128, 4], F32)
        for i in range(NT):
            nk = (i + 1) * 128
            nchunk = (nk + 511) // 512
            for j in range(nchunk):
                n = min(512, nk - j * 512)
                p.op("tensor", lambda t, i=i, j=j, n=n: t.matmul(
                    pA[:, 0:n], q2f[:, i * 128:(i + 1) * 128], k2f[:, j * 512:j * 512 + n],
                    start=True, stop=True), outs=[pA], ins=[q2f, k2f])
                p.op("vector", lambda v, j=j, n=n: v.tensor_reduce(
                    mtmp[:, j:j + 1], pA[:, 0:n], axis=AX.X, op=OP.max), outs=[mtmp], ins=[pA])
            p.op("vector", lambda v, i=i, nchunk=nchunk: v.tensor_reduce(
                mcols[:, i:i + 1], mtmp[:, 0:nchunk], axis=AX.X, op=OP.max), outs=[mcols], ins=[mtmp])
        negm = ctx.sbuf("l2_negm", [128, 1], F32)
        ctx.neg_scalar_max(mcols, NT, negm, bank=1)
        pS = ctx.bk(2)
        PT = ctx.sbuf("l2_PT", [128, 512], BF16)
        wo3 = ctx.view("l2_wo3", [128, DC, D], BF16, 49152)
        p.dma("gpsimd", lambda e: e.dma_start(
            out=wo3, in_=W["o_proj"].rearrange("(a e2) h d -> (e2 h) a d", e2=2)), outs=[wo3],
            ins=[W["o_proj"]], sem_key="D:l2wo3")
        att2 = ctx.sbuf("l2_att2", [128, DH], F32)
        z2 = ctx.sbuf("l2_z2", [128, NH, DH], F32)
        z2t = ctx.view("l2_z2t", [128, DC, 128], BF16, 66560)
        for qc in range(T // 512):
            tq0 = qc * 512
            ktiles = list(range(0, (tq0 + 512) // 128))
            for kk, k in enumerate(ktiles):
                st, sp = (kk == 0), (kk == len(ktiles) - 1)
                p.op("tensor", lambda t, k=k, tq0=tq0: t.matmul(
                    pS[:, 0:512], k2f[:, k * 128:(k + 1) * 128], q2f[:, tq0:tq0 + 512],
                    start=True, stop=True), outs=[pS], ins=[k2f, q2f])
                p.op("scalar", lambda s: s.activation(PT, pS[:, 0:512], AF.Exp, bias=negm),
                     outs=[PT], ins=[pS, negm])
                dsub = k - tq0 // 128
                if 0 <= dsub < 4:
                    p.op("vector", lambda v, dsub=dsub: v.tensor_tensor(
                        PT[:, dsub * 128:(dsub + 1) * 128], PT[:, dsub * 128:(dsub + 1) * 128],
                        ctx.tri, op=OP.mult), outs=[PT], ins=[PT, ctx.tri])
                    for z in range(dsub):
                        p.op("gpsimd", lambda g, z=z: g.memset(PT[:, z * 128:(z + 1) * 128], 0.0), outs=[PT])
                for sub in range(4):
                    p.op("tensor", lambda t, k=k, sub=sub, st=st, sp=sp: t.matmul(
                        ctx.bk(3 + sub)[:, 0:DH + 1], PT[:, sub * 128:(sub + 1) * 128], v2a[:, k, :],
                        start=st, stop=sp), outs=[ctx.bk(3 + sub)], ins=[PT, v2a])
            for sub in range(4):
                i = qc * 4 + sub
                rz = ctx.sbuf("l2_rz", [128, 1], F32)
                p.op("vector", lambda v, sub=sub: v.reciprocal(rz, ctx.bk(3 + sub)[:, DH:DH + 1]),
                     outs=[rz], ins=[ctx.bk(3 + sub)])
                p.op("vector", lambda v, sub=sub: v.tensor_scalar(att2, ctx.bk(3 + sub)[:, 0:DH], rz, None,
                                                                  op0=OP.mult, op1=OP.bypass),
                     outs=[att2], ins=[ctx.bk(3 + sub), rz])
                for e in range(NH):
                    p.op("vector", lambda v, i=i, e=e: v.tensor_scalar(
                        z2[:, e, :], att2, rw_all[:, i, e:e + 1], None, op0=OP.mult, op1=OP.bypass),
                        outs=[z2], ins=[att2, rw_all])
                for c in range(DC):
                    p.op("tensor", lambda t, c=c: t.transpose(pB[:, 0:128], z2[:, 2 * c:2 * c + 2, :], ctx.idn),
                         outs=[pB], ins=[z2, ctx.idn])
                    p.op("scalar", lambda s, c=c: s.copy(z2t[:, c, :], pB[:, 0:128]), outs=[z2t], ins=[pB])
                xt = ctx.sbuf(f"pn_x{sub % ctx.pnb}", [128, D], F32)
                st_ = ctx.sbuf(f"pn_xn{sub % ctx.pnb}", [128, D], F32)
                gt = ctx.sbuf("l2_ep_g", [128, D], F32)
                gs = ctx.sbuf("ln_sq", [128, D], F32)
                p.dma("sync", lambda e, b=b, i=i, xt=xt: e.dma_start(
                    out=xt, in_=x_in[b, i * 128:(i + 1) * 128, :]), outs=[xt], ins=[x_in])
                p.dma("sync", lambda e, b=b, i=i, st_=st_: e.dma_start(
                    out=st_, in_=scr["s"][b, i * 128:(i + 1) * 128, :]), outs=[st_], ins=[scr["s"]],
                    sem_key="D:l2s_rd")
                for half in range(2):
                    for c in range(DC):
                        p.op("tensor", lambda t, half=half, c=c: t.matmul(
                            pA[:, 0:512], z2t[:, c, :], wo3[:, c, half * 512:(half + 1) * 512],
                            start=(c == 0), stop=(c == DC - 1)), outs=[pA], ins=[z2t, wo3])
                    p.op("vector", lambda v, half=half, gt=gt: v.tensor_copy(
                        gt[:, half * 512:(half + 1) * 512], pA[:, 0:512]), outs=[gt], ins=[pA])
                p.op("scalar", lambda s, gt=gt, gs=gs: s.activation(gs, gt, AF.Silu), outs=[gs], ins=[gt])
                p.op("vector", lambda v, gs=gs, st_=st_: v.tensor_tensor(gs, gs, st_, op=OP.mult),
                     outs=[gs], ins=[gs, st_])
                p.op("vector", lambda v, gs=gs, xt=xt: v.tensor_tensor(gs, gs, xt, op=OP.add),
                     outs=[gs], ins=[gs, xt])
                p.dma("gpsimd", lambda e, b=b, i=i, gs=gs: e.dma_start(
                    out=y_out[b, i * 128:(i + 1) * 128, :], in_=gs), outs=[y_out], ins=[gs], sem_key="D:y")



def build_l3(ctx, x_in, y_out, W, scr):
    ctx.pnb = 1
    import os
    PH = int(os.environ.get("L3PH", "99"))
    p, nc = ctx.p, ctx.nc
    B, T, D, NT, DC = ctx.B, ctx.T, ctx.D, ctx.NT, ctx.DC
    NHEAD, C, DV = 8, 64, 128
    NG = 4
    L = 8
    NCH = T // L
    pA, pB = ctx.bk(0), ctx.bk(1)

    WaT = ctx.load_transposed("l3_WaT", W["Wa_w"], 512, D, stage_off=0)
    WqlT = ctx.load_transposed("l3_Wql", W["Wq"], C, D, stage_off=0)
    WklT = ctx.load_transposed("l3_Wkl", W["Wk"], C, D, stage_off=0)
    WvlT = ctx.load_transposed("l3_Wvl", W["Wv"], C, D, stage_off=0)
    wab = ctx.col_load("l3_wab", W["Wa_b"], NG)
    gmc = ctx.col_load("l3_gmc", W["gamma"], DC)
    btc = ctx.col_load("l3_btc", W["beta"], DC)
    mur = ctx.col_load("l3_mur", W["mu_r"], DC)
    muk = ctx.col_load("l3_muk", W["mu_k"], DC)
    Qc_sb = ctx.sbuf("l3_Qc", [C, NHEAD, C], BF16)
    p.dma("gpsimd", lambda e: e.dma_start(out=Qc_sb, in_=W["Qc"].rearrange("h c e -> c h e")),
          outs=[Qc_sb], ins=[W["Qc"]])
    Vc_sb = ctx.sbuf("l3_Vc", [C, NHEAD, DV], BF16)
    p.dma("gpsimd", lambda e: e.dma_start(out=Vc_sb, in_=W["Vc"].rearrange("h c d -> c h d")),
          outs=[Vc_sb], ins=[W["Vc"]])
    t8 = ctx.view("l3_t8", [128, T], F32, 0)
    p.op("gpsimd", lambda g: g.iota(t8, base=0, pattern=[[0, T // L], [1, L]], channel_multiplier=0,
                                    allow_small_or_imprecise_dtypes=True), outs=[t8])
    ms = ctx.sbuf("l3_ms", [128, T], BF16)
    p.op("vector", lambda v: v.tensor_scalar(ms, t8, 0.5, None, op0=OP.is_lt, op1=OP.bypass),
         outs=[ms], ins=[t8])
    bo2 = ctx.sbuf("l3_bo2", [128, 2], BF16)
    p.op("vector", lambda v: v.tensor_scalar(bo2[:, 0:1], ctx.pidx, 63.5, None, op0=OP.is_lt, op1=OP.bypass),
         outs=[bo2], ins=[ctx.pidx])
    p.op("vector", lambda v: v.tensor_scalar(bo2[:, 1:2], ctx.pidx, 63.5, None, op0=OP.is_gt, op1=OP.bypass),
         outs=[bo2], ins=[ctx.pidx])
    t2 = ctx.sbuf("l3_t2", [2, 128], F32)
    p.op("gpsimd", lambda g: g.iota(t2, base=0, pattern=[[0, 128]], channel_multiplier=1,
                                    allow_small_or_imprecise_dtypes=True), outs=[t2])
    sel = [ctx.sbuf(f"l3_sel{hh}", [2, 128], BF16) for hh in range(2)]
    p.op("vector", lambda v: v.tensor_scalar(sel[0], t2, 0.5, None, op0=OP.is_lt, op1=OP.bypass),
         outs=[sel[0]], ins=[t2])
    p.op("vector", lambda v: v.tensor_scalar(sel[1], t2, 0.5, None, op0=OP.is_gt, op1=OP.bypass),
         outs=[sel[1]], ins=[t2])

    D_cols = ctx.sbuf("l3_Dcols", [128, NG, NCH], F32)

    for b in range(B):
        f_cols = ctx.sbuf("l3_fcols", [128, NT], F32)

        def per_tile(bb, i, xn_tile):
            var, rstd = ctx.sb["ln_var"], ctx.sb["ln_rstd"]
            t1 = ctx.sbuf("l3_ft1", [128, 1], F32)
            p.op("vector", lambda v, i=i: v.tensor_tensor(t1, rstd, rstd, op=OP.mult), outs=[t1], ins=[rstd])
            p.op("vector", lambda v, i=i: v.tensor_tensor(t1, t1, var, op=OP.mult), outs=[t1], ins=[t1, var])
            p.op("scalar", lambda sc, i=i: sc.activation(t1, t1, AF.Sqrt, bias=ctx.sb["ln_eps"]),
                 outs=[t1], ins=[t1])
            p.op("vector", lambda v, i=i: v.reciprocal(f_cols[:, i:i + 1], t1), outs=[f_cols], ins=[t1])

        ctx.phase_norm(x_in, scr["xnb"], b, per_tile=per_tile, xnf=scr["xnf"])
        xn_fm = ctx.view("xn_fm_l3", [128, DC, T], BF16, 0)
        for c in range(DC):
            p.dma("sync", lambda e, b=b, c=c, t=xn_fm: e.dma_start_transpose(
                t[:, c, :], scr["xnb"][b, :, c * 128:(c + 1) * 128]),
                outs=[xn_fm], ins=[scr["xnb"]], sem_key="D:xn_fm")


        xn2 = ctx.view("l3_xn2", [128, DC, T], BF16, 77824)
        fbc = ctx.sbuf("l3_fbc", [128, 512], BF16)
        frow = ctx.sbuf("l3_rowb2", [1, 512], BF16)
        for tc in range(T // 512):
            for sub in range(4):
                i = tc * 4 + sub
                p.op("tensor", lambda t, i=i: t.transpose(pB[0:1, 0:128], f_cols[:, i:i + 1], ctx.idn),
                     outs=[pB], ins=[f_cols, ctx.idn])
                p.op("scalar", lambda s, sub=sub: s.copy(frow[0:1, sub * 128:(sub + 1) * 128], pB[0:1, 0:128]),
                     outs=[frow], ins=[pB])
            p.op("tensor", lambda t, tc=tc: t.matmul(pA[:, 0:512], ctx.ones_row[0:1, 0:128],
                                                     frow, start=True, stop=True),
                 outs=[pA], ins=[ctx.ones_row, frow])
            p.op("scalar", lambda s, tc=tc: s.copy(fbc, pA[:, 0:512]), outs=[fbc], ins=[pA])
            for c in range(DC):
                t1 = ctx.sbuf("l3_otmp", [128, 512], F32)
                p.op("vector", lambda v, c=c, tc=tc, t1=t1: v.tensor_tensor(
                    t1, xn_fm[:, c, tc * 512:(tc + 1) * 512], fbc, op=OP.mult), outs=[t1], ins=[xn_fm, fbc])
                p.op("vector", lambda v, c=c, tc=tc, t1=t1: v.tensor_scalar(
                    xn2[:, c, tc * 512:(tc + 1) * 512], t1, gmc[:, c:c + 1], btc[:, c:c + 1],
                    op0=OP.mult, op1=OP.add), outs=[xn2], ins=[t1, gmc, btc])
        q_lat = ctx.view("l3_qlat", [64, T], BF16, 65536)
        k_lat = ctx.view("l3_klat", [64, T], BF16, 69632)
        v_lat = ctx.view("l3_vlat", [64, T], BF16, 73728)
        for wT, dst in ((WqlT, q_lat), (WklT, k_lat), (WvlT, v_lat)):
            for tc in range(T // 512):
                for c in range(DC):
                    p.op("tensor", lambda t, tc=tc, c=c, wT=wT: t.matmul(
                        pA[0:64, 0:512], wT[:, c, :], xn2[:, c, tc * 512:(tc + 1) * 512],
                        start=(c == 0), stop=(c == DC - 1)), outs=[pA], ins=[wT, xn2])
                p.op("scalar", lambda s, tc=tc, dst=dst: s.copy(dst[:, tc * 512:(tc + 1) * 512], pA[0:64, 0:512]),
                     outs=[dst], ins=[pA])
        a_all = ctx.view("l3_aall", [128, NG, T], F32, 32768)
        for g in range(NG):
            for tc in range(T // 512):
                for c in range(DC):
                    p.op("tensor", lambda t, g=g, tc=tc, c=c: t.matmul(
                        pA[:, 0:512], WaT[:, c, g * 128:(g + 1) * 128], xn2[:, c, tc * 512:(tc + 1) * 512],
                        start=(c == 0), stop=(c == DC - 1)), outs=[pA], ins=[WaT, xn2])
                p.op("scalar", lambda s, g=g, tc=tc: s.activation(
                    a_all[:, g, tc * 512:(tc + 1) * 512], pA[:, 0:512], AF.Sigmoid, bias=wab[:, g:g + 1]),
                    outs=[a_all], ins=[pA, wab])
        rknbc = ctx.sbuf("l3_rknbc", [128, T], BF16)
        rkn_cols = ctx.sbuf("l3_rkncols", [128, NT], F32)
        rowb = ctx.sbuf("l3_rowb", [1, 512], F32)
        rowb2 = ctx.sbuf("l3_rowb2", [1, 512], BF16)
        ksq = ctx.sbuf("l3_otmp", [128, 512], F32)[0:64, :]
        ksqb = ctx.sbuf("l3_ksqb", [64, 512], BF16)
        for tc in range(T // 512):
            p.op("scalar", lambda s, tc=tc: s.activation(ksq, k_lat[:, tc * 512:(tc + 1) * 512], AF.Square),
                 outs=[ksq], ins=[k_lat])
            p.op("vector", lambda v: v.tensor_copy(ksqb, ksq), outs=[ksqb], ins=[ksq])
            p.op("tensor", lambda t, tc=tc: t.matmul(pB[0:1, 0:512], ctx.ones_col[0:64, :], ksqb,
                                                     start=True, stop=True), outs=[pB], ins=[ctx.ones_col, ksqb])
            p.op("scalar", lambda s, tc=tc: s.activation(rowb, pB[0:1, 0:512], AF.Sqrt),
                 outs=[rowb], ins=[pB])
            p.op("vector", lambda v: v.tensor_scalar(rowb, rowb, 1e-12, None, op0=OP.max, op1=OP.bypass),
                 outs=[rowb], ins=[rowb])
            p.op("vector", lambda v: v.reciprocal(rowb, rowb), outs=[rowb], ins=[rowb])
            p.op("vector", lambda v: v.tensor_copy(rowb2, rowb), outs=[rowb2], ins=[rowb])
            for sub in range(4):
                i = tc * 4 + sub
                p.op("tensor", lambda t, sub=sub: t.transpose(
                    pB[0:128, 0:1], rowb[0:1, sub * 128:(sub + 1) * 128], ctx.idn[0:1, 0:1]),
                    outs=[pB], ins=[rowb, ctx.idn])
                p.op("scalar", lambda s, i=i: s.copy(rkn_cols[:, i:i + 1], pB[0:128, 0:1]),
                     outs=[rkn_cols], ins=[pB])
            p.op("tensor", lambda t, tc=tc: t.matmul(pB[:, 0:512], ctx.ones_row[0:1, 0:128], rowb2,
                                                     start=True, stop=True), outs=[pB], ins=[ctx.ones_row, rowb2])
            p.op("scalar", lambda s, tc=tc: s.copy(rknbc[:, tc * 512:(tc + 1) * 512], pB[:, 0:512]),
                 outs=[rknbc], ins=[pB])
        k_tm2 = ctx.sbuf("l3_ktm2", [128, NT, 128], BF16)
        for i in range(NT):
            for c in range(DC):
                p.op("tensor", lambda t, i=i, c=c: t.matmul(
                    pA[:, 0:64], xn2[:, c, i * 128:(i + 1) * 128], WklT[:, c, :],
                    start=(c == 0), stop=(c == DC - 1)), outs=[pA], ins=[xn2, WklT])
            p.op("scalar", lambda s, i=i: s.copy(k_tm2[:, i, 0:64], pA[:, 0:64]), outs=[k_tm2], ins=[pA])
            p.op("scalar", lambda s, i=i: s.copy(k_tm2[:, i, 64:128], pA[:, 0:64]), outs=[k_tm2], ins=[pA])
        for h in range(NHEAD):
            for i in range(NT):
                p.op("tensor", lambda t, h=h, i=i: t.matmul(
                    pA[:, 0:DV], v_lat[:, i * 128:(i + 1) * 128], Vc_sb[:, h, :],
                    start=True, stop=True), outs=[pA], ins=[v_lat, Vc_sb])
                vt = ctx.sbuf(f"l3_vt{i % 2}", [128, DV], BF16)
                p.op("vector", lambda v, i=i, vt=vt: v.tensor_scalar(
                    vt, pA[:, 0:DV], rkn_cols[:, i:i + 1], None, op0=OP.mult, op1=OP.bypass),
                    outs=[vt], ins=[pA, rkn_cols])
                p.dma("gpsimd", lambda e, h=h, i=i, vt=vt: e.dma_start(
                    out=scr["vsd"][i * 128:(i + 1) * 128, h, :], in_=vt), outs=[scr["vsd"]], ins=[vt],
                    sem_key="D:l3vs")
        kk = ctx.sbuf("l3_kk", [128, L + T], BF16)
        p.op("gpsimd", lambda g: g.memset(kk[:, 0:L], 0.0), outs=[kk])
        p.op("scalar", lambda s: s.copy(kk[0:64, L:L + T], k_lat), outs=[kk], ins=[k_lat])
        p.op("scalar", lambda s: s.copy(kk[64:128, L:L + T], k_lat), outs=[kk], ins=[k_lat])

        for g in (range(NG) if PH >= 2 else []):
            ag = a_all[:, g, :]
            CP = ctx.view("l3_CP", [128, T], F32, 110656)
            d0 = ctx.view("l3_d0", [128, T], F32, 77824)
            d1 = ctx.view("l3_d1", [128, T], F32, 86016)
            ams = ctx.view("l3_ams", [128, T], F32, 86016)
            p.op("vector", lambda v, ag=ag: v.tensor_tensor(ams, ag, ms, op=OP.mult), outs=[ams], ins=[a_all, ms])
            p.op("vector", lambda v, ag=ag: v.tensor_tensor(d0, ag, ams, op=OP.subtract), outs=[d0], ins=[a_all, ams])
            p.op("vector", lambda v: v.tensor_tensor_scan(CP, d0, ams, 1.0, op0=OP.mult, op1=OP.add),
                 outs=[CP], ins=[d0, ams])
            p.op("vector", lambda v, g=g: v.tensor_copy(
                D_cols[:, g, :], CP.rearrange("p (n l) -> p n l", l=L)[:, :, L - 1]),
                outs=[D_cols], ins=[CP])
            for tc in range(T // 512):
                for hh in range(2):
                    p.op("tensor", lambda t, g=g, tc=tc, hh=hh: t.matmul(
                        pA[hh * 64:(hh + 1) * 64, 0:512], Qc_sb[:, 2 * g + hh, :],
                        q_lat[:, tc * 512:(tc + 1) * 512], start=True, stop=True),
                        outs=[pA], ins=[Qc_sb, q_lat])
                qcpt = ctx.sbuf("l3_otmp", [128, 512], F32)
                p.op("vector", lambda v, g=g, tc=tc, qcpt=qcpt: v.tensor_tensor(
                    qcpt, pA[:, 0:512], CP[:, tc * 512:(tc + 1) * 512],
                    op=OP.mult), outs=[qcpt], ins=[pA, CP])
                p.dma("gpsimd", lambda e, g=g, tc=tc, qcpt=qcpt: e.dma_start(
                    out=scr["qcpd"][g, :, tc * 512:(tc + 1) * 512], in_=qcpt),
                    outs=[scr["qcpd"]], ins=[qcpt], sem_key="D:l3qcp")
            a0z = ctx.view("l3_a0z", [128, L + T], F32, 118848)
            p.op("gpsimd", lambda g_: g_.memset(a0z[:, 0:L], 0.0), outs=[a0z], ins=[a_all])
            p.op("vector", lambda v, ag=ag: v.tensor_tensor(a0z[:, L:L + T], ag, ams, op=OP.subtract),
                 outs=[a0z], ins=[a_all, ams])
            Zr = ctx.view("l3_Zr", [128, T], F32, 94208)
            arev = ag[:, T - 1:0:-1]
            p.op("gpsimd", lambda g_: g_.memset(d0[:, 0:1], 0.0), outs=[d0], ins=[a_all])
            p.op("vector", lambda v, arev=arev: v.tensor_tensor(d1[:, 1:T], arev, ms[:, 1:T], op=OP.mult),
                 outs=[d1], ins=[a_all, ms])
            p.op("vector", lambda v, arev=arev: v.tensor_tensor(d0[:, 1:T], arev, d1[:, 1:T], op=OP.subtract),
                 outs=[d0], ins=[a_all, d1])
            p.op("vector", lambda v: v.tensor_copy(d1[:, 0:T], ms[:, 0:T]), outs=[d1], ins=[ms])
            p.op("vector", lambda v: v.tensor_tensor_scan(Zr, d0, d1, 0.0, op0=OP.mult, op1=OP.add),
                 outs=[Zr], ins=[d0, d1])
            for i in range(NT):
                znat = ctx.sbuf("l3_znat", [128, 128], F32)
                zstop = None if i == NT - 1 else T - 1 - i * 128 - 128
                p.op("vector", lambda v, i=i, zstop=zstop: v.tensor_copy(
                    znat, Zr[:, T - 1 - i * 128: zstop:-1]), outs=[znat], ins=[Zr])
                p.op("tensor", lambda t, i=i: t.transpose(pB[:, 0:128], znat, ctx.idn),
                     outs=[pB], ins=[znat, ctx.idn])
                zt_t = ctx.sbuf("l3_ztt", [128, 128], F32)
                p.op("scalar", lambda s, i=i: s.copy(zt_t, pB[:, 0:128]), outs=[zt_t], ins=[pB])
                kzt = ctx.sbuf("l3_kzt", [128, 128], BF16)
                p.op("vector", lambda v, i=i: v.tensor_tensor(kzt, zt_t, k_tm2[:, i, :], op=OP.mult),
                     outs=[kzt], ins=[zt_t, k_tm2])
                p.dma("gpsimd", lambda e, i=i, g=g, kzt=kzt: e.dma_start(
                    out=scr["kzd"][i * 128:(i + 1) * 128, g, :], in_=kzt), outs=[scr["kzd"]], ins=[kzt],
                    sem_key="D:l3kz")
            vsf = [ctx.view(f"l3_vsf{hh}", [128, L + 2048], BF16, 102400 + hh * 4128) for hh in range(2)]
            for hh in range(2):
                p.op("gpsimd", lambda g_, hh=hh: g_.memset(vsf[hh][:, 0:L], 0.0), outs=[vsf[hh]])
                for tc in range(T // 512):
                    p.op("tensor", lambda t, g=g, hh=hh, tc=tc: t.matmul(
                        pA[:, 0:512], Vc_sb[:, 2 * g + hh, :], v_lat[:, tc * 512:(tc + 1) * 512],
                        start=True, stop=True), outs=[pA], ins=[Vc_sb, v_lat])
                    p.op("vector", lambda v, hh=hh, tc=tc: v.tensor_tensor(
                        vsf[hh][:, L + tc * 512:L + (tc + 1) * 512], pA[:, 0:512],
                        rknbc[:, tc * 512:(tc + 1) * 512], op=OP.mult),
                        outs=[vsf[hh]], ins=[pA, rknbc])
            Rbuf = [ctx.view(f"l3_R{x}", [128, 512], F32, 127072 + x * 2048) for x in range(2)]
            prod1 = ctx.sbuf("l3_prod1", [128, 512], BF16)
            prod = ctx.sbuf("l3_prod", [128, 512], BF16)
            s2 = ctx.sbuf("l3_s2", [2, 512], BF16)
            q2tc = ctx.sbuf("l3_q2tc", [128, 512], BF16)
            oacc = [ctx.view(f"l3_oacc{hh}", [128, 512], F32, 131168 + hh * 2048) for hh in range(2)]
            pS, pP = ctx.bk(2), ctx.bk(3)
            for tc in range(T // 512):
                t0 = tc * 512
                for d in range(L):
                    if d == 0:
                        for hh in range(2):
                            p.op("tensor", lambda t, g=g, hh=hh, t0=t0: t.matmul(
                                pA[hh * 64:(hh + 1) * 64, 0:512], Qc_sb[:, 2 * g + hh, :],
                                q_lat[:, t0:t0 + 512], start=True, stop=True), outs=[pA], ins=[Qc_sb, q_lat])
                        p.op("vector", lambda v, t0=t0: v.tensor_copy(q2tc, pA[:, 0:512]),
                             outs=[q2tc], ins=[pA])
                        p.op("vector", lambda v, t0=t0: v.tensor_tensor(
                            prod, q2tc, kk[:, L + t0:L + t0 + 512], op=OP.mult), outs=[prod], ins=[q2tc, kk])
                    else:
                        Rc, Rp = Rbuf[d % 2], Rbuf[(d + 1) % 2]
                        if d == 1:
                            p.op("vector", lambda v, t0=t0, Rc=Rc: v.tensor_copy(
                                Rc, a0z[:, L + t0:L + t0 + 512]), outs=[Rc], ins=[a0z])
                        else:
                            p.op("vector", lambda v, d=d, t0=t0, Rc=Rc, Rp=Rp: v.tensor_tensor(
                                Rc, Rp, a0z[:, L + t0 - d + 1:L + t0 - d + 1 + 512], op=OP.mult),
                                outs=[Rc], ins=[Rp, a0z])
                        p.op("gpsimd", lambda g_, d=d, t0=t0: g_.tensor_tensor(
                            prod1, q2tc, kk[:, L + t0 - d:L + t0 - d + 512], op=OP.mult),
                            outs=[prod1], ins=[q2tc, kk])
                        p.op("vector", lambda v, d=d, Rc=Rc: v.tensor_tensor(prod, prod1, Rc, op=OP.mult),
                             outs=[prod], ins=[prod1, Rc])
                    p.op("tensor", lambda t, d=d: t.matmul(pS[0:2, 0:512], bo2, prod, start=True, stop=True),
                         outs=[pS], ins=[bo2, prod])
                    p.op("scalar", lambda s, d=d: s.copy(s2, pS[0:2, 0:512]), outs=[s2], ins=[pS])
                    for hh in range(2):
                        p.op("tensor", lambda t, hh=hh, d=d: t.matmul(pP[:, 0:512], sel[hh], s2,
                                                                      start=True, stop=True),
                             outs=[pP], ins=[sel[hh], s2])
                        if d == 0:
                            p.op("vector", lambda v, hh=hh, t0=t0: v.tensor_tensor(
                                oacc[hh], vsf[hh][:, L + t0:L + t0 + 512], pP[:, 0:512], op=OP.mult),
                                outs=[oacc[hh]], ins=[vsf[hh], pP])
                        else:
                            tmp = ctx.sbuf("l3_otmp", [128, 512], F32)
                            p.op("vector", lambda v, hh=hh, d=d, t0=t0, tmp=tmp: v.tensor_tensor(
                                tmp, vsf[hh][:, L + t0 - d:L + t0 - d + 512], pP[:, 0:512], op=OP.mult),
                                outs=[tmp], ins=[vsf[hh], pP])
                            p.op("gpsimd", lambda g_, hh=hh, tmp=tmp: g_.tensor_tensor(
                                oacc[hh], oacc[hh], tmp, op=OP.add), outs=[oacc[hh]], ins=[oacc[hh], tmp])
                for hh in range(2):
                    ob = ctx.sbuf("l3_prod", [128, 512], BF16)
                    p.op("vector", lambda v, hh=hh: v.tensor_copy(ob, oacc[hh]), outs=[ob], ins=[oacc[hh]])
                    p.dma("gpsimd", lambda e, g=g, hh=hh, tc=tc, ob=ob: e.dma_start(
                        out=scr["oin"][2 * g + hh, :, tc * 512:(tc + 1) * 512], in_=ob),
                        outs=[scr["oin"]], ins=[ob], sem_key="D:l3oi")

        if PH < 3:
            continue
        S_g = [ctx.view(f"l3_S{g}", [128, DV], F32, 126976 + g * 512) for g in range(NG)]
        for g in range(NG):
            p.op("gpsimd", lambda g_, g2=g: g_.memset(S_g[g2], 0.0), outs=[S_g[g]])
        KZT = ctx.view("l3_KZT", [8, 16, 512], BF16, 77824)
        VST = ctx.view("l3_VST", [8, 16, 1024], BF16, 94208)
        QCPs = ctx.sbuf("l3_QCPs", [128, NG, 128], F32)
        QCPb = ctx.sbuf("l3_QCPb", [128, NG * 128], BF16)
        pDS = ctx.bk(4)
        poA, poB = ctx.bk(5), ctx.bk(6)
        for sp in range(T // 128):
            p.dma("sync", lambda e, sp=sp: e.dma_start(
                out=KZT, in_=scr["kzd"].rearrange("(n l) g c -> l n (g c)", l=L)[:, sp * 16:(sp + 1) * 16, :]),
                outs=[KZT], ins=[scr["kzd"]], sem_key="D:l3kzt")
            p.dma("sync", lambda e, sp=sp: e.dma_start(
                out=VST, in_=scr["vsd"].rearrange("(n l) h v -> l n (h v)", l=L)[:, sp * 16:(sp + 1) * 16, :]),
                outs=[VST], ins=[scr["vsd"]], sem_key="D:l3vst")
            p.dma("sync", lambda e, sp=sp: e.dma_start(
                out=QCPs, in_=scr["qcpd"].rearrange("g p t -> p g t")[:, :, sp * 128:(sp + 1) * 128]),
                outs=[QCPs], ins=[scr["qcpd"]], sem_key="D:l3qcps")
            p.op("vector", lambda v: v.tensor_copy(QCPb.rearrange("p (g t) -> p g t", g=NG), QCPs), outs=[QCPb], ins=[QCPs])
            L3I = int(os.environ.get("L3I", "99"))
            for ii in range(16):
                i = sp * 16 + ii
                if L3I < 2:
                    continue
                for g in range(NG):
                    Sb = ctx.sbuf(f"l3_Sb{g}", [128, DV], BF16)
                    p.op("scalar", lambda sc, g=g, Sb=Sb: sc.copy(Sb, S_g[g]), outs=[Sb], ins=[S_g[g]])
                    for hh in range(2):
                        if L3I < 4:
                            continue
                        h = 2 * g + hh
                        po = poA if hh == 0 else poB
                        p.op("tensor", lambda t, g=g, hh=hh, i=i, ii=ii, h=h, po=po, Sb=Sb: t.matmul(
                            po[:, g * 128 + ii * 8:g * 128 + ii * 8 + 8],
                            Sb[hh * 64:(hh + 1) * 64, :],
                            QCPb[hh * 64:(hh + 1) * 64, g * 128 + ii * L:g * 128 + (ii + 1) * L],
                            start=True, stop=True), outs=[po], ins=[Sb, QCPb])
                    for hh in range(2):
                        if L3I < 3:
                            continue
                        h = 2 * g + hh
                        p.op("tensor", lambda t, g=g, hh=hh, ii=ii, h=h: t.matmul(
                            pDS[hh * 64:(hh + 1) * 64, g * 128:(g + 1) * 128],
                            KZT[:, ii, g * 128 + hh * 64:g * 128 + (hh + 1) * 64],
                            VST[:, ii, h * 128:(h + 1) * 128], start=True, stop=True),
                            outs=[pDS], ins=[KZT, VST])
                for g in (range(NG) if L3I >= 3 else []):
                    p.op("vector", lambda v, g=g, i=i: v.scalar_tensor_tensor(
                        S_g[g], S_g[g], D_cols[:, g, i:i + 1], pDS[:, g * 128:(g + 1) * 128],
                        op0=OP.mult, op1=OP.add), outs=[S_g[g]], ins=[S_g[g], D_cols, pDS])
            for h in (range(NHEAD) if L3I >= 5 else []):
                g, hh = h // 2, h % 2
                po = poA if hh == 0 else poB
                ob2 = ctx.sbuf("l3_ob2", [128, 128], BF16)
                p.op("vector", lambda v, h=h, po=po, g=g: v.tensor_copy(
                    ob2, po[:, g * 128:g * 128 + 128]), outs=[ob2], ins=[po])
                p.dma("gpsimd", lambda e, h=h, sp=sp, ob2=ob2: e.dma_start(
                    out=scr["oit"][h, :, sp * 128:(sp + 1) * 128], in_=ob2), outs=[scr["oit"]], ins=[ob2],
                    sem_key="D:l3ot")

        if PH < 4:
            continue
        WoT3 = ctx.load_transposed("l3_WoT3", W["Wo"], D, NHEAD * DV, stage_off=94208, out_off=32768)
        os_sl = ctx.view("l3_osl", [128, NHEAD, 512], BF16, 77824)
        os_s2 = ctx.view("l3_os2", [128, NHEAD, 512], BF16, 86016)
        s_t = ctx.sbuf("ln_sq", [128, D], F32)
        for tc in range(T // 512):
            for h in range(NHEAD):
                p.dma("sync", lambda e, h=h, tc=tc: e.dma_start(
                    out=os_sl[:, h, :], in_=scr["oin"][h, :, tc * 512:(tc + 1) * 512]),
                    outs=[os_sl], ins=[scr["oin"]], sem_key="D:l3or1")
                p.dma("sync", lambda e, h=h, tc=tc: e.dma_start(
                    out=os_s2[:, h, :], in_=scr["oit"][h, :, tc * 512:(tc + 1) * 512]),
                    outs=[os_s2], ins=[scr["oit"]], sem_key="D:l3or2")
            for sub in range(4):
                i = tc * 4 + sub
                xnt = ctx.sbuf("pn_x0", [128, D], F32)
                p.dma("sync", lambda e, b=b, i=i, xnt=xnt: e.dma_start(
                    out=xnt, in_=scr["xnf"][b, i * 128:(i + 1) * 128, :]), outs=[xnt], ins=[scr["xnf"]],
                    sem_key="D:l3xn")
                for half in range(2):
                    for h in range(NHEAD):
                        p.op("tensor", lambda t, h=h, sub=sub, half=half: t.matmul(
                            pA[:, 0:512], os_sl[:, h, sub * 128:(sub + 1) * 128],
                            WoT3[:, h, half * 512:(half + 1) * 512], start=(h == 0), stop=False),
                            outs=[pA], ins=[os_sl, WoT3])
                    for h in range(NHEAD):
                        p.op("tensor", lambda t, h=h, sub=sub, half=half: t.matmul(
                            pA[:, 0:512], os_s2[:, h, sub * 128:(sub + 1) * 128],
                            WoT3[:, h, half * 512:(half + 1) * 512], start=False, stop=(h == NHEAD - 1)),
                            outs=[pA], ins=[os_s2, WoT3])
                    p.op("vector", lambda v, half=half: v.tensor_tensor(
                        s_t[:, half * 512:(half + 1) * 512], pA[:, 0:512],
                        xnt[:, half * 512:(half + 1) * 512], op=OP.add), outs=[s_t], ins=[pA, xnt])
                p.dma("gpsimd", lambda e, i=i, b=b: e.dma_start(
                    out=scr["s"][b, i * 128:(i + 1) * 128, :], in_=s_t), outs=[scr["s"]], ins=[s_t],
                    sem_key="D:l3s")

        if PH < 5:
            continue
        WrT3 = ctx.view("l3_WrT3", [128, DC, D], BF16, 49152)
        WkT3 = ctx.view("l3_WkT3", [128, DC, D], BF16, 65536)
        WvT3 = ctx.view("l3_WvT3", [128, DC, D], BF16, 81920)
        for dst, wsrc in ((WrT3, "W_r"), (WkT3, "W_k"), (WvT3, "W_v")):
            stg = ctx.view("l3_fstg", [128, DC, D], F32, 98304)
            p.dma("gpsimd", lambda e, stg=stg, wsrc=wsrc: e.dma_start(
                out=stg, in_=W[wsrc].rearrange("(a p) m -> p a m", p=128)), outs=[stg], ins=[W[wsrc]],
                sem_key="D:l3fw")
            for r in range(DC):
                for c in range(DC):
                    p.op("tensor", lambda tt, r=r, c=c, stg=stg: tt.transpose(
                        pB[:, 0:128], stg[:, r, c * 128:(c + 1) * 128], ctx.idn), outs=[pB], ins=[stg, ctx.idn])
                    p.op("scalar", lambda s, r=r, c=c, dst=dst: s.copy(
                        dst[:, c, r * 128:(r + 1) * 128], pB[:, 0:128]), outs=[dst], ins=[pB])
        xk = ctx.view("l3_xk", [128, DC, T], BF16, 98304)
        for mcol, dst in ((muk, xk),):
            for c in range(DC):
                omm = ctx.sbuf("l3_omm", [128, 1], F32)
                p.op("vector", lambda v, c=c, mcol=mcol, omm=omm: v.tensor_scalar(
                    omm, mcol[:, c:c + 1], 1.0, -1.0, op0=OP.subtract, op1=OP.mult),
                    outs=[omm], ins=[mcol])
                for tc in range(T // 512):
                    t1 = ctx.sbuf("l3_otmp", [128, 512], F32)
                    lo = tc * 512
                    p.op("vector", lambda v, c=c, lo=lo, mcol=mcol, t1=t1: v.tensor_scalar(
                        t1, xn_fm[:, c, lo:lo + 512], mcol[:, c:c + 1], None, op0=OP.mult, op1=OP.bypass),
                        outs=[t1], ins=[xn_fm, mcol])
                    if tc == 0:
                        p.op("vector", lambda v, c=c, dst=dst, t1=t1, omm=omm: v.scalar_tensor_tensor(
                            dst[:, c, 1:512], xn_fm[:, c, 0:511], omm, t1[:, 1:512], op0=OP.mult, op1=OP.add),
                            outs=[dst], ins=[xn_fm, omm, t1])
                        p.op("vector", lambda v, c=c, dst=dst, t1=t1: v.tensor_copy(dst[:, c, 0:1], t1[:, 0:1]),
                             outs=[dst], ins=[t1])
                    else:
                        p.op("vector", lambda v, c=c, lo=lo, dst=dst, t1=t1, omm=omm: v.scalar_tensor_tensor(
                            dst[:, c, lo:lo + 512], xn_fm[:, c, lo - 1:lo + 511], omm, t1,
                            op0=OP.mult, op1=OP.add), outs=[dst], ins=[xn_fm, omm, t1])
        for i in range(NT):
            for half in range(2):
                for c in range(DC):
                    p.op("tensor", lambda t, i=i, half=half, c=c: t.matmul(
                        pA[:, 0:512], xk[:, c, i * 128:(i + 1) * 128], WkT3[:, c, half * 512:(half + 1) * 512],
                        start=(c == 0), stop=(c == DC - 1)), outs=[pA], ins=[xk, WkT3])
                kr = ctx.sbuf("l3_otmp", [128, 512], F32)
                p.op("vector", lambda v, kr=kr: v.tensor_scalar(kr, pA[:, 0:512], 0.0, None,
                                                                op0=OP.max, op1=OP.bypass), outs=[kr], ins=[pA])
                ka = ctx.sbuf("l3_ka", [128, 512], BF16)
                p.op("scalar", lambda s, ka=ka, kr=kr: s.activation(ka, kr, AF.Square), outs=[ka], ins=[kr])
                p.dma("gpsimd", lambda e, i=i, half=half, ka=ka: e.dma_start(
                    out=scr["kad"][i * 128:(i + 1) * 128, half * 512:(half + 1) * 512], in_=ka),
                    outs=[scr["kad"]], ins=[ka], sem_key="D:l3ka")
        xr = ctx.view("l3_xr", [128, DC, T], BF16, 98304)
        for mcol, dst in ((mur, xr),):
            for c in range(DC):
                omm = ctx.sbuf("l3_omm", [128, 1], F32)
                p.op("vector", lambda v, c=c, mcol=mcol, omm=omm: v.tensor_scalar(
                    omm, mcol[:, c:c + 1], 1.0, -1.0, op0=OP.subtract, op1=OP.mult),
                    outs=[omm], ins=[mcol])
                for tc in range(T // 512):
                    t1 = ctx.sbuf("l3_otmp", [128, 512], F32)
                    lo = tc * 512
                    p.op("vector", lambda v, c=c, lo=lo, mcol=mcol, t1=t1: v.tensor_scalar(
                        t1, xn_fm[:, c, lo:lo + 512], mcol[:, c:c + 1], None, op0=OP.mult, op1=OP.bypass),
                        outs=[t1], ins=[xn_fm, mcol])
                    if tc == 0:
                        p.op("vector", lambda v, c=c, dst=dst, t1=t1, omm=omm: v.scalar_tensor_tensor(
                            dst[:, c, 1:512], xn_fm[:, c, 0:511], omm, t1[:, 1:512], op0=OP.mult, op1=OP.add),
                            outs=[dst], ins=[xn_fm, omm, t1])
                        p.op("vector", lambda v, c=c, dst=dst, t1=t1: v.tensor_copy(dst[:, c, 0:1], t1[:, 0:1]),
                             outs=[dst], ins=[t1])
                    else:
                        p.op("vector", lambda v, c=c, lo=lo, dst=dst, t1=t1, omm=omm: v.scalar_tensor_tensor(
                            dst[:, c, lo:lo + 512], xn_fm[:, c, lo - 1:lo + 511], omm, t1,
                            op0=OP.mult, op1=OP.add), outs=[dst], ins=[xn_fm, omm, t1])
        kaf = ctx.view("l3_kaf", [128, DC, T], BF16, 0)
        for c in range(DC):
            p.dma("sync", lambda e, c=c: e.dma_start_transpose(
                kaf[:, c, :], scr["kad"][:, c * 128:(c + 1) * 128]), outs=[kaf], ins=[scr["kad"]],
                sem_key="D:l3kaf")
        for i in range(NT):
            rt_ = ctx.sbuf("pn_x0", [128, D], F32)
            o2 = ctx.sbuf("l3_ep_o2", [128, D], F32)
            for half in range(2):
                for c in range(DC):
                    p.op("tensor", lambda t, i=i, half=half, c=c: t.matmul(
                        pA[:, 0:512], xr[:, c, i * 128:(i + 1) * 128], WrT3[:, c, half * 512:(half + 1) * 512],
                        start=(c == 0), stop=(c == DC - 1)), outs=[pA], ins=[xr, WrT3])
                p.op("scalar", lambda s, half=half, rt_=rt_: s.activation(
                    rt_[:, half * 512:(half + 1) * 512], pA[:, 0:512], AF.Sigmoid), outs=[rt_], ins=[pA])
                for c in range(DC):
                    p.op("tensor", lambda t, i=i, half=half, c=c: t.matmul(
                        pA[:, 0:512], kaf[:, c, i * 128:(i + 1) * 128], WvT3[:, c, half * 512:(half + 1) * 512],
                        start=(c == 0), stop=(c == DC - 1)), outs=[pA], ins=[kaf, WvT3])
                p.op("vector", lambda v, half=half, o2=o2: v.tensor_copy(
                    o2[:, half * 512:(half + 1) * 512], pA[:, 0:512]), outs=[o2], ins=[pA])
            p.op("vector", lambda v, rt_=rt_, o2=o2: v.tensor_tensor(o2, o2, rt_, op=OP.mult),
                 outs=[o2], ins=[o2, rt_])
            xt = ctx.sbuf(f"pn_x{i % ctx.pnb}", [128, D], F32)
            st_ = ctx.sbuf(f"pn_xn{i % ctx.pnb}", [128, D], F32)
            gs = ctx.sbuf("ln_sq", [128, D], F32)
            p.dma("sync", lambda e, b=b, i=i, xt=xt: e.dma_start(
                out=xt, in_=x_in[b, i * 128:(i + 1) * 128, :]), outs=[xt], ins=[x_in])
            p.dma("sync", lambda e, b=b, i=i, st_=st_: e.dma_start(
                out=st_, in_=scr["s"][b, i * 128:(i + 1) * 128, :]), outs=[st_], ins=[scr["s"]],
                sem_key="D:l3s_rd")
            p.op("scalar", lambda s, o2=o2, gs=gs: s.activation(gs, o2, AF.Silu), outs=[gs], ins=[o2])
            p.op("vector", lambda v, gs=gs, st_=st_: v.tensor_tensor(gs, gs, st_, op=OP.mult),
                 outs=[gs], ins=[gs, st_])
            p.op("vector", lambda v, gs=gs, xt=xt: v.tensor_tensor(gs, gs, xt, op=OP.add),
                 outs=[gs], ins=[gs, xt])
            p.dma("gpsimd", lambda e, b=b, i=i, gs=gs: e.dma_start(
                out=y_out[b, i * 128:(i + 1) * 128, :], in_=gs), outs=[y_out], ins=[gs], sem_key="D:y")


from concourse.bass_utils import run_bass_kernel_spmd

_CACHE = {}

B, T, D = 2, 2048, 1024


def _mk_layer(lid):
    key = ("layer", lid)
    if key in _CACHE:
        return _CACHE[key]
    nc = bass.Bass("TRN2", target_bir_lowering=False)
    dx = nc.dram_tensor("x", [B, T, D], F32, kind="ExternalInput").ap()
    dy = nc.dram_tensor("y", [B, T, D], F32, kind="ExternalOutput").ap()
    scr = {
        "xnb": nc.dram_tensor("scr_xnb", [B, T, D], BF16, kind="Internal").ap(),
        "s": nc.dram_tensor("scr_s", [B, T, D], F32, kind="Internal").ap(),
    }
    p = Prog(nc)
    if lid == 0:
        W = {k: nc.dram_tensor(k, shp, F32, kind="ExternalInput").ap() for k, shp in [
            ("Wq", [2, 128, D]), ("Wk", [128, D]), ("Wv", [128, D]), ("Wo", [D, 256]),
            ("sg", [D, 2816]), ("su", [D, 2816]), ("sd", [2816, D]),
            ("rg", [7, D, 2816]), ("ru", [7, D, 2816]), ("rd", [7, 2816, D]),
            ("Wr", [D, 7]), ("rb", [7])]}
        C = {k: nc.dram_tensor("c_" + k, [T, 64], F32, kind="ExternalInput").ap()
             for k in ("cosk", "sink", "cosq", "sinq")}
        scr["xnb_lo"] = nc.dram_tensor("scr_xnb_lo", [B, T, D], BF16, kind="Internal").ap()
        scr["h"] = nc.dram_tensor("scr_h", [128, 22, T // 2], BF16, kind="Internal").ap()
        ctx = Ctx(nc, p, B=B, T=T, D=D, arena_bytes=116000)
        build_l0(ctx, dx, dy, W, C, scr)
    elif lid == 1:
        W = {k: nc.dram_tensor(k, shp, F32, kind="ExternalInput").ap() for k, shp in [
            ("w_qkv", [D, 3072]), ("w_out", [1024, D]), ("routing_token", [D]),
            ("gamma_light", [D]), ("w1_light", [D, 512]), ("b1_light", [512]),
            ("w2_light", [512, D]), ("b2_light", [D]), ("gamma_heavy", [D]),
            ("w1_heavy", [D, 4096]), ("b1_heavy", [4096]), ("w2_heavy", [4096, D]),
            ("b2_heavy", [D])]}
        ctx = Ctx(nc, p, B=B, T=T, D=D, arena_bytes=104000)
        build_l1(ctx, dx, dy, W, scr)
    elif lid == 2:
        NH, DH = 16, 64
        W = {k: nc.dram_tensor(k, shp, F32, kind="ExternalInput").ap() for k, shp in [
            ("W_Q", [NH, D, DH]), ("b_Q", [NH, DH]), ("W_K", [NH, D, DH]), ("b_K", [NH, DH]),
            ("W_V", [NH, D, DH]), ("b_V", [NH, DH]), ("W_O", [NH, DH, D]), ("b_O", [D]),
            ("sim_matrix", [D, NH]), ("gates", [NH]),
            ("q_proj", [NH, D, DH]), ("k_proj", [NH, D, DH]), ("v_proj", [NH, D, DH]),
            ("o_proj", [NH, DH, D])]}
        scr["xnb_lo"] = nc.dram_tensor("scr_xnb_lo", [B, T, D], BF16, kind="Internal").ap()
        ctx = Ctx(nc, p, B=B, T=T, D=D, arena_bytes=116000)
        build_l2(ctx, dx, dy, W, scr)
    else:
        H, C_, DV = 8, 64, 128
        W = {k: nc.dram_tensor(k, shp, F32, kind="ExternalInput").ap() for k, shp in [
            ("Wq", [C_, D]), ("Wk", [C_, D]), ("Wv", [C_, D]), ("Qc", [H, C_, C_]),
            ("Vc", [H, C_, DV]), ("Wa_w", [512, D]), ("Wa_b", [512]), ("Wo", [D, H * DV]),
            ("gamma", [D]), ("beta", [D]), ("W_r", [D, D]), ("W_k", [D, D]), ("W_v", [D, D]),
            ("mu_r", [D]), ("mu_k", [D])]}
        scr.update({
            "xnf": nc.dram_tensor("scr_xnf", [B, T, D], F32, kind="Internal").ap(),
            "vsd": nc.dram_tensor("scr_vsd", [T, H, DV], BF16, kind="Internal").ap(),
            "kzd": nc.dram_tensor("scr_kzd", [T, 4, 128], BF16, kind="Internal").ap(),
            "oin": nc.dram_tensor("scr_oin", [H, DV, T], BF16, kind="Internal").ap(),
            "oit": nc.dram_tensor("scr_oit", [H, DV, T], BF16, kind="Internal").ap(),
            "kad": nc.dram_tensor("scr_kad", [T, D], BF16, kind="Internal").ap(),
            "qcpd": nc.dram_tensor("scr_qcpd", [4, 128, T], F32, kind="Internal").ap(),
        })
        ctx = Ctx(nc, p, B=B, T=T, D=D, arena_bytes=136000)
        build_l3(ctx, dx, dy, W, scr)
    p.build()
    _CACHE[key] = nc
    return nc


def kernel(**inputs):
    n = 8
    x = np.ascontiguousarray(inputs["L0_attn_x"], dtype=np.float32)
    shards = [x[2 * c:2 * c + 2] for c in range(n)]
    consts = l0_consts(T)
    wmaps = [
        {"Wq": "L0_attn_Wq", "Wk": "L0_attn_Wk", "Wv": "L0_attn_Wv", "Wo": "L0_attn_Wo",
         "sg": "L0_ffn_shared_gate", "su": "L0_ffn_shared_up", "sd": "L0_ffn_shared_down",
         "rg": "L0_ffn_routed_gate", "ru": "L0_ffn_routed_up", "rd": "L0_ffn_routed_down",
         "Wr": "L0_ffn_Wr", "rb": "L0_ffn_routing_bias"},
        {"w_qkv": "L1_attn_w_qkv", "w_out": "L1_attn_w_out",
         "routing_token": "L1_ffn_routing_token", "gamma_light": "L1_ffn_gamma_light",
         "w1_light": "L1_ffn_w1_light", "b1_light": "L1_ffn_b1_light",
         "w2_light": "L1_ffn_w2_light", "b2_light": "L1_ffn_b2_light",
         "gamma_heavy": "L1_ffn_gamma_heavy", "w1_heavy": "L1_ffn_w1_heavy",
         "b1_heavy": "L1_ffn_b1_heavy", "w2_heavy": "L1_ffn_w2_heavy",
         "b2_heavy": "L1_ffn_b2_heavy"},
        {"W_Q": "L2_attn_W_Q", "b_Q": "L2_attn_b_Q", "W_K": "L2_attn_W_K", "b_K": "L2_attn_b_K",
         "W_V": "L2_attn_W_V", "b_V": "L2_attn_b_V", "W_O": "L2_attn_W_O", "b_O": "L2_attn_b_O",
         "sim_matrix": "L2_ffn_sim_matrix", "gates": "L2_ffn_gates",
         "q_proj": "L2_ffn_q_proj", "k_proj": "L2_ffn_k_proj", "v_proj": "L2_ffn_v_proj",
         "o_proj": "L2_ffn_o_proj"},
        {"Wq": "L3_attn_Wq", "Wk": "L3_attn_Wk", "Wv": "L3_attn_Wv", "Qc": "L3_attn_Qc",
         "Vc": "L3_attn_Vc", "Wa_w": "L3_attn_Wa_w", "Wa_b": "L3_attn_Wa_b",
         "Wo": "L3_attn_Wo", "gamma": "L3_attn_gamma", "beta": "L3_attn_beta",
         "W_r": "L3_ffn_W_r", "W_k": "L3_ffn_W_k", "W_v": "L3_ffn_W_v",
         "mu_r": "L3_ffn_mu_r", "mu_k": "L3_ffn_mu_k"},
    ]
    cur = shards
    for lid in range(4):
        nc = _mk_layer(lid)
        wvals = {k: np.ascontiguousarray(inputs[v], dtype=np.float32) for k, v in wmaps[lid].items()}
        in_maps = []
        for c in range(n):
            m = {"x": np.ascontiguousarray(cur[c], dtype=np.float32)}
            m.update(wvals)
            if lid == 0:
                m.update({"c_" + k: v for k, v in consts.items()})
            in_maps.append(m)
        res = run_bass_kernel_spmd(nc, in_maps, core_ids=list(range(n)))
        cur = [res.results[c]["y"] for c in range(n)]
    return np.concatenate([np.asarray(c, dtype=np.float32) for c in cur], axis=0)


# version 08
# speedup vs baseline: 1.0918x; 1.0918x over previous
import numpy as np
"""Auto-sync layer over raw Bass: records ops with read/write sets, inserts
semaphore edges (vector clocks for dedup), plays back per engine in one Block.

Rules encoded:
- every compute op incs its engine sem by 1 (value = per-engine op index)
- every dma incs a per-destination-tensor sem by 16 (value = 16 * count)
- RAW/WAW/WAR edges all get waits; same-engine edges too (except TensorE,
  whose queue is FIFO with pc-ordered completion)
- waits deduped via per-engine vector clocks with transitive closure through
  producer snapshots
- finalize(): sync engine waits every dma sem to its final total (timing contract)
"""
import concourse.bass as bass
import concourse.mybir as mybir
from concourse.alu_op_type import AluOpType as OP

F32 = mybir.dt.float32
BF16 = mybir.dt.bfloat16
AF = mybir.ActivationFunctionType
AX = mybir.AxisListType

ENGINES = ("tensor", "vector", "scalar", "gpsimd", "sync")


class Prog:
    def __init__(self, nc, same_engine_sync=True):
        self.nc = nc
        self.same_engine_sync = same_engine_sync
        self.seq = {e: [] for e in ENGINES}
        self.eng_count = {e: 0 for e in ENGINES}
        self.clock = {e: {} for e in ENGINES}
        self.bufs = {}
        self.alias = {}
        self.dma_count = {}
        self.snapshots = {}
        self.n_waits = 0

    def _buf(self, ap):
        name = ap.tensor.name
        name = self.alias.get(name, name)
        b = self.bufs.get(name)
        if b is None:
            b = {"w": None, "r": []}
            self.bufs[name] = b
        return b

    def register_view(self, name, region, lo, hi):
        self.alias.setdefault("__regions__", {})
        regs = self.alias["__regions__"]
        regs.setdefault(region, {})[name] = (lo, hi)

    def _targets(self, ap):
        name = ap.tensor.name
        out = [self._buf(ap)]
        regs = self.alias.get("__regions__", {})
        for region, members in regs.items():
            if name in members:
                lo, hi = members[name]
                for other, (lo2, hi2) in members.items():
                    if other != name and lo < hi2 and lo2 < hi:
                        b = self.bufs.get(other)
                        if b is None:
                            b = {"w": None, "r": []}
                            self.bufs[other] = b
                        out.append(b)
        return out

    def _wait(self, eng, sem_key, val):
        c = self.clock[eng]
        if c.get(sem_key, 0) >= val:
            return
        self.seq[eng].append(("wait", sem_key, val))
        self.n_waits += 1
        if c.get(sem_key, 0) < val:
            c[sem_key] = val
        snap = self.snapshots.get((sem_key, val))
        if snap:
            for k, v in snap.items():
                if c.get(k, 0) < v:
                    c[k] = v

    def _skip_same(self, eng, sem_key):
        if sem_key != "E:" + eng:
            return False
        if eng == "tensor":
            return True
        return not self.same_engine_sync

    def _read_dep(self, eng, ap):
        for b in self._targets(ap):
            w = b["w"]
            if w is None:
                continue
            sem, val = w
            if self._skip_same(eng, sem):
                continue
            self._wait(eng, sem, val)

    def _write_dep(self, eng, ap):
        for b in self._targets(ap):
            if b["w"] is not None:
                sem, val = b["w"]
                if not self._skip_same(eng, sem):
                    self._wait(eng, sem, val)
            for sem, val in b["r"]:
                if not self._skip_same(eng, sem):
                    self._wait(eng, sem, val)

    def _record_effects(self, ev, outs, ins):
        for ap in ins:
            self._buf(ap)["r"].append(ev)
        for ap in outs:
            b = self._buf(ap)
            b["w"] = ev
            b["r"] = []

    def op(self, eng, emit, outs=(), ins=()):
        for ap in ins:
            self._read_dep(eng, ap)
        for ap in outs:
            self._write_dep(eng, ap)
        self.eng_count[eng] += 1
        v = self.eng_count[eng]
        key = "E:" + eng
        self.seq[eng].append(("op", emit, key, v))
        ev = (key, v)
        self._record_effects(ev, outs, ins)
        snap = dict(self.clock[eng])
        snap[key] = v
        self.snapshots[ev] = snap
        return ev

    def dma(self, queue, emit, outs=(), ins=(), sem_key=None):
        assert queue in ("sync", "gpsimd", "scalar")
        for ap in ins:
            self._read_dep(queue, ap)
        for ap in outs:
            self._write_dep(queue, ap)
        sem_key = "D:" + outs[0].tensor.name
        cnt = self.dma_count.get(sem_key, 0) + 1
        self.dma_count[sem_key] = cnt
        val = 16 * cnt
        self.seq[queue].append(("dma", emit, sem_key, val))
        ev = (sem_key, val)
        self._record_effects(ev, outs, ins)
        snap = dict(self.clock[queue])
        snap[sem_key] = val
        self.snapshots[ev] = snap
        return ev

    def barrier_read(self, eng, ap):
        self._read_dep(eng, ap)

    def build(self):
        for key, cnt in self.dma_count.items():
            self._wait("sync", key, 16 * cnt)
        nc = self.nc
        sems = {}
        for name in ["E:" + e for e in ENGINES] + list(self.dma_count.keys()):
            sems[name] = nc.alloc_semaphore(name=name.replace(":", "_").replace("/", "_"))
        nsems = len(sems)
        assert nsems <= 100, f"too many semaphores: {nsems}"
        with nc.Block() as block:
            for e in ENGINES:
                seq = self.seq[e]
                if not seq:
                    continue

                def mk_body(seq=seq):
                    def body(engobj):
                        for item in seq:
                            if item[0] == "wait":
                                engobj.wait_ge(sems[item[1]], item[2])
                            else:
                                kind, emit, key, val = item
                                inst = emit(engobj)
                                inst.then_inc(sems[key], 16 if kind == "dma" else 1)
                    return body

                getattr(block, e)(mk_body())
        return sems


"""Layer builders for mk_recurrent_d4. Conventions:
- token-major tile: [128 tokens, feat]; feature-major (fm): [128 feat-chunk, tokens]
- biases / -rowmax / per-token shifts folded in via K=1 ones-matmul PSUM accumulation
- softmax: two-pass (pass1 rowmax token-major; pass2 transposed scores, exp->P^T bf16,
  PV fm accumulation + ones-col Z row; normalize via K=1 broadcast of 1/Z)
- ALL lambdas must bind loop vars as default args (deferred emission).
"""

TILE = 128


class Ctx:
    def __init__(self, nc, p, B=2, T=2048, D=1024, arena_bytes=116000):
        self.nc, self.p = nc, p
        self.B, self.T, self.D = B, T, D
        self.NT = T // TILE
        self.DC = D // TILE
        self.sb = {}
        self.pnb = 2
        self.banks = [nc.alloc_psum_tensor(f"bank{i}", [128, 512], F32).ap() for i in range(8)]
        self.arena_h = nc.alloc_sbuf_tensor("arena", [128, arena_bytes], mybir.dt.uint8)
        self.arena_base = nc.sbuf_base - arena_bytes
        self.arena_sz = arena_bytes
        self._mkcommon()

    def bk(self, i):
        return self.banks[i]

    def sbuf(self, name, shape, dtype):
        if name not in self.sb:
            self.sb[name] = self.nc.alloc_sbuf_tensor(name, shape, dtype).ap()
        return self.sb[name]

    def view(self, name, shape, dtype, off):
        if name not in self.sb:
            n = int(np.prod(shape[1:]))
            esz = {F32: 4, BF16: 2}[dtype]
            assert off + n * esz <= self.arena_sz, (name, off, n * esz, self.arena_sz)
            assert (self.arena_base + off) % 32 == 0, (name, off)
            h = self.nc.alloc_sbuf_tensor_at(name, shape, dtype, offset=self.arena_base + off)
            self.sb[name] = h.ap()
            self.p.register_view(self.sb[name].tensor.name, "arena", off, off + n * esz)
        return self.sb[name]

    def _mkcommon(self):
        p = self.p
        self.idn = self.sbuf("c_idn", [128, 128], F32)
        self.pidx = self.sbuf("c_pidx", [128, 1], F32)
        self.ones_row = self.sbuf("c_ones_row", [1, 512], BF16)
        self.neg_row = self.sbuf("c_neg_row", [1, 128], BF16)
        self.ones_col = self.sbuf("c_ones_col", [128, 1], BF16)
        self.tri = self.sbuf("c_tri", [128, 128], BF16)
        p.op("gpsimd", lambda g: g.iota(self.idn, base=0, pattern=[[1, 128]], channel_multiplier=0,
                                        allow_small_or_imprecise_dtypes=True), outs=[self.idn])
        p.op("gpsimd", lambda g: g.iota(self.pidx, base=0, pattern=[[0, 1]], channel_multiplier=1,
                                        allow_small_or_imprecise_dtypes=True), outs=[self.pidx])
        p.op("vector", lambda v: v.tensor_scalar(self.tri, self.idn, self.pidx, None,
                                                 op0=OP.is_ge, op1=OP.bypass),
             outs=[self.tri], ins=[self.idn, self.pidx])
        p.op("vector", lambda v: v.tensor_scalar(self.idn, self.idn, self.pidx, None,
                                                 op0=OP.is_equal, op1=OP.bypass),
             outs=[self.idn], ins=[self.idn, self.pidx])
        p.op("gpsimd", lambda g: g.memset(self.ones_row, 1.0), outs=[self.ones_row])
        p.op("gpsimd", lambda g: g.memset(self.neg_row, -1.0), outs=[self.neg_row])
        p.op("gpsimd", lambda g: g.memset(self.ones_col, 1.0), outs=[self.ones_col])

    def load_cast(self, name, dram, kchunks, mcols, queue="gpsimd", view_off=None, sem_key=None):
        if view_off is None:
            t = self.sbuf(name, [128, kchunks, mcols], BF16)
        else:
            t = self.view(name, [128, kchunks, mcols], BF16, view_off)
        self.p.dma(queue, lambda e, t=t, dram=dram: e.dma_start(
            out=t, in_=dram.rearrange("(a p) m -> p a m", p=128)),
            outs=[t], ins=[dram], sem_key=sem_key)
        return t

    def load_transposed(self, name, dram, dout, din, stage_off=0, bank=1, out_off=None):
        nc, p = self.nc, self.p
        rc, cc = (dout + 127) // 128, din // 128
        rpart = min(dout, 128)
        stage = self.view(name + "_stg", [128, rc, din], F32, stage_off)
        p.dma("gpsimd", lambda e, stage=stage, dram=dram: e.dma_start(
            out=stage[0:rpart, :, :] if rc == 1 else stage,
            in_=dram.rearrange("(a p) m -> p a m", p=rpart)),
            outs=[stage], ins=[dram], sem_key="D:wstg")
        if out_off is None:
            t = self.sbuf(name, [128, cc, dout], BF16)
        else:
            t = self.view(name, [128, cc, dout], BF16, out_off)
        pb = self.bk(bank)
        for r in range(rc):
            for c in range(cc):
                p.op("tensor", lambda tt, r=r, c=c, stage=stage, t=t: tt.transpose(
                    self.bk(bank)[:, 0:rpart], stage[0:rpart, r, c * 128:(c + 1) * 128],
                    self.idn[0:rpart, 0:rpart]), outs=[pb], ins=[stage, self.idn])
                p.op("scalar", lambda s, r=r, c=c, t=t: s.copy(
                    t[:, c, r * 128:r * 128 + rpart], self.bk(bank)[:, 0:rpart]),
                    outs=[t], ins=[pb])
        return t

    def ln_tile(self, x_tile, xn_tile, eps=1e-5):
        p, D = self.p, self.D
        epsc = self.sbuf("ln_eps", [128, 1], F32)
        if not getattr(self, "_eps_init", False):
            self._eps_init = True
            p.op("gpsimd", lambda g: g.memset(epsc, float(eps)), outs=[epsc])
        s = self.sbuf("ln_s", [128, 1], F32)
        nm = self.sbuf("ln_nm", [128, 1], F32)
        sq = self.sbuf("ln_sq", [128, D], F32)
        ssq = self.sbuf("ln_ssq", [128, 1], F32)
        var = self.sbuf("ln_var", [128, 1], F32)
        rstd = self.sbuf("ln_rstd", [128, 1], F32)
        p.op("vector", lambda v: v.tensor_reduce(s, x_tile, axis=AX.X, op=OP.add), outs=[s], ins=[x_tile])
        p.op("vector", lambda v: v.tensor_scalar(nm, s, -1.0 / D, None, op0=OP.mult, op1=OP.bypass),
             outs=[nm], ins=[s])
        p.op("scalar", lambda sc: sc.activation(sq, x_tile, AF.Square, accum_out=ssq),
             outs=[sq, ssq], ins=[x_tile])
        p.op("vector", lambda v: v.tensor_tensor(var, nm, nm, op=OP.mult), outs=[var], ins=[nm])
        p.op("vector", lambda v: v.scalar_tensor_tensor(var, ssq, 1.0 / D, var, op0=OP.mult, op1=OP.subtract),
             outs=[var], ins=[ssq, var])
        p.op("scalar", lambda sc: sc.activation(rstd, var, AF.Sqrt, bias=epsc), outs=[rstd], ins=[var, epsc])
        p.op("vector", lambda v: v.reciprocal(rstd, rstd), outs=[rstd], ins=[rstd])
        p.op("vector", lambda v: v.tensor_scalar(xn_tile, x_tile, nm, rstd, op0=OP.add, op1=OP.mult),
             outs=[xn_tile], ins=[x_tile, nm, rstd])

    def phase_norm(self, x_in, xnb, b, xnb_lo=None, per_tile=None, xnf=None):
        p = self.p
        for i in range(self.NT):
            xt = self.sbuf(f"pn_x{i % self.pnb}", [128, self.D], F32)
            xn = self.sbuf(f"pn_xn{i % self.pnb}", [128, self.D], F32)
            p.dma("sync", lambda e, b=b, i=i, xt=xt: e.dma_start(
                out=xt, in_=x_in[b, i * 128:(i + 1) * 128, :]), outs=[xt], ins=[x_in])
            self.ln_tile(xt, xn)
            p.dma("gpsimd", lambda e, b=b, i=i, xn=xn, xnb=xnb: e.dma_start(
                out=xnb[b, i * 128:(i + 1) * 128, :], in_=xn), outs=[xnb], ins=[xn], sem_key="D:xnb")
            if xnf is not None:
                p.dma("scalar", lambda e, b=b, i=i, xn=xn, xnf=xnf: e.dma_start(
                    out=xnf[b, i * 128:(i + 1) * 128, :], in_=xn), outs=[xnf], ins=[xn], sem_key="D:xnf")
            if xnb_lo is not None:
                hi = self.sbuf("pn_hi", [128, self.D], BF16)
                lo = self.sbuf("pn_lo", [128, self.D], F32)
                p.op("vector", lambda v, hi=hi, xn=xn: v.tensor_copy(hi, xn), outs=[hi], ins=[xn])
                p.op("vector", lambda v, lo=lo, xn=xn, hi=hi: v.tensor_tensor(lo, xn, hi, op=OP.subtract),
                     outs=[lo], ins=[xn, hi])
                p.dma("gpsimd", lambda e, b=b, i=i, lo=lo, xnb_lo=xnb_lo: e.dma_start(
                    out=xnb_lo[b, i * 128:(i + 1) * 128, :], in_=lo), outs=[xnb_lo], ins=[lo],
                    sem_key="D:xnblo")
            if per_tile is not None:
                per_tile(b, i, xn)

    def load_xn_fm_b(self, xnb, b, name="xn_fm"):
        p = self.p
        t = self.sbuf(name, [128, self.DC, self.T], BF16)
        for c in range(self.DC):
            p.dma("sync", lambda e, b=b, c=c, t=t, xnb=xnb: e.dma_start_transpose(
                t[:, c, :], xnb[b, :, c * 128:(c + 1) * 128]),
                outs=[t], ins=[xnb], sem_key="D:" + name)
        return t

    def hilo_split(self, name, dram, kchunks, mcols, queue="gpsimd"):
        p = self.p
        stg = self.sbuf(name + "_stg", [128, kchunks, mcols], F32)
        p.dma(queue, lambda e, stg=stg, dram=dram: e.dma_start(
            out=stg, in_=dram.rearrange("(a p) m -> p a m", p=128)), outs=[stg], ins=[dram])
        hi = self.sbuf(name + "_hi", [128, kchunks, mcols], BF16)
        lo = self.sbuf(name + "_lo", [128, kchunks, mcols], BF16)
        tmp = self.sbuf(name + "_tmp", [128, kchunks, mcols], F32)
        p.op("vector", lambda v: v.tensor_copy(hi, stg), outs=[hi], ins=[stg])
        p.op("vector", lambda v: v.tensor_tensor(tmp, stg, hi, op=OP.subtract), outs=[tmp], ins=[stg, hi])
        p.op("vector", lambda v: v.tensor_copy(lo, tmp), outs=[lo], ins=[tmp])
        return hi, lo

    def mrow_of(self, mcols, mrow, n, bank=1):
        p = self.p
        pb = self.bk(bank)
        for i in range(n):
            p.op("tensor", lambda t, i=i: t.transpose(pb[0:1, 0:128], mcols[:, i:i + 1], self.idn),
                 outs=[pb], ins=[mcols, self.idn])
            p.op("scalar", lambda s, i=i: s.copy(mrow[0:1, i * 128:(i + 1) * 128], pb[0:1, 0:128]),
                 outs=[mrow], ins=[pb])

    def col_load(self, name, dram, nchunks, bank=1):
        p = self.p
        row = self.sbuf("c_rowbuf", [1, 512], F32)
        t = self.sbuf(name, [128, nchunks], F32)
        pb = self.bk(bank)
        for c0 in range(0, nchunks, 4):
            cn = min(4, nchunks - c0)
            p.dma("sync", lambda e, c0=c0, cn=cn, row=row, dram=dram: e.dma_start(
                out=row[0:1, 0:cn * 128], in_=dram[None, c0 * 128:(c0 + cn) * 128]),
                outs=[row], ins=[dram], sem_key="D:colload")
            for c in range(cn):
                p.op("tensor", lambda tt, c=c: tt.transpose(pb[0:128, 0:1], row[0:1, c * 128:(c + 1) * 128],
                                                            self.idn[0:1, 0:1]), outs=[pb], ins=[row, self.idn])
                p.op("scalar", lambda sc, c=c, c0=c0: sc.copy(t[:, c0 + c:c0 + c + 1], pb[0:128, 0:1]),
                     outs=[t], ins=[pb])
        return t

    def neg_scalar_max(self, mcols, n, negm, bank=1):
        p = self.p
        pb = self.bk(bank)
        mm = self.sbuf("nsm_mm", [128, 1], F32)
        mr = self.sbuf("nsm_mr", [1, 1], F32)
        mrb = self.sbuf("nsm_mrb", [1, 1], BF16)
        p.op("vector", lambda v: v.tensor_reduce(mm, mcols[:, 0:n], axis=AX.X, op=OP.max),
             outs=[mm], ins=[mcols])
        p.op("tensor", lambda t: t.transpose(pb[0:1, 0:128], mm, self.idn), outs=[pb], ins=[mm, self.idn])
        p.op("vector", lambda v: v.tensor_reduce(mr, pb[0:1, 0:128], axis=AX.X, op=OP.max, negate=True),
             outs=[mr], ins=[pb])
        p.op("vector", lambda v: v.tensor_copy(mrb, mr), outs=[mrb], ins=[mr])
        p.op("tensor", lambda t: t.matmul(pb[:, 0:1], self.ones_row[0:1, 0:128], mrb, start=True, stop=True),
             outs=[pb], ins=[self.ones_row, mrb])
        p.op("scalar", lambda sc: sc.copy(negm, pb[:, 0:1]), outs=[negm], ins=[pb])

    def bcast_row(self, row_f32, out_sbuf, n, bank=4, row_bf=None):
        p = self.p
        if row_bf is None:
            row_bf = self.sbuf("bc_rowbf", [1, 512], BF16)
        p.op("vector", lambda v: v.tensor_copy(row_bf[:, 0:n], row_f32[:, 0:n]),
             outs=[row_bf], ins=[row_f32])
        pb = self.bk(bank)
        p.op("tensor", lambda t: t.matmul(pb[:, 0:n], self.ones_row[0:1, 0:128], row_bf[:, 0:n],
                                          start=True, stop=True), outs=[pb], ins=[self.ones_row, row_bf])
        p.op("scalar", lambda s: s.copy(out_sbuf[:, 0:n], pb[:, 0:n]), outs=[out_sbuf], ins=[pb])



def build_l0(ctx, x_in, y_out, W, C, scr):
    p, nc = ctx.p, ctx.nc
    B, T, D, NT, DC = ctx.B, ctx.T, ctx.D, ctx.NT, ctx.DC
    HID = 2816
    HC = HID // 128
    NKV = 2

    WkT = ctx.load_transposed("l0_WkT", W["Wk"], 128, D, stage_off=0)
    WvT = ctx.load_transposed("l0_WvT", W["Wv"], 128, D, stage_off=0)
    WqT = [ctx.load_transposed(f"l0_WqT{h}", W["Wq"][h], 128, D, stage_off=0) for h in range(NKV)]
    WoT = ctx.load_transposed("l0_WoT", W["Wo"], D, NKV * 128, stage_off=0)
    Wr_hi, Wr_lo = ctx.hilo_split("l0_Wr", W["Wr"], DC, 7)
    rbias = ctx.sbuf("l0_rbias", [1, 7], F32)
    p.dma("sync", lambda e: e.dma_start(out=rbias, in_=W["rb"][None, :]), outs=[rbias], ins=[W["rb"]])
    rb_bc = ctx.sbuf("l0_rb_bc", [128, 7], F32)
    ctx.bcast_row(rbias, rb_bc, 7, bank=0)

    ctab = {}
    for nm in ("cosk", "sink", "cosq", "sinq"):
        t = ctx.sbuf("l0_" + nm, [128, NT, 64], BF16)
        p.dma("gpsimd", lambda e, t=t, nm=nm: e.dma_start(
            out=t, in_=C[nm].rearrange("(n p) i -> p n i", p=128)), outs=[t], ins=[C[nm]], sem_key="D:ctab")
        ctab[nm] = t

    def rope_evac(ps, out_t, i, cos, sin):
        E = ctx.sbuf("rp_e", [128, 64], F32)
        O = ctx.sbuf("rp_o", [128, 64], F32)
        t1 = ctx.sbuf("rp_t1", [128, 64], F32)
        p.op("vector", lambda v, i=i: v.tensor_copy(E, ps[:, 0:128:2]), outs=[E], ins=[ps])
        p.op("vector", lambda v, i=i: v.tensor_copy(O, ps[:, 1:128:2]), outs=[O], ins=[ps])
        p.op("vector", lambda v, i=i: v.tensor_tensor(t1, O, sin[:, i, :], op=OP.mult), outs=[t1], ins=[O, sin])
        p.op("vector", lambda v, i=i: v.tensor_tensor(out_t[:, 0:128:2], E, cos[:, i, :], op=OP.mult),
             outs=[out_t], ins=[E, cos])
        p.op("vector", lambda v: v.tensor_tensor(out_t[:, 0:128:2], out_t[:, 0:128:2], t1, op=OP.subtract),
             outs=[out_t], ins=[out_t, t1])
        p.op("vector", lambda v, i=i: v.tensor_tensor(t1, E, sin[:, i, :], op=OP.mult), outs=[t1], ins=[E, sin])
        p.op("vector", lambda v, i=i: v.tensor_tensor(out_t[:, 1:128:2], O, cos[:, i, :], op=OP.mult),
             outs=[out_t], ins=[O, cos])
        p.op("vector", lambda v: v.tensor_tensor(out_t[:, 1:128:2], out_t[:, 1:128:2], t1, op=OP.add),
             outs=[out_t], ins=[out_t, t1])

    for b in range(B):
        ctx.phase_norm(x_in, scr["xnb"], b, xnb_lo=scr["xnb_lo"])
        xn_fm = ctx.load_xn_fm_b(scr["xnb"], b)

        K_fm = ctx.view("l0_Kfm", [128, T], BF16, 0)
        Q_fm = [ctx.view(f"l0_Qfm{h}", [128, T], BF16, 4096 + h * 4096) for h in range(NKV)]
        V_tm = ctx.view("l0_Vtm", [128, NT, 128], BF16, 12288)
        attnout = ctx.view("l0_attnout", [128, NKV, T], BF16, 16384)
        rope_t = ctx.sbuf("l0_rope_t", [128, 128], F32)
        pA, pB = ctx.bk(0), ctx.bk(1)
        for i in range(NT):
            for c in range(DC):
                p.op("tensor", lambda t, i=i, c=c, xn_fm=xn_fm: t.matmul(
                    pA[:, 0:128], xn_fm[:, c, i * 128:(i + 1) * 128], WkT[:, c, :],
                    start=(c == 0), stop=(c == DC - 1)), outs=[pA], ins=[xn_fm, WkT])
            rope_evac(pA, rope_t, i, ctab["cosk"], ctab["sink"])
            p.op("tensor", lambda t, i=i: t.transpose(pB[:, 0:128], rope_t, ctx.idn),
                 outs=[pB], ins=[rope_t, ctx.idn])
            p.op("scalar", lambda s, i=i: s.copy(K_fm[:, i * 128:(i + 1) * 128], pB[:, 0:128]),
                 outs=[K_fm], ins=[pB])
            for c in range(DC):
                p.op("tensor", lambda t, i=i, c=c, xn_fm=xn_fm: t.matmul(
                    pA[:, 0:128], xn_fm[:, c, i * 128:(i + 1) * 128], WvT[:, c, :],
                    start=(c == 0), stop=(c == DC - 1)), outs=[pA], ins=[xn_fm, WvT])
            p.op("scalar", lambda s, i=i: s.copy(V_tm[:, i, :], pA[:, 0:128]), outs=[V_tm], ins=[pA])
            for h in range(NKV):
                for c in range(DC):
                    p.op("tensor", lambda t, i=i, c=c, h=h, xn_fm=xn_fm: t.matmul(
                        pA[:, 0:128], xn_fm[:, c, i * 128:(i + 1) * 128], WqT[h][:, c, :],
                        start=(c == 0), stop=(c == DC - 1)), outs=[pA], ins=[xn_fm, WqT[h]])
                rope_evac(pA, rope_t, i, ctab["cosq"], ctab["sinq"])
                p.op("tensor", lambda t, i=i, h=h: t.transpose(pB[:, 0:128], rope_t, ctx.idn),
                     outs=[pB], ins=[rope_t, ctx.idn])
                p.op("scalar", lambda s, i=i, h=h: s.copy(Q_fm[h][:, i * 128:(i + 1) * 128], pB[:, 0:128]),
                     outs=[Q_fm[h]], ins=[pB])

        for h in range(NKV):
            mcols = ctx.sbuf("l0_mcols", [128, NT], F32)
            mtmp = ctx.sbuf("l0_mtmp", [128, 4], F32)
            for i in range(NT):
                nk = (i + 1) * 128
                nchunk = (nk + 511) // 512
                for j in range(nchunk):
                    n = min(512, nk - j * 512)
                    p.op("tensor", lambda t, i=i, j=j, n=n, h=h: t.matmul(
                        pA[:, 0:n], Q_fm[h][:, i * 128:(i + 1) * 128], K_fm[:, j * 512:j * 512 + n],
                        start=True, stop=True), outs=[pA], ins=[Q_fm[h], K_fm])
                    p.op("vector", lambda v, j=j, n=n: v.tensor_reduce(
                        mtmp[:, j:j + 1], pA[:, 0:n], axis=AX.X, op=OP.max), outs=[mtmp], ins=[pA])
                p.op("vector", lambda v, i=i, nchunk=nchunk: v.tensor_reduce(
                    mcols[:, i:i + 1], mtmp[:, 0:nchunk], axis=AX.X, op=OP.max), outs=[mcols], ins=[mtmp])
            negm = ctx.sbuf("l0_negm", [128, 1], F32)
            ctx.neg_scalar_max(mcols, NT, negm, bank=1)

            pS, pO, pZ = ctx.bk(2), ctx.bk(3), ctx.bk(4)
            PT = ctx.sbuf("l0_PT", [128, 512], BF16)
            zrow = ctx.sbuf("l0_zrow", [1, 512], F32)
            zbc = ctx.sbuf("l0_zbc", [128, 512], F32)
            for qc in range(T // 512):
                tq0 = qc * 512
                ktiles = list(range(0, (tq0 + 512) // 128))
                for kk, k in enumerate(ktiles):
                    st, sp = (kk == 0), (kk == len(ktiles) - 1)
                    p.op("tensor", lambda t, k=k, tq0=tq0, h=h: t.matmul(
                        pS[:, 0:512], K_fm[:, k * 128:(k + 1) * 128], Q_fm[h][:, tq0:tq0 + 512],
                        start=True, stop=True), outs=[pS], ins=[K_fm, Q_fm[h]])
                    p.op("scalar", lambda s: s.activation(PT, pS[:, 0:512], AF.Exp, bias=negm),
                         outs=[PT], ins=[pS, negm])
                    dsub = k - tq0 // 128
                    if 0 <= dsub < 4:
                        p.op("vector", lambda v, dsub=dsub: v.tensor_tensor(
                            PT[:, dsub * 128:(dsub + 1) * 128], PT[:, dsub * 128:(dsub + 1) * 128],
                            ctx.tri, op=OP.mult), outs=[PT], ins=[PT, ctx.tri])
                        for z in range(dsub):
                            p.op("gpsimd", lambda g, z=z: g.memset(PT[:, z * 128:(z + 1) * 128], 0.0),
                                 outs=[PT])
                    p.op("tensor", lambda t, k=k, st=st, sp=sp: t.matmul(
                        pO[:, 0:512], V_tm[:, k, :], PT, start=st, stop=sp), outs=[pO], ins=[V_tm, PT])
                    p.op("tensor", lambda t, st=st, sp=sp: t.matmul(
                        pZ[0:1, 0:512], ctx.ones_col, PT, start=st, stop=sp),
                        outs=[pZ], ins=[ctx.ones_col, PT])
                p.op("vector", lambda v: v.reciprocal(zrow, pZ[0:1, 0:512]), outs=[zrow], ins=[pZ])
                ctx.bcast_row(zrow, zbc, 512, bank=4)
                p.op("vector", lambda v, h=h, tq0=tq0: v.tensor_tensor(
                    attnout[:, h, tq0:tq0 + 512], pO[:, 0:512], zbc, op=OP.mult),
                    outs=[attnout], ins=[pO, zbc])

        s_t = ctx.sbuf("ln_sq", [128, D], F32)
        for i in range(NT):
            for half in range(2):
                for hc in range(NKV):
                    p.op("tensor", lambda t, i=i, half=half, hc=hc: t.matmul(
                        pA[:, 0:512], attnout[:, hc, i * 128:(i + 1) * 128],
                        WoT[:, hc, half * 512:(half + 1) * 512],
                        start=(hc == 0), stop=(hc == NKV - 1)), outs=[pA], ins=[attnout, WoT])
                p.op("vector", lambda v, half=half: v.tensor_copy(
                    s_t[:, half * 512:(half + 1) * 512], pA[:, 0:512]), outs=[s_t], ins=[pA])
            p.dma("scalar", lambda e, i=i, b=b: e.dma_start(
                out=scr["s"][b, i * 128:(i + 1) * 128, :], in_=s_t), outs=[scr["s"]], ins=[s_t],
                sem_key="D:l0s")

        xn_lo_fm = ctx.view("xn_lo_fm", [128, DC, T], BF16, 24576)
        for c in range(DC):
            p.dma("sync", lambda e, b=b, c=c, t=xn_lo_fm: e.dma_start_transpose(
                t[:, c, :], scr["xnb_lo"][b, :, c * 128:(c + 1) * 128]),
                outs=[xn_lo_fm], ins=[scr["xnb_lo"]], sem_key="D:xnlofm")
        w_all = ctx.sbuf("l0_w_all", [128, NT, 7], F32)
        for i in range(NT):
            for ci, (xs, ws) in enumerate([(xn_fm, Wr_hi), (xn_lo_fm, Wr_hi), (xn_fm, Wr_lo)]):
                for c in range(DC):
                    p.op("tensor", lambda t, i=i, c=c, xs=xs, ws=ws, ci=ci: t.matmul(
                        pA[:, 0:7], xs[:, c, i * 128:(i + 1) * 128], ws[:, c, :],
                        start=(ci == 0 and c == 0), stop=(ci == 2 and c == DC - 1)),
                        outs=[pA], ins=[xs, ws])
            lg = ctx.sbuf("l0_lg", [128, 7], F32)
            pr = ctx.sbuf("l0_pr", [128, 7], F32)
            m1 = ctx.sbuf("l0_m1", [128, 1], F32)
            m2 = ctx.sbuf("l0_m2", [128, 1], F32)
            eq1 = ctx.sbuf("l0_eq1", [128, 7], F32)
            eq2 = ctx.sbuf("l0_eq2", [128, 7], F32)
            pr2 = ctx.sbuf("l0_pr2", [128, 7], F32)
            rs = ctx.sbuf("l0_rs", [128, 1], F32)
            p.op("vector", lambda v: v.tensor_tensor(lg, pA[:, 0:7], rb_bc, op=OP.mult),
                 outs=[lg], ins=[pA, rb_bc])
            p.op("scalar", lambda s: s.activation(pr, lg, AF.Sigmoid), outs=[pr], ins=[lg])
            p.op("vector", lambda v: v.tensor_reduce(m1, pr, axis=AX.X, op=OP.max), outs=[m1], ins=[pr])
            p.op("vector", lambda v: v.tensor_scalar(eq1, pr, m1, None, op0=OP.is_ge, op1=OP.bypass),
                 outs=[eq1], ins=[pr, m1])
            p.op("vector", lambda v: v.scalar_tensor_tensor(pr2, eq1, -4.0, pr, op0=OP.mult, op1=OP.add),
                 outs=[pr2], ins=[eq1, pr])
            p.op("vector", lambda v: v.tensor_reduce(m2, pr2, axis=AX.X, op=OP.max), outs=[m2], ins=[pr2])
            p.op("vector", lambda v: v.tensor_scalar(eq2, pr2, m2, None, op0=OP.is_ge, op1=OP.bypass),
                 outs=[eq2], ins=[pr2, m2])
            p.op("vector", lambda v: v.tensor_tensor(rs, m1, m2, op=OP.add), outs=[rs], ins=[m1, m2])
            p.op("vector", lambda v: v.reciprocal(rs, rs), outs=[rs], ins=[rs])
            p.op("vector", lambda v: v.tensor_tensor(eq1, eq1, eq2, op=OP.add), outs=[eq1], ins=[eq1, eq2])
            p.op("vector", lambda v: v.tensor_tensor(eq1, eq1, pr, op=OP.mult), outs=[eq1], ins=[eq1, pr])
            p.op("vector", lambda v, i=i: v.tensor_scalar(
                w_all[:, i, :], eq1, rs, None, op0=OP.mult, op1=OP.bypass), outs=[w_all], ins=[eq1, rs])

        UN = [3, 3, 3, 3, 3, 3, 3, 1]
        pg, pu, pd = ctx.bk(5), ctx.bk(6), ctx.bk(7)
        sil = ctx.sbuf("l0_sil", [128, 512], BF16)
        hdram = scr["h"]
        HT = T // 2
        NSUB = HT // 128
        for half_t in range(2):
            hb = half_t * HT
            comb = ctx.view("l0_comb", [128, NSUB, D], F32, 82048)
            for e in range(8):
                if e == 0:
                    gd, ud, dd = W["sg"], W["su"], W["sd"]
                else:
                    gd, ud, dd = W["rg"][e - 1], W["ru"][e - 1], W["rd"][e - 1]
                Wd = ctx.view("l0_Wd", [128, HC, D], BF16, 24576)
                p.dma("gpsimd", lambda ee, dd=dd, Wd=Wd: ee.dma_start(
                    out=Wd, in_=dd.rearrange("(a p) m -> p a m", p=128)),
                    outs=[Wd], ins=[dd], sem_key="D:l0wd")
                hc0 = 0
                for u, un in enumerate(UN):
                    ub = u % 2
                    Wg = ctx.view(f"l0_Wg{ub}", [128, DC, 3 * 128], BF16, 0 + ub * 6144)
                    Wu = ctx.view(f"l0_Wu{ub}", [128, DC, 3 * 128], BF16, 12288 + ub * 6144)
                    p.dma("gpsimd", lambda ee, gd=gd, Wg=Wg, hc0=hc0, un=un: ee.dma_start(
                        out=Wg[:, :, 0:un * 128],
                        in_=gd[:, hc0 * 128:(hc0 + un) * 128].rearrange("(a p) m -> p a m", p=128)),
                        outs=[Wg], ins=[gd], sem_key=f"D:l0wg{ub}")
                    p.dma("gpsimd", lambda ee, ud=ud, Wu=Wu, hc0=hc0, un=un: ee.dma_start(
                        out=Wu[:, :, 0:un * 128],
                        in_=ud[:, hc0 * 128:(hc0 + un) * 128].rearrange("(a p) m -> p a m", p=128)),
                        outs=[Wu], ins=[ud], sem_key=f"D:l0wu{ub}")
                    for uc in range(un):
                        hc = hc0 + uc
                        for tcc in range(HT // 512):
                            t0 = hb + tcc * 512
                            ht = ctx.sbuf(f"l0_ht{(hc * 2 + tcc) % 2}", [128, 512], BF16)
                            for c in range(DC):
                                p.op("tensor", lambda t, uc=uc, c=c, t0=t0, Wg=Wg, xn_fm=xn_fm: t.matmul(
                                    pg[:, 0:512], Wg[:, c, uc * 128:(uc + 1) * 128], xn_fm[:, c, t0:t0 + 512],
                                    start=(c == 0), stop=(c == DC - 1)), outs=[pg], ins=[Wg, xn_fm])
                            for c in range(DC):
                                p.op("tensor", lambda t, uc=uc, c=c, t0=t0, Wu=Wu, xn_fm=xn_fm: t.matmul(
                                    pu[:, 0:512], Wu[:, c, uc * 128:(uc + 1) * 128], xn_fm[:, c, t0:t0 + 512],
                                    start=(c == 0), stop=(c == DC - 1)), outs=[pu], ins=[Wu, xn_fm])
                            p.op("scalar", lambda s, ht=ht: s.activation(sil, pg, AF.Silu), outs=[sil], ins=[pg])
                            p.op("vector", lambda v, ht=ht: v.tensor_tensor(ht, pu, sil, op=OP.mult),
                                 outs=[ht], ins=[pu, sil])
                            p.dma("scalar", lambda ee, hc=hc, tcc=tcc, ht=ht: ee.dma_start(
                                out=hdram[:, hc, tcc * 512:(tcc + 1) * 512], in_=ht),
                                outs=[hdram], ins=[ht], sem_key="D:l0h")
                    hc0 += un
                for sub in range(NSUB):
                    i = half_t * NSUB + sub
                    hsub = ctx.view(f"l0_hs{sub % 2}", [128, HC, 128], BF16, 69664 + (sub % 2) * 5888)
                    for hh in range(2):
                        p.dma("sync", lambda ee, sub=sub, hsub=hsub, hh=hh: ee.dma_start(
                            out=hsub[:, hh * 11:(hh + 1) * 11, :],
                            in_=hdram[:, hh * 11:(hh + 1) * 11, sub * 128:(sub + 1) * 128]),
                            outs=[hsub], ins=[hdram], sem_key=f"D:l0hs{sub % 2}")
                    for halfd in range(2):
                        for hc in range(HC):
                            p.op("tensor", lambda t, sub=sub, halfd=halfd, Wd=Wd, hc=hc, hsub=hsub: t.matmul(
                                pd[:, 0:512], hsub[:, hc, :], Wd[:, hc, halfd * 512:(halfd + 1) * 512],
                                start=(hc == 0), stop=(hc == HC - 1)), outs=[pd], ins=[hsub, Wd])
                        if e == 0:
                            p.op("vector", lambda v, sub=sub, halfd=halfd: v.tensor_copy(
                                comb[:, sub, halfd * 512:(halfd + 1) * 512], pd[:, 0:512]),
                                outs=[comb], ins=[pd])
                        else:
                            p.op("vector", lambda v, i=i, sub=sub, halfd=halfd, e=e: v.scalar_tensor_tensor(
                                comb[:, sub, halfd * 512:(halfd + 1) * 512], pd[:, 0:512],
                                w_all[:, i, e - 1:e], comb[:, sub, halfd * 512:(halfd + 1) * 512],
                                op0=OP.mult, op1=OP.add), outs=[comb], ins=[pd, w_all, comb])
            for sub in range(NSUB):
                i = half_t * NSUB + sub
                xt = ctx.sbuf(f"pn_x{sub % ctx.pnb}", [128, D], F32)
                st = ctx.sbuf(f"pn_xn{sub % ctx.pnb}", [128, D], F32)
                gt = ctx.sbuf("ln_sq", [128, D], F32)
                p.dma("sync", lambda e, b=b, i=i, xt=xt: e.dma_start(
                    out=xt, in_=x_in[b, i * 128:(i + 1) * 128, :]), outs=[xt], ins=[x_in])
                p.dma("sync", lambda e, b=b, i=i, st=st: e.dma_start(
                    out=st, in_=scr["s"][b, i * 128:(i + 1) * 128, :]), outs=[st], ins=[scr["s"]],
                    sem_key="D:l0s_rd")
                p.op("scalar", lambda s, sub=sub, gt=gt: s.activation(gt, comb[:, sub, :], AF.Silu),
                     outs=[gt], ins=[comb])
                p.op("vector", lambda v, gt=gt, st=st: v.tensor_tensor(gt, gt, st, op=OP.mult),
                     outs=[gt], ins=[gt, st])
                p.op("vector", lambda v, gt=gt, xt=xt: v.tensor_tensor(gt, gt, xt, op=OP.add),
                     outs=[gt], ins=[gt, xt])
                p.dma("scalar", lambda e, b=b, i=i, gt=gt: e.dma_start(
                    out=y_out[b, i * 128:(i + 1) * 128, :], in_=gt), outs=[y_out], ins=[gt], sem_key="D:y")


def l0_consts(T=2048):
    pos = np.arange(0, 128, 2, dtype=np.float32)
    theta = (10000.0 ** (-2.0 * pos / 128.0)).astype(np.float32)
    ang = (np.arange(T, dtype=np.float32)[:, None] * theta[None, :]).astype(np.float32)
    cos = np.cos(ang.astype(np.float64)).astype(np.float32)
    sin = np.sin(ang.astype(np.float64)).astype(np.float32)
    s = np.float32(128.0 ** -0.5)
    return {"cosk": cos, "sink": sin, "cosq": (cos * s).astype(np.float32),
            "sinq": (sin * s).astype(np.float32)}



def build_l1(ctx, x_in, y_out, W, scr):
    ctx.pnb = 1
    p, nc = ctx.p, ctx.nc
    B, T, D, NT, DC = ctx.B, ctx.T, ctx.D, ctx.NT, ctx.DC
    NH, DH, WIN = 16, 64, 256
    NHP = NH // 2
    pA, pB = ctx.bk(0), ctx.bk(1)

    wout = ctx.load_cast("l1_wout", W["w_out"], DC, D)
    w1l = ctx.load_cast("l1_w1l", W["w1_light"], DC, 512)
    w2l = ctx.load_cast("l1_w2l", W["w2_light"], 4, D)
    b1l = ctx.col_load("l1_b1l", W["b1_light"], 4)
    b1h = ctx.col_load("l1_b1h", W["b1_heavy"], 32)
    b2l_row = ctx.sbuf("l1_b2l", [1, D], BF16)
    p.dma("gpsimd", lambda e: e.dma_start(out=b2l_row, in_=W["b2_light"][None, :]),
          outs=[b2l_row], ins=[W["b2_light"]])
    b2h_row = ctx.sbuf("l1_b2h", [1, D], BF16)
    p.dma("gpsimd", lambda e: e.dma_start(out=b2h_row, in_=W["b2_heavy"][None, :]),
          outs=[b2h_row], ins=[W["b2_heavy"]])
    gml = ctx.col_load("l1_gml", W["gamma_light"], DC)
    gmh = ctx.col_load("l1_gmh", W["gamma_heavy"], DC)
    rt_row = ctx.sbuf("c_rowbuf", [1, 512], F32)
    ones_row_f = ctx.sbuf("c_ones_row_f", [1, 128], F32)
    p.op("gpsimd", lambda g: g.memset(ones_row_f, 1.0), outs=[ones_row_f])
    rt_bc = ctx.sbuf("l1_rt_bc", [128, D], F32)
    for hf in range(2):
        p.dma("sync", lambda e, hf=hf: e.dma_start(
            out=rt_row, in_=W["routing_token"][None, hf * 512:(hf + 1) * 512]),
            outs=[rt_row], ins=[W["routing_token"]], sem_key="D:colload")
        p.op("tensor", lambda t, hf=hf: t.matmul(pA[:, 0:512], ones_row_f, rt_row,
                                                 start=True, stop=True), outs=[pA], ins=[ones_row_f, rt_row])
        p.op("vector", lambda v, hf=hf: v.tensor_copy(rt_bc[:, hf * 512:(hf + 1) * 512], pA[:, 0:512]),
             outs=[rt_bc], ins=[pA])

    msks = {}
    c2v = ctx.sbuf("l1_c2v", [128, 256], F32)
    for dt in (-128, 0, 128, 256):
        m = ctx.sbuf(f"l1_msk{dt}", [128, 256], BF16)
        p.op("gpsimd", lambda g, dt=dt: g.iota(c2v, base=dt, pattern=[[1, 256]], channel_multiplier=-1,
                                               allow_small_or_imprecise_dtypes=True), outs=[c2v])
        t1 = ctx.sbuf("l1_mt1", [128, 256], F32)
        p.op("vector", lambda v: v.tensor_scalar(t1, c2v, -0.5, None, op0=OP.is_gt, op1=OP.bypass),
             outs=[t1], ins=[c2v])
        p.op("vector", lambda v, m=m: v.tensor_scalar(m, c2v, 255.5, None, op0=OP.is_lt, op1=OP.bypass),
             outs=[m], ins=[c2v])
        p.op("vector", lambda v, m=m: v.tensor_tensor(m, m, t1, op=OP.mult), outs=[m], ins=[m, t1])
        msks[dt] = m

    HT = T // 2
    NSUB = HT // 128
    for b in range(B):
        s_cols = ctx.sbuf("l1_scols", [128, NT], F32)
        sdump = ctx.sbuf("ln_sq", [128, self.D] if False else [128, D], F32)

        def score_tile(bb, i, xn_tile):
            p.op("vector", lambda v, i=i, xn_tile=xn_tile: v.tensor_tensor(
                sdump, xn_tile, rt_bc, op=OP.mult), outs=[sdump], ins=[xn_tile, rt_bc])
            p.op("vector", lambda v, i=i: v.tensor_reduce(
                s_cols[:, i:i + 1], sdump, axis=AX.X, op=OP.add), outs=[s_cols], ins=[sdump])

        rms_cols = ctx.sbuf("l1_rmscols", [128, NT], F32)

        def rms_tile(bb, i, xn_tile):
            var, rstd = ctx.sb["ln_var"], ctx.sb["ln_rstd"]
            t1 = ctx.sbuf("l1_rt1", [128, 1], F32)
            p.op("scalar", lambda sc, i=i: sc.activation(t1, var, AF.Sqrt, scale=float(D)),
                 outs=[t1], ins=[var])
            p.op("vector", lambda v, i=i: v.tensor_tensor(t1, t1, rstd, op=OP.mult), outs=[t1], ins=[t1, rstd])
            p.op("vector", lambda v, i=i: v.tensor_scalar(t1, t1, 1e-12, None, op0=OP.max, op1=OP.bypass),
                 outs=[t1], ins=[t1])
            p.op("vector", lambda v, i=i: v.reciprocal(t1, t1), outs=[t1], ins=[t1])
            p.op("vector", lambda v, i=i: v.tensor_scalar(
                rms_cols[:, i:i + 1], t1, float(np.sqrt(D)), None, op0=OP.mult, op1=OP.bypass),
                outs=[rms_cols], ins=[t1])

        def per_tile(bb, i, xn_tile):
            score_tile(bb, i, xn_tile)
            rms_tile(bb, i, xn_tile)

        ctx.phase_norm(x_in, scr["xnb"], b, per_tile=per_tile)
        xn_fm = ctx.load_xn_fm_b(scr["xnb"], b)

        thr = ctx.sbuf("l1_thr", [128, 1], F32)
        cnt = ctx.sbuf("l1_cnt", [128, 1], F32)
        cntb = ctx.sbuf("l1_cntb", [128, 1], BF16)
        tot = ctx.sbuf("l1_tot", [128, 1], F32)
        d01 = ctx.sbuf("l1_d01", [128, 1], F32)
        bmask = ctx.sbuf("l1_bmask", [128, NT], F32)
        ones_mat = ctx.sbuf("c_ones_mat", [128, 128], BF16)
        p.op("gpsimd", lambda g: g.memset(ones_mat, 1.0), outs=[ones_mat])
        p.op("gpsimd", lambda g: g.memset(thr, 0.0), outs=[thr])
        step = 256.0
        for it in range(40):
            p.op("vector", lambda v: v.tensor_scalar(bmask, s_cols, thr, None, op0=OP.is_gt, op1=OP.bypass),
                 outs=[bmask], ins=[s_cols, thr])
            p.op("vector", lambda v: v.tensor_reduce(cnt, bmask, axis=AX.X, op=OP.add), outs=[cnt], ins=[bmask])
            p.op("vector", lambda v: v.tensor_copy(cntb, cnt), outs=[cntb], ins=[cnt])
            p.op("tensor", lambda t: t.matmul(pB[:, 0:1], ones_mat, cntb, start=True, stop=True),
                 outs=[pB], ins=[ones_mat, cntb])
            p.op("scalar", lambda s: s.copy(tot, pB[:, 0:1]), outs=[tot], ins=[pB])
            p.op("vector", lambda v: v.tensor_scalar(d01, tot, float(T // 2) + 0.5, None,
                                                     op0=OP.is_gt, op1=OP.bypass), outs=[d01], ins=[tot])
            p.op("vector", lambda v, step=step: v.scalar_tensor_tensor(
                thr, d01, 2.0 * step, thr, op0=OP.mult, op1=OP.add), outs=[thr], ins=[d01, thr])
            p.op("vector", lambda v, step=step: v.tensor_scalar(
                thr, thr, -step, None, op0=OP.add, op1=OP.bypass), outs=[thr], ins=[thr])
            step = step / 2.0
        p.op("vector", lambda v: v.tensor_scalar(bmask, s_cols, thr, None, op0=OP.is_gt, op1=OP.bypass),
             outs=[bmask], ins=[s_cols, thr])

        v_tm = ctx.view("l1_vtm", [128, NT, D], BF16, 0)
        attnout = ctx.view("l1_attnout", [128, NHP, T], BF16, 32768)
        wv = ctx.view("l1_wv", [128, DC, D], BF16, 32768)
        p.dma("gpsimd", lambda e: e.dma_start(
            out=wv, in_=W["w_qkv"][:, 2048:3072].rearrange("(a p) m -> p a m", p=128)),
            outs=[wv], ins=[W["w_qkv"]], sem_key="D:l1wv")
        for i in range(NT):
            for half in range(2):
                for c in range(DC):
                    p.op("tensor", lambda t, i=i, half=half, c=c, xn_fm=xn_fm: t.matmul(
                        pA[:, 0:512], xn_fm[:, c, i * 128:(i + 1) * 128], wv[:, c, half * 512:(half + 1) * 512],
                        start=(c == 0), stop=(c == DC - 1)), outs=[pA], ins=[xn_fm, wv])
                p.op("scalar", lambda s, i=i, half=half: s.copy(
                    v_tm[:, i, half * 512:(half + 1) * 512], pA[:, 0:512]), outs=[v_tm], ins=[pA])

        for hp in range(NHP):
            qw = ctx.view(f"l1_qw{hp % 2}", [128, DC, 128], BF16, 65536 + (hp % 2) * 2048)
            kw = ctx.view(f"l1_kw{hp % 2}", [128, DC, 128], BF16, 69632 + (hp % 2) * 2048)
            p.dma("gpsimd", lambda e, hp=hp, qw=qw: e.dma_start(
                out=qw, in_=W["w_qkv"][:, hp * 128:(hp + 1) * 128].rearrange("(a p) m -> p a m", p=128)),
                outs=[qw], ins=[W["w_qkv"]], sem_key=f"D:l1qw{hp % 2}")
            p.dma("gpsimd", lambda e, hp=hp, kw=kw: e.dma_start(
                out=kw, in_=W["w_qkv"][:, 1024 + hp * 128:1024 + (hp + 1) * 128].rearrange(
                    "(a p) m -> p a m", p=128)),
                outs=[kw], ins=[W["w_qkv"]], sem_key=f"D:l1kw{hp % 2}")
            q_fm = ctx.view("l1_qfm", [128, T], BF16, 73728)
            k_fm = ctx.view("l1_kfm", [128, T], BF16, 77824)
            for tc in range(T // 512):
                for c in range(DC):
                    p.op("tensor", lambda t, tc=tc, c=c, qw=qw, xn_fm=xn_fm: t.matmul(
                        pA[:, 0:512], qw[:, c, :], xn_fm[:, c, tc * 512:(tc + 1) * 512],
                        start=(c == 0), stop=(c == DC - 1)), outs=[pA], ins=[qw, xn_fm])
                p.op("scalar", lambda s, tc=tc: s.activation(
                    q_fm[:, tc * 512:(tc + 1) * 512], pA[:, 0:512], AF.Copy, scale=float(DH ** -0.5)),
                    outs=[q_fm], ins=[pA])
                for c in range(DC):
                    p.op("tensor", lambda t, tc=tc, c=c, kw=kw, xn_fm=xn_fm: t.matmul(
                        pA[:, 0:512], kw[:, c, :], xn_fm[:, c, tc * 512:(tc + 1) * 512],
                        start=(c == 0), stop=(c == DC - 1)), outs=[pA], ins=[kw, xn_fm])
                p.op("scalar", lambda s, tc=tc: s.copy(k_fm[:, tc * 512:(tc + 1) * 512], pA[:, 0:512]),
                     outs=[k_fm], ins=[pA])
            for hh in range(2):
                h = hp * 2 + hh
                hb64 = hh * 64
                mcols = ctx.sbuf("l1_mcols", [128, NT], F32)
                for i in range(NT):
                    w0 = max(0, (i // 2) * 256 - 256)
                    wn = (i // 2) * 256 + 256 - w0
                    p.op("tensor", lambda t, i=i, w0=w0, wn=wn, hb64=hb64: t.matmul(
                        pA[:, 0:wn], q_fm[hb64:hb64 + 64, i * 128:(i + 1) * 128],
                        k_fm[hb64:hb64 + 64, w0:w0 + wn], start=True, stop=True),
                        outs=[pA], ins=[q_fm, k_fm])
                    p.op("vector", lambda v, i=i, wn=wn: v.tensor_reduce(
                        mcols[:, i:i + 1], pA[:, 0:wn], axis=AX.X, op=OP.max), outs=[mcols], ins=[pA])
                negm = ctx.sbuf("l1_negm", [128, 1], F32)
                ctx.neg_scalar_max(mcols, NT, negm, bank=1)
                pS, pO, pZ = ctx.bk(2), ctx.bk(3), ctx.bk(4)
                PT = ctx.sbuf("l1_PT", [128, 256], BF16)
                zrow = ctx.sbuf("l1_zrow", [1, 256], F32)
                zbc = ctx.sbuf("l1_zbc", [128, 256], F32)
                for blk in range(T // 256):
                    tq0 = blk * 256
                    kts = [k for k in range(2 * blk - 2, 2 * blk + 2) if k >= 0]
                    for kk, kt in enumerate(kts):
                        st, sp = (kk == 0), (kk == len(kts) - 1)
                        p.op("tensor", lambda t, kt=kt, tq0=tq0, hb64=hb64: t.matmul(
                            pS[:, 0:256], k_fm[hb64:hb64 + 64, kt * 128:(kt + 1) * 128],
                            q_fm[hb64:hb64 + 64, tq0:tq0 + 256], start=True, stop=True),
                            outs=[pS], ins=[k_fm, q_fm])
                        p.op("scalar", lambda s: s.activation(PT, pS[:, 0:256], AF.Exp, bias=negm),
                             outs=[PT], ins=[pS, negm])
                        dt = tq0 - kt * 128
                        p.op("vector", lambda v, dt=dt: v.tensor_tensor(PT, PT, msks[dt], op=OP.mult),
                             outs=[PT], ins=[PT, msks[dt]])
                        p.op("tensor", lambda t, kt=kt, st=st, sp=sp, h=h: t.matmul(
                            pO[0:64, 0:256], v_tm[:, kt, h * 64:(h + 1) * 64], PT, start=st, stop=sp),
                            outs=[pO], ins=[v_tm, PT])
                        p.op("tensor", lambda t, st=st, sp=sp: t.matmul(
                            pZ[0:1, 0:256], ctx.ones_col, PT, start=st, stop=sp),
                            outs=[pZ], ins=[ctx.ones_col, PT])
                    p.op("vector", lambda v: v.reciprocal(zrow, pZ[0:1, 0:256]), outs=[zrow], ins=[pZ])
                    zrow_bf = ctx.sbuf("l1_zrow_bf", [1, 256], BF16)
                    p.op("vector", lambda v: v.tensor_copy(zrow_bf, zrow), outs=[zrow_bf], ins=[zrow])
                    p.op("tensor", lambda t: t.matmul(pZ[:, 0:256], ctx.ones_row[0:1, 0:128], zrow_bf,
                                                      start=True, stop=True), outs=[pZ], ins=[ctx.ones_row, zrow_bf])
                    p.op("scalar", lambda s: s.copy(zbc, pZ[:, 0:256]), outs=[zbc], ins=[pZ])
                    p.op("vector", lambda v, hp=hp, hb64=hb64, tq0=tq0: v.tensor_tensor(
                        attnout[hb64:hb64 + 64, hp, tq0:tq0 + 256], pO[0:64, 0:256], zbc[0:64, :], op=OP.mult),
                        outs=[attnout], ins=[pO, zbc])
        s_t = ctx.sbuf("ln_sq", [128, D], F32)
        for i in range(NT):
            for half in range(2):
                for hc in range(NHP):
                    p.op("tensor", lambda t, i=i, half=half, hc=hc: t.matmul(
                        pA[:, 0:512], attnout[:, hc, i * 128:(i + 1) * 128],
                        wout[:, hc, half * 512:(half + 1) * 512],
                        start=(hc == 0), stop=(hc == NHP - 1)), outs=[pA], ins=[attnout, wout])
                p.op("vector", lambda v, half=half: v.tensor_copy(
                    s_t[:, half * 512:(half + 1) * 512], pA[:, 0:512]), outs=[s_t], ins=[pA])
            p.dma("scalar", lambda e, i=i, b=b: e.dma_start(
                out=scr["s"][b, i * 128:(i + 1) * 128, :], in_=s_t), outs=[scr["s"]], ins=[s_t],
                sem_key="D:l1s")

        rms_row = ctx.sbuf("l1_rms_row", [1, T], BF16)
        for i in range(NT):
            p.op("tensor", lambda t, i=i: t.transpose(pB[0:1, 0:128], rms_cols[:, i:i + 1], ctx.idn),
                 outs=[pB], ins=[rms_cols, ctx.idn])
            p.op("scalar", lambda s, i=i: s.copy(rms_row[0:1, i * 128:(i + 1) * 128], pB[0:1, 0:128]),
                 outs=[rms_row], ins=[pB])
        rmsbc = ctx.sbuf("l1_rmsbc", [128, 512], BF16)
        pg = ctx.bk(5)
        for tc in range(T // 512):
            t0 = tc * 512
            i0 = tc * 4
            p.op("tensor", lambda t, t0=t0: t.matmul(pg[:, 0:512], ctx.ones_row[0:1, 0:128],
                                                     rms_row[0:1, t0:t0 + 512], start=True, stop=True),
                 outs=[pg], ins=[ctx.ones_row, rms_row])
            p.op("scalar", lambda s: s.copy(rmsbc, pg[:, 0:512]), outs=[rmsbc], ins=[pg])
            rnl = ctx.view("l1_rnl", [128, DC, 512], BF16, 0)
            rnh = ctx.view("l1_rnh", [128, DC, 512], BF16, 8192)
            for c in range(DC):
                p.op("vector", lambda v, c=c, t0=t0: v.tensor_tensor(
                    rnl[:, c, :], xn_fm[:, c, t0:t0 + 512], rmsbc, op=OP.mult), outs=[rnl], ins=[xn_fm, rmsbc])
                p.op("vector", lambda v, c=c: v.tensor_scalar(
                    rnh[:, c, :], rnl[:, c, :], gmh[:, c:c + 1], None, op0=OP.mult, op1=OP.bypass),
                    outs=[rnh], ins=[rnl, gmh])
                p.op("vector", lambda v, c=c: v.tensor_scalar(
                    rnl[:, c, :], rnl[:, c, :], gml[:, c:c + 1], None, op0=OP.mult, op1=OP.bypass),
                    outs=[rnl], ins=[rnl, gml])
            h1l = ctx.view("l1_h1l", [128, 4, 512], BF16, 16384)
            for hc in range(4):
                for c in range(DC):
                    p.op("tensor", lambda t, hc=hc, c=c: t.matmul(
                        pg[:, 0:512], w1l[:, c, hc * 128:(hc + 1) * 128], rnl[:, c, :],
                        start=(c == 0), stop=(c == DC - 1)), outs=[pg], ins=[w1l, rnl])
                p.op("scalar", lambda s, hc=hc: s.activation(
                    h1l[:, hc, :], pg[:, 0:512], AF.Gelu, bias=b1l[:, hc:hc + 1]),
                    outs=[h1l], ins=[pg, b1l])
            lt = ctx.view("l1_lt", [128, 4, D], F32, 20480)
            for sub in range(4):
                for half in range(2):
                    for hc in range(4):
                        p.op("tensor", lambda t, sub=sub, half=half, hc=hc: t.matmul(
                            pg[:, 0:512], h1l[:, hc, sub * 128:(sub + 1) * 128],
                            w2l[:, hc, half * 512:(half + 1) * 512],
                            start=(hc == 0), stop=False), outs=[pg], ins=[h1l, w2l])
                    p.op("tensor", lambda t, sub=sub, half=half: t.matmul(
                        pg[:, 0:512], ctx.ones_row[0:1, 0:128], b2l_row[0:1, half * 512:(half + 1) * 512],
                        start=False, stop=True), outs=[pg], ins=[ctx.ones_row, b2l_row])
                    p.op("vector", lambda v, sub=sub, half=half: v.tensor_copy(
                        lt[:, sub, half * 512:(half + 1) * 512], pg[:, 0:512]), outs=[lt], ins=[pg])
            h1h = ctx.view("l1_h1h", [128, 32, 512], BF16, 36864)
            for q in range(4):
                w1h = ctx.view("l1_w1h", [128, DC, 1024], BF16, 69632)
                p.dma("gpsimd", lambda e, q=q, w1h=w1h: e.dma_start(
                    out=w1h, in_=W["w1_heavy"][:, q * 1024:(q + 1) * 1024].rearrange("(a p) m -> p a m", p=128)),
                    outs=[w1h], ins=[W["w1_heavy"]], sem_key="D:l1w1h")
                for hcq in range(8):
                    hc = q * 8 + hcq
                    for c in range(DC):
                        p.op("tensor", lambda t, hcq=hcq, c=c, w1h=w1h: t.matmul(
                            pg[:, 0:512], w1h[:, c, hcq * 128:(hcq + 1) * 128], rnh[:, c, :],
                            start=(c == 0), stop=(c == DC - 1)), outs=[pg], ins=[w1h, rnh])
                    p.op("scalar", lambda s, hc=hc: s.activation(
                        h1h[:, hc, :], pg[:, 0:512], AF.Gelu, bias=b1h[:, hc:hc + 1]),
                        outs=[h1h], ins=[pg, b1h])
            for u in range(8):
                w2h = ctx.view(f"l1_w2h{u % 2}", [128, 4, D], BF16, 86016 + (u % 2) * 8192)
                p.dma("gpsimd", lambda e, u=u, w2h=w2h: e.dma_start(
                    out=w2h, in_=W["w2_heavy"][u * 512:(u + 1) * 512, :].rearrange("(a p) m -> p a m", p=128)),
                    outs=[w2h], ins=[W["w2_heavy"]], sem_key=f"D:l1w2h{u % 2}")
                for sub in range(4):
                    for half in range(2):
                        bidx = sub * 2 + half
                        for hcu in range(4):
                            hc = u * 4 + hcu
                            p.op("tensor", lambda t, sub=sub, half=half, hcu=hcu, hc=hc, w2h=w2h, u=u,
                                 bidx=bidx: t.matmul(
                                ctx.bk(bidx)[:, 0:512], h1h[:, hc, sub * 128:(sub + 1) * 128],
                                w2h[:, hcu, half * 512:(half + 1) * 512],
                                start=(u == 0 and hcu == 0), stop=False),
                                outs=[ctx.bk(bidx)], ins=[h1h, w2h])
            for sub in range(4):
                i = i0 + sub
                for half in range(2):
                    bidx = sub * 2 + half
                    p.op("tensor", lambda t, half=half, bidx=bidx: t.matmul(
                        ctx.bk(bidx)[:, 0:512], ctx.ones_row[0:1, 0:128],
                        b2h_row[0:1, half * 512:(half + 1) * 512], start=False, stop=True),
                        outs=[ctx.bk(bidx)], ins=[ctx.ones_row, b2h_row])
                xt = ctx.sbuf(f"pn_x{sub % ctx.pnb}", [128, D], F32)
                st = ctx.sbuf(f"pn_xn{sub % ctx.pnb}", [128, D], F32)
                gt = ctx.sbuf("l1_ep_g", [128, D], F32)
                p.dma("sync", lambda e, b=b, i=i, xt=xt: e.dma_start(
                    out=xt, in_=x_in[b, i * 128:(i + 1) * 128, :]), outs=[xt], ins=[x_in])
                p.dma("sync", lambda e, b=b, i=i, st=st: e.dma_start(
                    out=st, in_=scr["s"][b, i * 128:(i + 1) * 128, :]), outs=[st], ins=[scr["s"]],
                    sem_key="D:l1s_rd")
                for half in range(2):
                    bidx = sub * 2 + half
                    p.op("vector", lambda v, sub=sub, half=half, i=i, bidx=bidx: v.scalar_tensor_tensor(
                        gt[:, half * 512:(half + 1) * 512], ctx.bk(bidx)[:, 0:512], bmask[:, i:i + 1],
                        lt[:, sub, half * 512:(half + 1) * 512], op0=OP.mult, op1=OP.add),
                        outs=[gt], ins=[ctx.bk(bidx), bmask, lt])
                gs = ctx.sbuf("ln_sq", [128, D], F32)
                p.op("scalar", lambda s, gt=gt, gs=gs: s.activation(gs, gt, AF.Silu), outs=[gs], ins=[gt])
                p.op("vector", lambda v, gs=gs, st=st: v.tensor_tensor(gs, gs, st, op=OP.mult),
                     outs=[gs], ins=[gs, st])
                p.op("vector", lambda v, gs=gs, xt=xt: v.tensor_tensor(gs, gs, xt, op=OP.add),
                     outs=[gs], ins=[gs, xt])
                p.dma("scalar", lambda e, b=b, i=i, gs=gs: e.dma_start(
                    out=y_out[b, i * 128:(i + 1) * 128, :], in_=gs), outs=[y_out], ins=[gs], sem_key="D:y")



def build_l2(ctx, x_in, y_out, W, scr):
    ctx.pnb = 1
    p, nc = ctx.p, ctx.nc
    B, T, D, NT, DC = ctx.B, ctx.T, ctx.D, ctx.NT, ctx.DC
    NH, DH = 16, 64
    NHP = NH // 2
    pA, pB = ctx.bk(0), ctx.bk(1)

    sim_stg = ctx.sbuf("l2_sim_stg", [128, DC, 16], F32)
    p.dma("sync", lambda e: e.dma_start(out=sim_stg, in_=W["sim_matrix"].rearrange("(a p) m -> p a m", p=128)),
          outs=[sim_stg], ins=[W["sim_matrix"]])
    ones_col_f = ctx.sbuf("c_ones_col_f", [128, 1], F32)
    p.op("gpsimd", lambda g: g.memset(ones_col_f, 1.0), outs=[ones_col_f])
    simsq = ctx.sbuf("l2_simsq", [128, DC, 16], F32)
    p.op("scalar", lambda s: s.activation(simsq, sim_stg, AF.Square), outs=[simsq], ins=[sim_stg])
    for c in range(DC):
        p.op("tensor", lambda t, c=c: t.matmul(pB[0:16, 0:1], simsq[:, c, :], ones_col_f,
                                               start=(c == 0), stop=(c == DC - 1)),
             outs=[pB], ins=[simsq, ones_col_f])
    cn = ctx.sbuf("l2_cn", [16, 1], F32)
    p.op("scalar", lambda s: s.activation(cn, pB[0:16, 0:1], AF.Sqrt), outs=[cn], ins=[pB])
    p.op("vector", lambda v: v.tensor_scalar(cn, cn, 1e-12, None, op0=OP.max, op1=OP.bypass),
         outs=[cn], ins=[cn])
    p.op("vector", lambda v: v.reciprocal(cn, cn), outs=[cn], ins=[cn])
    cnr = ctx.sbuf("l2_cnr", [1, 16], F32)
    p.op("tensor", lambda t: t.transpose(pB[0:1, 0:16], cn, ctx.idn[0:16, 0:16]), outs=[pB], ins=[cn, ctx.idn])
    p.op("scalar", lambda s: s.copy(cnr, pB[0:1, 0:16]), outs=[cnr], ins=[pB])
    ones_row_f = ctx.sbuf("c_ones_row_f", [1, 128], F32)
    p.op("gpsimd", lambda g: g.memset(ones_row_f, 1.0), outs=[ones_row_f])
    cnbc = ctx.sbuf("l2_cnbc", [128, 16], F32)
    p.op("tensor", lambda t: t.matmul(pB[:, 0:16], ones_row_f, cnr, start=True, stop=True),
         outs=[pB], ins=[ones_row_f, cnr])
    p.op("scalar", lambda s: s.copy(cnbc, pB[:, 0:16]), outs=[cnbc], ins=[pB])
    simn = ctx.sbuf("l2_simn", [128, DC, 16], F32)
    for c in range(DC):
        p.op("vector", lambda v, c=c: v.tensor_tensor(simn[:, c, :], sim_stg[:, c, :], cnbc, op=OP.mult),
             outs=[simn], ins=[sim_stg, cnbc])
    sim_hi = ctx.sbuf("l2_sim_hi", [128, DC, 16], BF16)
    sim_lo = ctx.sbuf("l2_sim_lo", [128, DC, 16], BF16)
    p.op("vector", lambda v: v.tensor_copy(sim_hi, simn), outs=[sim_hi], ins=[simn])
    p.op("vector", lambda v: v.tensor_tensor(simsq, simn, sim_hi, op=OP.subtract), outs=[simsq], ins=[simn, sim_hi])
    p.op("vector", lambda v: v.tensor_copy(sim_lo, simsq), outs=[sim_lo], ins=[simsq])
    grow = ctx.sbuf("c_rowbuf", [1, 512], F32)
    p.dma("sync", lambda e: e.dma_start(out=grow[0:1, 0:16], in_=W["gates"][None, :]),
          outs=[grow], ins=[W["gates"]], sem_key="D:colload")
    gsig = ctx.sbuf("l2_gsig", [1, 16], F32)
    p.op("scalar", lambda s: s.activation(gsig, grow[0:1, 0:16], AF.Sigmoid), outs=[gsig], ins=[grow])
    sigbc = ctx.sbuf("l2_sigbc", [128, 16], F32)
    p.op("tensor", lambda t: t.matmul(pB[:, 0:16], ones_row_f, gsig, start=True, stop=True),
         outs=[pB], ins=[ones_row_f, gsig])
    p.op("scalar", lambda s: s.copy(sigbc, pB[:, 0:16]), outs=[sigbc], ins=[pB])

    bq = ctx.col_load("l2_bq", W["b_Q"].rearrange("e h -> (e h)"), 8)
    bk_ = ctx.col_load("l2_bk", W["b_K"].rearrange("e h -> (e h)"), 8)
    bv_row = ctx.sbuf("l2_bv_row", [1, NH * DH], BF16)
    p.dma("gpsimd", lambda e: e.dma_start(out=bv_row, in_=W["b_V"].rearrange("e h -> (e h)")[None, :]),
          outs=[bv_row], ins=[W["b_V"]])
    bo_row = ctx.sbuf("l2_bo_row", [1, D], BF16)
    p.dma("gpsimd", lambda e: e.dma_start(out=bo_row, in_=W["b_O"][None, :]),
          outs=[bo_row], ins=[W["b_O"]])

    HT = T // 2
    for b in range(B):
        rn_cols = ctx.sbuf("l2_rncols", [128, NT], F32)

        def per_tile(bb, i, xn_tile):
            var, rstd = ctx.sb["ln_var"], ctx.sb["ln_rstd"]
            t1 = ctx.sbuf("l2_rt1", [128, 1], F32)
            p.op("scalar", lambda sc, i=i: sc.activation(t1, var, AF.Sqrt, scale=float(D)),
                 outs=[t1], ins=[var])
            p.op("vector", lambda v, i=i: v.tensor_tensor(t1, t1, rstd, op=OP.mult), outs=[t1], ins=[t1, rstd])
            p.op("vector", lambda v, i=i: v.tensor_scalar(t1, t1, 1e-12, None, op0=OP.max, op1=OP.bypass),
                 outs=[t1], ins=[t1])
            p.op("vector", lambda v, i=i: v.reciprocal(rn_cols[:, i:i + 1], t1), outs=[rn_cols], ins=[t1])

        ctx.phase_norm(x_in, scr["xnb"], b, xnb_lo=scr["xnb_lo"], per_tile=per_tile)
        xn_fm = ctx.load_xn_fm_b(scr["xnb"], b)
        xn_lo_fm = ctx.view("xn_lo_fm", [128, DC, T], BF16, 0)
        for c in range(DC):
            p.dma("sync", lambda e, b=b, c=c, t=xn_lo_fm: e.dma_start_transpose(
                t[:, c, :], scr["xnb_lo"][b, :, c * 128:(c + 1) * 128]),
                outs=[xn_lo_fm], ins=[scr["xnb_lo"]], sem_key="D:xnlofm")

        rw_all = ctx.sbuf("l2_rw_all", [128, NT, 16], F32)
        for i in range(NT):
            for ci, (xs, ws) in enumerate([(xn_fm, sim_hi), (xn_lo_fm, sim_hi), (xn_fm, sim_lo)]):
                for c in range(DC):
                    p.op("tensor", lambda t, i=i, c=c, xs=xs, ws=ws, ci=ci: t.matmul(
                        pA[:, 0:16], xs[:, c, i * 128:(i + 1) * 128], ws[:, c, :],
                        start=(ci == 0 and c == 0), stop=(ci == 2 and c == DC - 1)),
                        outs=[pA], ins=[xs, ws])
            lg = ctx.sbuf("l2_lg", [128, 16], F32)
            gated = ctx.sbuf("l2_gated", [128, 16], F32)
            m1 = ctx.sbuf("l2_m1", [128, 1], F32)
            m2 = ctx.sbuf("l2_m2", [128, 1], F32)
            eq1 = ctx.sbuf("l2_eq1", [128, 16], F32)
            eq2 = ctx.sbuf("l2_eq2", [128, 16], F32)
            lg2 = ctx.sbuf("l2_lg2", [128, 16], F32)
            act01 = ctx.sbuf("l2_act01", [128, 1], F32)
            fm = ctx.sbuf("l2_fm", [128, 16], F32)
            ml = ctx.sbuf("l2_ml", [128, 16], F32)
            nm3 = ctx.sbuf("l2_nm3", [128, 1], F32)
            zs = ctx.sbuf("l2_zs", [128, 1], F32)
            p.op("vector", lambda v, i=i: v.tensor_scalar(lg, pA[:, 0:16], rn_cols[:, i:i + 1], None,
                                                          op0=OP.mult, op1=OP.bypass), outs=[lg], ins=[pA, rn_cols])
            p.op("vector", lambda v: v.tensor_tensor(lg, lg, sigbc, op=OP.subtract), outs=[lg], ins=[lg, sigbc])
            p.op("vector", lambda v: v.tensor_scalar(gated, lg, 0.0, None, op0=OP.max, op1=OP.bypass),
                 outs=[gated], ins=[lg])
            p.op("vector", lambda v: v.tensor_reduce(m1, lg, axis=AX.X, op=OP.max), outs=[m1], ins=[lg])
            p.op("vector", lambda v: v.tensor_scalar(eq1, lg, m1, None, op0=OP.is_ge, op1=OP.bypass),
                 outs=[eq1], ins=[lg, m1])
            p.op("vector", lambda v: v.scalar_tensor_tensor(lg2, eq1, -1e30, lg, op0=OP.mult, op1=OP.add),
                 outs=[lg2], ins=[eq1, lg])
            p.op("vector", lambda v: v.tensor_reduce(m2, lg2, axis=AX.X, op=OP.max), outs=[m2], ins=[lg2])
            p.op("vector", lambda v: v.tensor_scalar(eq2, lg2, m2, None, op0=OP.is_ge, op1=OP.bypass),
                 outs=[eq2], ins=[lg2, m2])
            p.op("vector", lambda v: v.tensor_scalar(act01, m1, 0.0, None, op0=OP.is_gt, op1=OP.bypass),
                 outs=[act01], ins=[m1])
            p.op("vector", lambda v: v.tensor_tensor(fm, eq1, eq2, op=OP.add), outs=[fm], ins=[eq1, eq2])
            p.op("vector", lambda v: v.tensor_scalar(ml, lg, 0.0, None, op0=OP.is_gt, op1=OP.bypass),
                 outs=[ml], ins=[lg])
            p.op("vector", lambda v: v.tensor_tensor(ml, ml, fm, op=OP.subtract), outs=[ml], ins=[ml, fm])
            p.op("vector", lambda v: v.tensor_scalar(ml, ml, act01, None, op0=OP.mult, op1=OP.bypass),
                 outs=[ml], ins=[ml, act01])
            p.op("vector", lambda v: v.tensor_tensor(fm, fm, ml, op=OP.add), outs=[fm], ins=[fm, ml])
            p.op("vector", lambda v: v.tensor_scalar(ml, fm, 1e30, -1e30, op0=OP.mult, op1=OP.add),
                 outs=[ml], ins=[fm])
            p.op("vector", lambda v: v.scalar_tensor_tensor(ml, gated, 1.0, ml, op0=OP.mult, op1=OP.add)
                 if False else v.tensor_tensor(lg2, gated, fm, op=OP.mult), outs=[lg2], ins=[gated, fm])
            p.op("vector", lambda v: v.tensor_tensor(ml, ml, lg2, op=OP.add), outs=[ml], ins=[ml, lg2])
            p.op("vector", lambda v: v.tensor_reduce(nm3, ml, axis=AX.X, op=OP.max, negate=True),
                 outs=[nm3], ins=[ml])
            p.op("scalar", lambda s, i=i: s.activation(rw_all[:, i, :], ml, AF.Exp, bias=nm3,
                                                       accum_out=zs), outs=[rw_all, zs], ins=[ml, nm3])
            p.op("vector", lambda v: v.reciprocal(zs, zs), outs=[zs], ins=[zs])
            p.op("vector", lambda v, i=i: v.tensor_scalar(rw_all[:, i, :], rw_all[:, i, :], zs, None,
                                                          op0=OP.mult, op1=OP.bypass),
                 outs=[rw_all], ins=[rw_all, zs])

        wv = ctx.view("l2_wv", [128, DC, NH, DH], BF16, 0)
        for c in range(DC):
            p.dma("gpsimd", lambda e, c=c: e.dma_start(
                out=wv[:, c, :, :], in_=W["W_V"].rearrange("e (a p) h -> a p e h", p=128)[c]),
                outs=[wv], ins=[W["W_V"]], sem_key="D:l2wv")
        v_tm = ctx.view("l2_vtm", [128, NT, NH * DH], BF16, 16384)
        for i in range(NT):
            for half in range(2):
                for c in range(DC):
                    p.op("tensor", lambda t, i=i, half=half, c=c, xn_fm=xn_fm: t.matmul(
                        pA[:, 0:512], xn_fm[:, c, i * 128:(i + 1) * 128],
                        wv[:, c, half * 8:(half + 1) * 8, :], start=(c == 0), stop=False),
                        outs=[pA], ins=[xn_fm, wv])
                p.op("tensor", lambda t, i=i, half=half: t.matmul(
                    pA[:, 0:512], ctx.ones_row[0:1, 0:128], bv_row[0:1, half * 512:(half + 1) * 512],
                    start=False, stop=True), outs=[pA], ins=[ctx.ones_row, bv_row])
                p.op("scalar", lambda s, i=i, half=half: s.copy(
                    v_tm[:, i, half * 512:(half + 1) * 512], pA[:, 0:512]), outs=[v_tm], ins=[pA])
        attnout = ctx.view("l2_attnout", [128, NHP, T], BF16, 49152)
        for hp in range(NHP):
            wqs = ctx.view(f"l2_wqs{hp % 2}", [128, DC, 128], BF16, 81920 + (hp % 2) * 2048)
            wks = ctx.view(f"l2_wks{hp % 2}", [128, DC, 128], BF16, 86016 + (hp % 2) * 2048)
            for c in range(DC):
                p.dma("gpsimd", lambda e, hp=hp, c=c, wqs=wqs: e.dma_start(
                    out=wqs[:, c, :].rearrange("p (e h) -> p e h", e=2),
                    in_=W["W_Q"][2 * hp:2 * hp + 2].rearrange("e (a p) h -> a p e h", p=128)[c]),
                    outs=[wqs], ins=[W["W_Q"]], sem_key=f"D:l2wq{hp % 2}")
                p.dma("gpsimd", lambda e, hp=hp, c=c, wks=wks: e.dma_start(
                    out=wks[:, c, :].rearrange("p (e h) -> p e h", e=2),
                    in_=W["W_K"][2 * hp:2 * hp + 2].rearrange("e (a p) h -> a p e h", p=128)[c]),
                    outs=[wks], ins=[W["W_K"]], sem_key=f"D:l2wk{hp % 2}")
            q_fm = ctx.view("l2_qfm", [128, T], BF16, 90112)
            k_fm = ctx.view("l2_kfm", [128, T], BF16, 94208)
            for tc in range(T // 512):
                for c in range(DC):
                    p.op("tensor", lambda t, tc=tc, c=c, wqs=wqs, xn_fm=xn_fm: t.matmul(
                        pA[:, 0:512], wqs[:, c, :], xn_fm[:, c, tc * 512:(tc + 1) * 512],
                        start=(c == 0), stop=(c == DC - 1)), outs=[pA], ins=[wqs, xn_fm])
                p.op("scalar", lambda s, tc=tc, hp=hp: s.activation(
                    q_fm[:, tc * 512:(tc + 1) * 512], pA[:, 0:512], AF.Copy, bias=0.0, scale=0.125),
                    outs=[q_fm], ins=[pA])
                for c in range(DC):
                    p.op("tensor", lambda t, tc=tc, c=c, wks=wks, xn_fm=xn_fm: t.matmul(
                        pA[:, 0:512], wks[:, c, :], xn_fm[:, c, tc * 512:(tc + 1) * 512],
                        start=(c == 0), stop=(c == DC - 1)), outs=[pA], ins=[wks, xn_fm])
                p.op("scalar", lambda s, tc=tc, hp=hp: s.copy(k_fm[:, tc * 512:(tc + 1) * 512], pA[:, 0:512]),
                     outs=[k_fm], ins=[pA])
            bqc = ctx.sbuf("l2_bqc", [128, 1], F32)
            p.op("vector", lambda v, hp=hp: v.tensor_scalar(bqc, bq[:, hp:hp + 1], 0.125, None,
                                                            op0=OP.mult, op1=OP.bypass), outs=[bqc], ins=[bq])
            for tc in range(T // 512):
                p.op("vector", lambda v, tc=tc: v.tensor_scalar(
                    q_fm[:, tc * 512:(tc + 1) * 512], q_fm[:, tc * 512:(tc + 1) * 512], bqc, None,
                    op0=OP.add, op1=OP.bypass), outs=[q_fm], ins=[q_fm, bqc])
                p.op("vector", lambda v, tc=tc, hp=hp: v.tensor_scalar(
                    k_fm[:, tc * 512:(tc + 1) * 512], k_fm[:, tc * 512:(tc + 1) * 512], bk_[:, hp:hp + 1],
                    None, op0=OP.add, op1=OP.bypass), outs=[k_fm], ins=[k_fm, bk_])
            for hh in range(2):
                h = 2 * hp + hh
                hb64 = hh * 64
                mcols = ctx.sbuf("l2_mcols", [128, NT], F32)
                mtmp = ctx.sbuf("l2_mtmp", [128, 4], F32)
                for i in range(NT):
                    nk = (i + 1) * 128
                    nchunk = (nk + 511) // 512
                    for j in range(nchunk):
                        n = min(512, nk - j * 512)
                        p.op("tensor", lambda t, i=i, j=j, n=n, hb64=hb64: t.matmul(
                            pA[:, 0:n], q_fm[hb64:hb64 + 64, i * 128:(i + 1) * 128],
                            k_fm[hb64:hb64 + 64, j * 512:j * 512 + n], start=True, stop=True),
                            outs=[pA], ins=[q_fm, k_fm])
                        p.op("vector", lambda v, j=j, n=n: v.tensor_reduce(
                            mtmp[:, j:j + 1], pA[:, 0:n], axis=AX.X, op=OP.max), outs=[mtmp], ins=[pA])
                    p.op("vector", lambda v, i=i, nchunk=nchunk: v.tensor_reduce(
                        mcols[:, i:i + 1], mtmp[:, 0:nchunk], axis=AX.X, op=OP.max), outs=[mcols], ins=[mtmp])
                negm = ctx.sbuf("l2_negm", [128, 1], F32)
                ctx.neg_scalar_max(mcols, NT, negm, bank=1)
                pS, pO, pZ = ctx.bk(2), ctx.bk(3), ctx.bk(4)
                PT = ctx.sbuf("l2_PT", [128, 512], BF16)
                zrow = ctx.sbuf("l2_zrow", [1, 512], F32)
                zbc = ctx.sbuf("l2_zbc", [128, 512], F32)
                for qc in range(T // 512):
                    tq0 = qc * 512
                    ktiles = list(range(0, (tq0 + 512) // 128))
                    for kk, k in enumerate(ktiles):
                        st, sp = (kk == 0), (kk == len(ktiles) - 1)
                        p.op("tensor", lambda t, k=k, tq0=tq0, hb64=hb64: t.matmul(
                            pS[:, 0:512], k_fm[hb64:hb64 + 64, k * 128:(k + 1) * 128],
                            q_fm[hb64:hb64 + 64, tq0:tq0 + 512], start=True, stop=True),
                            outs=[pS], ins=[k_fm, q_fm])
                        p.op("scalar", lambda s: s.activation(PT, pS[:, 0:512], AF.Exp, bias=negm),
                             outs=[PT], ins=[pS, negm])
                        dsub = k - tq0 // 128
                        if 0 <= dsub < 4:
                            p.op("vector", lambda v, dsub=dsub: v.tensor_tensor(
                                PT[:, dsub * 128:(dsub + 1) * 128], PT[:, dsub * 128:(dsub + 1) * 128],
                                ctx.tri, op=OP.mult), outs=[PT], ins=[PT, ctx.tri])
                            for z in range(dsub):
                                p.op("gpsimd", lambda g, z=z: g.memset(PT[:, z * 128:(z + 1) * 128], 0.0),
                                     outs=[PT])
                        p.op("tensor", lambda t, k=k, st=st, sp=sp, h=h: t.matmul(
                            pO[0:64, 0:512], v_tm[:, k, h * 64:(h + 1) * 64], PT, start=st, stop=sp),
                            outs=[pO], ins=[v_tm, PT])
                        p.op("tensor", lambda t, st=st, sp=sp: t.matmul(
                            pZ[0:1, 0:512], ctx.ones_col, PT, start=st, stop=sp),
                            outs=[pZ], ins=[ctx.ones_col, PT])
                    p.op("vector", lambda v: v.reciprocal(zrow, pZ[0:1, 0:512]), outs=[zrow], ins=[pZ])
                    zrow_bf = ctx.sbuf("l2_zrow_bf", [1, 512], BF16)
                    p.op("vector", lambda v: v.tensor_copy(zrow_bf, zrow), outs=[zrow_bf], ins=[zrow])
                    p.op("tensor", lambda t: t.matmul(pZ[:, 0:512], ctx.ones_row[0:1, 0:128], zrow_bf,
                                                      start=True, stop=True), outs=[pZ], ins=[ctx.ones_row, zrow_bf])
                    p.op("scalar", lambda s: s.copy(zbc, pZ[:, 0:512]), outs=[zbc], ins=[pZ])
                    p.op("vector", lambda v, hp=hp, hb64=hb64, tq0=tq0: v.tensor_tensor(
                        attnout[hb64:hb64 + 64, hp, tq0:tq0 + 512], pO[0:64, 0:512], zbc[0:64, :], op=OP.mult),
                        outs=[attnout], ins=[pO, zbc])
        wo = ctx.view("l2_wo", [128, NHP, D], BF16, 98304)
        p.dma("gpsimd", lambda e: e.dma_start(
            out=wo.rearrange("p a d -> p a d"),
            in_=W["W_O"].rearrange("(a e2) h d -> (e2 h) a d", e2=2)), outs=[wo], ins=[W["W_O"]],
            sem_key="D:l2wo")
        s_t = ctx.sbuf("ln_sq", [128, D], F32)
        for i in range(NT):
            for half in range(2):
                for hc in range(NHP):
                    p.op("tensor", lambda t, i=i, half=half, hc=hc: t.matmul(
                        pA[:, 0:512], attnout[:, hc, i * 128:(i + 1) * 128],
                        wo[:, hc, half * 512:(half + 1) * 512],
                        start=(hc == 0), stop=False), outs=[pA], ins=[attnout, wo])
                p.op("tensor", lambda t, half=half: t.matmul(
                    pA[:, 0:512], ctx.ones_row[0:1, 0:128], bo_row[0:1, half * 512:(half + 1) * 512],
                    start=False, stop=True), outs=[pA], ins=[ctx.ones_row, bo_row])
                p.op("vector", lambda v, half=half: v.tensor_copy(
                    s_t[:, half * 512:(half + 1) * 512], pA[:, 0:512]), outs=[s_t], ins=[pA])
            p.dma("scalar", lambda e, i=i, b=b: e.dma_start(
                out=scr["s"][b, i * 128:(i + 1) * 128, :], in_=s_t), outs=[scr["s"]], ins=[s_t],
                sem_key="D:l2s")

        wq3 = ctx.view("l2_wq3", [128, DC, NH, DH], BF16, 0)
        wk3 = ctx.view("l2_wk3", [128, DC, NH, DH], BF16, 16384)
        wv3 = ctx.view("l2_wv3", [128, DC, NH, DH], BF16, 32768)
        for nm, dst, src in (("q", wq3, "q_proj"), ("k", wk3, "k_proj"), ("v", wv3, "v_proj")):
            for c in range(DC):
                p.dma("gpsimd", lambda e, dst=dst, src=src, c=c: e.dma_start(
                    out=dst[:, c, :, :], in_=W[src].rearrange("e (a p) h -> a p e h", p=128)[c]),
                    outs=[dst], ins=[W[src]], sem_key=f"D:l2w3{nm}")
        q2f = ctx.view("l2_q2f", [64, T], BF16, 73728)
        k2f = ctx.view("l2_k2f", [64, T], BF16, 77824)
        v2a = ctx.view("l2_v2a", [128, NT, DH + 1], BF16, 81920)
        mixq = ctx.sbuf("l2_mixq", [128, DH], F32)
        mixk = ctx.sbuf("l2_mixk", [128, DH], F32)
        mixv = ctx.sbuf("l2_mixv", [128, DH], F32)
        pall = ctx.view("l2_pall", [128, 3, NH * DH], F32, 86016)
        for i in range(NT):
            for ti, (w3, mx) in enumerate([(wq3, mixq), (wk3, mixk), (wv3, mixv)]):
                for half in range(2):
                    for c in range(DC):
                        p.op("tensor", lambda t, i=i, ti=ti, half=half, c=c, w3=w3, xn_fm=xn_fm: t.matmul(
                            pA[:, 0:512], xn_fm[:, c, i * 128:(i + 1) * 128],
                            w3[:, c, half * 8:(half + 1) * 8, :], start=(c == 0), stop=(c == DC - 1)),
                            outs=[pA], ins=[xn_fm, w3])
                    p.op("scalar", lambda s, i=i, ti=ti, half=half: s.copy(
                        pall[:, ti, half * 512:(half + 1) * 512], pA[:, 0:512]), outs=[pall], ins=[pA])
                for e in range(NH):
                    if e == 0:
                        p.op("vector", lambda v, i=i, ti=ti, mx=mx: v.tensor_scalar(
                            mx, pall[:, ti, 0:DH], rw_all[:, i, 0:1], None, op0=OP.mult, op1=OP.bypass),
                            outs=[mx], ins=[pall, rw_all])
                    else:
                        p.op("vector", lambda v, i=i, ti=ti, e=e, mx=mx: v.scalar_tensor_tensor(
                            mx, pall[:, ti, e * DH:(e + 1) * DH], rw_all[:, i, e:e + 1], mx,
                            op0=OP.mult, op1=OP.add), outs=[mx], ins=[pall, rw_all, mx])
            p.op("vector", lambda v, i=i: v.tensor_scalar(mixq, mixq, 0.125, None, op0=OP.mult, op1=OP.bypass),
                 outs=[mixq], ins=[mixq])
            p.op("tensor", lambda t, i=i: t.transpose(pB[0:64, 0:128], mixq, ctx.idn), outs=[pB], ins=[mixq, ctx.idn])
            p.op("scalar", lambda s, i=i: s.copy(q2f[:, i * 128:(i + 1) * 128], pB[0:64, 0:128]),
                 outs=[q2f], ins=[pB])
            p.op("tensor", lambda t, i=i: t.transpose(pB[0:64, 0:128], mixk, ctx.idn), outs=[pB], ins=[mixk, ctx.idn])
            p.op("scalar", lambda s, i=i: s.copy(k2f[:, i * 128:(i + 1) * 128], pB[0:64, 0:128]),
                 outs=[k2f], ins=[pB])
            p.op("vector", lambda v, i=i: v.tensor_copy(v2a[:, i, 0:DH], mixv), outs=[v2a], ins=[mixv])
            p.op("gpsimd", lambda g, i=i: g.memset(v2a[:, i, DH:DH + 1], 1.0), outs=[v2a])
        mcols = ctx.sbuf("l2_mcols", [128, NT], F32)
        mtmp = ctx.sbuf("l2_mtmp", [128, 4], F32)
        for i in range(NT):
            nk = (i + 1) * 128
            nchunk = (nk + 511) // 512
            for j in range(nchunk):
                n = min(512, nk - j * 512)
                p.op("tensor", lambda t, i=i, j=j, n=n: t.matmul(
                    pA[:, 0:n], q2f[:, i * 128:(i + 1) * 128], k2f[:, j * 512:j * 512 + n],
                    start=True, stop=True), outs=[pA], ins=[q2f, k2f])
                p.op("vector", lambda v, j=j, n=n: v.tensor_reduce(
                    mtmp[:, j:j + 1], pA[:, 0:n], axis=AX.X, op=OP.max), outs=[mtmp], ins=[pA])
            p.op("vector", lambda v, i=i, nchunk=nchunk: v.tensor_reduce(
                mcols[:, i:i + 1], mtmp[:, 0:nchunk], axis=AX.X, op=OP.max), outs=[mcols], ins=[mtmp])
        negm = ctx.sbuf("l2_negm", [128, 1], F32)
        ctx.neg_scalar_max(mcols, NT, negm, bank=1)
        pS = ctx.bk(2)
        PT = ctx.sbuf("l2_PT", [128, 512], BF16)
        wo3 = ctx.view("l2_wo3", [128, DC, D], BF16, 49152)
        p.dma("gpsimd", lambda e: e.dma_start(
            out=wo3, in_=W["o_proj"].rearrange("(a e2) h d -> (e2 h) a d", e2=2)), outs=[wo3],
            ins=[W["o_proj"]], sem_key="D:l2wo3")
        att2 = ctx.sbuf("l2_att2", [128, DH], F32)
        z2 = ctx.sbuf("l2_z2", [128, NH, DH], F32)
        z2t = ctx.view("l2_z2t", [128, DC, 128], BF16, 66560)
        for qc in range(T // 512):
            tq0 = qc * 512
            ktiles = list(range(0, (tq0 + 512) // 128))
            for kk, k in enumerate(ktiles):
                st, sp = (kk == 0), (kk == len(ktiles) - 1)
                p.op("tensor", lambda t, k=k, tq0=tq0: t.matmul(
                    pS[:, 0:512], k2f[:, k * 128:(k + 1) * 128], q2f[:, tq0:tq0 + 512],
                    start=True, stop=True), outs=[pS], ins=[k2f, q2f])
                p.op("scalar", lambda s: s.activation(PT, pS[:, 0:512], AF.Exp, bias=negm),
                     outs=[PT], ins=[pS, negm])
                dsub = k - tq0 // 128
                if 0 <= dsub < 4:
                    p.op("vector", lambda v, dsub=dsub: v.tensor_tensor(
                        PT[:, dsub * 128:(dsub + 1) * 128], PT[:, dsub * 128:(dsub + 1) * 128],
                        ctx.tri, op=OP.mult), outs=[PT], ins=[PT, ctx.tri])
                    for z in range(dsub):
                        p.op("gpsimd", lambda g, z=z: g.memset(PT[:, z * 128:(z + 1) * 128], 0.0), outs=[PT])
                for sub in range(4):
                    p.op("tensor", lambda t, k=k, sub=sub, st=st, sp=sp: t.matmul(
                        ctx.bk(3 + sub)[:, 0:DH + 1], PT[:, sub * 128:(sub + 1) * 128], v2a[:, k, :],
                        start=st, stop=sp), outs=[ctx.bk(3 + sub)], ins=[PT, v2a])
            for sub in range(4):
                i = qc * 4 + sub
                rz = ctx.sbuf("l2_rz", [128, 1], F32)
                p.op("vector", lambda v, sub=sub: v.reciprocal(rz, ctx.bk(3 + sub)[:, DH:DH + 1]),
                     outs=[rz], ins=[ctx.bk(3 + sub)])
                p.op("vector", lambda v, sub=sub: v.tensor_scalar(att2, ctx.bk(3 + sub)[:, 0:DH], rz, None,
                                                                  op0=OP.mult, op1=OP.bypass),
                     outs=[att2], ins=[ctx.bk(3 + sub), rz])
                for e in range(NH):
                    p.op("vector", lambda v, i=i, e=e: v.tensor_scalar(
                        z2[:, e, :], att2, rw_all[:, i, e:e + 1], None, op0=OP.mult, op1=OP.bypass),
                        outs=[z2], ins=[att2, rw_all])
                for c in range(DC):
                    p.op("tensor", lambda t, c=c: t.transpose(pB[:, 0:128], z2[:, 2 * c:2 * c + 2, :], ctx.idn),
                         outs=[pB], ins=[z2, ctx.idn])
                    p.op("scalar", lambda s, c=c: s.copy(z2t[:, c, :], pB[:, 0:128]), outs=[z2t], ins=[pB])
                xt = ctx.sbuf(f"pn_x{sub % ctx.pnb}", [128, D], F32)
                st_ = ctx.sbuf(f"pn_xn{sub % ctx.pnb}", [128, D], F32)
                gt = ctx.sbuf("l2_ep_g", [128, D], F32)
                gs = ctx.sbuf("ln_sq", [128, D], F32)
                p.dma("sync", lambda e, b=b, i=i, xt=xt: e.dma_start(
                    out=xt, in_=x_in[b, i * 128:(i + 1) * 128, :]), outs=[xt], ins=[x_in])
                p.dma("sync", lambda e, b=b, i=i, st_=st_: e.dma_start(
                    out=st_, in_=scr["s"][b, i * 128:(i + 1) * 128, :]), outs=[st_], ins=[scr["s"]],
                    sem_key="D:l2s_rd")
                for half in range(2):
                    for c in range(DC):
                        p.op("tensor", lambda t, half=half, c=c: t.matmul(
                            pA[:, 0:512], z2t[:, c, :], wo3[:, c, half * 512:(half + 1) * 512],
                            start=(c == 0), stop=(c == DC - 1)), outs=[pA], ins=[z2t, wo3])
                    p.op("vector", lambda v, half=half, gt=gt: v.tensor_copy(
                        gt[:, half * 512:(half + 1) * 512], pA[:, 0:512]), outs=[gt], ins=[pA])
                p.op("scalar", lambda s, gt=gt, gs=gs: s.activation(gs, gt, AF.Silu), outs=[gs], ins=[gt])
                p.op("vector", lambda v, gs=gs, st_=st_: v.tensor_tensor(gs, gs, st_, op=OP.mult),
                     outs=[gs], ins=[gs, st_])
                p.op("vector", lambda v, gs=gs, xt=xt: v.tensor_tensor(gs, gs, xt, op=OP.add),
                     outs=[gs], ins=[gs, xt])
                p.dma("scalar", lambda e, b=b, i=i, gs=gs: e.dma_start(
                    out=y_out[b, i * 128:(i + 1) * 128, :], in_=gs), outs=[y_out], ins=[gs], sem_key="D:y")



def build_l3(ctx, x_in, y_out, W, scr):
    ctx.pnb = 1
    import os
    PH = int(os.environ.get("L3PH", "99"))
    p, nc = ctx.p, ctx.nc
    B, T, D, NT, DC = ctx.B, ctx.T, ctx.D, ctx.NT, ctx.DC
    NHEAD, C, DV = 8, 64, 128
    NG = 4
    L = 8
    NCH = T // L
    pA, pB = ctx.bk(0), ctx.bk(1)

    WaT = ctx.load_transposed("l3_WaT", W["Wa_w"], 512, D, stage_off=0)
    WqlT = ctx.load_transposed("l3_Wql", W["Wq"], C, D, stage_off=0)
    WklT = ctx.load_transposed("l3_Wkl", W["Wk"], C, D, stage_off=0)
    WvlT = ctx.load_transposed("l3_Wvl", W["Wv"], C, D, stage_off=0)
    wab = ctx.col_load("l3_wab", W["Wa_b"], NG)
    gmc = ctx.col_load("l3_gmc", W["gamma"], DC)
    btc = ctx.col_load("l3_btc", W["beta"], DC)
    mur = ctx.col_load("l3_mur", W["mu_r"], DC)
    muk = ctx.col_load("l3_muk", W["mu_k"], DC)
    Qc_sb = ctx.sbuf("l3_Qc", [C, NHEAD, C], BF16)
    p.dma("gpsimd", lambda e: e.dma_start(out=Qc_sb, in_=W["Qc"].rearrange("h c e -> c h e")),
          outs=[Qc_sb], ins=[W["Qc"]])
    Vc_sb = ctx.sbuf("l3_Vc", [C, NHEAD, DV], BF16)
    p.dma("gpsimd", lambda e: e.dma_start(out=Vc_sb, in_=W["Vc"].rearrange("h c d -> c h d")),
          outs=[Vc_sb], ins=[W["Vc"]])
    t8 = ctx.view("l3_t8", [128, T], F32, 0)
    p.op("gpsimd", lambda g: g.iota(t8, base=0, pattern=[[0, T // L], [1, L]], channel_multiplier=0,
                                    allow_small_or_imprecise_dtypes=True), outs=[t8])
    ms = ctx.sbuf("l3_ms", [128, T], BF16)
    p.op("vector", lambda v: v.tensor_scalar(ms, t8, 0.5, None, op0=OP.is_lt, op1=OP.bypass),
         outs=[ms], ins=[t8])
    bo2 = ctx.sbuf("l3_bo2", [128, 2], BF16)
    p.op("vector", lambda v: v.tensor_scalar(bo2[:, 0:1], ctx.pidx, 63.5, None, op0=OP.is_lt, op1=OP.bypass),
         outs=[bo2], ins=[ctx.pidx])
    p.op("vector", lambda v: v.tensor_scalar(bo2[:, 1:2], ctx.pidx, 63.5, None, op0=OP.is_gt, op1=OP.bypass),
         outs=[bo2], ins=[ctx.pidx])
    t2 = ctx.sbuf("l3_t2", [2, 128], F32)
    p.op("gpsimd", lambda g: g.iota(t2, base=0, pattern=[[0, 128]], channel_multiplier=1,
                                    allow_small_or_imprecise_dtypes=True), outs=[t2])
    sel = [ctx.sbuf(f"l3_sel{hh}", [2, 128], BF16) for hh in range(2)]
    p.op("vector", lambda v: v.tensor_scalar(sel[0], t2, 0.5, None, op0=OP.is_lt, op1=OP.bypass),
         outs=[sel[0]], ins=[t2])
    p.op("vector", lambda v: v.tensor_scalar(sel[1], t2, 0.5, None, op0=OP.is_gt, op1=OP.bypass),
         outs=[sel[1]], ins=[t2])

    D_cols = ctx.sbuf("l3_Dcols", [128, NG, NCH], F32)

    for b in range(B):
        f_cols = ctx.sbuf("l3_fcols", [128, NT], F32)

        def per_tile(bb, i, xn_tile):
            var, rstd = ctx.sb["ln_var"], ctx.sb["ln_rstd"]
            t1 = ctx.sbuf("l3_ft1", [128, 1], F32)
            p.op("vector", lambda v, i=i: v.tensor_tensor(t1, rstd, rstd, op=OP.mult), outs=[t1], ins=[rstd])
            p.op("vector", lambda v, i=i: v.tensor_tensor(t1, t1, var, op=OP.mult), outs=[t1], ins=[t1, var])
            p.op("scalar", lambda sc, i=i: sc.activation(t1, t1, AF.Sqrt, bias=ctx.sb["ln_eps"]),
                 outs=[t1], ins=[t1])
            p.op("vector", lambda v, i=i: v.reciprocal(f_cols[:, i:i + 1], t1), outs=[f_cols], ins=[t1])

        ctx.phase_norm(x_in, scr["xnb"], b, per_tile=per_tile, xnf=scr["xnf"])
        xn_fm = ctx.view("xn_fm_l3", [128, DC, T], BF16, 0)
        for c in range(DC):
            p.dma("sync", lambda e, b=b, c=c, t=xn_fm: e.dma_start_transpose(
                t[:, c, :], scr["xnb"][b, :, c * 128:(c + 1) * 128]),
                outs=[xn_fm], ins=[scr["xnb"]], sem_key="D:xn_fm")


        xn2 = ctx.view("l3_xn2", [128, DC, T], BF16, 77824)
        fbc = ctx.sbuf("l3_fbc", [128, 512], BF16)
        frow = ctx.sbuf("l3_rowb2", [1, 512], BF16)
        for tc in range(T // 512):
            for sub in range(4):
                i = tc * 4 + sub
                p.op("tensor", lambda t, i=i: t.transpose(pB[0:1, 0:128], f_cols[:, i:i + 1], ctx.idn),
                     outs=[pB], ins=[f_cols, ctx.idn])
                p.op("scalar", lambda s, sub=sub: s.copy(frow[0:1, sub * 128:(sub + 1) * 128], pB[0:1, 0:128]),
                     outs=[frow], ins=[pB])
            p.op("tensor", lambda t, tc=tc: t.matmul(pA[:, 0:512], ctx.ones_row[0:1, 0:128],
                                                     frow, start=True, stop=True),
                 outs=[pA], ins=[ctx.ones_row, frow])
            p.op("scalar", lambda s, tc=tc: s.copy(fbc, pA[:, 0:512]), outs=[fbc], ins=[pA])
            for c in range(DC):
                t1 = ctx.sbuf("l3_otmp", [128, 512], F32)
                p.op("vector", lambda v, c=c, tc=tc, t1=t1: v.tensor_tensor(
                    t1, xn_fm[:, c, tc * 512:(tc + 1) * 512], fbc, op=OP.mult), outs=[t1], ins=[xn_fm, fbc])
                p.op("vector", lambda v, c=c, tc=tc, t1=t1: v.tensor_scalar(
                    xn2[:, c, tc * 512:(tc + 1) * 512], t1, gmc[:, c:c + 1], btc[:, c:c + 1],
                    op0=OP.mult, op1=OP.add), outs=[xn2], ins=[t1, gmc, btc])
        q_lat = ctx.view("l3_qlat", [64, T], BF16, 65536)
        k_lat = ctx.view("l3_klat", [64, T], BF16, 69632)
        v_lat = ctx.view("l3_vlat", [64, T], BF16, 73728)
        for wT, dst in ((WqlT, q_lat), (WklT, k_lat), (WvlT, v_lat)):
            for tc in range(T // 512):
                for c in range(DC):
                    p.op("tensor", lambda t, tc=tc, c=c, wT=wT: t.matmul(
                        pA[0:64, 0:512], wT[:, c, :], xn2[:, c, tc * 512:(tc + 1) * 512],
                        start=(c == 0), stop=(c == DC - 1)), outs=[pA], ins=[wT, xn2])
                p.op("scalar", lambda s, tc=tc, dst=dst: s.copy(dst[:, tc * 512:(tc + 1) * 512], pA[0:64, 0:512]),
                     outs=[dst], ins=[pA])
        a_all = ctx.view("l3_aall", [128, NG, T], F32, 32768)
        for g in range(NG):
            for tc in range(T // 512):
                for c in range(DC):
                    p.op("tensor", lambda t, g=g, tc=tc, c=c: t.matmul(
                        pA[:, 0:512], WaT[:, c, g * 128:(g + 1) * 128], xn2[:, c, tc * 512:(tc + 1) * 512],
                        start=(c == 0), stop=(c == DC - 1)), outs=[pA], ins=[WaT, xn2])
                p.op("scalar", lambda s, g=g, tc=tc: s.activation(
                    a_all[:, g, tc * 512:(tc + 1) * 512], pA[:, 0:512], AF.Sigmoid, bias=wab[:, g:g + 1]),
                    outs=[a_all], ins=[pA, wab])
        rknbc = ctx.sbuf("l3_rknbc", [128, T], BF16)
        rkn_cols = ctx.sbuf("l3_rkncols", [128, NT], F32)
        rowb = ctx.sbuf("l3_rowb", [1, 512], F32)
        rowb2 = ctx.sbuf("l3_rowb2", [1, 512], BF16)
        ksq = ctx.sbuf("l3_otmp", [128, 512], F32)[0:64, :]
        ksqb = ctx.sbuf("l3_ksqb", [64, 512], BF16)
        for tc in range(T // 512):
            p.op("scalar", lambda s, tc=tc: s.activation(ksq, k_lat[:, tc * 512:(tc + 1) * 512], AF.Square),
                 outs=[ksq], ins=[k_lat])
            p.op("vector", lambda v: v.tensor_copy(ksqb, ksq), outs=[ksqb], ins=[ksq])
            p.op("tensor", lambda t, tc=tc: t.matmul(pB[0:1, 0:512], ctx.ones_col[0:64, :], ksqb,
                                                     start=True, stop=True), outs=[pB], ins=[ctx.ones_col, ksqb])
            p.op("scalar", lambda s, tc=tc: s.activation(rowb, pB[0:1, 0:512], AF.Sqrt),
                 outs=[rowb], ins=[pB])
            p.op("vector", lambda v: v.tensor_scalar(rowb, rowb, 1e-12, None, op0=OP.max, op1=OP.bypass),
                 outs=[rowb], ins=[rowb])
            p.op("vector", lambda v: v.reciprocal(rowb, rowb), outs=[rowb], ins=[rowb])
            p.op("vector", lambda v: v.tensor_copy(rowb2, rowb), outs=[rowb2], ins=[rowb])
            for sub in range(4):
                i = tc * 4 + sub
                p.op("tensor", lambda t, sub=sub: t.transpose(
                    pB[0:128, 0:1], rowb[0:1, sub * 128:(sub + 1) * 128], ctx.idn[0:1, 0:1]),
                    outs=[pB], ins=[rowb, ctx.idn])
                p.op("scalar", lambda s, i=i: s.copy(rkn_cols[:, i:i + 1], pB[0:128, 0:1]),
                     outs=[rkn_cols], ins=[pB])
            p.op("tensor", lambda t, tc=tc: t.matmul(pB[:, 0:512], ctx.ones_row[0:1, 0:128], rowb2,
                                                     start=True, stop=True), outs=[pB], ins=[ctx.ones_row, rowb2])
            p.op("scalar", lambda s, tc=tc: s.copy(rknbc[:, tc * 512:(tc + 1) * 512], pB[:, 0:512]),
                 outs=[rknbc], ins=[pB])
        k_tm2 = ctx.sbuf("l3_ktm2", [128, NT, 128], BF16)
        for i in range(NT):
            for c in range(DC):
                p.op("tensor", lambda t, i=i, c=c: t.matmul(
                    pA[:, 0:64], xn2[:, c, i * 128:(i + 1) * 128], WklT[:, c, :],
                    start=(c == 0), stop=(c == DC - 1)), outs=[pA], ins=[xn2, WklT])
            p.op("scalar", lambda s, i=i: s.copy(k_tm2[:, i, 0:64], pA[:, 0:64]), outs=[k_tm2], ins=[pA])
            p.op("scalar", lambda s, i=i: s.copy(k_tm2[:, i, 64:128], pA[:, 0:64]), outs=[k_tm2], ins=[pA])
        for h in range(NHEAD):
            for i in range(NT):
                p.op("tensor", lambda t, h=h, i=i: t.matmul(
                    pA[:, 0:DV], v_lat[:, i * 128:(i + 1) * 128], Vc_sb[:, h, :],
                    start=True, stop=True), outs=[pA], ins=[v_lat, Vc_sb])
                vt = ctx.sbuf(f"l3_vt{i % 2}", [128, DV], BF16)
                p.op("vector", lambda v, i=i, vt=vt: v.tensor_scalar(
                    vt, pA[:, 0:DV], rkn_cols[:, i:i + 1], None, op0=OP.mult, op1=OP.bypass),
                    outs=[vt], ins=[pA, rkn_cols])
                p.dma("scalar", lambda e, h=h, i=i, vt=vt: e.dma_start(
                    out=scr["vsd"][i * 128:(i + 1) * 128, h, :], in_=vt), outs=[scr["vsd"]], ins=[vt],
                    sem_key="D:l3vs")
        kk = ctx.sbuf("l3_kk", [128, L + T], BF16)
        p.op("gpsimd", lambda g: g.memset(kk[:, 0:L], 0.0), outs=[kk])
        p.op("scalar", lambda s: s.copy(kk[0:64, L:L + T], k_lat), outs=[kk], ins=[k_lat])
        p.op("scalar", lambda s: s.copy(kk[64:128, L:L + T], k_lat), outs=[kk], ins=[k_lat])

        for g in (range(NG) if PH >= 2 else []):
            ag = a_all[:, g, :]
            CP = ctx.view("l3_CP", [128, T], F32, 110656)
            d0 = ctx.view("l3_d0", [128, T], F32, 77824)
            d1 = ctx.view("l3_d1", [128, T], F32, 86016)
            ams = ctx.view("l3_ams", [128, T], F32, 86016)
            p.op("vector", lambda v, ag=ag: v.tensor_tensor(ams, ag, ms, op=OP.mult), outs=[ams], ins=[a_all, ms])
            p.op("vector", lambda v, ag=ag: v.tensor_tensor(d0, ag, ams, op=OP.subtract), outs=[d0], ins=[a_all, ams])
            p.op("vector", lambda v: v.tensor_tensor_scan(CP, d0, ams, 1.0, op0=OP.mult, op1=OP.add),
                 outs=[CP], ins=[d0, ams])
            p.op("vector", lambda v, g=g: v.tensor_copy(
                D_cols[:, g, :], CP.rearrange("p (n l) -> p n l", l=L)[:, :, L - 1]),
                outs=[D_cols], ins=[CP])
            for tc in range(T // 512):
                for hh in range(2):
                    p.op("tensor", lambda t, g=g, tc=tc, hh=hh: t.matmul(
                        pA[hh * 64:(hh + 1) * 64, 0:512], Qc_sb[:, 2 * g + hh, :],
                        q_lat[:, tc * 512:(tc + 1) * 512], start=True, stop=True),
                        outs=[pA], ins=[Qc_sb, q_lat])
                qcpt = ctx.sbuf("l3_otmp", [128, 512], F32)
                p.op("vector", lambda v, g=g, tc=tc, qcpt=qcpt: v.tensor_tensor(
                    qcpt, pA[:, 0:512], CP[:, tc * 512:(tc + 1) * 512],
                    op=OP.mult), outs=[qcpt], ins=[pA, CP])
                p.dma("scalar", lambda e, g=g, tc=tc, qcpt=qcpt: e.dma_start(
                    out=scr["qcpd"][g, :, tc * 512:(tc + 1) * 512], in_=qcpt),
                    outs=[scr["qcpd"]], ins=[qcpt], sem_key="D:l3qcp")
            a0z = ctx.view("l3_a0z", [128, L + T], F32, 118848)
            p.op("gpsimd", lambda g_: g_.memset(a0z[:, 0:L], 0.0), outs=[a0z], ins=[a_all])
            p.op("vector", lambda v, ag=ag: v.tensor_tensor(a0z[:, L:L + T], ag, ams, op=OP.subtract),
                 outs=[a0z], ins=[a_all, ams])
            Zr = ctx.view("l3_Zr", [128, T], F32, 94208)
            arev = ag[:, T - 1:0:-1]
            p.op("gpsimd", lambda g_: g_.memset(d0[:, 0:1], 0.0), outs=[d0], ins=[a_all])
            p.op("vector", lambda v, arev=arev: v.tensor_tensor(d1[:, 1:T], arev, ms[:, 1:T], op=OP.mult),
                 outs=[d1], ins=[a_all, ms])
            p.op("vector", lambda v, arev=arev: v.tensor_tensor(d0[:, 1:T], arev, d1[:, 1:T], op=OP.subtract),
                 outs=[d0], ins=[a_all, d1])
            p.op("vector", lambda v: v.tensor_copy(d1[:, 0:T], ms[:, 0:T]), outs=[d1], ins=[ms])
            p.op("vector", lambda v: v.tensor_tensor_scan(Zr, d0, d1, 0.0, op0=OP.mult, op1=OP.add),
                 outs=[Zr], ins=[d0, d1])
            for i in range(NT):
                znat = ctx.sbuf("l3_znat", [128, 128], F32)
                zstop = None if i == NT - 1 else T - 1 - i * 128 - 128
                p.op("vector", lambda v, i=i, zstop=zstop: v.tensor_copy(
                    znat, Zr[:, T - 1 - i * 128: zstop:-1]), outs=[znat], ins=[Zr])
                p.op("tensor", lambda t, i=i: t.transpose(pB[:, 0:128], znat, ctx.idn),
                     outs=[pB], ins=[znat, ctx.idn])
                zt_t = ctx.sbuf("l3_ztt", [128, 128], F32)
                p.op("scalar", lambda s, i=i: s.copy(zt_t, pB[:, 0:128]), outs=[zt_t], ins=[pB])
                kzt = ctx.sbuf("l3_kzt", [128, 128], BF16)
                p.op("vector", lambda v, i=i: v.tensor_tensor(kzt, zt_t, k_tm2[:, i, :], op=OP.mult),
                     outs=[kzt], ins=[zt_t, k_tm2])
                p.dma("scalar", lambda e, i=i, g=g, kzt=kzt: e.dma_start(
                    out=scr["kzd"][i * 128:(i + 1) * 128, g, :], in_=kzt), outs=[scr["kzd"]], ins=[kzt],
                    sem_key="D:l3kz")
            vsf = [ctx.view(f"l3_vsf{hh}", [128, L + 2048], BF16, 102400 + hh * 4128) for hh in range(2)]
            for hh in range(2):
                p.op("gpsimd", lambda g_, hh=hh: g_.memset(vsf[hh][:, 0:L], 0.0), outs=[vsf[hh]])
                for tc in range(T // 512):
                    p.op("tensor", lambda t, g=g, hh=hh, tc=tc: t.matmul(
                        pA[:, 0:512], Vc_sb[:, 2 * g + hh, :], v_lat[:, tc * 512:(tc + 1) * 512],
                        start=True, stop=True), outs=[pA], ins=[Vc_sb, v_lat])
                    p.op("vector", lambda v, hh=hh, tc=tc: v.tensor_tensor(
                        vsf[hh][:, L + tc * 512:L + (tc + 1) * 512], pA[:, 0:512],
                        rknbc[:, tc * 512:(tc + 1) * 512], op=OP.mult),
                        outs=[vsf[hh]], ins=[pA, rknbc])
            Rbuf = [ctx.view(f"l3_R{x}", [128, 512], F32, 127072 + x * 2048) for x in range(2)]
            prod1 = ctx.sbuf("l3_prod1", [128, 512], BF16)
            prod = ctx.sbuf("l3_prod", [128, 512], BF16)
            s2 = ctx.sbuf("l3_s2", [2, 512], BF16)
            q2tc = ctx.sbuf("l3_q2tc", [128, 512], BF16)
            oacc = [ctx.view(f"l3_oacc{hh}", [128, 512], F32, 131168 + hh * 2048) for hh in range(2)]
            pS, pP = ctx.bk(2), ctx.bk(3)
            for tc in range(T // 512):
                t0 = tc * 512
                for d in range(L):
                    if d == 0:
                        for hh in range(2):
                            p.op("tensor", lambda t, g=g, hh=hh, t0=t0: t.matmul(
                                pA[hh * 64:(hh + 1) * 64, 0:512], Qc_sb[:, 2 * g + hh, :],
                                q_lat[:, t0:t0 + 512], start=True, stop=True), outs=[pA], ins=[Qc_sb, q_lat])
                        p.op("vector", lambda v, t0=t0: v.tensor_copy(q2tc, pA[:, 0:512]),
                             outs=[q2tc], ins=[pA])
                        p.op("vector", lambda v, t0=t0: v.tensor_tensor(
                            prod, q2tc, kk[:, L + t0:L + t0 + 512], op=OP.mult), outs=[prod], ins=[q2tc, kk])
                    else:
                        Rc, Rp = Rbuf[d % 2], Rbuf[(d + 1) % 2]
                        if d == 1:
                            p.op("vector", lambda v, t0=t0, Rc=Rc: v.tensor_copy(
                                Rc, a0z[:, L + t0:L + t0 + 512]), outs=[Rc], ins=[a0z])
                        else:
                            p.op("vector", lambda v, d=d, t0=t0, Rc=Rc, Rp=Rp: v.tensor_tensor(
                                Rc, Rp, a0z[:, L + t0 - d + 1:L + t0 - d + 1 + 512], op=OP.mult),
                                outs=[Rc], ins=[Rp, a0z])
                        p.op("gpsimd", lambda g_, d=d, t0=t0: g_.tensor_tensor(
                            prod1, q2tc, kk[:, L + t0 - d:L + t0 - d + 512], op=OP.mult),
                            outs=[prod1], ins=[q2tc, kk])
                        p.op("vector", lambda v, d=d, Rc=Rc: v.tensor_tensor(prod, prod1, Rc, op=OP.mult),
                             outs=[prod], ins=[prod1, Rc])
                    p.op("tensor", lambda t, d=d: t.matmul(pS[0:2, 0:512], bo2, prod, start=True, stop=True),
                         outs=[pS], ins=[bo2, prod])
                    p.op("scalar", lambda s, d=d: s.copy(s2, pS[0:2, 0:512]), outs=[s2], ins=[pS])
                    for hh in range(2):
                        p.op("tensor", lambda t, hh=hh, d=d: t.matmul(pP[:, 0:512], sel[hh], s2,
                                                                      start=True, stop=True),
                             outs=[pP], ins=[sel[hh], s2])
                        if d == 0:
                            p.op("vector", lambda v, hh=hh, t0=t0: v.tensor_tensor(
                                oacc[hh], vsf[hh][:, L + t0:L + t0 + 512], pP[:, 0:512], op=OP.mult),
                                outs=[oacc[hh]], ins=[vsf[hh], pP])
                        else:
                            tmp = ctx.sbuf("l3_otmp", [128, 512], F32)
                            p.op("vector", lambda v, hh=hh, d=d, t0=t0, tmp=tmp: v.tensor_tensor(
                                tmp, vsf[hh][:, L + t0 - d:L + t0 - d + 512], pP[:, 0:512], op=OP.mult),
                                outs=[tmp], ins=[vsf[hh], pP])
                            p.op("gpsimd", lambda g_, hh=hh, tmp=tmp: g_.tensor_tensor(
                                oacc[hh], oacc[hh], tmp, op=OP.add), outs=[oacc[hh]], ins=[oacc[hh], tmp])
                for hh in range(2):
                    ob = ctx.sbuf("l3_prod", [128, 512], BF16)
                    p.op("vector", lambda v, hh=hh: v.tensor_copy(ob, oacc[hh]), outs=[ob], ins=[oacc[hh]])
                    p.dma("scalar", lambda e, g=g, hh=hh, tc=tc, ob=ob: e.dma_start(
                        out=scr["oin"][2 * g + hh, :, tc * 512:(tc + 1) * 512], in_=ob),
                        outs=[scr["oin"]], ins=[ob], sem_key="D:l3oi")

        if PH < 3:
            continue
        S_g = [ctx.view(f"l3_S{g}", [128, DV], F32, 126976 + g * 512) for g in range(NG)]
        for g in range(NG):
            p.op("gpsimd", lambda g_, g2=g: g_.memset(S_g[g2], 0.0), outs=[S_g[g]])
        KZT = ctx.view("l3_KZT", [8, 16, 512], BF16, 77824)
        VST = ctx.view("l3_VST", [8, 16, 1024], BF16, 94208)
        QCPs = ctx.sbuf("l3_QCPs", [128, NG, 128], F32)
        QCPb = ctx.sbuf("l3_QCPb", [128, NG * 128], BF16)
        pDS = ctx.bk(4)
        poA, poB = ctx.bk(5), ctx.bk(6)
        for sp in range(T // 128):
            p.dma("sync", lambda e, sp=sp: e.dma_start(
                out=KZT, in_=scr["kzd"].rearrange("(n l) g c -> l n (g c)", l=L)[:, sp * 16:(sp + 1) * 16, :]),
                outs=[KZT], ins=[scr["kzd"]], sem_key="D:l3kzt")
            p.dma("sync", lambda e, sp=sp: e.dma_start(
                out=VST, in_=scr["vsd"].rearrange("(n l) h v -> l n (h v)", l=L)[:, sp * 16:(sp + 1) * 16, :]),
                outs=[VST], ins=[scr["vsd"]], sem_key="D:l3vst")
            p.dma("sync", lambda e, sp=sp: e.dma_start(
                out=QCPs, in_=scr["qcpd"].rearrange("g p t -> p g t")[:, :, sp * 128:(sp + 1) * 128]),
                outs=[QCPs], ins=[scr["qcpd"]], sem_key="D:l3qcps")
            p.op("vector", lambda v: v.tensor_copy(QCPb.rearrange("p (g t) -> p g t", g=NG), QCPs), outs=[QCPb], ins=[QCPs])
            L3I = int(os.environ.get("L3I", "99"))
            for ii in range(16):
                i = sp * 16 + ii
                if L3I < 2:
                    continue
                for g in range(NG):
                    Sb = ctx.sbuf(f"l3_Sb{g}", [128, DV], BF16)
                    p.op("scalar", lambda sc, g=g, Sb=Sb: sc.copy(Sb, S_g[g]), outs=[Sb], ins=[S_g[g]])
                    for hh in range(2):
                        if L3I < 4:
                            continue
                        h = 2 * g + hh
                        po = poA if hh == 0 else poB
                        p.op("tensor", lambda t, g=g, hh=hh, i=i, ii=ii, h=h, po=po, Sb=Sb: t.matmul(
                            po[:, g * 128 + ii * 8:g * 128 + ii * 8 + 8],
                            Sb[hh * 64:(hh + 1) * 64, :],
                            QCPb[hh * 64:(hh + 1) * 64, g * 128 + ii * L:g * 128 + (ii + 1) * L],
                            start=True, stop=True), outs=[po], ins=[Sb, QCPb])
                    for hh in range(2):
                        if L3I < 3:
                            continue
                        h = 2 * g + hh
                        p.op("tensor", lambda t, g=g, hh=hh, ii=ii, h=h: t.matmul(
                            pDS[hh * 64:(hh + 1) * 64, g * 128:(g + 1) * 128],
                            KZT[:, ii, g * 128 + hh * 64:g * 128 + (hh + 1) * 64],
                            VST[:, ii, h * 128:(h + 1) * 128], start=True, stop=True),
                            outs=[pDS], ins=[KZT, VST])
                for g in (range(NG) if L3I >= 3 else []):
                    p.op("vector", lambda v, g=g, i=i: v.scalar_tensor_tensor(
                        S_g[g], S_g[g], D_cols[:, g, i:i + 1], pDS[:, g * 128:(g + 1) * 128],
                        op0=OP.mult, op1=OP.add), outs=[S_g[g]], ins=[S_g[g], D_cols, pDS])
            for h in (range(NHEAD) if L3I >= 5 else []):
                g, hh = h // 2, h % 2
                po = poA if hh == 0 else poB
                ob2 = ctx.sbuf("l3_ob2", [128, 128], BF16)
                p.op("vector", lambda v, h=h, po=po, g=g: v.tensor_copy(
                    ob2, po[:, g * 128:g * 128 + 128]), outs=[ob2], ins=[po])
                p.dma("scalar", lambda e, h=h, sp=sp, ob2=ob2: e.dma_start(
                    out=scr["oit"][h, :, sp * 128:(sp + 1) * 128], in_=ob2), outs=[scr["oit"]], ins=[ob2],
                    sem_key="D:l3ot")

        if PH < 4:
            continue
        WoT3 = ctx.load_transposed("l3_WoT3", W["Wo"], D, NHEAD * DV, stage_off=94208, out_off=32768)
        os_sl = ctx.view("l3_osl", [128, NHEAD, 512], BF16, 77824)
        os_s2 = ctx.view("l3_os2", [128, NHEAD, 512], BF16, 86016)
        s_t = ctx.sbuf("ln_sq", [128, D], F32)
        for tc in range(T // 512):
            for h in range(NHEAD):
                p.dma("sync", lambda e, h=h, tc=tc: e.dma_start(
                    out=os_sl[:, h, :], in_=scr["oin"][h, :, tc * 512:(tc + 1) * 512]),
                    outs=[os_sl], ins=[scr["oin"]], sem_key="D:l3or1")
                p.dma("sync", lambda e, h=h, tc=tc: e.dma_start(
                    out=os_s2[:, h, :], in_=scr["oit"][h, :, tc * 512:(tc + 1) * 512]),
                    outs=[os_s2], ins=[scr["oit"]], sem_key="D:l3or2")
            for sub in range(4):
                i = tc * 4 + sub
                xnt = ctx.sbuf("pn_x0", [128, D], F32)
                p.dma("sync", lambda e, b=b, i=i, xnt=xnt: e.dma_start(
                    out=xnt, in_=scr["xnf"][b, i * 128:(i + 1) * 128, :]), outs=[xnt], ins=[scr["xnf"]],
                    sem_key="D:l3xn")
                for half in range(2):
                    for h in range(NHEAD):
                        p.op("tensor", lambda t, h=h, sub=sub, half=half: t.matmul(
                            pA[:, 0:512], os_sl[:, h, sub * 128:(sub + 1) * 128],
                            WoT3[:, h, half * 512:(half + 1) * 512], start=(h == 0), stop=False),
                            outs=[pA], ins=[os_sl, WoT3])
                    for h in range(NHEAD):
                        p.op("tensor", lambda t, h=h, sub=sub, half=half: t.matmul(
                            pA[:, 0:512], os_s2[:, h, sub * 128:(sub + 1) * 128],
                            WoT3[:, h, half * 512:(half + 1) * 512], start=False, stop=(h == NHEAD - 1)),
                            outs=[pA], ins=[os_s2, WoT3])
                    p.op("vector", lambda v, half=half: v.tensor_tensor(
                        s_t[:, half * 512:(half + 1) * 512], pA[:, 0:512],
                        xnt[:, half * 512:(half + 1) * 512], op=OP.add), outs=[s_t], ins=[pA, xnt])
                p.dma("scalar", lambda e, i=i, b=b: e.dma_start(
                    out=scr["s"][b, i * 128:(i + 1) * 128, :], in_=s_t), outs=[scr["s"]], ins=[s_t],
                    sem_key="D:l3s")

        if PH < 5:
            continue
        WrT3 = ctx.view("l3_WrT3", [128, DC, D], BF16, 49152)
        WkT3 = ctx.view("l3_WkT3", [128, DC, D], BF16, 65536)
        WvT3 = ctx.view("l3_WvT3", [128, DC, D], BF16, 81920)
        for dst, wsrc in ((WrT3, "W_r"), (WkT3, "W_k"), (WvT3, "W_v")):
            stg = ctx.view("l3_fstg", [128, DC, D], F32, 98304)
            p.dma("gpsimd", lambda e, stg=stg, wsrc=wsrc: e.dma_start(
                out=stg, in_=W[wsrc].rearrange("(a p) m -> p a m", p=128)), outs=[stg], ins=[W[wsrc]],
                sem_key="D:l3fw")
            for r in range(DC):
                for c in range(DC):
                    p.op("tensor", lambda tt, r=r, c=c, stg=stg: tt.transpose(
                        pB[:, 0:128], stg[:, r, c * 128:(c + 1) * 128], ctx.idn), outs=[pB], ins=[stg, ctx.idn])
                    p.op("scalar", lambda s, r=r, c=c, dst=dst: s.copy(
                        dst[:, c, r * 128:(r + 1) * 128], pB[:, 0:128]), outs=[dst], ins=[pB])
        xk = ctx.view("l3_xk", [128, DC, T], BF16, 98304)
        for mcol, dst in ((muk, xk),):
            for c in range(DC):
                omm = ctx.sbuf("l3_omm", [128, 1], F32)
                p.op("vector", lambda v, c=c, mcol=mcol, omm=omm: v.tensor_scalar(
                    omm, mcol[:, c:c + 1], 1.0, -1.0, op0=OP.subtract, op1=OP.mult),
                    outs=[omm], ins=[mcol])
                for tc in range(T // 512):
                    t1 = ctx.sbuf("l3_otmp", [128, 512], F32)
                    lo = tc * 512
                    p.op("vector", lambda v, c=c, lo=lo, mcol=mcol, t1=t1: v.tensor_scalar(
                        t1, xn_fm[:, c, lo:lo + 512], mcol[:, c:c + 1], None, op0=OP.mult, op1=OP.bypass),
                        outs=[t1], ins=[xn_fm, mcol])
                    if tc == 0:
                        p.op("vector", lambda v, c=c, dst=dst, t1=t1, omm=omm: v.scalar_tensor_tensor(
                            dst[:, c, 1:512], xn_fm[:, c, 0:511], omm, t1[:, 1:512], op0=OP.mult, op1=OP.add),
                            outs=[dst], ins=[xn_fm, omm, t1])
                        p.op("vector", lambda v, c=c, dst=dst, t1=t1: v.tensor_copy(dst[:, c, 0:1], t1[:, 0:1]),
                             outs=[dst], ins=[t1])
                    else:
                        p.op("vector", lambda v, c=c, lo=lo, dst=dst, t1=t1, omm=omm: v.scalar_tensor_tensor(
                            dst[:, c, lo:lo + 512], xn_fm[:, c, lo - 1:lo + 511], omm, t1,
                            op0=OP.mult, op1=OP.add), outs=[dst], ins=[xn_fm, omm, t1])
        for i in range(NT):
            for half in range(2):
                for c in range(DC):
                    p.op("tensor", lambda t, i=i, half=half, c=c: t.matmul(
                        pA[:, 0:512], xk[:, c, i * 128:(i + 1) * 128], WkT3[:, c, half * 512:(half + 1) * 512],
                        start=(c == 0), stop=(c == DC - 1)), outs=[pA], ins=[xk, WkT3])
                kr = ctx.sbuf("l3_otmp", [128, 512], F32)
                p.op("vector", lambda v, kr=kr: v.tensor_scalar(kr, pA[:, 0:512], 0.0, None,
                                                                op0=OP.max, op1=OP.bypass), outs=[kr], ins=[pA])
                ka = ctx.sbuf("l3_ka", [128, 512], BF16)
                p.op("scalar", lambda s, ka=ka, kr=kr: s.activation(ka, kr, AF.Square), outs=[ka], ins=[kr])
                p.dma("scalar", lambda e, i=i, half=half, ka=ka: e.dma_start(
                    out=scr["kad"][i * 128:(i + 1) * 128, half * 512:(half + 1) * 512], in_=ka),
                    outs=[scr["kad"]], ins=[ka], sem_key="D:l3ka")
        xr = ctx.view("l3_xr", [128, DC, T], BF16, 98304)
        for mcol, dst in ((mur, xr),):
            for c in range(DC):
                omm = ctx.sbuf("l3_omm", [128, 1], F32)
                p.op("vector", lambda v, c=c, mcol=mcol, omm=omm: v.tensor_scalar(
                    omm, mcol[:, c:c + 1], 1.0, -1.0, op0=OP.subtract, op1=OP.mult),
                    outs=[omm], ins=[mcol])
                for tc in range(T // 512):
                    t1 = ctx.sbuf("l3_otmp", [128, 512], F32)
                    lo = tc * 512
                    p.op("vector", lambda v, c=c, lo=lo, mcol=mcol, t1=t1: v.tensor_scalar(
                        t1, xn_fm[:, c, lo:lo + 512], mcol[:, c:c + 1], None, op0=OP.mult, op1=OP.bypass),
                        outs=[t1], ins=[xn_fm, mcol])
                    if tc == 0:
                        p.op("vector", lambda v, c=c, dst=dst, t1=t1, omm=omm: v.scalar_tensor_tensor(
                            dst[:, c, 1:512], xn_fm[:, c, 0:511], omm, t1[:, 1:512], op0=OP.mult, op1=OP.add),
                            outs=[dst], ins=[xn_fm, omm, t1])
                        p.op("vector", lambda v, c=c, dst=dst, t1=t1: v.tensor_copy(dst[:, c, 0:1], t1[:, 0:1]),
                             outs=[dst], ins=[t1])
                    else:
                        p.op("vector", lambda v, c=c, lo=lo, dst=dst, t1=t1, omm=omm: v.scalar_tensor_tensor(
                            dst[:, c, lo:lo + 512], xn_fm[:, c, lo - 1:lo + 511], omm, t1,
                            op0=OP.mult, op1=OP.add), outs=[dst], ins=[xn_fm, omm, t1])
        kaf = ctx.view("l3_kaf", [128, DC, T], BF16, 0)
        for c in range(DC):
            p.dma("sync", lambda e, c=c: e.dma_start_transpose(
                kaf[:, c, :], scr["kad"][:, c * 128:(c + 1) * 128]), outs=[kaf], ins=[scr["kad"]],
                sem_key="D:l3kaf")
        for i in range(NT):
            rt_ = ctx.sbuf("pn_x0", [128, D], F32)
            o2 = ctx.sbuf("l3_ep_o2", [128, D], F32)
            for half in range(2):
                for c in range(DC):
                    p.op("tensor", lambda t, i=i, half=half, c=c: t.matmul(
                        pA[:, 0:512], xr[:, c, i * 128:(i + 1) * 128], WrT3[:, c, half * 512:(half + 1) * 512],
                        start=(c == 0), stop=(c == DC - 1)), outs=[pA], ins=[xr, WrT3])
                p.op("scalar", lambda s, half=half, rt_=rt_: s.activation(
                    rt_[:, half * 512:(half + 1) * 512], pA[:, 0:512], AF.Sigmoid), outs=[rt_], ins=[pA])
                for c in range(DC):
                    p.op("tensor", lambda t, i=i, half=half, c=c: t.matmul(
                        pA[:, 0:512], kaf[:, c, i * 128:(i + 1) * 128], WvT3[:, c, half * 512:(half + 1) * 512],
                        start=(c == 0), stop=(c == DC - 1)), outs=[pA], ins=[kaf, WvT3])
                p.op("vector", lambda v, half=half, o2=o2: v.tensor_copy(
                    o2[:, half * 512:(half + 1) * 512], pA[:, 0:512]), outs=[o2], ins=[pA])
            p.op("vector", lambda v, rt_=rt_, o2=o2: v.tensor_tensor(o2, o2, rt_, op=OP.mult),
                 outs=[o2], ins=[o2, rt_])
            xt = ctx.sbuf(f"pn_x{i % ctx.pnb}", [128, D], F32)
            st_ = ctx.sbuf(f"pn_xn{i % ctx.pnb}", [128, D], F32)
            gs = ctx.sbuf("ln_sq", [128, D], F32)
            p.dma("sync", lambda e, b=b, i=i, xt=xt: e.dma_start(
                out=xt, in_=x_in[b, i * 128:(i + 1) * 128, :]), outs=[xt], ins=[x_in])
            p.dma("sync", lambda e, b=b, i=i, st_=st_: e.dma_start(
                out=st_, in_=scr["s"][b, i * 128:(i + 1) * 128, :]), outs=[st_], ins=[scr["s"]],
                sem_key="D:l3s_rd")
            p.op("scalar", lambda s, o2=o2, gs=gs: s.activation(gs, o2, AF.Silu), outs=[gs], ins=[o2])
            p.op("vector", lambda v, gs=gs, st_=st_: v.tensor_tensor(gs, gs, st_, op=OP.mult),
                 outs=[gs], ins=[gs, st_])
            p.op("vector", lambda v, gs=gs, xt=xt: v.tensor_tensor(gs, gs, xt, op=OP.add),
                 outs=[gs], ins=[gs, xt])
            p.dma("scalar", lambda e, b=b, i=i, gs=gs: e.dma_start(
                out=y_out[b, i * 128:(i + 1) * 128, :], in_=gs), outs=[y_out], ins=[gs], sem_key="D:y")


from concourse.bass_utils import run_bass_kernel_spmd

_CACHE = {}

B, T, D = 2, 2048, 1024


def _mk_layer(lid):
    key = ("layer", lid)
    if key in _CACHE:
        return _CACHE[key]
    nc = bass.Bass("TRN2", target_bir_lowering=False)
    dx = nc.dram_tensor("x", [B, T, D], F32, kind="ExternalInput").ap()
    dy = nc.dram_tensor("y", [B, T, D], F32, kind="ExternalOutput").ap()
    scr = {
        "xnb": nc.dram_tensor("scr_xnb", [B, T, D], BF16, kind="Internal").ap(),
        "s": nc.dram_tensor("scr_s", [B, T, D], F32, kind="Internal").ap(),
    }
    p = Prog(nc)
    if lid == 0:
        W = {k: nc.dram_tensor(k, shp, F32, kind="ExternalInput").ap() for k, shp in [
            ("Wq", [2, 128, D]), ("Wk", [128, D]), ("Wv", [128, D]), ("Wo", [D, 256]),
            ("sg", [D, 2816]), ("su", [D, 2816]), ("sd", [2816, D]),
            ("rg", [7, D, 2816]), ("ru", [7, D, 2816]), ("rd", [7, 2816, D]),
            ("Wr", [D, 7]), ("rb", [7])]}
        C = {k: nc.dram_tensor("c_" + k, [T, 64], F32, kind="ExternalInput").ap()
             for k in ("cosk", "sink", "cosq", "sinq")}
        scr["xnb_lo"] = nc.dram_tensor("scr_xnb_lo", [B, T, D], BF16, kind="Internal").ap()
        scr["h"] = nc.dram_tensor("scr_h", [128, 22, T // 2], BF16, kind="Internal").ap()
        ctx = Ctx(nc, p, B=B, T=T, D=D, arena_bytes=116000)
        build_l0(ctx, dx, dy, W, C, scr)
    elif lid == 1:
        W = {k: nc.dram_tensor(k, shp, F32, kind="ExternalInput").ap() for k, shp in [
            ("w_qkv", [D, 3072]), ("w_out", [1024, D]), ("routing_token", [D]),
            ("gamma_light", [D]), ("w1_light", [D, 512]), ("b1_light", [512]),
            ("w2_light", [512, D]), ("b2_light", [D]), ("gamma_heavy", [D]),
            ("w1_heavy", [D, 4096]), ("b1_heavy", [4096]), ("w2_heavy", [4096, D]),
            ("b2_heavy", [D])]}
        ctx = Ctx(nc, p, B=B, T=T, D=D, arena_bytes=104000)
        build_l1(ctx, dx, dy, W, scr)
    elif lid == 2:
        NH, DH = 16, 64
        W = {k: nc.dram_tensor(k, shp, F32, kind="ExternalInput").ap() for k, shp in [
            ("W_Q", [NH, D, DH]), ("b_Q", [NH, DH]), ("W_K", [NH, D, DH]), ("b_K", [NH, DH]),
            ("W_V", [NH, D, DH]), ("b_V", [NH, DH]), ("W_O", [NH, DH, D]), ("b_O", [D]),
            ("sim_matrix", [D, NH]), ("gates", [NH]),
            ("q_proj", [NH, D, DH]), ("k_proj", [NH, D, DH]), ("v_proj", [NH, D, DH]),
            ("o_proj", [NH, DH, D])]}
        scr["xnb_lo"] = nc.dram_tensor("scr_xnb_lo", [B, T, D], BF16, kind="Internal").ap()
        ctx = Ctx(nc, p, B=B, T=T, D=D, arena_bytes=116000)
        build_l2(ctx, dx, dy, W, scr)
    else:
        H, C_, DV = 8, 64, 128
        W = {k: nc.dram_tensor(k, shp, F32, kind="ExternalInput").ap() for k, shp in [
            ("Wq", [C_, D]), ("Wk", [C_, D]), ("Wv", [C_, D]), ("Qc", [H, C_, C_]),
            ("Vc", [H, C_, DV]), ("Wa_w", [512, D]), ("Wa_b", [512]), ("Wo", [D, H * DV]),
            ("gamma", [D]), ("beta", [D]), ("W_r", [D, D]), ("W_k", [D, D]), ("W_v", [D, D]),
            ("mu_r", [D]), ("mu_k", [D])]}
        scr.update({
            "xnf": nc.dram_tensor("scr_xnf", [B, T, D], F32, kind="Internal").ap(),
            "vsd": nc.dram_tensor("scr_vsd", [T, H, DV], BF16, kind="Internal").ap(),
            "kzd": nc.dram_tensor("scr_kzd", [T, 4, 128], BF16, kind="Internal").ap(),
            "oin": nc.dram_tensor("scr_oin", [H, DV, T], BF16, kind="Internal").ap(),
            "oit": nc.dram_tensor("scr_oit", [H, DV, T], BF16, kind="Internal").ap(),
            "kad": nc.dram_tensor("scr_kad", [T, D], BF16, kind="Internal").ap(),
            "qcpd": nc.dram_tensor("scr_qcpd", [4, 128, T], F32, kind="Internal").ap(),
        })
        ctx = Ctx(nc, p, B=B, T=T, D=D, arena_bytes=136000)
        build_l3(ctx, dx, dy, W, scr)
    p.build()
    _CACHE[key] = nc
    return nc


def kernel(**inputs):
    n = 8
    x = np.ascontiguousarray(inputs["L0_attn_x"], dtype=np.float32)
    shards = [x[2 * c:2 * c + 2] for c in range(n)]
    consts = l0_consts(T)
    wmaps = [
        {"Wq": "L0_attn_Wq", "Wk": "L0_attn_Wk", "Wv": "L0_attn_Wv", "Wo": "L0_attn_Wo",
         "sg": "L0_ffn_shared_gate", "su": "L0_ffn_shared_up", "sd": "L0_ffn_shared_down",
         "rg": "L0_ffn_routed_gate", "ru": "L0_ffn_routed_up", "rd": "L0_ffn_routed_down",
         "Wr": "L0_ffn_Wr", "rb": "L0_ffn_routing_bias"},
        {"w_qkv": "L1_attn_w_qkv", "w_out": "L1_attn_w_out",
         "routing_token": "L1_ffn_routing_token", "gamma_light": "L1_ffn_gamma_light",
         "w1_light": "L1_ffn_w1_light", "b1_light": "L1_ffn_b1_light",
         "w2_light": "L1_ffn_w2_light", "b2_light": "L1_ffn_b2_light",
         "gamma_heavy": "L1_ffn_gamma_heavy", "w1_heavy": "L1_ffn_w1_heavy",
         "b1_heavy": "L1_ffn_b1_heavy", "w2_heavy": "L1_ffn_w2_heavy",
         "b2_heavy": "L1_ffn_b2_heavy"},
        {"W_Q": "L2_attn_W_Q", "b_Q": "L2_attn_b_Q", "W_K": "L2_attn_W_K", "b_K": "L2_attn_b_K",
         "W_V": "L2_attn_W_V", "b_V": "L2_attn_b_V", "W_O": "L2_attn_W_O", "b_O": "L2_attn_b_O",
         "sim_matrix": "L2_ffn_sim_matrix", "gates": "L2_ffn_gates",
         "q_proj": "L2_ffn_q_proj", "k_proj": "L2_ffn_k_proj", "v_proj": "L2_ffn_v_proj",
         "o_proj": "L2_ffn_o_proj"},
        {"Wq": "L3_attn_Wq", "Wk": "L3_attn_Wk", "Wv": "L3_attn_Wv", "Qc": "L3_attn_Qc",
         "Vc": "L3_attn_Vc", "Wa_w": "L3_attn_Wa_w", "Wa_b": "L3_attn_Wa_b",
         "Wo": "L3_attn_Wo", "gamma": "L3_attn_gamma", "beta": "L3_attn_beta",
         "W_r": "L3_ffn_W_r", "W_k": "L3_ffn_W_k", "W_v": "L3_ffn_W_v",
         "mu_r": "L3_ffn_mu_r", "mu_k": "L3_ffn_mu_k"},
    ]
    cur = shards
    for lid in range(4):
        nc = _mk_layer(lid)
        wvals = {k: np.ascontiguousarray(inputs[v], dtype=np.float32) for k, v in wmaps[lid].items()}
        in_maps = []
        for c in range(n):
            m = {"x": np.ascontiguousarray(cur[c], dtype=np.float32)}
            m.update(wvals)
            if lid == 0:
                m.update({"c_" + k: v for k, v in consts.items()})
            in_maps.append(m)
        res = run_bass_kernel_spmd(nc, in_maps, core_ids=list(range(n)))
        cur = [res.results[c]["y"] for c in range(n)]
    return np.concatenate([np.asarray(c, dtype=np.float32) for c in cur], axis=0)


# version 20
# speedup vs baseline: 1.1443x; 1.0481x over previous
import numpy as np
"""Auto-sync layer over raw Bass: records ops with read/write sets, inserts
semaphore edges (vector clocks for dedup), plays back per engine in one Block.

Rules encoded:
- every compute op incs its engine sem by 1 (value = per-engine op index)
- every dma incs a per-destination-tensor sem by 16 (value = 16 * count)
- RAW/WAW/WAR edges all get waits; same-engine edges too (except TensorE,
  whose queue is FIFO with pc-ordered completion)
- waits deduped via per-engine vector clocks with transitive closure through
  producer snapshots
- finalize(): sync engine waits every dma sem to its final total (timing contract)
"""
import concourse.bass as bass
import concourse.mybir as mybir
from concourse.alu_op_type import AluOpType as OP

F32 = mybir.dt.float32
BF16 = mybir.dt.bfloat16
AF = mybir.ActivationFunctionType
AX = mybir.AxisListType

ENGINES = ("tensor", "vector", "scalar", "gpsimd", "sync")


class Prog:
    def __init__(self, nc, same_engine_sync=True):
        self.nc = nc
        self.same_engine_sync = same_engine_sync
        self.seq = {e: [] for e in ENGINES}
        self.eng_count = {e: 0 for e in ENGINES}
        self.clock = {e: {} for e in ENGINES}
        self.bufs = {}
        self.alias = {}
        self.dma_count = {}
        self.snapshots = {}
        self.n_waits = 0

    def _buf(self, ap):
        name = ap.tensor.name
        name = self.alias.get(name, name)
        b = self.bufs.get(name)
        if b is None:
            b = {"w": None, "r": []}
            self.bufs[name] = b
        return b

    def register_view(self, name, region, lo, hi):
        self.alias.setdefault("__regions__", {})
        regs = self.alias["__regions__"]
        regs.setdefault(region, {})[name] = (lo, hi)

    def _targets(self, ap):
        name = ap.tensor.name
        out = [self._buf(ap)]
        regs = self.alias.get("__regions__", {})
        for region, members in regs.items():
            if name in members:
                lo, hi = members[name]
                for other, (lo2, hi2) in members.items():
                    if other != name and lo < hi2 and lo2 < hi:
                        b = self.bufs.get(other)
                        if b is None:
                            b = {"w": None, "r": []}
                            self.bufs[other] = b
                        out.append(b)
        return out

    def _wait(self, eng, sem_key, val):
        c = self.clock[eng]
        if c.get(sem_key, 0) >= val:
            return
        self.seq[eng].append(("wait", sem_key, val))
        self.n_waits += 1
        if c.get(sem_key, 0) < val:
            c[sem_key] = val
        snap = self.snapshots.get((sem_key, val))
        if snap:
            for k, v in snap.items():
                if c.get(k, 0) < v:
                    c[k] = v

    def _skip_same(self, eng, sem_key):
        if sem_key != "E:" + eng:
            return False
        if eng == "tensor":
            return True
        return not self.same_engine_sync

    def _read_dep(self, eng, ap):
        for b in self._targets(ap):
            w = b["w"]
            if w is None:
                continue
            sem, val = w
            if self._skip_same(eng, sem):
                continue
            self._wait(eng, sem, val)

    def _write_dep(self, eng, ap):
        for b in self._targets(ap):
            if b["w"] is not None:
                sem, val = b["w"]
                if not self._skip_same(eng, sem):
                    self._wait(eng, sem, val)
            for sem, val in b["r"]:
                if not self._skip_same(eng, sem):
                    self._wait(eng, sem, val)

    def _record_effects(self, ev, outs, ins):
        for ap in ins:
            self._buf(ap)["r"].append(ev)
        for ap in outs:
            b = self._buf(ap)
            b["w"] = ev
            b["r"] = []

    def op(self, eng, emit, outs=(), ins=()):
        for ap in ins:
            self._read_dep(eng, ap)
        for ap in outs:
            self._write_dep(eng, ap)
        self.eng_count[eng] += 1
        v = self.eng_count[eng]
        key = "E:" + eng
        self.seq[eng].append(("op", emit, key, v))
        ev = (key, v)
        self._record_effects(ev, outs, ins)
        snap = dict(self.clock[eng])
        snap[key] = v
        self.snapshots[ev] = snap
        return ev

    def dma(self, queue, emit, outs=(), ins=(), sem_key=None):
        assert queue in ("sync", "gpsimd", "scalar")
        for ap in ins:
            self._read_dep(queue, ap)
        for ap in outs:
            self._write_dep(queue, ap)
        sem_key = "D:" + outs[0].tensor.name
        if sem_key not in self.dma_count and len(self.dma_count) >= 88:
            sem_key = "D:__misc__"
            prev = self.dma_count.get(sem_key, 0)
            if prev:
                self._wait(queue, sem_key, 16 * prev)
        cnt = self.dma_count.get(sem_key, 0) + 1
        self.dma_count[sem_key] = cnt
        val = 16 * cnt
        self.seq[queue].append(("dma", emit, sem_key, val))
        ev = (sem_key, val)
        self._record_effects(ev, outs, ins)
        snap = dict(self.clock[queue])
        snap[sem_key] = val
        self.snapshots[ev] = snap
        return ev

    def barrier_read(self, eng, ap):
        self._read_dep(eng, ap)

    def build(self):
        for key, cnt in self.dma_count.items():
            self._wait("sync", key, 16 * cnt)
        nc = self.nc
        sems = {}
        for name in ["E:" + e for e in ENGINES] + list(self.dma_count.keys()):
            sems[name] = nc.alloc_semaphore(name=name.replace(":", "_").replace("/", "_"))
        nsems = len(sems)
        assert nsems <= 100, f"too many semaphores: {nsems}"
        with nc.Block() as block:
            for e in ENGINES:
                seq = self.seq[e]
                if not seq:
                    continue

                def mk_body(seq=seq):
                    def body(engobj):
                        for item in seq:
                            if item[0] == "wait":
                                engobj.wait_ge(sems[item[1]], item[2])
                            else:
                                kind, emit, key, val = item
                                inst = emit(engobj)
                                inst.then_inc(sems[key], 16 if kind == "dma" else 1)
                    return body

                getattr(block, e)(mk_body())
        return sems


"""Layer builders for mk_recurrent_d4. Conventions:
- token-major tile: [128 tokens, feat]; feature-major (fm): [128 feat-chunk, tokens]
- biases / -rowmax / per-token shifts folded in via K=1 ones-matmul PSUM accumulation
- softmax: two-pass (pass1 rowmax token-major; pass2 transposed scores, exp->P^T bf16,
  PV fm accumulation + ones-col Z row; normalize via K=1 broadcast of 1/Z)
- ALL lambdas must bind loop vars as default args (deferred emission).
"""

TILE = 128


class Ctx:
    def __init__(self, nc, p, B=2, T=2048, D=1024, arena_bytes=136000, bump_bytes=75000):
        self.nc, self.p = nc, p
        self.B, self.T, self.D = B, T, D
        self.NT = T // TILE
        self.DC = D // TILE
        self.sb = {}
        self.pnb = 2
        self.banks = [nc.alloc_psum_tensor(f"bank{i}", [128, 512], F32).ap() for i in range(8)]
        total = arena_bytes + bump_bytes
        self.arena_h = nc.alloc_sbuf_tensor("arena", [128, total], mybir.dt.uint8)
        self.arena_base = nc.sbuf_base - total
        self.arena_sz = arena_bytes
        self.bump_lo = arena_bytes
        self.bump_ptr = arena_bytes
        self.bump_hi = total
        self._mkcommon()

    def fresh_layer(self):
        self.sb = {}
        self.pnb = 2
        self.bump_ptr = self.bump_lo
        self._eps_init = False
        self._mkcommon()

    def bk(self, i):
        return self.banks[i]

    def sbuf(self, name, shape, dtype):
        if name not in self.sb:
            n = int(np.prod(shape[1:]))
            esz = {F32: 4, BF16: 2}[dtype]
            off = (self.bump_ptr + 31) & ~31
            assert off + n * esz <= self.bump_hi, (name, off, n * esz, self.bump_hi)
            h = self.nc.alloc_sbuf_tensor_at(name, shape, dtype, offset=self.arena_base + off)
            self.bump_ptr = off + n * esz
            self.sb[name] = h.ap()
            self.p.register_view(self.sb[name].tensor.name, "arena", off, off + n * esz)
        return self.sb[name]

    def view(self, name, shape, dtype, off):
        if name not in self.sb:
            n = int(np.prod(shape[1:]))
            esz = {F32: 4, BF16: 2}[dtype]
            assert off + n * esz <= self.arena_sz, (name, off, n * esz, self.arena_sz)
            assert (self.arena_base + off) % 32 == 0, (name, off)
            h = self.nc.alloc_sbuf_tensor_at(name, shape, dtype, offset=self.arena_base + off)
            self.sb[name] = h.ap()
            self.p.register_view(self.sb[name].tensor.name, "arena", off, off + n * esz)
        return self.sb[name]

    def _mkcommon(self):
        p = self.p
        self.idn = self.sbuf("c_idn", [128, 128], F32)
        self.pidx = self.sbuf("c_pidx", [128, 1], F32)
        self.ones_row = self.sbuf("c_ones_row", [1, 512], BF16)
        self.neg_row = self.sbuf("c_neg_row", [1, 128], BF16)
        self.ones_col = self.sbuf("c_ones_col", [128, 1], BF16)
        self.tri = self.sbuf("c_tri", [128, 128], BF16)
        p.op("gpsimd", lambda g: g.iota(self.idn, base=0, pattern=[[1, 128]], channel_multiplier=0,
                                        allow_small_or_imprecise_dtypes=True), outs=[self.idn])
        p.op("gpsimd", lambda g: g.iota(self.pidx, base=0, pattern=[[0, 1]], channel_multiplier=1,
                                        allow_small_or_imprecise_dtypes=True), outs=[self.pidx])
        p.op("vector", lambda v: v.tensor_scalar(self.tri, self.idn, self.pidx, None,
                                                 op0=OP.is_ge, op1=OP.bypass),
             outs=[self.tri], ins=[self.idn, self.pidx])
        p.op("vector", lambda v: v.tensor_scalar(self.idn, self.idn, self.pidx, None,
                                                 op0=OP.is_equal, op1=OP.bypass),
             outs=[self.idn], ins=[self.idn, self.pidx])
        p.op("gpsimd", lambda g: g.memset(self.ones_row, 1.0), outs=[self.ones_row])
        p.op("gpsimd", lambda g: g.memset(self.neg_row, -1.0), outs=[self.neg_row])
        p.op("gpsimd", lambda g: g.memset(self.ones_col, 1.0), outs=[self.ones_col])

    def load_cast(self, name, dram, kchunks, mcols, queue="gpsimd", view_off=None, sem_key=None):
        if view_off is None:
            t = self.sbuf(name, [128, kchunks, mcols], BF16)
        else:
            t = self.view(name, [128, kchunks, mcols], BF16, view_off)
        self.p.dma(queue, lambda e, t=t, dram=dram: e.dma_start(
            out=t, in_=dram.rearrange("(a p) m -> p a m", p=128)),
            outs=[t], ins=[dram], sem_key=sem_key)
        return t

    def load_transposed(self, name, dram, dout, din, stage_off=0, bank=1, out_off=None):
        nc, p = self.nc, self.p
        rc, cc = (dout + 127) // 128, din // 128
        rpart = min(dout, 128)
        stage = self.view(name + "_stg", [128, rc, din], F32, stage_off)
        p.dma("gpsimd", lambda e, stage=stage, dram=dram: e.dma_start(
            out=stage[0:rpart, :, :] if rc == 1 else stage,
            in_=dram.rearrange("(a p) m -> p a m", p=rpart)),
            outs=[stage], ins=[dram], sem_key="D:wstg")
        if out_off is None:
            t = self.sbuf(name, [128, cc, dout], BF16)
        else:
            t = self.view(name, [128, cc, dout], BF16, out_off)
        pb = self.bk(bank)
        for r in range(rc):
            for c in range(cc):
                p.op("tensor", lambda tt, r=r, c=c, stage=stage, t=t: tt.transpose(
                    self.bk(bank)[:, 0:rpart], stage[0:rpart, r, c * 128:(c + 1) * 128],
                    self.idn[0:rpart, 0:rpart]), outs=[pb], ins=[stage, self.idn])
                p.op("scalar", lambda s, r=r, c=c, t=t: s.copy(
                    t[:, c, r * 128:r * 128 + rpart], self.bk(bank)[:, 0:rpart]),
                    outs=[t], ins=[pb])
        return t

    def ln_tile(self, x_tile, xn_tile, eps=1e-5):
        p, D = self.p, self.D
        epsc = self.sbuf("ln_eps", [128, 1], F32)
        if not getattr(self, "_eps_init", False):
            self._eps_init = True
            p.op("gpsimd", lambda g: g.memset(epsc, float(eps)), outs=[epsc])
        s = self.sbuf("ln_s", [128, 1], F32)
        nm = self.sbuf("ln_nm", [128, 1], F32)
        sq = self.sbuf("ln_sq", [128, D], F32)
        ssq = self.sbuf("ln_ssq", [128, 1], F32)
        var = self.sbuf("ln_var", [128, 1], F32)
        rstd = self.sbuf("ln_rstd", [128, 1], F32)
        p.op("vector", lambda v: v.tensor_reduce(s, x_tile, axis=AX.X, op=OP.add), outs=[s], ins=[x_tile])
        p.op("vector", lambda v: v.tensor_scalar(nm, s, -1.0 / D, None, op0=OP.mult, op1=OP.bypass),
             outs=[nm], ins=[s])
        p.op("scalar", lambda sc: sc.activation(sq, x_tile, AF.Square, accum_out=ssq),
             outs=[sq, ssq], ins=[x_tile])
        p.op("vector", lambda v: v.tensor_tensor(var, nm, nm, op=OP.mult), outs=[var], ins=[nm])
        p.op("vector", lambda v: v.scalar_tensor_tensor(var, ssq, 1.0 / D, var, op0=OP.mult, op1=OP.subtract),
             outs=[var], ins=[ssq, var])
        p.op("scalar", lambda sc: sc.activation(rstd, var, AF.Sqrt, bias=epsc), outs=[rstd], ins=[var, epsc])
        p.op("vector", lambda v: v.reciprocal(rstd, rstd), outs=[rstd], ins=[rstd])
        p.op("vector", lambda v: v.tensor_scalar(xn_tile, x_tile, nm, rstd, op0=OP.add, op1=OP.mult),
             outs=[xn_tile], ins=[x_tile, nm, rstd])

    def phase_norm(self, x_in, xnb, b, xnb_lo=None, per_tile=None, xnf=None):
        p = self.p
        for i in range(self.NT):
            xt = self.sbuf(f"pn_x{i % self.pnb}", [128, self.D], F32)
            xn = self.sbuf(f"pn_xn{i % self.pnb}", [128, self.D], F32)
            p.dma("sync", lambda e, b=b, i=i, xt=xt: e.dma_start(
                out=xt, in_=x_in[b, i * 128:(i + 1) * 128, :]), outs=[xt], ins=[x_in])
            self.ln_tile(xt, xn)
            p.dma("gpsimd", lambda e, b=b, i=i, xn=xn, xnb=xnb: e.dma_start(
                out=xnb[b, i * 128:(i + 1) * 128, :], in_=xn), outs=[xnb], ins=[xn], sem_key="D:xnb")
            if xnf is not None:
                p.dma("scalar", lambda e, b=b, i=i, xn=xn, xnf=xnf: e.dma_start(
                    out=xnf[b, i * 128:(i + 1) * 128, :], in_=xn), outs=[xnf], ins=[xn], sem_key="D:xnf")
            if xnb_lo is not None:
                hi = self.sbuf("pn_hi", [128, self.D], BF16)
                lo = self.sbuf("pn_lo", [128, self.D], F32)
                p.op("vector", lambda v, hi=hi, xn=xn: v.tensor_copy(hi, xn), outs=[hi], ins=[xn])
                p.op("vector", lambda v, lo=lo, xn=xn, hi=hi: v.tensor_tensor(lo, xn, hi, op=OP.subtract),
                     outs=[lo], ins=[xn, hi])
                p.dma("gpsimd", lambda e, b=b, i=i, lo=lo, xnb_lo=xnb_lo: e.dma_start(
                    out=xnb_lo[b, i * 128:(i + 1) * 128, :], in_=lo), outs=[xnb_lo], ins=[lo],
                    sem_key="D:xnblo")
            if per_tile is not None:
                per_tile(b, i, xn)

    def load_xn_fm_b(self, xnb, b, name="xn_fm"):
        p = self.p
        t = self.sbuf(name, [128, self.DC, self.T], BF16)
        for c in range(self.DC):
            p.dma("sync", lambda e, b=b, c=c, t=t, xnb=xnb: e.dma_start_transpose(
                t[:, c, :], xnb[b, :, c * 128:(c + 1) * 128]),
                outs=[t], ins=[xnb], sem_key="D:" + name)
        return t

    def hilo_split(self, name, dram, kchunks, mcols, queue="gpsimd"):
        p = self.p
        stg = self.sbuf(name + "_stg", [128, kchunks, mcols], F32)
        p.dma(queue, lambda e, stg=stg, dram=dram: e.dma_start(
            out=stg, in_=dram.rearrange("(a p) m -> p a m", p=128)), outs=[stg], ins=[dram])
        hi = self.sbuf(name + "_hi", [128, kchunks, mcols], BF16)
        lo = self.sbuf(name + "_lo", [128, kchunks, mcols], BF16)
        tmp = self.sbuf(name + "_tmp", [128, kchunks, mcols], F32)
        p.op("vector", lambda v: v.tensor_copy(hi, stg), outs=[hi], ins=[stg])
        p.op("vector", lambda v: v.tensor_tensor(tmp, stg, hi, op=OP.subtract), outs=[tmp], ins=[stg, hi])
        p.op("vector", lambda v: v.tensor_copy(lo, tmp), outs=[lo], ins=[tmp])
        return hi, lo

    def mrow_of(self, mcols, mrow, n, bank=1):
        p = self.p
        pb = self.bk(bank)
        for i in range(n):
            p.op("tensor", lambda t, i=i: t.transpose(pb[0:1, 0:128], mcols[:, i:i + 1], self.idn),
                 outs=[pb], ins=[mcols, self.idn])
            p.op("scalar", lambda s, i=i: s.copy(mrow[0:1, i * 128:(i + 1) * 128], pb[0:1, 0:128]),
                 outs=[mrow], ins=[pb])

    def col_load(self, name, dram, nchunks, bank=1):
        p = self.p
        row = self.sbuf("c_rowbuf", [1, 512], F32)
        t = self.sbuf(name, [128, nchunks], F32)
        pb = self.bk(bank)
        for c0 in range(0, nchunks, 4):
            cn = min(4, nchunks - c0)
            p.dma("sync", lambda e, c0=c0, cn=cn, row=row, dram=dram: e.dma_start(
                out=row[0:1, 0:cn * 128], in_=dram[None, c0 * 128:(c0 + cn) * 128]),
                outs=[row], ins=[dram], sem_key="D:colload")
            for c in range(cn):
                p.op("tensor", lambda tt, c=c: tt.transpose(pb[0:128, 0:1], row[0:1, c * 128:(c + 1) * 128],
                                                            self.idn[0:1, 0:1]), outs=[pb], ins=[row, self.idn])
                p.op("scalar", lambda sc, c=c, c0=c0: sc.copy(t[:, c0 + c:c0 + c + 1], pb[0:128, 0:1]),
                     outs=[t], ins=[pb])
        return t

    def neg_scalar_max(self, mcols, n, negm, bank=1):
        p = self.p
        pb = self.bk(bank)
        mm = self.sbuf("nsm_mm", [128, 1], F32)
        mr = self.sbuf("nsm_mr", [1, 1], F32)
        mrb = self.sbuf("nsm_mrb", [1, 1], BF16)
        p.op("vector", lambda v: v.tensor_reduce(mm, mcols[:, 0:n], axis=AX.X, op=OP.max),
             outs=[mm], ins=[mcols])
        p.op("tensor", lambda t: t.transpose(pb[0:1, 0:128], mm, self.idn), outs=[pb], ins=[mm, self.idn])
        p.op("vector", lambda v: v.tensor_reduce(mr, pb[0:1, 0:128], axis=AX.X, op=OP.max, negate=True),
             outs=[mr], ins=[pb])
        p.op("vector", lambda v: v.tensor_copy(mrb, mr), outs=[mrb], ins=[mr])
        p.op("tensor", lambda t: t.matmul(pb[:, 0:1], self.ones_row[0:1, 0:128], mrb, start=True, stop=True),
             outs=[pb], ins=[self.ones_row, mrb])
        p.op("scalar", lambda sc: sc.copy(negm, pb[:, 0:1]), outs=[negm], ins=[pb])

    def recip_bf(self, v, out, in_):
        with self.nc.allow_low_precision(reason="1/Z is broadcast via bf16 matmul operand"):
            return v.reciprocal(out, in_)

    def bcast_row(self, row_f32, out_sbuf, n, bank=4, row_bf=None):
        p = self.p
        if row_bf is None:
            row_bf = self.sbuf("bc_rowbf", [1, 512], BF16)
        p.op("vector", lambda v: v.tensor_copy(row_bf[:, 0:n], row_f32[:, 0:n]),
             outs=[row_bf], ins=[row_f32])
        pb = self.bk(bank)
        p.op("tensor", lambda t: t.matmul(pb[:, 0:n], self.ones_row[0:1, 0:128], row_bf[:, 0:n],
                                          start=True, stop=True), outs=[pb], ins=[self.ones_row, row_bf])
        p.op("scalar", lambda s: s.copy(out_sbuf[:, 0:n], pb[:, 0:n]), outs=[out_sbuf], ins=[pb])



def build_l0(ctx, x_in, y_out, W, C, scr):
    p, nc = ctx.p, ctx.nc
    B, T, D, NT, DC = ctx.B, ctx.T, ctx.D, ctx.NT, ctx.DC
    HID = 2816
    HC = HID // 128
    NKV = 2

    WkT = ctx.load_transposed("l0_WkT", W["Wk"], 128, D, stage_off=0, out_off=124192)
    WvT = ctx.load_transposed("l0_WvT", W["Wv"], 128, D, stage_off=0, out_off=126240)
    WqT = [ctx.load_transposed(f"l0_WqT{h}", W["Wq"][h], 128, D, stage_off=0, out_off=128288 + 2048 * h) for h in range(NKV)]
    WoT = ctx.load_transposed("l0_WoT", W["Wo"], D, NKV * 128, stage_off=0)
    Wr_hi, Wr_lo = ctx.hilo_split("l0_Wr", W["Wr"], DC, 7)
    rbias = ctx.sbuf("l0_rbias", [1, 7], F32)
    p.dma("sync", lambda e: e.dma_start(out=rbias, in_=W["rb"][None, :]), outs=[rbias], ins=[W["rb"]])
    rb_bc = ctx.sbuf("l0_rb_bc", [128, 7], F32)
    ctx.bcast_row(rbias, rb_bc, 7, bank=0)

    ctab = {}
    for nm in ("cosk", "sink", "cosq", "sinq"):
        t = ctx.view("l0_" + nm, [128, NT, 64], BF16,
                     116000 + {"cosk": 0, "sink": 2048, "cosq": 4096, "sinq": 6144}[nm])
        p.dma("gpsimd", lambda e, t=t, nm=nm: e.dma_start(
            out=t, in_=C[nm].rearrange("(n p) i -> p n i", p=128)), outs=[t], ins=[C[nm]], sem_key="D:ctab")
        ctab[nm] = t

    def rope_evac(ps, out_t, i, cos, sin):
        E = ctx.sbuf("rp_e", [128, 64], F32)
        O = ctx.sbuf("rp_o", [128, 64], F32)
        t1 = ctx.sbuf("rp_t1", [128, 64], F32)
        p.op("vector", lambda v, i=i: v.tensor_copy(E, ps[:, 0:128:2]), outs=[E], ins=[ps])
        p.op("vector", lambda v, i=i: v.tensor_copy(O, ps[:, 1:128:2]), outs=[O], ins=[ps])
        p.op("vector", lambda v, i=i: v.tensor_tensor(t1, O, sin[:, i, :], op=OP.mult), outs=[t1], ins=[O, sin])
        p.op("vector", lambda v, i=i: v.tensor_tensor(out_t[:, 0:128:2], E, cos[:, i, :], op=OP.mult),
             outs=[out_t], ins=[E, cos])
        p.op("vector", lambda v: v.tensor_tensor(out_t[:, 0:128:2], out_t[:, 0:128:2], t1, op=OP.subtract),
             outs=[out_t], ins=[out_t, t1])
        p.op("vector", lambda v, i=i: v.tensor_tensor(t1, E, sin[:, i, :], op=OP.mult), outs=[t1], ins=[E, sin])
        p.op("vector", lambda v, i=i: v.tensor_tensor(out_t[:, 1:128:2], O, cos[:, i, :], op=OP.mult),
             outs=[out_t], ins=[O, cos])
        p.op("vector", lambda v: v.tensor_tensor(out_t[:, 1:128:2], out_t[:, 1:128:2], t1, op=OP.add),
             outs=[out_t], ins=[out_t, t1])

    for b in range(B):
        ctx.phase_norm(x_in, scr["xnb"], b, xnb_lo=scr["xnb_lo"])
        xn_fm = ctx.load_xn_fm_b(scr["xnb"], b)

        K_fm = ctx.view("l0_Kfm", [128, T], BF16, 0)
        Q_fm = [ctx.view(f"l0_Qfm{h}", [128, T], BF16, 4096 + h * 4096) for h in range(NKV)]
        V_tm = ctx.view("l0_Vtm", [128, NT, 128], BF16, 12288)
        attnout = ctx.view("l0_attnout", [128, NKV, T], BF16, 16384)
        rope_t = ctx.sbuf("l0_rope_t", [128, 128], F32)
        pA, pB = ctx.bk(0), ctx.bk(1)
        for i in range(NT):
            for c in range(DC):
                p.op("tensor", lambda t, i=i, c=c, xn_fm=xn_fm: t.matmul(
                    pA[:, 0:128], xn_fm[:, c, i * 128:(i + 1) * 128], WkT[:, c, :],
                    start=(c == 0), stop=(c == DC - 1)), outs=[pA], ins=[xn_fm, WkT])
            rope_evac(pA, rope_t, i, ctab["cosk"], ctab["sink"])
            p.op("tensor", lambda t, i=i: t.transpose(pB[:, 0:128], rope_t, ctx.idn),
                 outs=[pB], ins=[rope_t, ctx.idn])
            p.op("scalar", lambda s, i=i: s.copy(K_fm[:, i * 128:(i + 1) * 128], pB[:, 0:128]),
                 outs=[K_fm], ins=[pB])
            for c in range(DC):
                p.op("tensor", lambda t, i=i, c=c, xn_fm=xn_fm: t.matmul(
                    pA[:, 0:128], xn_fm[:, c, i * 128:(i + 1) * 128], WvT[:, c, :],
                    start=(c == 0), stop=(c == DC - 1)), outs=[pA], ins=[xn_fm, WvT])
            p.op("scalar", lambda s, i=i: s.copy(V_tm[:, i, :], pA[:, 0:128]), outs=[V_tm], ins=[pA])
            for h in range(NKV):
                for c in range(DC):
                    p.op("tensor", lambda t, i=i, c=c, h=h, xn_fm=xn_fm: t.matmul(
                        pA[:, 0:128], xn_fm[:, c, i * 128:(i + 1) * 128], WqT[h][:, c, :],
                        start=(c == 0), stop=(c == DC - 1)), outs=[pA], ins=[xn_fm, WqT[h]])
                rope_evac(pA, rope_t, i, ctab["cosq"], ctab["sinq"])
                p.op("tensor", lambda t, i=i, h=h: t.transpose(pB[:, 0:128], rope_t, ctx.idn),
                     outs=[pB], ins=[rope_t, ctx.idn])
                p.op("scalar", lambda s, i=i, h=h: s.copy(Q_fm[h][:, i * 128:(i + 1) * 128], pB[:, 0:128]),
                     outs=[Q_fm[h]], ins=[pB])

        for h in range(NKV):
            mcols = ctx.sbuf("l0_mcols", [128, NT], F32)
            mtmp = ctx.sbuf("l0_mtmp", [128, 4], F32)
            for i in range(NT):
                nk = (i + 1) * 128
                nchunk = (nk + 511) // 512
                for j in range(nchunk):
                    n = min(512, nk - j * 512)
                    p.op("tensor", lambda t, i=i, j=j, n=n, h=h: t.matmul(
                        pA[:, 0:n], Q_fm[h][:, i * 128:(i + 1) * 128], K_fm[:, j * 512:j * 512 + n],
                        start=True, stop=True), outs=[pA], ins=[Q_fm[h], K_fm])
                    p.op("vector", lambda v, j=j, n=n: v.tensor_reduce(
                        mtmp[:, j:j + 1], pA[:, 0:n], axis=AX.X, op=OP.max), outs=[mtmp], ins=[pA])
                p.op("vector", lambda v, i=i, nchunk=nchunk: v.tensor_reduce(
                    mcols[:, i:i + 1], mtmp[:, 0:nchunk], axis=AX.X, op=OP.max), outs=[mcols], ins=[mtmp])
            negm = ctx.sbuf("l0_negm", [128, 1], F32)
            ctx.neg_scalar_max(mcols, NT, negm, bank=1)

            pO, pZ = ctx.bk(3 if h == 0 else 6), ctx.bk(4 if h == 0 else 7)
            zrow = ctx.sbuf("l0_zrow", [1, 512], BF16)
            zbc = ctx.view("l0_zbc", [128, 512], BF16, 134432)
            for qc in range(T // 512):
                tq0 = qc * 512
                ktiles = list(range(0, (tq0 + 512) // 128))
                for kk, k in enumerate(ktiles):
                    st, sp = (kk == 0), (kk == len(ktiles) - 1)
                    pS = ctx.bk(2 if kk % 2 == 0 else 5)
                    PT = ctx.view(f"l0_PT{kk % 2}", [128, 512], BF16, 132384 + (kk % 2) * 1024)
                    p.op("tensor", lambda t, k=k, tq0=tq0, h=h, pS=pS: t.matmul(
                        pS[:, 0:512], K_fm[:, k * 128:(k + 1) * 128], Q_fm[h][:, tq0:tq0 + 512],
                        start=True, stop=True), outs=[pS], ins=[K_fm, Q_fm[h]])
                    p.op("scalar", lambda s, PT=PT, pS=pS: s.activation(PT, pS[:, 0:512], AF.Exp, bias=negm),
                         outs=[PT], ins=[pS, negm])
                    dsub = k - tq0 // 128
                    if 0 <= dsub < 4:
                        p.op("vector", lambda v, dsub=dsub, PT=PT: v.tensor_tensor(
                            PT[:, dsub * 128:(dsub + 1) * 128], PT[:, dsub * 128:(dsub + 1) * 128],
                            ctx.tri, op=OP.mult), outs=[PT], ins=[PT, ctx.tri])
                        for z in range(dsub):
                            p.op("gpsimd", lambda g, z=z, PT=PT: g.memset(PT[:, z * 128:(z + 1) * 128], 0.0),
                                 outs=[PT])
                    p.op("tensor", lambda t, k=k, st=st, sp=sp, PT=PT, pO=pO: t.matmul(
                        pO[:, 0:512], V_tm[:, k, :], PT, start=st, stop=sp), outs=[pO], ins=[V_tm, PT])
                    p.op("tensor", lambda t, st=st, sp=sp, PT=PT, pZ=pZ: t.matmul(
                        pZ[0:1, 0:512], ctx.ones_col, PT, start=st, stop=sp),
                        outs=[pZ], ins=[ctx.ones_col, PT])
                p.op("vector", lambda v, zrow=zrow, pZ=pZ: ctx.recip_bf(v, zrow, pZ[0:1, 0:512]),
                     outs=[zrow], ins=[pZ])
                p.op("tensor", lambda t, zrow=zrow, pZ=pZ: t.matmul(
                    pZ[:, 0:512], ctx.ones_row[0:1, 0:128], zrow, start=True, stop=True),
                    outs=[pZ], ins=[ctx.ones_row, zrow])
                p.op("scalar", lambda s, pZ=pZ, zbc=zbc: s.copy(zbc, pZ[:, 0:512]), outs=[zbc], ins=[pZ])
                p.op("vector", lambda v, h=h, tq0=tq0, pO=pO, zbc=zbc: v.tensor_tensor(
                    attnout[:, h, tq0:tq0 + 512], pO[:, 0:512], zbc, op=OP.mult),
                    outs=[attnout], ins=[pO, zbc])

        s_t = ctx.sbuf("ln_sq", [128, D], F32)
        for i in range(NT):
            for half in range(2):
                for hc in range(NKV):
                    p.op("tensor", lambda t, i=i, half=half, hc=hc: t.matmul(
                        pA[:, 0:512], attnout[:, hc, i * 128:(i + 1) * 128],
                        WoT[:, hc, half * 512:(half + 1) * 512],
                        start=(hc == 0), stop=(hc == NKV - 1)), outs=[pA], ins=[attnout, WoT])
                p.op("vector", lambda v, half=half: v.tensor_copy(
                    s_t[:, half * 512:(half + 1) * 512], pA[:, 0:512]), outs=[s_t], ins=[pA])
            p.dma("scalar", lambda e, i=i, b=b: e.dma_start(
                out=scr["s"][b, i * 128:(i + 1) * 128, :], in_=s_t), outs=[scr["s"]], ins=[s_t],
                sem_key="D:l0s")

        xn_lo_fm = ctx.view("xn_lo_fm", [128, DC, T], BF16, 24576)
        for c in range(DC):
            p.dma("sync", lambda e, b=b, c=c, t=xn_lo_fm: e.dma_start_transpose(
                t[:, c, :], scr["xnb_lo"][b, :, c * 128:(c + 1) * 128]),
                outs=[xn_lo_fm], ins=[scr["xnb_lo"]], sem_key="D:xnlofm")
        w_all = ctx.sbuf("l0_w_all", [128, NT, 7], F32)
        for i in range(NT):
            for ci, (xs, ws) in enumerate([(xn_fm, Wr_hi), (xn_lo_fm, Wr_hi), (xn_fm, Wr_lo)]):
                for c in range(DC):
                    p.op("tensor", lambda t, i=i, c=c, xs=xs, ws=ws, ci=ci: t.matmul(
                        pA[:, 0:7], xs[:, c, i * 128:(i + 1) * 128], ws[:, c, :],
                        start=(ci == 0 and c == 0), stop=(ci == 2 and c == DC - 1)),
                        outs=[pA], ins=[xs, ws])
            lg = ctx.sbuf("l0_lg", [128, 7], F32)
            pr = ctx.sbuf("l0_pr", [128, 7], F32)
            m1 = ctx.sbuf("l0_m1", [128, 1], F32)
            m2 = ctx.sbuf("l0_m2", [128, 1], F32)
            eq1 = ctx.sbuf("l0_eq1", [128, 7], F32)
            eq2 = ctx.sbuf("l0_eq2", [128, 7], F32)
            pr2 = ctx.sbuf("l0_pr2", [128, 7], F32)
            rs = ctx.sbuf("l0_rs", [128, 1], F32)
            p.op("vector", lambda v: v.tensor_tensor(lg, pA[:, 0:7], rb_bc, op=OP.mult),
                 outs=[lg], ins=[pA, rb_bc])
            p.op("scalar", lambda s: s.activation(pr, lg, AF.Sigmoid), outs=[pr], ins=[lg])
            p.op("vector", lambda v: v.tensor_reduce(m1, pr, axis=AX.X, op=OP.max), outs=[m1], ins=[pr])
            p.op("vector", lambda v: v.tensor_scalar(eq1, pr, m1, None, op0=OP.is_ge, op1=OP.bypass),
                 outs=[eq1], ins=[pr, m1])
            p.op("vector", lambda v: v.scalar_tensor_tensor(pr2, eq1, -4.0, pr, op0=OP.mult, op1=OP.add),
                 outs=[pr2], ins=[eq1, pr])
            p.op("vector", lambda v: v.tensor_reduce(m2, pr2, axis=AX.X, op=OP.max), outs=[m2], ins=[pr2])
            p.op("vector", lambda v: v.tensor_scalar(eq2, pr2, m2, None, op0=OP.is_ge, op1=OP.bypass),
                 outs=[eq2], ins=[pr2, m2])
            p.op("vector", lambda v: v.tensor_tensor(rs, m1, m2, op=OP.add), outs=[rs], ins=[m1, m2])
            p.op("vector", lambda v: v.reciprocal(rs, rs), outs=[rs], ins=[rs])
            p.op("vector", lambda v: v.tensor_tensor(eq1, eq1, eq2, op=OP.add), outs=[eq1], ins=[eq1, eq2])
            p.op("vector", lambda v: v.tensor_tensor(eq1, eq1, pr, op=OP.mult), outs=[eq1], ins=[eq1, pr])
            p.op("vector", lambda v, i=i: v.tensor_scalar(
                w_all[:, i, :], eq1, rs, None, op0=OP.mult, op1=OP.bypass), outs=[w_all], ins=[eq1, rs])

        UN = [3, 3, 3, 3, 3, 3, 3, 1]
        hdram = scr["h"]
        HT = T // 2
        NSUB = HT // 128
        for half_t in range(2):
            hb = half_t * HT
            comb = ctx.view("l0_comb", [128, NSUB, D], F32, 82048)
            for e in range(8):
                if e == 0:
                    gd, ud, dd = W["sg"], W["su"], W["sd"]
                else:
                    gd, ud, dd = W["rg"][e - 1], W["ru"][e - 1], W["rd"][e - 1]
                Wd = ctx.view("l0_Wd", [128, HC, D], BF16, 24576)
                p.dma("gpsimd", lambda ee, dd=dd, Wd=Wd: ee.dma_start(
                    out=Wd, in_=dd.rearrange("(a p) m -> p a m", p=128)),
                    outs=[Wd], ins=[dd], sem_key="D:l0wd")
                hc0 = 0
                for u, un in enumerate(UN):
                    ub = u % 2
                    Wg = ctx.view(f"l0_Wg{ub}", [128, DC, 3 * 128], BF16, 0 + ub * 6144)
                    Wu = ctx.view(f"l0_Wu{ub}", [128, DC, 3 * 128], BF16, 12288 + ub * 6144)
                    p.dma("gpsimd", lambda ee, gd=gd, Wg=Wg, hc0=hc0, un=un: ee.dma_start(
                        out=Wg[:, :, 0:un * 128],
                        in_=gd[:, hc0 * 128:(hc0 + un) * 128].rearrange("(a p) m -> p a m", p=128)),
                        outs=[Wg], ins=[gd], sem_key=f"D:l0wg{ub}")
                    p.dma("gpsimd", lambda ee, ud=ud, Wu=Wu, hc0=hc0, un=un: ee.dma_start(
                        out=Wu[:, :, 0:un * 128],
                        in_=ud[:, hc0 * 128:(hc0 + un) * 128].rearrange("(a p) m -> p a m", p=128)),
                        outs=[Wu], ins=[ud], sem_key=f"D:l0wu{ub}")
                    for uc in range(un):
                        hc = hc0 + uc
                        for tcc in range(HT // 512):
                            t0 = hb + tcc * 512
                            par = (hc * 2 + tcc) % 2
                            ht = ctx.sbuf(f"l0_ht{par}", [128, 512], BF16)
                            sil = ctx.sbuf(f"l0_sil{par}", [128, 512], BF16)
                            pg, pu = ctx.bk(5 if par == 0 else 0), ctx.bk(6 if par == 0 else 1)
                            for c in range(DC):
                                p.op("tensor", lambda t, uc=uc, c=c, t0=t0, Wg=Wg, xn_fm=xn_fm, pg=pg: t.matmul(
                                    pg[:, 0:512], Wg[:, c, uc * 128:(uc + 1) * 128], xn_fm[:, c, t0:t0 + 512],
                                    start=(c == 0), stop=(c == DC - 1)), outs=[pg], ins=[Wg, xn_fm])
                            for c in range(DC):
                                p.op("tensor", lambda t, uc=uc, c=c, t0=t0, Wu=Wu, xn_fm=xn_fm, pu=pu: t.matmul(
                                    pu[:, 0:512], Wu[:, c, uc * 128:(uc + 1) * 128], xn_fm[:, c, t0:t0 + 512],
                                    start=(c == 0), stop=(c == DC - 1)), outs=[pu], ins=[Wu, xn_fm])
                            p.op("scalar", lambda s, ht=ht, sil=sil, pg=pg: s.activation(sil, pg, AF.Silu),
                                 outs=[sil], ins=[pg])
                            p.op("vector", lambda v, ht=ht, sil=sil, pu=pu: v.tensor_tensor(ht, pu, sil, op=OP.mult),
                                 outs=[ht], ins=[pu, sil])
                            p.dma("scalar", lambda ee, hc=hc, tcc=tcc, ht=ht: ee.dma_start(
                                out=hdram[:, hc, tcc * 512:(tcc + 1) * 512], in_=ht),
                                outs=[hdram], ins=[ht], sem_key="D:l0h")
                    hc0 += un
                for sub in range(NSUB):
                    i = half_t * NSUB + sub
                    hsub = ctx.view(f"l0_hs{sub % 2}", [128, HC, 128], BF16, 69664 + (sub % 2) * 5888)
                    for hh in range(2):
                        p.dma("sync", lambda ee, sub=sub, hsub=hsub, hh=hh: ee.dma_start(
                            out=hsub[:, hh * 11:(hh + 1) * 11, :],
                            in_=hdram[:, hh * 11:(hh + 1) * 11, sub * 128:(sub + 1) * 128]),
                            outs=[hsub], ins=[hdram], sem_key=f"D:l0hs{sub % 2}")
                    for halfd in range(2):
                        pd = ctx.bk(7 if (sub * 2 + halfd) % 2 == 0 else 2)
                        for hc in range(HC):
                            p.op("tensor", lambda t, sub=sub, halfd=halfd, Wd=Wd, hc=hc, hsub=hsub, pd=pd: t.matmul(
                                pd[:, 0:512], hsub[:, hc, :], Wd[:, hc, halfd * 512:(halfd + 1) * 512],
                                start=(hc == 0), stop=(hc == HC - 1)), outs=[pd], ins=[hsub, Wd])
                        if e == 0:
                            p.op("vector", lambda v, sub=sub, halfd=halfd, pd=pd: v.tensor_copy(
                                comb[:, sub, halfd * 512:(halfd + 1) * 512], pd[:, 0:512]),
                                outs=[comb], ins=[pd])
                        else:
                            p.op("vector", lambda v, i=i, sub=sub, halfd=halfd, e=e, pd=pd: v.scalar_tensor_tensor(
                                comb[:, sub, halfd * 512:(halfd + 1) * 512], pd[:, 0:512],
                                w_all[:, i, e - 1:e], comb[:, sub, halfd * 512:(halfd + 1) * 512],
                                op0=OP.mult, op1=OP.add), outs=[comb], ins=[pd, w_all, comb])
            for sub in range(NSUB):
                i = half_t * NSUB + sub
                xt = ctx.sbuf(f"pn_x{sub % ctx.pnb}", [128, D], F32)
                st = ctx.sbuf(f"pn_xn{sub % ctx.pnb}", [128, D], F32)
                gt = ctx.sbuf("ln_sq", [128, D], F32)
                p.dma("sync", lambda e, b=b, i=i, xt=xt: e.dma_start(
                    out=xt, in_=x_in[b, i * 128:(i + 1) * 128, :]), outs=[xt], ins=[x_in])
                p.dma("sync", lambda e, b=b, i=i, st=st: e.dma_start(
                    out=st, in_=scr["s"][b, i * 128:(i + 1) * 128, :]), outs=[st], ins=[scr["s"]],
                    sem_key="D:l0s_rd")
                p.op("scalar", lambda s, sub=sub, gt=gt: s.activation(gt, comb[:, sub, :], AF.Silu),
                     outs=[gt], ins=[comb])
                p.op("vector", lambda v, gt=gt, st=st: v.tensor_tensor(gt, gt, st, op=OP.mult),
                     outs=[gt], ins=[gt, st])
                p.op("vector", lambda v, gt=gt, xt=xt: v.tensor_tensor(gt, gt, xt, op=OP.add),
                     outs=[gt], ins=[gt, xt])
                p.dma("scalar", lambda e, b=b, i=i, gt=gt: e.dma_start(
                    out=y_out[b, i * 128:(i + 1) * 128, :], in_=gt), outs=[y_out], ins=[gt], sem_key="D:y")


def l0_consts(T=2048):
    pos = np.arange(0, 128, 2, dtype=np.float32)
    theta = (10000.0 ** (-2.0 * pos / 128.0)).astype(np.float32)
    ang = (np.arange(T, dtype=np.float32)[:, None] * theta[None, :]).astype(np.float32)
    cos = np.cos(ang.astype(np.float64)).astype(np.float32)
    sin = np.sin(ang.astype(np.float64)).astype(np.float32)
    s = np.float32(128.0 ** -0.5)
    return {"cosk": cos, "sink": sin, "cosq": (cos * s).astype(np.float32),
            "sinq": (sin * s).astype(np.float32)}



def build_l1(ctx, x_in, y_out, W, scr):
    ctx.pnb = 1
    p, nc = ctx.p, ctx.nc
    B, T, D, NT, DC = ctx.B, ctx.T, ctx.D, ctx.NT, ctx.DC
    NH, DH, WIN = 16, 64, 256
    NHP = NH // 2
    pA, pB = ctx.bk(0), ctx.bk(1)

    wout = ctx.load_cast("l1_wout", W["w_out"], DC, D, view_off=102400)
    w1l = ctx.load_cast("l1_w1l", W["w1_light"], DC, 512, view_off=118784)
    w2l = ctx.load_cast("l1_w2l", W["w2_light"], 4, D, view_off=126976)
    b1l = ctx.col_load("l1_b1l", W["b1_light"], 4)
    b1h = ctx.col_load("l1_b1h", W["b1_heavy"], 32)
    b2l_row = ctx.sbuf("l1_b2l", [1, D], BF16)
    p.dma("gpsimd", lambda e: e.dma_start(out=b2l_row, in_=W["b2_light"][None, :]),
          outs=[b2l_row], ins=[W["b2_light"]])
    b2h_row = ctx.sbuf("l1_b2h", [1, D], BF16)
    p.dma("gpsimd", lambda e: e.dma_start(out=b2h_row, in_=W["b2_heavy"][None, :]),
          outs=[b2h_row], ins=[W["b2_heavy"]])
    gml = ctx.col_load("l1_gml", W["gamma_light"], DC)
    gmh = ctx.col_load("l1_gmh", W["gamma_heavy"], DC)
    rt_row = ctx.sbuf("c_rowbuf", [1, 512], F32)
    ones_row_f = ctx.sbuf("c_ones_row_f", [1, 128], F32)
    p.op("gpsimd", lambda g: g.memset(ones_row_f, 1.0), outs=[ones_row_f])
    rt_bc = ctx.sbuf("l1_rt_bc", [128, D], F32)
    for hf in range(2):
        p.dma("sync", lambda e, hf=hf: e.dma_start(
            out=rt_row, in_=W["routing_token"][None, hf * 512:(hf + 1) * 512]),
            outs=[rt_row], ins=[W["routing_token"]], sem_key="D:colload")
        p.op("tensor", lambda t, hf=hf: t.matmul(pA[:, 0:512], ones_row_f, rt_row,
                                                 start=True, stop=True), outs=[pA], ins=[ones_row_f, rt_row])
        p.op("vector", lambda v, hf=hf: v.tensor_copy(rt_bc[:, hf * 512:(hf + 1) * 512], pA[:, 0:512]),
             outs=[rt_bc], ins=[pA])

    msks = {}
    c2v = ctx.sbuf("l1_c2v", [128, 256], F32)
    for dt in (-128, 0, 128, 256):
        m = ctx.sbuf(f"l1_msk{dt}", [128, 256], BF16)
        p.op("gpsimd", lambda g, dt=dt: g.iota(c2v, base=dt, pattern=[[1, 256]], channel_multiplier=-1,
                                               allow_small_or_imprecise_dtypes=True), outs=[c2v])
        t1 = ctx.sbuf("l1_mt1", [128, 256], F32)
        p.op("vector", lambda v: v.tensor_scalar(t1, c2v, -0.5, None, op0=OP.is_gt, op1=OP.bypass),
             outs=[t1], ins=[c2v])
        p.op("vector", lambda v, m=m: v.tensor_scalar(m, c2v, 255.5, None, op0=OP.is_lt, op1=OP.bypass),
             outs=[m], ins=[c2v])
        p.op("vector", lambda v, m=m: v.tensor_tensor(m, m, t1, op=OP.mult), outs=[m], ins=[m, t1])
        msks[dt] = m

    HT = T // 2
    NSUB = HT // 128
    for b in range(B):
        s_cols = ctx.sbuf("l1_scols", [128, NT], F32)
        sdump = ctx.sbuf("ln_sq", [128, self.D] if False else [128, D], F32)

        def score_tile(bb, i, xn_tile):
            p.op("vector", lambda v, i=i, xn_tile=xn_tile: v.tensor_tensor(
                sdump, xn_tile, rt_bc, op=OP.mult), outs=[sdump], ins=[xn_tile, rt_bc])
            p.op("vector", lambda v, i=i: v.tensor_reduce(
                s_cols[:, i:i + 1], sdump, axis=AX.X, op=OP.add), outs=[s_cols], ins=[sdump])

        rms_cols = ctx.sbuf("l1_rmscols", [128, NT], F32)

        def rms_tile(bb, i, xn_tile):
            var, rstd = ctx.sb["ln_var"], ctx.sb["ln_rstd"]
            t1 = ctx.sbuf("l1_rt1", [128, 1], F32)
            p.op("scalar", lambda sc, i=i: sc.activation(t1, var, AF.Sqrt, scale=float(D)),
                 outs=[t1], ins=[var])
            p.op("vector", lambda v, i=i: v.tensor_tensor(t1, t1, rstd, op=OP.mult), outs=[t1], ins=[t1, rstd])
            p.op("vector", lambda v, i=i: v.tensor_scalar(t1, t1, 1e-12, None, op0=OP.max, op1=OP.bypass),
                 outs=[t1], ins=[t1])
            p.op("vector", lambda v, i=i: v.reciprocal(t1, t1), outs=[t1], ins=[t1])
            p.op("vector", lambda v, i=i: v.tensor_scalar(
                rms_cols[:, i:i + 1], t1, float(np.sqrt(D)), None, op0=OP.mult, op1=OP.bypass),
                outs=[rms_cols], ins=[t1])

        def per_tile(bb, i, xn_tile):
            score_tile(bb, i, xn_tile)
            rms_tile(bb, i, xn_tile)

        ctx.phase_norm(x_in, scr["xnb"], b, per_tile=per_tile)
        xn_fm = ctx.load_xn_fm_b(scr["xnb"], b)

        thr = ctx.sbuf("l1_thr", [128, 1], F32)
        cnt = ctx.sbuf("l1_cnt", [128, 1], F32)
        cntb = ctx.sbuf("l1_cntb", [128, 1], BF16)
        tot = ctx.sbuf("l1_tot", [128, 1], F32)
        d01 = ctx.sbuf("l1_d01", [128, 1], F32)
        bmask = ctx.sbuf("l1_bmask", [128, NT], F32)
        ones_mat = ctx.sbuf("c_ones_mat", [128, 128], BF16)
        p.op("gpsimd", lambda g: g.memset(ones_mat, 1.0), outs=[ones_mat])
        p.op("gpsimd", lambda g: g.memset(thr, 0.0), outs=[thr])
        step = 256.0
        for it in range(40):
            p.op("vector", lambda v: v.tensor_scalar(bmask, s_cols, thr, None, op0=OP.is_gt, op1=OP.bypass),
                 outs=[bmask], ins=[s_cols, thr])
            p.op("vector", lambda v: v.tensor_reduce(cnt, bmask, axis=AX.X, op=OP.add), outs=[cnt], ins=[bmask])
            p.op("vector", lambda v: v.tensor_copy(cntb, cnt), outs=[cntb], ins=[cnt])
            p.op("tensor", lambda t: t.matmul(pB[:, 0:1], ones_mat, cntb, start=True, stop=True),
                 outs=[pB], ins=[ones_mat, cntb])
            p.op("scalar", lambda s: s.copy(tot, pB[:, 0:1]), outs=[tot], ins=[pB])
            p.op("vector", lambda v: v.tensor_scalar(d01, tot, float(T // 2) + 0.5, None,
                                                     op0=OP.is_gt, op1=OP.bypass), outs=[d01], ins=[tot])
            p.op("vector", lambda v, step=step: v.scalar_tensor_tensor(
                thr, d01, 2.0 * step, thr, op0=OP.mult, op1=OP.add), outs=[thr], ins=[d01, thr])
            p.op("vector", lambda v, step=step: v.tensor_scalar(
                thr, thr, -step, None, op0=OP.add, op1=OP.bypass), outs=[thr], ins=[thr])
            step = step / 2.0
        p.op("vector", lambda v: v.tensor_scalar(bmask, s_cols, thr, None, op0=OP.is_gt, op1=OP.bypass),
             outs=[bmask], ins=[s_cols, thr])

        v_tm = ctx.view("l1_vtm", [128, NT, D], BF16, 0)
        attnout = ctx.view("l1_attnout", [128, NHP, T], BF16, 32768)
        wv = ctx.view("l1_wv", [128, DC, D], BF16, 32768)
        p.dma("gpsimd", lambda e: e.dma_start(
            out=wv, in_=W["w_qkv"][:, 2048:3072].rearrange("(a p) m -> p a m", p=128)),
            outs=[wv], ins=[W["w_qkv"]], sem_key="D:l1wv")
        for i in range(NT):
            for half in range(2):
                for c in range(DC):
                    p.op("tensor", lambda t, i=i, half=half, c=c, xn_fm=xn_fm: t.matmul(
                        pA[:, 0:512], xn_fm[:, c, i * 128:(i + 1) * 128], wv[:, c, half * 512:(half + 1) * 512],
                        start=(c == 0), stop=(c == DC - 1)), outs=[pA], ins=[xn_fm, wv])
                p.op("scalar", lambda s, i=i, half=half: s.copy(
                    v_tm[:, i, half * 512:(half + 1) * 512], pA[:, 0:512]), outs=[v_tm], ins=[pA])

        for hp in range(NHP):
            qw = ctx.view(f"l1_qw{hp % 2}", [128, DC, 128], BF16, 65536 + (hp % 2) * 2048)
            kw = ctx.view(f"l1_kw{hp % 2}", [128, DC, 128], BF16, 69632 + (hp % 2) * 2048)
            p.dma("gpsimd", lambda e, hp=hp, qw=qw: e.dma_start(
                out=qw, in_=W["w_qkv"][:, hp * 128:(hp + 1) * 128].rearrange("(a p) m -> p a m", p=128)),
                outs=[qw], ins=[W["w_qkv"]], sem_key=f"D:l1qw{hp % 2}")
            p.dma("gpsimd", lambda e, hp=hp, kw=kw: e.dma_start(
                out=kw, in_=W["w_qkv"][:, 1024 + hp * 128:1024 + (hp + 1) * 128].rearrange(
                    "(a p) m -> p a m", p=128)),
                outs=[kw], ins=[W["w_qkv"]], sem_key=f"D:l1kw{hp % 2}")
            q_fm = ctx.view("l1_qfm", [128, T], BF16, 73728)
            k_fm = ctx.view("l1_kfm", [128, T], BF16, 77824)
            for tc in range(T // 512):
                for c in range(DC):
                    p.op("tensor", lambda t, tc=tc, c=c, qw=qw, xn_fm=xn_fm: t.matmul(
                        pA[:, 0:512], qw[:, c, :], xn_fm[:, c, tc * 512:(tc + 1) * 512],
                        start=(c == 0), stop=(c == DC - 1)), outs=[pA], ins=[qw, xn_fm])
                p.op("scalar", lambda s, tc=tc: s.activation(
                    q_fm[:, tc * 512:(tc + 1) * 512], pA[:, 0:512], AF.Copy, scale=float(DH ** -0.5)),
                    outs=[q_fm], ins=[pA])
                for c in range(DC):
                    p.op("tensor", lambda t, tc=tc, c=c, kw=kw, xn_fm=xn_fm: t.matmul(
                        pA[:, 0:512], kw[:, c, :], xn_fm[:, c, tc * 512:(tc + 1) * 512],
                        start=(c == 0), stop=(c == DC - 1)), outs=[pA], ins=[kw, xn_fm])
                p.op("scalar", lambda s, tc=tc: s.copy(k_fm[:, tc * 512:(tc + 1) * 512], pA[:, 0:512]),
                     outs=[k_fm], ins=[pA])
            for hh in range(2):
                h = hp * 2 + hh
                hb64 = hh * 64
                mcols = ctx.sbuf("l1_mcols", [128, NT], F32)
                for i in range(NT):
                    w0 = max(0, (i // 2) * 256 - 256)
                    wn = (i // 2) * 256 + 256 - w0
                    p.op("tensor", lambda t, i=i, w0=w0, wn=wn, hb64=hb64: t.matmul(
                        pA[:, 0:wn], q_fm[hb64:hb64 + 64, i * 128:(i + 1) * 128],
                        k_fm[hb64:hb64 + 64, w0:w0 + wn], start=True, stop=True),
                        outs=[pA], ins=[q_fm, k_fm])
                    p.op("vector", lambda v, i=i, wn=wn: v.tensor_reduce(
                        mcols[:, i:i + 1], pA[:, 0:wn], axis=AX.X, op=OP.max), outs=[mcols], ins=[pA])
                negm = ctx.sbuf("l1_negm", [128, 1], F32)
                ctx.neg_scalar_max(mcols, NT, negm, bank=1)
                pO, pZ = ctx.bk(3 if hh == 0 else 6), ctx.bk(4 if hh == 0 else 7)
                zrow = ctx.sbuf(f"l1_zrow{hh}", [1, 256], BF16)
                zbc = ctx.sbuf(f"l1_zbc{hh}", [128, 256], BF16)
                for blk in range(T // 256):
                    tq0 = blk * 256
                    kts = [k for k in range(2 * blk - 2, 2 * blk + 2) if k >= 0]
                    for kk, kt in enumerate(kts):
                        st, sp = (kk == 0), (kk == len(kts) - 1)
                        pS = ctx.bk(2 if kk % 2 == 0 else 5)
                        PT = ctx.sbuf(f"l1_PT{kk % 2}", [128, 256], BF16)
                        p.op("tensor", lambda t, kt=kt, tq0=tq0, hb64=hb64, pS=pS: t.matmul(
                            pS[:, 0:256], k_fm[hb64:hb64 + 64, kt * 128:(kt + 1) * 128],
                            q_fm[hb64:hb64 + 64, tq0:tq0 + 256], start=True, stop=True),
                            outs=[pS], ins=[k_fm, q_fm])
                        p.op("scalar", lambda s, PT=PT, pS=pS: s.activation(PT, pS[:, 0:256], AF.Exp, bias=negm),
                             outs=[PT], ins=[pS, negm])
                        dt = tq0 - kt * 128
                        p.op("vector", lambda v, dt=dt, PT=PT: v.tensor_tensor(PT, PT, msks[dt], op=OP.mult),
                             outs=[PT], ins=[PT, msks[dt]])
                        p.op("tensor", lambda t, kt=kt, st=st, sp=sp, h=h, PT=PT, pO=pO: t.matmul(
                            pO[0:64, 0:256], v_tm[:, kt, h * 64:(h + 1) * 64], PT, start=st, stop=sp),
                            outs=[pO], ins=[v_tm, PT])
                        p.op("tensor", lambda t, st=st, sp=sp, PT=PT, pZ=pZ: t.matmul(
                            pZ[0:1, 0:256], ctx.ones_col, PT, start=st, stop=sp),
                            outs=[pZ], ins=[ctx.ones_col, PT])
                    p.op("vector", lambda v, pZ=pZ, zrow=zrow: ctx.recip_bf(v, zrow, pZ[0:1, 0:256]),
                         outs=[zrow], ins=[pZ])
                    p.op("tensor", lambda t, pZ=pZ, zrow=zrow: t.matmul(
                        pZ[:, 0:256], ctx.ones_row[0:1, 0:128], zrow,
                        start=True, stop=True), outs=[pZ], ins=[ctx.ones_row, zrow])
                    p.op("scalar", lambda s, pZ=pZ, zbc=zbc: s.copy(zbc, pZ[:, 0:256]), outs=[zbc], ins=[pZ])
                    p.op("vector", lambda v, hp=hp, hb64=hb64, tq0=tq0, pO=pO, zbc=zbc: v.tensor_tensor(
                        attnout[hb64:hb64 + 64, hp, tq0:tq0 + 256], pO[0:64, 0:256], zbc[0:64, :], op=OP.mult),
                        outs=[attnout], ins=[pO, zbc])
        s_t = ctx.sbuf("ln_sq", [128, D], F32)
        for i in range(NT):
            for half in range(2):
                for hc in range(NHP):
                    p.op("tensor", lambda t, i=i, half=half, hc=hc: t.matmul(
                        pA[:, 0:512], attnout[:, hc, i * 128:(i + 1) * 128],
                        wout[:, hc, half * 512:(half + 1) * 512],
                        start=(hc == 0), stop=(hc == NHP - 1)), outs=[pA], ins=[attnout, wout])
                p.op("vector", lambda v, half=half: v.tensor_copy(
                    s_t[:, half * 512:(half + 1) * 512], pA[:, 0:512]), outs=[s_t], ins=[pA])
            p.dma("scalar", lambda e, i=i, b=b: e.dma_start(
                out=scr["s"][b, i * 128:(i + 1) * 128, :], in_=s_t), outs=[scr["s"]], ins=[s_t],
                sem_key="D:l1s")

        rms_row = ctx.sbuf("l1_rms_row", [1, T], BF16)
        for i in range(NT):
            p.op("tensor", lambda t, i=i: t.transpose(pB[0:1, 0:128], rms_cols[:, i:i + 1], ctx.idn),
                 outs=[pB], ins=[rms_cols, ctx.idn])
            p.op("scalar", lambda s, i=i: s.copy(rms_row[0:1, i * 128:(i + 1) * 128], pB[0:1, 0:128]),
                 outs=[rms_row], ins=[pB])
        rmsbc = ctx.sbuf("l1_rmsbc", [128, 512], BF16)
        pg = ctx.bk(5)
        for tc in range(T // 512):
            t0 = tc * 512
            i0 = tc * 4
            p.op("tensor", lambda t, t0=t0: t.matmul(pg[:, 0:512], ctx.ones_row[0:1, 0:128],
                                                     rms_row[0:1, t0:t0 + 512], start=True, stop=True),
                 outs=[pg], ins=[ctx.ones_row, rms_row])
            p.op("scalar", lambda s: s.copy(rmsbc, pg[:, 0:512]), outs=[rmsbc], ins=[pg])
            rnl = ctx.view("l1_rnl", [128, DC, 512], BF16, 0)
            rnh = ctx.view("l1_rnh", [128, DC, 512], BF16, 8192)
            for c in range(DC):
                p.op("vector", lambda v, c=c, t0=t0: v.tensor_tensor(
                    rnl[:, c, :], xn_fm[:, c, t0:t0 + 512], rmsbc, op=OP.mult), outs=[rnl], ins=[xn_fm, rmsbc])
                p.op("vector", lambda v, c=c: v.tensor_scalar(
                    rnh[:, c, :], rnl[:, c, :], gmh[:, c:c + 1], None, op0=OP.mult, op1=OP.bypass),
                    outs=[rnh], ins=[rnl, gmh])
                p.op("vector", lambda v, c=c: v.tensor_scalar(
                    rnl[:, c, :], rnl[:, c, :], gml[:, c:c + 1], None, op0=OP.mult, op1=OP.bypass),
                    outs=[rnl], ins=[rnl, gml])
            h1l = ctx.view("l1_h1l", [128, 4, 512], BF16, 16384)
            for hc in range(4):
                for c in range(DC):
                    p.op("tensor", lambda t, hc=hc, c=c: t.matmul(
                        pg[:, 0:512], w1l[:, c, hc * 128:(hc + 1) * 128], rnl[:, c, :],
                        start=(c == 0), stop=(c == DC - 1)), outs=[pg], ins=[w1l, rnl])
                p.op("scalar", lambda s, hc=hc: s.activation(
                    h1l[:, hc, :], pg[:, 0:512], AF.Gelu, bias=b1l[:, hc:hc + 1]),
                    outs=[h1l], ins=[pg, b1l])
            lt = ctx.view("l1_lt", [128, 4, D], F32, 20480)
            for sub in range(4):
                for half in range(2):
                    for hc in range(4):
                        p.op("tensor", lambda t, sub=sub, half=half, hc=hc: t.matmul(
                            pg[:, 0:512], h1l[:, hc, sub * 128:(sub + 1) * 128],
                            w2l[:, hc, half * 512:(half + 1) * 512],
                            start=(hc == 0), stop=False), outs=[pg], ins=[h1l, w2l])
                    p.op("tensor", lambda t, sub=sub, half=half: t.matmul(
                        pg[:, 0:512], ctx.ones_row[0:1, 0:128], b2l_row[0:1, half * 512:(half + 1) * 512],
                        start=False, stop=True), outs=[pg], ins=[ctx.ones_row, b2l_row])
                    p.op("vector", lambda v, sub=sub, half=half: v.tensor_copy(
                        lt[:, sub, half * 512:(half + 1) * 512], pg[:, 0:512]), outs=[lt], ins=[pg])
            h1h = ctx.view("l1_h1h", [128, 32, 512], BF16, 36864)
            for q in range(4):
                w1h = ctx.view("l1_w1h", [128, DC, 1024], BF16, 69632)
                p.dma("gpsimd", lambda e, q=q, w1h=w1h: e.dma_start(
                    out=w1h, in_=W["w1_heavy"][:, q * 1024:(q + 1) * 1024].rearrange("(a p) m -> p a m", p=128)),
                    outs=[w1h], ins=[W["w1_heavy"]], sem_key="D:l1w1h")
                for hcq in range(8):
                    hc = q * 8 + hcq
                    for c in range(DC):
                        p.op("tensor", lambda t, hcq=hcq, c=c, w1h=w1h: t.matmul(
                            pg[:, 0:512], w1h[:, c, hcq * 128:(hcq + 1) * 128], rnh[:, c, :],
                            start=(c == 0), stop=(c == DC - 1)), outs=[pg], ins=[w1h, rnh])
                    p.op("scalar", lambda s, hc=hc: s.activation(
                        h1h[:, hc, :], pg[:, 0:512], AF.Gelu, bias=b1h[:, hc:hc + 1]),
                        outs=[h1h], ins=[pg, b1h])
            for u in range(8):
                w2h = ctx.view(f"l1_w2h{u % 2}", [128, 4, D], BF16, 86016 + (u % 2) * 8192)
                p.dma("gpsimd", lambda e, u=u, w2h=w2h: e.dma_start(
                    out=w2h, in_=W["w2_heavy"][u * 512:(u + 1) * 512, :].rearrange("(a p) m -> p a m", p=128)),
                    outs=[w2h], ins=[W["w2_heavy"]], sem_key=f"D:l1w2h{u % 2}")
                for sub in range(4):
                    for half in range(2):
                        bidx = sub * 2 + half
                        for hcu in range(4):
                            hc = u * 4 + hcu
                            p.op("tensor", lambda t, sub=sub, half=half, hcu=hcu, hc=hc, w2h=w2h, u=u,
                                 bidx=bidx: t.matmul(
                                ctx.bk(bidx)[:, 0:512], h1h[:, hc, sub * 128:(sub + 1) * 128],
                                w2h[:, hcu, half * 512:(half + 1) * 512],
                                start=(u == 0 and hcu == 0), stop=False),
                                outs=[ctx.bk(bidx)], ins=[h1h, w2h])
            for sub in range(4):
                i = i0 + sub
                for half in range(2):
                    bidx = sub * 2 + half
                    p.op("tensor", lambda t, half=half, bidx=bidx: t.matmul(
                        ctx.bk(bidx)[:, 0:512], ctx.ones_row[0:1, 0:128],
                        b2h_row[0:1, half * 512:(half + 1) * 512], start=False, stop=True),
                        outs=[ctx.bk(bidx)], ins=[ctx.ones_row, b2h_row])
                xt = ctx.sbuf(f"pn_x{sub % ctx.pnb}", [128, D], F32)
                st = ctx.sbuf(f"pn_xn{sub % ctx.pnb}", [128, D], F32)
                gt = ctx.sbuf("l1_rt_bc", [128, D], F32)
                p.dma("sync", lambda e, b=b, i=i, xt=xt: e.dma_start(
                    out=xt, in_=x_in[b, i * 128:(i + 1) * 128, :]), outs=[xt], ins=[x_in])
                p.dma("sync", lambda e, b=b, i=i, st=st: e.dma_start(
                    out=st, in_=scr["s"][b, i * 128:(i + 1) * 128, :]), outs=[st], ins=[scr["s"]],
                    sem_key="D:l1s_rd")
                for half in range(2):
                    bidx = sub * 2 + half
                    p.op("vector", lambda v, sub=sub, half=half, i=i, bidx=bidx: v.scalar_tensor_tensor(
                        gt[:, half * 512:(half + 1) * 512], ctx.bk(bidx)[:, 0:512], bmask[:, i:i + 1],
                        lt[:, sub, half * 512:(half + 1) * 512], op0=OP.mult, op1=OP.add),
                        outs=[gt], ins=[ctx.bk(bidx), bmask, lt])
                gs = ctx.sbuf("ln_sq", [128, D], F32)
                p.op("scalar", lambda s, gt=gt, gs=gs: s.activation(gs, gt, AF.Silu), outs=[gs], ins=[gt])
                p.op("vector", lambda v, gs=gs, st=st: v.tensor_tensor(gs, gs, st, op=OP.mult),
                     outs=[gs], ins=[gs, st])
                p.op("vector", lambda v, gs=gs, xt=xt: v.tensor_tensor(gs, gs, xt, op=OP.add),
                     outs=[gs], ins=[gs, xt])
                p.dma("scalar", lambda e, b=b, i=i, gs=gs: e.dma_start(
                    out=y_out[b, i * 128:(i + 1) * 128, :], in_=gs), outs=[y_out], ins=[gs], sem_key="D:y")



def build_l2(ctx, x_in, y_out, W, scr):
    ctx.pnb = 1
    p, nc = ctx.p, ctx.nc
    B, T, D, NT, DC = ctx.B, ctx.T, ctx.D, ctx.NT, ctx.DC
    NH, DH = 16, 64
    NHP = NH // 2
    pA, pB = ctx.bk(0), ctx.bk(1)

    sim_stg = ctx.sbuf("l2_sim_stg", [128, DC, 16], F32)
    p.dma("sync", lambda e: e.dma_start(out=sim_stg, in_=W["sim_matrix"].rearrange("(a p) m -> p a m", p=128)),
          outs=[sim_stg], ins=[W["sim_matrix"]])
    ones_col_f = ctx.sbuf("c_ones_col_f", [128, 1], F32)
    p.op("gpsimd", lambda g: g.memset(ones_col_f, 1.0), outs=[ones_col_f])
    simsq = ctx.sbuf("l2_simsq", [128, DC, 16], F32)
    p.op("scalar", lambda s: s.activation(simsq, sim_stg, AF.Square), outs=[simsq], ins=[sim_stg])
    for c in range(DC):
        p.op("tensor", lambda t, c=c: t.matmul(pB[0:16, 0:1], simsq[:, c, :], ones_col_f,
                                               start=(c == 0), stop=(c == DC - 1)),
             outs=[pB], ins=[simsq, ones_col_f])
    cn = ctx.sbuf("l2_cn", [16, 1], F32)
    p.op("scalar", lambda s: s.activation(cn, pB[0:16, 0:1], AF.Sqrt), outs=[cn], ins=[pB])
    p.op("vector", lambda v: v.tensor_scalar(cn, cn, 1e-12, None, op0=OP.max, op1=OP.bypass),
         outs=[cn], ins=[cn])
    p.op("vector", lambda v: v.reciprocal(cn, cn), outs=[cn], ins=[cn])
    cnr = ctx.sbuf("l2_cnr", [1, 16], F32)
    p.op("tensor", lambda t: t.transpose(pB[0:1, 0:16], cn, ctx.idn[0:16, 0:16]), outs=[pB], ins=[cn, ctx.idn])
    p.op("scalar", lambda s: s.copy(cnr, pB[0:1, 0:16]), outs=[cnr], ins=[pB])
    ones_row_f = ctx.sbuf("c_ones_row_f", [1, 128], F32)
    p.op("gpsimd", lambda g: g.memset(ones_row_f, 1.0), outs=[ones_row_f])
    cnbc = ctx.sbuf("l2_cnbc", [128, 16], F32)
    p.op("tensor", lambda t: t.matmul(pB[:, 0:16], ones_row_f, cnr, start=True, stop=True),
         outs=[pB], ins=[ones_row_f, cnr])
    p.op("scalar", lambda s: s.copy(cnbc, pB[:, 0:16]), outs=[cnbc], ins=[pB])
    simn = ctx.sbuf("l2_simn", [128, DC, 16], F32)
    for c in range(DC):
        p.op("vector", lambda v, c=c: v.tensor_tensor(simn[:, c, :], sim_stg[:, c, :], cnbc, op=OP.mult),
             outs=[simn], ins=[sim_stg, cnbc])
    sim_hi = ctx.sbuf("l2_sim_hi", [128, DC, 16], BF16)
    sim_lo = ctx.sbuf("l2_sim_lo", [128, DC, 16], BF16)
    p.op("vector", lambda v: v.tensor_copy(sim_hi, simn), outs=[sim_hi], ins=[simn])
    p.op("vector", lambda v: v.tensor_tensor(simsq, simn, sim_hi, op=OP.subtract), outs=[simsq], ins=[simn, sim_hi])
    p.op("vector", lambda v: v.tensor_copy(sim_lo, simsq), outs=[sim_lo], ins=[simsq])
    grow = ctx.sbuf("c_rowbuf", [1, 512], F32)
    p.dma("sync", lambda e: e.dma_start(out=grow[0:1, 0:16], in_=W["gates"][None, :]),
          outs=[grow], ins=[W["gates"]], sem_key="D:colload")
    gsig = ctx.sbuf("l2_gsig", [1, 16], F32)
    p.op("scalar", lambda s: s.activation(gsig, grow[0:1, 0:16], AF.Sigmoid), outs=[gsig], ins=[grow])
    sigbc = ctx.sbuf("l2_sigbc", [128, 16], F32)
    p.op("tensor", lambda t: t.matmul(pB[:, 0:16], ones_row_f, gsig, start=True, stop=True),
         outs=[pB], ins=[ones_row_f, gsig])
    p.op("scalar", lambda s: s.copy(sigbc, pB[:, 0:16]), outs=[sigbc], ins=[pB])

    bq = ctx.col_load("l2_bq", W["b_Q"].rearrange("e h -> (e h)"), 8)
    bk_ = ctx.col_load("l2_bk", W["b_K"].rearrange("e h -> (e h)"), 8)
    bv_row = ctx.sbuf("l2_bv_row", [1, NH * DH], BF16)
    p.dma("gpsimd", lambda e: e.dma_start(out=bv_row, in_=W["b_V"].rearrange("e h -> (e h)")[None, :]),
          outs=[bv_row], ins=[W["b_V"]])
    bo_row = ctx.sbuf("l2_bo_row", [1, D], BF16)
    p.dma("gpsimd", lambda e: e.dma_start(out=bo_row, in_=W["b_O"][None, :]),
          outs=[bo_row], ins=[W["b_O"]])

    HT = T // 2
    for b in range(B):
        rn_cols = ctx.sbuf("l2_rncols", [128, NT], F32)

        def per_tile(bb, i, xn_tile):
            var, rstd = ctx.sb["ln_var"], ctx.sb["ln_rstd"]
            t1 = ctx.sbuf("l2_rt1", [128, 1], F32)
            p.op("scalar", lambda sc, i=i: sc.activation(t1, var, AF.Sqrt, scale=float(D)),
                 outs=[t1], ins=[var])
            p.op("vector", lambda v, i=i: v.tensor_tensor(t1, t1, rstd, op=OP.mult), outs=[t1], ins=[t1, rstd])
            p.op("vector", lambda v, i=i: v.tensor_scalar(t1, t1, 1e-12, None, op0=OP.max, op1=OP.bypass),
                 outs=[t1], ins=[t1])
            p.op("vector", lambda v, i=i: v.reciprocal(rn_cols[:, i:i + 1], t1), outs=[rn_cols], ins=[t1])

        ctx.phase_norm(x_in, scr["xnb"], b, xnb_lo=scr["xnb_lo"], per_tile=per_tile)
        xn_fm = ctx.load_xn_fm_b(scr["xnb"], b)
        xn_lo_fm = ctx.view("xn_lo_fm", [128, DC, T], BF16, 0)
        for c in range(DC):
            p.dma("sync", lambda e, b=b, c=c, t=xn_lo_fm: e.dma_start_transpose(
                t[:, c, :], scr["xnb_lo"][b, :, c * 128:(c + 1) * 128]),
                outs=[xn_lo_fm], ins=[scr["xnb_lo"]], sem_key="D:xnlofm")

        rw_all = ctx.sbuf("l2_rw_all", [128, NT, 16], F32)
        for i in range(NT):
            for ci, (xs, ws) in enumerate([(xn_fm, sim_hi), (xn_lo_fm, sim_hi), (xn_fm, sim_lo)]):
                for c in range(DC):
                    p.op("tensor", lambda t, i=i, c=c, xs=xs, ws=ws, ci=ci: t.matmul(
                        pA[:, 0:16], xs[:, c, i * 128:(i + 1) * 128], ws[:, c, :],
                        start=(ci == 0 and c == 0), stop=(ci == 2 and c == DC - 1)),
                        outs=[pA], ins=[xs, ws])
            lg = ctx.sbuf("l2_lg", [128, 16], F32)
            gated = ctx.sbuf("l2_gated", [128, 16], F32)
            m1 = ctx.sbuf("l2_m1", [128, 1], F32)
            m2 = ctx.sbuf("l2_m2", [128, 1], F32)
            eq1 = ctx.sbuf("l2_eq1", [128, 16], F32)
            eq2 = ctx.sbuf("l2_eq2", [128, 16], F32)
            lg2 = ctx.sbuf("l2_lg2", [128, 16], F32)
            act01 = ctx.sbuf("l2_act01", [128, 1], F32)
            fm = ctx.sbuf("l2_fm", [128, 16], F32)
            ml = ctx.sbuf("l2_ml", [128, 16], F32)
            nm3 = ctx.sbuf("l2_nm3", [128, 1], F32)
            zs = ctx.sbuf("l2_zs", [128, 1], F32)
            p.op("vector", lambda v, i=i: v.tensor_scalar(lg, pA[:, 0:16], rn_cols[:, i:i + 1], None,
                                                          op0=OP.mult, op1=OP.bypass), outs=[lg], ins=[pA, rn_cols])
            p.op("vector", lambda v: v.tensor_tensor(lg, lg, sigbc, op=OP.subtract), outs=[lg], ins=[lg, sigbc])
            p.op("vector", lambda v: v.tensor_scalar(gated, lg, 0.0, None, op0=OP.max, op1=OP.bypass),
                 outs=[gated], ins=[lg])
            p.op("vector", lambda v: v.tensor_reduce(m1, lg, axis=AX.X, op=OP.max), outs=[m1], ins=[lg])
            p.op("vector", lambda v: v.tensor_scalar(eq1, lg, m1, None, op0=OP.is_ge, op1=OP.bypass),
                 outs=[eq1], ins=[lg, m1])
            p.op("vector", lambda v: v.scalar_tensor_tensor(lg2, eq1, -1e30, lg, op0=OP.mult, op1=OP.add),
                 outs=[lg2], ins=[eq1, lg])
            p.op("vector", lambda v: v.tensor_reduce(m2, lg2, axis=AX.X, op=OP.max), outs=[m2], ins=[lg2])
            p.op("vector", lambda v: v.tensor_scalar(eq2, lg2, m2, None, op0=OP.is_ge, op1=OP.bypass),
                 outs=[eq2], ins=[lg2, m2])
            p.op("vector", lambda v: v.tensor_scalar(act01, m1, 0.0, None, op0=OP.is_gt, op1=OP.bypass),
                 outs=[act01], ins=[m1])
            p.op("vector", lambda v: v.tensor_tensor(fm, eq1, eq2, op=OP.add), outs=[fm], ins=[eq1, eq2])
            p.op("vector", lambda v: v.tensor_scalar(ml, lg, 0.0, None, op0=OP.is_gt, op1=OP.bypass),
                 outs=[ml], ins=[lg])
            p.op("vector", lambda v: v.tensor_tensor(ml, ml, fm, op=OP.subtract), outs=[ml], ins=[ml, fm])
            p.op("vector", lambda v: v.tensor_scalar(ml, ml, act01, None, op0=OP.mult, op1=OP.bypass),
                 outs=[ml], ins=[ml, act01])
            p.op("vector", lambda v: v.tensor_tensor(fm, fm, ml, op=OP.add), outs=[fm], ins=[fm, ml])
            p.op("vector", lambda v: v.tensor_scalar(ml, fm, 1e30, -1e30, op0=OP.mult, op1=OP.add),
                 outs=[ml], ins=[fm])
            p.op("vector", lambda v: v.scalar_tensor_tensor(ml, gated, 1.0, ml, op0=OP.mult, op1=OP.add)
                 if False else v.tensor_tensor(lg2, gated, fm, op=OP.mult), outs=[lg2], ins=[gated, fm])
            p.op("vector", lambda v: v.tensor_tensor(ml, ml, lg2, op=OP.add), outs=[ml], ins=[ml, lg2])
            p.op("vector", lambda v: v.tensor_reduce(nm3, ml, axis=AX.X, op=OP.max, negate=True),
                 outs=[nm3], ins=[ml])
            p.op("scalar", lambda s, i=i: s.activation(rw_all[:, i, :], ml, AF.Exp, bias=nm3,
                                                       accum_out=zs), outs=[rw_all, zs], ins=[ml, nm3])
            p.op("vector", lambda v: v.reciprocal(zs, zs), outs=[zs], ins=[zs])
            p.op("vector", lambda v, i=i: v.tensor_scalar(rw_all[:, i, :], rw_all[:, i, :], zs, None,
                                                          op0=OP.mult, op1=OP.bypass),
                 outs=[rw_all], ins=[rw_all, zs])

        wv = ctx.view("l2_wv", [128, DC, NH, DH], BF16, 0)
        for c in range(DC):
            p.dma("gpsimd", lambda e, c=c: e.dma_start(
                out=wv[:, c, :, :], in_=W["W_V"].rearrange("e (a p) h -> a p e h", p=128)[c]),
                outs=[wv], ins=[W["W_V"]], sem_key="D:l2wv")
        v_tm = ctx.view("l2_vtm", [128, NT, NH * DH], BF16, 16384)
        for i in range(NT):
            for half in range(2):
                for c in range(DC):
                    p.op("tensor", lambda t, i=i, half=half, c=c, xn_fm=xn_fm: t.matmul(
                        pA[:, 0:512], xn_fm[:, c, i * 128:(i + 1) * 128],
                        wv[:, c, half * 8:(half + 1) * 8, :], start=(c == 0), stop=False),
                        outs=[pA], ins=[xn_fm, wv])
                p.op("tensor", lambda t, i=i, half=half: t.matmul(
                    pA[:, 0:512], ctx.ones_row[0:1, 0:128], bv_row[0:1, half * 512:(half + 1) * 512],
                    start=False, stop=True), outs=[pA], ins=[ctx.ones_row, bv_row])
                p.op("scalar", lambda s, i=i, half=half: s.copy(
                    v_tm[:, i, half * 512:(half + 1) * 512], pA[:, 0:512]), outs=[v_tm], ins=[pA])
        attnout = ctx.view("l2_attnout", [128, NHP, T], BF16, 49152)
        for hp in range(NHP):
            wqs = ctx.view(f"l2_wqs{hp % 2}", [128, DC, 128], BF16, 81920 + (hp % 2) * 2048)
            wks = ctx.view(f"l2_wks{hp % 2}", [128, DC, 128], BF16, 86016 + (hp % 2) * 2048)
            for c in range(DC):
                p.dma("gpsimd", lambda e, hp=hp, c=c, wqs=wqs: e.dma_start(
                    out=wqs[:, c, :].rearrange("p (e h) -> p e h", e=2),
                    in_=W["W_Q"][2 * hp:2 * hp + 2].rearrange("e (a p) h -> a p e h", p=128)[c]),
                    outs=[wqs], ins=[W["W_Q"]], sem_key=f"D:l2wq{hp % 2}")
                p.dma("gpsimd", lambda e, hp=hp, c=c, wks=wks: e.dma_start(
                    out=wks[:, c, :].rearrange("p (e h) -> p e h", e=2),
                    in_=W["W_K"][2 * hp:2 * hp + 2].rearrange("e (a p) h -> a p e h", p=128)[c]),
                    outs=[wks], ins=[W["W_K"]], sem_key=f"D:l2wk{hp % 2}")
            q_fm = ctx.view("l2_qfm", [128, T], BF16, 90112)
            k_fm = ctx.view("l2_kfm", [128, T], BF16, 94208)
            for tc in range(T // 512):
                for c in range(DC):
                    p.op("tensor", lambda t, tc=tc, c=c, wqs=wqs, xn_fm=xn_fm: t.matmul(
                        pA[:, 0:512], wqs[:, c, :], xn_fm[:, c, tc * 512:(tc + 1) * 512],
                        start=(c == 0), stop=(c == DC - 1)), outs=[pA], ins=[wqs, xn_fm])
                p.op("scalar", lambda s, tc=tc, hp=hp: s.activation(
                    q_fm[:, tc * 512:(tc + 1) * 512], pA[:, 0:512], AF.Copy, bias=0.0, scale=0.125),
                    outs=[q_fm], ins=[pA])
                for c in range(DC):
                    p.op("tensor", lambda t, tc=tc, c=c, wks=wks, xn_fm=xn_fm: t.matmul(
                        pA[:, 0:512], wks[:, c, :], xn_fm[:, c, tc * 512:(tc + 1) * 512],
                        start=(c == 0), stop=(c == DC - 1)), outs=[pA], ins=[wks, xn_fm])
                p.op("scalar", lambda s, tc=tc, hp=hp: s.copy(k_fm[:, tc * 512:(tc + 1) * 512], pA[:, 0:512]),
                     outs=[k_fm], ins=[pA])
            bqc = ctx.sbuf("l2_bqc", [128, 1], F32)
            p.op("vector", lambda v, hp=hp: v.tensor_scalar(bqc, bq[:, hp:hp + 1], 0.125, None,
                                                            op0=OP.mult, op1=OP.bypass), outs=[bqc], ins=[bq])
            for tc in range(T // 512):
                p.op("vector", lambda v, tc=tc: v.tensor_scalar(
                    q_fm[:, tc * 512:(tc + 1) * 512], q_fm[:, tc * 512:(tc + 1) * 512], bqc, None,
                    op0=OP.add, op1=OP.bypass), outs=[q_fm], ins=[q_fm, bqc])
                p.op("vector", lambda v, tc=tc, hp=hp: v.tensor_scalar(
                    k_fm[:, tc * 512:(tc + 1) * 512], k_fm[:, tc * 512:(tc + 1) * 512], bk_[:, hp:hp + 1],
                    None, op0=OP.add, op1=OP.bypass), outs=[k_fm], ins=[k_fm, bk_])
            for hh in range(2):
                h = 2 * hp + hh
                hb64 = hh * 64
                mcols = ctx.sbuf("l2_mcols", [128, NT], F32)
                mtmp = ctx.sbuf("l2_mtmp", [128, 4], F32)
                for i in range(NT):
                    nk = (i + 1) * 128
                    nchunk = (nk + 511) // 512
                    for j in range(nchunk):
                        n = min(512, nk - j * 512)
                        p.op("tensor", lambda t, i=i, j=j, n=n, hb64=hb64: t.matmul(
                            pA[:, 0:n], q_fm[hb64:hb64 + 64, i * 128:(i + 1) * 128],
                            k_fm[hb64:hb64 + 64, j * 512:j * 512 + n], start=True, stop=True),
                            outs=[pA], ins=[q_fm, k_fm])
                        p.op("vector", lambda v, j=j, n=n: v.tensor_reduce(
                            mtmp[:, j:j + 1], pA[:, 0:n], axis=AX.X, op=OP.max), outs=[mtmp], ins=[pA])
                    p.op("vector", lambda v, i=i, nchunk=nchunk: v.tensor_reduce(
                        mcols[:, i:i + 1], mtmp[:, 0:nchunk], axis=AX.X, op=OP.max), outs=[mcols], ins=[mtmp])
                negm = ctx.sbuf("l2_negm", [128, 1], F32)
                ctx.neg_scalar_max(mcols, NT, negm, bank=1)
                pO, pZ = ctx.bk(3 if hh == 0 else 6), ctx.bk(4 if hh == 0 else 7)
                zrow = ctx.sbuf(f"l2_zrow{hh}", [1, 512], BF16)
                zbc = ctx.sbuf(f"l2_zbc{hh}", [128, 512], BF16)
                for qc in range(T // 512):
                    tq0 = qc * 512
                    ktiles = list(range(0, (tq0 + 512) // 128))
                    for kk, k in enumerate(ktiles):
                        st, sp = (kk == 0), (kk == len(ktiles) - 1)
                        pS = ctx.bk(2 if kk % 2 == 0 else 5)
                        PT = ctx.sbuf(f"l2_PT{kk % 2}", [128, 512], BF16)
                        p.op("tensor", lambda t, k=k, tq0=tq0, hb64=hb64, pS=pS: t.matmul(
                            pS[:, 0:512], k_fm[hb64:hb64 + 64, k * 128:(k + 1) * 128],
                            q_fm[hb64:hb64 + 64, tq0:tq0 + 512], start=True, stop=True),
                            outs=[pS], ins=[k_fm, q_fm])
                        p.op("scalar", lambda s, PT=PT, pS=pS: s.activation(PT, pS[:, 0:512], AF.Exp, bias=negm),
                             outs=[PT], ins=[pS, negm])
                        dsub = k - tq0 // 128
                        if 0 <= dsub < 4:
                            p.op("vector", lambda v, dsub=dsub, PT=PT: v.tensor_tensor(
                                PT[:, dsub * 128:(dsub + 1) * 128], PT[:, dsub * 128:(dsub + 1) * 128],
                                ctx.tri, op=OP.mult), outs=[PT], ins=[PT, ctx.tri])
                            for z in range(dsub):
                                p.op("gpsimd", lambda g, z=z, PT=PT: g.memset(PT[:, z * 128:(z + 1) * 128], 0.0),
                                     outs=[PT])
                        p.op("tensor", lambda t, k=k, st=st, sp=sp, h=h, PT=PT, pO=pO: t.matmul(
                            pO[0:64, 0:512], v_tm[:, k, h * 64:(h + 1) * 64], PT, start=st, stop=sp),
                            outs=[pO], ins=[v_tm, PT])
                        p.op("tensor", lambda t, st=st, sp=sp, PT=PT, pZ=pZ: t.matmul(
                            pZ[0:1, 0:512], ctx.ones_col, PT, start=st, stop=sp),
                            outs=[pZ], ins=[ctx.ones_col, PT])
                    p.op("vector", lambda v, pZ=pZ, zrow=zrow: ctx.recip_bf(v, zrow, pZ[0:1, 0:512]),
                         outs=[zrow], ins=[pZ])
                    p.op("tensor", lambda t, pZ=pZ, zrow=zrow: t.matmul(
                        pZ[:, 0:512], ctx.ones_row[0:1, 0:128], zrow,
                        start=True, stop=True), outs=[pZ], ins=[ctx.ones_row, zrow])
                    p.op("scalar", lambda s, pZ=pZ, zbc=zbc: s.copy(zbc, pZ[:, 0:512]), outs=[zbc], ins=[pZ])
                    p.op("vector", lambda v, hp=hp, hb64=hb64, tq0=tq0, pO=pO, zbc=zbc: v.tensor_tensor(
                        attnout[hb64:hb64 + 64, hp, tq0:tq0 + 512], pO[0:64, 0:512], zbc[0:64, :], op=OP.mult),
                        outs=[attnout], ins=[pO, zbc])
        wo = ctx.view("l2_wo", [128, NHP, D], BF16, 98304)
        p.dma("gpsimd", lambda e: e.dma_start(
            out=wo.rearrange("p a d -> p a d"),
            in_=W["W_O"].rearrange("(a e2) h d -> (e2 h) a d", e2=2)), outs=[wo], ins=[W["W_O"]],
            sem_key="D:l2wo")
        s_t = ctx.sbuf("ln_sq", [128, D], F32)
        for i in range(NT):
            for half in range(2):
                for hc in range(NHP):
                    p.op("tensor", lambda t, i=i, half=half, hc=hc: t.matmul(
                        pA[:, 0:512], attnout[:, hc, i * 128:(i + 1) * 128],
                        wo[:, hc, half * 512:(half + 1) * 512],
                        start=(hc == 0), stop=False), outs=[pA], ins=[attnout, wo])
                p.op("tensor", lambda t, half=half: t.matmul(
                    pA[:, 0:512], ctx.ones_row[0:1, 0:128], bo_row[0:1, half * 512:(half + 1) * 512],
                    start=False, stop=True), outs=[pA], ins=[ctx.ones_row, bo_row])
                p.op("vector", lambda v, half=half: v.tensor_copy(
                    s_t[:, half * 512:(half + 1) * 512], pA[:, 0:512]), outs=[s_t], ins=[pA])
            p.dma("scalar", lambda e, i=i, b=b: e.dma_start(
                out=scr["s"][b, i * 128:(i + 1) * 128, :], in_=s_t), outs=[scr["s"]], ins=[s_t],
                sem_key="D:l2s")

        wq3 = ctx.view("l2_wq3", [128, DC, NH, DH], BF16, 0)
        wk3 = ctx.view("l2_wk3", [128, DC, NH, DH], BF16, 16384)
        wv3 = ctx.view("l2_wv3", [128, DC, NH, DH], BF16, 32768)
        for nm, dst, src in (("q", wq3, "q_proj"), ("k", wk3, "k_proj"), ("v", wv3, "v_proj")):
            for c in range(DC):
                p.dma("gpsimd", lambda e, dst=dst, src=src, c=c: e.dma_start(
                    out=dst[:, c, :, :], in_=W[src].rearrange("e (a p) h -> a p e h", p=128)[c]),
                    outs=[dst], ins=[W[src]], sem_key=f"D:l2w3{nm}")
        q2f = ctx.view("l2_q2f", [64, T], BF16, 73728)
        k2f = ctx.view("l2_k2f", [64, T], BF16, 77824)
        v2a = ctx.view("l2_v2a", [128, NT, DH + 1], BF16, 81920)
        mixq = ctx.sbuf("l2_mixq", [128, DH], F32)
        mixk = ctx.sbuf("l2_mixk", [128, DH], F32)
        mixv = ctx.sbuf("l2_mixv", [128, DH], F32)
        pall = ctx.view("l2_pall", [128, 3, NH * DH], F32, 86016)
        for i in range(NT):
            for ti, (w3, mx) in enumerate([(wq3, mixq), (wk3, mixk), (wv3, mixv)]):
                for half in range(2):
                    for c in range(DC):
                        p.op("tensor", lambda t, i=i, ti=ti, half=half, c=c, w3=w3, xn_fm=xn_fm: t.matmul(
                            pA[:, 0:512], xn_fm[:, c, i * 128:(i + 1) * 128],
                            w3[:, c, half * 8:(half + 1) * 8, :], start=(c == 0), stop=(c == DC - 1)),
                            outs=[pA], ins=[xn_fm, w3])
                    p.op("scalar", lambda s, i=i, ti=ti, half=half: s.copy(
                        pall[:, ti, half * 512:(half + 1) * 512], pA[:, 0:512]), outs=[pall], ins=[pA])
                for e in range(NH):
                    if e == 0:
                        p.op("vector", lambda v, i=i, ti=ti, mx=mx: v.tensor_scalar(
                            mx, pall[:, ti, 0:DH], rw_all[:, i, 0:1], None, op0=OP.mult, op1=OP.bypass),
                            outs=[mx], ins=[pall, rw_all])
                    else:
                        p.op("vector", lambda v, i=i, ti=ti, e=e, mx=mx: v.scalar_tensor_tensor(
                            mx, pall[:, ti, e * DH:(e + 1) * DH], rw_all[:, i, e:e + 1], mx,
                            op0=OP.mult, op1=OP.add), outs=[mx], ins=[pall, rw_all, mx])
            p.op("vector", lambda v, i=i: v.tensor_scalar(mixq, mixq, 0.125, None, op0=OP.mult, op1=OP.bypass),
                 outs=[mixq], ins=[mixq])
            p.op("tensor", lambda t, i=i: t.transpose(pB[0:64, 0:128], mixq, ctx.idn), outs=[pB], ins=[mixq, ctx.idn])
            p.op("scalar", lambda s, i=i: s.copy(q2f[:, i * 128:(i + 1) * 128], pB[0:64, 0:128]),
                 outs=[q2f], ins=[pB])
            p.op("tensor", lambda t, i=i: t.transpose(pB[0:64, 0:128], mixk, ctx.idn), outs=[pB], ins=[mixk, ctx.idn])
            p.op("scalar", lambda s, i=i: s.copy(k2f[:, i * 128:(i + 1) * 128], pB[0:64, 0:128]),
                 outs=[k2f], ins=[pB])
            p.op("vector", lambda v, i=i: v.tensor_copy(v2a[:, i, 0:DH], mixv), outs=[v2a], ins=[mixv])
            p.op("gpsimd", lambda g, i=i: g.memset(v2a[:, i, DH:DH + 1], 1.0), outs=[v2a])
        mcols = ctx.sbuf("l2_mcols", [128, NT], F32)
        mtmp = ctx.sbuf("l2_mtmp", [128, 4], F32)
        for i in range(NT):
            nk = (i + 1) * 128
            nchunk = (nk + 511) // 512
            for j in range(nchunk):
                n = min(512, nk - j * 512)
                p.op("tensor", lambda t, i=i, j=j, n=n: t.matmul(
                    pA[:, 0:n], q2f[:, i * 128:(i + 1) * 128], k2f[:, j * 512:j * 512 + n],
                    start=True, stop=True), outs=[pA], ins=[q2f, k2f])
                p.op("vector", lambda v, j=j, n=n: v.tensor_reduce(
                    mtmp[:, j:j + 1], pA[:, 0:n], axis=AX.X, op=OP.max), outs=[mtmp], ins=[pA])
            p.op("vector", lambda v, i=i, nchunk=nchunk: v.tensor_reduce(
                mcols[:, i:i + 1], mtmp[:, 0:nchunk], axis=AX.X, op=OP.max), outs=[mcols], ins=[mtmp])
        negm = ctx.sbuf("l2_negm", [128, 1], F32)
        ctx.neg_scalar_max(mcols, NT, negm, bank=1)
        pS = ctx.bk(2)
        PT = ctx.sbuf("l2_PT", [128, 512], BF16)
        wo3 = ctx.view("l2_wo3", [128, DC, D], BF16, 49152)
        p.dma("gpsimd", lambda e: e.dma_start(
            out=wo3, in_=W["o_proj"].rearrange("(a e2) h d -> (e2 h) a d", e2=2)), outs=[wo3],
            ins=[W["o_proj"]], sem_key="D:l2wo3")
        att2 = ctx.sbuf("l2_att2", [128, DH], F32)
        z2 = ctx.view("l2_z2", [128, NH, DH], F32, 98304)
        z2t = ctx.view("l2_z2t", [128, DC, 128], BF16, 66560)
        for qc in range(T // 512):
            tq0 = qc * 512
            ktiles = list(range(0, (tq0 + 512) // 128))
            for kk, k in enumerate(ktiles):
                st, sp = (kk == 0), (kk == len(ktiles) - 1)
                p.op("tensor", lambda t, k=k, tq0=tq0: t.matmul(
                    pS[:, 0:512], k2f[:, k * 128:(k + 1) * 128], q2f[:, tq0:tq0 + 512],
                    start=True, stop=True), outs=[pS], ins=[k2f, q2f])
                p.op("scalar", lambda s: s.activation(PT, pS[:, 0:512], AF.Exp, bias=negm),
                     outs=[PT], ins=[pS, negm])
                dsub = k - tq0 // 128
                if 0 <= dsub < 4:
                    p.op("vector", lambda v, dsub=dsub: v.tensor_tensor(
                        PT[:, dsub * 128:(dsub + 1) * 128], PT[:, dsub * 128:(dsub + 1) * 128],
                        ctx.tri, op=OP.mult), outs=[PT], ins=[PT, ctx.tri])
                    for z in range(dsub):
                        p.op("gpsimd", lambda g, z=z: g.memset(PT[:, z * 128:(z + 1) * 128], 0.0), outs=[PT])
                for sub in range(4):
                    p.op("tensor", lambda t, k=k, sub=sub, st=st, sp=sp: t.matmul(
                        ctx.bk(3 + sub)[:, 0:DH + 1], PT[:, sub * 128:(sub + 1) * 128], v2a[:, k, :],
                        start=st, stop=sp), outs=[ctx.bk(3 + sub)], ins=[PT, v2a])
            for sub in range(4):
                i = qc * 4 + sub
                rz = ctx.sbuf("l2_rz", [128, 1], F32)
                p.op("vector", lambda v, sub=sub: v.reciprocal(rz, ctx.bk(3 + sub)[:, DH:DH + 1]),
                     outs=[rz], ins=[ctx.bk(3 + sub)])
                p.op("vector", lambda v, sub=sub: v.tensor_scalar(att2, ctx.bk(3 + sub)[:, 0:DH], rz, None,
                                                                  op0=OP.mult, op1=OP.bypass),
                     outs=[att2], ins=[ctx.bk(3 + sub), rz])
                for e in range(NH):
                    p.op("vector", lambda v, i=i, e=e: v.tensor_scalar(
                        z2[:, e, :], att2, rw_all[:, i, e:e + 1], None, op0=OP.mult, op1=OP.bypass),
                        outs=[z2], ins=[att2, rw_all])
                for c in range(DC):
                    p.op("tensor", lambda t, c=c: t.transpose(pB[:, 0:128], z2[:, 2 * c:2 * c + 2, :], ctx.idn),
                         outs=[pB], ins=[z2, ctx.idn])
                    p.op("scalar", lambda s, c=c: s.copy(z2t[:, c, :], pB[:, 0:128]), outs=[z2t], ins=[pB])
                xt = ctx.sbuf(f"pn_x{sub % ctx.pnb}", [128, D], F32)
                st_ = ctx.sbuf(f"pn_xn{sub % ctx.pnb}", [128, D], F32)
                gt = ctx.view("l2_ep_g", [128, D], F32, 102400)
                gs = ctx.sbuf("ln_sq", [128, D], F32)
                p.dma("sync", lambda e, b=b, i=i, xt=xt: e.dma_start(
                    out=xt, in_=x_in[b, i * 128:(i + 1) * 128, :]), outs=[xt], ins=[x_in])
                p.dma("sync", lambda e, b=b, i=i, st_=st_: e.dma_start(
                    out=st_, in_=scr["s"][b, i * 128:(i + 1) * 128, :]), outs=[st_], ins=[scr["s"]],
                    sem_key="D:l2s_rd")
                for half in range(2):
                    for c in range(DC):
                        p.op("tensor", lambda t, half=half, c=c: t.matmul(
                            pA[:, 0:512], z2t[:, c, :], wo3[:, c, half * 512:(half + 1) * 512],
                            start=(c == 0), stop=(c == DC - 1)), outs=[pA], ins=[z2t, wo3])
                    p.op("vector", lambda v, half=half, gt=gt: v.tensor_copy(
                        gt[:, half * 512:(half + 1) * 512], pA[:, 0:512]), outs=[gt], ins=[pA])
                p.op("scalar", lambda s, gt=gt, gs=gs: s.activation(gs, gt, AF.Silu), outs=[gs], ins=[gt])
                p.op("vector", lambda v, gs=gs, st_=st_: v.tensor_tensor(gs, gs, st_, op=OP.mult),
                     outs=[gs], ins=[gs, st_])
                p.op("vector", lambda v, gs=gs, xt=xt: v.tensor_tensor(gs, gs, xt, op=OP.add),
                     outs=[gs], ins=[gs, xt])
                p.dma("scalar", lambda e, b=b, i=i, gs=gs: e.dma_start(
                    out=y_out[b, i * 128:(i + 1) * 128, :], in_=gs), outs=[y_out], ins=[gs], sem_key="D:y")



def build_l3(ctx, x_in, y_out, W, scr):
    ctx.pnb = 1
    import os
    PH = int(os.environ.get("L3PH", "99"))
    p, nc = ctx.p, ctx.nc
    B, T, D, NT, DC = ctx.B, ctx.T, ctx.D, ctx.NT, ctx.DC
    NHEAD, C, DV = 8, 64, 128
    NG = 4
    L = 8
    NCH = T // L
    pA, pB = ctx.bk(0), ctx.bk(1)

    WaT = ctx.load_transposed("l3_WaT", W["Wa_w"], 512, D, stage_off=0)
    WqlT = ctx.load_transposed("l3_Wql", W["Wq"], C, D, stage_off=0)
    WklT = ctx.load_transposed("l3_Wkl", W["Wk"], C, D, stage_off=0)
    WvlT = ctx.load_transposed("l3_Wvl", W["Wv"], C, D, stage_off=0)
    wab = ctx.col_load("l3_wab", W["Wa_b"], NG)
    gmc = ctx.col_load("l3_gmc", W["gamma"], DC)
    btc = ctx.col_load("l3_btc", W["beta"], DC)
    mur = ctx.col_load("l3_mur", W["mu_r"], DC)
    muk = ctx.col_load("l3_muk", W["mu_k"], DC)
    Qc_sb = ctx.sbuf("l3_Qc", [C, NHEAD, C], BF16)
    p.dma("gpsimd", lambda e: e.dma_start(out=Qc_sb, in_=W["Qc"].rearrange("h c e -> c h e")),
          outs=[Qc_sb], ins=[W["Qc"]])
    Vc_sb = ctx.sbuf("l3_Vc", [C, NHEAD, DV], BF16)
    p.dma("gpsimd", lambda e: e.dma_start(out=Vc_sb, in_=W["Vc"].rearrange("h c d -> c h d")),
          outs=[Vc_sb], ins=[W["Vc"]])
    t8 = ctx.view("l3_t8", [128, T], F32, 0)
    p.op("gpsimd", lambda g: g.iota(t8, base=0, pattern=[[0, T // L], [1, L]], channel_multiplier=0,
                                    allow_small_or_imprecise_dtypes=True), outs=[t8])
    ms = ctx.sbuf("l3_ms", [128, T], BF16)
    p.op("vector", lambda v: v.tensor_scalar(ms, t8, 0.5, None, op0=OP.is_lt, op1=OP.bypass),
         outs=[ms], ins=[t8])
    bo2 = ctx.sbuf("l3_bo2", [128, 2], BF16)
    p.op("vector", lambda v: v.tensor_scalar(bo2[:, 0:1], ctx.pidx, 63.5, None, op0=OP.is_lt, op1=OP.bypass),
         outs=[bo2], ins=[ctx.pidx])
    p.op("vector", lambda v: v.tensor_scalar(bo2[:, 1:2], ctx.pidx, 63.5, None, op0=OP.is_gt, op1=OP.bypass),
         outs=[bo2], ins=[ctx.pidx])
    t2 = ctx.sbuf("l3_t2", [2, 128], F32)
    p.op("gpsimd", lambda g: g.iota(t2, base=0, pattern=[[0, 128]], channel_multiplier=1,
                                    allow_small_or_imprecise_dtypes=True), outs=[t2])
    sel = [ctx.sbuf(f"l3_sel{hh}", [2, 128], BF16) for hh in range(2)]
    p.op("vector", lambda v: v.tensor_scalar(sel[0], t2, 0.5, None, op0=OP.is_lt, op1=OP.bypass),
         outs=[sel[0]], ins=[t2])
    p.op("vector", lambda v: v.tensor_scalar(sel[1], t2, 0.5, None, op0=OP.is_gt, op1=OP.bypass),
         outs=[sel[1]], ins=[t2])

    D_cols = ctx.sbuf("l3_Dcols", [128, NG, NCH], F32)

    for b in range(B):
        f_cols = ctx.sbuf("l3_fcols", [128, NT], F32)

        def per_tile(bb, i, xn_tile):
            var, rstd = ctx.sb["ln_var"], ctx.sb["ln_rstd"]
            t1 = ctx.sbuf("l3_ft1", [128, 1], F32)
            p.op("vector", lambda v, i=i: v.tensor_tensor(t1, rstd, rstd, op=OP.mult), outs=[t1], ins=[rstd])
            p.op("vector", lambda v, i=i: v.tensor_tensor(t1, t1, var, op=OP.mult), outs=[t1], ins=[t1, var])
            p.op("scalar", lambda sc, i=i: sc.activation(t1, t1, AF.Sqrt, bias=ctx.sb["ln_eps"]),
                 outs=[t1], ins=[t1])
            p.op("vector", lambda v, i=i: v.reciprocal(f_cols[:, i:i + 1], t1), outs=[f_cols], ins=[t1])

        ctx.phase_norm(x_in, scr["xnb"], b, per_tile=per_tile, xnf=scr["xnf"])
        xn_fm = ctx.view("xn_fm_l3", [128, DC, T], BF16, 0)
        for c in range(DC):
            p.dma("sync", lambda e, b=b, c=c, t=xn_fm: e.dma_start_transpose(
                t[:, c, :], scr["xnb"][b, :, c * 128:(c + 1) * 128]),
                outs=[xn_fm], ins=[scr["xnb"]], sem_key="D:xn_fm")


        xn2 = ctx.view("l3_xn2", [128, DC, T], BF16, 77824)
        fbc = ctx.sbuf("l3_fbc", [128, 512], BF16)
        frow = ctx.sbuf("l3_rowb2", [1, 512], BF16)
        for tc in range(T // 512):
            for sub in range(4):
                i = tc * 4 + sub
                p.op("tensor", lambda t, i=i: t.transpose(pB[0:1, 0:128], f_cols[:, i:i + 1], ctx.idn),
                     outs=[pB], ins=[f_cols, ctx.idn])
                p.op("scalar", lambda s, sub=sub: s.copy(frow[0:1, sub * 128:(sub + 1) * 128], pB[0:1, 0:128]),
                     outs=[frow], ins=[pB])
            p.op("tensor", lambda t, tc=tc: t.matmul(pA[:, 0:512], ctx.ones_row[0:1, 0:128],
                                                     frow, start=True, stop=True),
                 outs=[pA], ins=[ctx.ones_row, frow])
            p.op("scalar", lambda s, tc=tc: s.copy(fbc, pA[:, 0:512]), outs=[fbc], ins=[pA])
            for c in range(DC):
                t1 = ctx.sbuf("l3_otmp", [128, 512], F32)
                p.op("vector", lambda v, c=c, tc=tc, t1=t1: v.tensor_tensor(
                    t1, xn_fm[:, c, tc * 512:(tc + 1) * 512], fbc, op=OP.mult), outs=[t1], ins=[xn_fm, fbc])
                p.op("vector", lambda v, c=c, tc=tc, t1=t1: v.tensor_scalar(
                    xn2[:, c, tc * 512:(tc + 1) * 512], t1, gmc[:, c:c + 1], btc[:, c:c + 1],
                    op0=OP.mult, op1=OP.add), outs=[xn2], ins=[t1, gmc, btc])
        q_lat = ctx.view("l3_qlat", [64, T], BF16, 65536)
        k_lat = ctx.view("l3_klat", [64, T], BF16, 69632)
        v_lat = ctx.view("l3_vlat", [64, T], BF16, 73728)
        for wT, dst in ((WqlT, q_lat), (WklT, k_lat), (WvlT, v_lat)):
            for tc in range(T // 512):
                for c in range(DC):
                    p.op("tensor", lambda t, tc=tc, c=c, wT=wT: t.matmul(
                        pA[0:64, 0:512], wT[:, c, :], xn2[:, c, tc * 512:(tc + 1) * 512],
                        start=(c == 0), stop=(c == DC - 1)), outs=[pA], ins=[wT, xn2])
                p.op("scalar", lambda s, tc=tc, dst=dst: s.copy(dst[:, tc * 512:(tc + 1) * 512], pA[0:64, 0:512]),
                     outs=[dst], ins=[pA])
        a_all = ctx.view("l3_aall", [128, NG, T], F32, 32768)
        for g in range(NG):
            for tc in range(T // 512):
                for c in range(DC):
                    p.op("tensor", lambda t, g=g, tc=tc, c=c: t.matmul(
                        pA[:, 0:512], WaT[:, c, g * 128:(g + 1) * 128], xn2[:, c, tc * 512:(tc + 1) * 512],
                        start=(c == 0), stop=(c == DC - 1)), outs=[pA], ins=[WaT, xn2])
                p.op("scalar", lambda s, g=g, tc=tc: s.activation(
                    a_all[:, g, tc * 512:(tc + 1) * 512], pA[:, 0:512], AF.Sigmoid, bias=wab[:, g:g + 1]),
                    outs=[a_all], ins=[pA, wab])
        rknbc = ctx.sbuf("l3_rknbc", [128, T], BF16)
        rkn_cols = ctx.sbuf("l3_rkncols", [128, NT], F32)
        rowb = ctx.sbuf("l3_rowb", [1, 512], F32)
        rowb2 = ctx.sbuf("l3_rowb2", [1, 512], BF16)
        ksq = ctx.sbuf("l3_otmp", [128, 512], F32)[0:64, :]
        ksqb = ctx.sbuf("l3_ksqb", [64, 512], BF16)
        for tc in range(T // 512):
            p.op("scalar", lambda s, tc=tc: s.activation(ksq, k_lat[:, tc * 512:(tc + 1) * 512], AF.Square),
                 outs=[ksq], ins=[k_lat])
            p.op("vector", lambda v: v.tensor_copy(ksqb, ksq), outs=[ksqb], ins=[ksq])
            p.op("tensor", lambda t, tc=tc: t.matmul(pB[0:1, 0:512], ctx.ones_col[0:64, :], ksqb,
                                                     start=True, stop=True), outs=[pB], ins=[ctx.ones_col, ksqb])
            p.op("scalar", lambda s, tc=tc: s.activation(rowb, pB[0:1, 0:512], AF.Sqrt),
                 outs=[rowb], ins=[pB])
            p.op("vector", lambda v: v.tensor_scalar(rowb, rowb, 1e-12, None, op0=OP.max, op1=OP.bypass),
                 outs=[rowb], ins=[rowb])
            p.op("vector", lambda v: v.reciprocal(rowb, rowb), outs=[rowb], ins=[rowb])
            p.op("vector", lambda v: v.tensor_copy(rowb2, rowb), outs=[rowb2], ins=[rowb])
            for sub in range(4):
                i = tc * 4 + sub
                p.op("tensor", lambda t, sub=sub: t.transpose(
                    pB[0:128, 0:1], rowb[0:1, sub * 128:(sub + 1) * 128], ctx.idn[0:1, 0:1]),
                    outs=[pB], ins=[rowb, ctx.idn])
                p.op("scalar", lambda s, i=i: s.copy(rkn_cols[:, i:i + 1], pB[0:128, 0:1]),
                     outs=[rkn_cols], ins=[pB])
            p.op("tensor", lambda t, tc=tc: t.matmul(pB[:, 0:512], ctx.ones_row[0:1, 0:128], rowb2,
                                                     start=True, stop=True), outs=[pB], ins=[ctx.ones_row, rowb2])
            p.op("scalar", lambda s, tc=tc: s.copy(rknbc[:, tc * 512:(tc + 1) * 512], pB[:, 0:512]),
                 outs=[rknbc], ins=[pB])
        k_tm2 = ctx.sbuf("l3_ktm2", [128, NT, 128], BF16)
        for i in range(NT):
            for c in range(DC):
                p.op("tensor", lambda t, i=i, c=c: t.matmul(
                    pA[:, 0:64], xn2[:, c, i * 128:(i + 1) * 128], WklT[:, c, :],
                    start=(c == 0), stop=(c == DC - 1)), outs=[pA], ins=[xn2, WklT])
            p.op("scalar", lambda s, i=i: s.copy(k_tm2[:, i, 0:64], pA[:, 0:64]), outs=[k_tm2], ins=[pA])
            p.op("scalar", lambda s, i=i: s.copy(k_tm2[:, i, 64:128], pA[:, 0:64]), outs=[k_tm2], ins=[pA])
        for h in range(NHEAD):
            for i in range(NT):
                p.op("tensor", lambda t, h=h, i=i: t.matmul(
                    pA[:, 0:DV], v_lat[:, i * 128:(i + 1) * 128], Vc_sb[:, h, :],
                    start=True, stop=True), outs=[pA], ins=[v_lat, Vc_sb])
                vt = ctx.sbuf(f"l3_vt{i % 2}", [128, DV], BF16)
                p.op("vector", lambda v, i=i, vt=vt: v.tensor_scalar(
                    vt, pA[:, 0:DV], rkn_cols[:, i:i + 1], None, op0=OP.mult, op1=OP.bypass),
                    outs=[vt], ins=[pA, rkn_cols])
                p.dma("scalar", lambda e, h=h, i=i, vt=vt: e.dma_start(
                    out=scr["vsd"][i * 128:(i + 1) * 128, h, :], in_=vt), outs=[scr["vsd"]], ins=[vt],
                    sem_key="D:l3vs")
        kk = ctx.sbuf("l3_kk", [128, L + T], BF16)
        p.op("gpsimd", lambda g: g.memset(kk[:, 0:L], 0.0), outs=[kk])
        p.op("scalar", lambda s: s.copy(kk[0:64, L:L + T], k_lat), outs=[kk], ins=[k_lat])
        p.op("scalar", lambda s: s.copy(kk[64:128, L:L + T], k_lat), outs=[kk], ins=[k_lat])

        for g in (range(NG) if PH >= 2 else []):
            ag = a_all[:, g, :]
            CP = ctx.view("l3_CP", [128, T], F32, 110656)
            d0 = ctx.view("l3_d0", [128, T], F32, 77824)
            d1 = ctx.view("l3_d1", [128, T], F32, 86016)
            ams = ctx.view("l3_ams", [128, T], F32, 86016)
            p.op("vector", lambda v, ag=ag: v.tensor_tensor(ams, ag, ms, op=OP.mult), outs=[ams], ins=[a_all, ms])
            p.op("vector", lambda v, ag=ag: v.tensor_tensor(d0, ag, ams, op=OP.subtract), outs=[d0], ins=[a_all, ams])
            p.op("vector", lambda v: v.tensor_tensor_scan(CP, d0, ams, 1.0, op0=OP.mult, op1=OP.add),
                 outs=[CP], ins=[d0, ams])
            p.op("vector", lambda v, g=g: v.tensor_copy(
                D_cols[:, g, :], CP.rearrange("p (n l) -> p n l", l=L)[:, :, L - 1]),
                outs=[D_cols], ins=[CP])
            for tc in range(T // 512):
                for hh in range(2):
                    p.op("tensor", lambda t, g=g, tc=tc, hh=hh: t.matmul(
                        pA[hh * 64:(hh + 1) * 64, 0:512], Qc_sb[:, 2 * g + hh, :],
                        q_lat[:, tc * 512:(tc + 1) * 512], start=True, stop=True),
                        outs=[pA], ins=[Qc_sb, q_lat])
                qcpt = ctx.sbuf("l3_otmp", [128, 512], F32)
                p.op("vector", lambda v, g=g, tc=tc, qcpt=qcpt: v.tensor_tensor(
                    qcpt, pA[:, 0:512], CP[:, tc * 512:(tc + 1) * 512],
                    op=OP.mult), outs=[qcpt], ins=[pA, CP])
                p.dma("scalar", lambda e, g=g, tc=tc, qcpt=qcpt: e.dma_start(
                    out=scr["qcpd"][g, :, tc * 512:(tc + 1) * 512], in_=qcpt),
                    outs=[scr["qcpd"]], ins=[qcpt], sem_key="D:l3qcp")
            a0z = ctx.view("l3_a0z", [128, L + T], F32, 118848)
            p.op("gpsimd", lambda g_: g_.memset(a0z[:, 0:L], 0.0), outs=[a0z], ins=[a_all])
            p.op("vector", lambda v, ag=ag: v.tensor_tensor(a0z[:, L:L + T], ag, ams, op=OP.subtract),
                 outs=[a0z], ins=[a_all, ams])
            Zr = ctx.view("l3_Zr", [128, T], F32, 94208)
            arev = ag[:, T - 1:0:-1]
            p.op("gpsimd", lambda g_: g_.memset(d0[:, 0:1], 0.0), outs=[d0], ins=[a_all])
            p.op("vector", lambda v, arev=arev: v.tensor_tensor(d1[:, 1:T], arev, ms[:, 1:T], op=OP.mult),
                 outs=[d1], ins=[a_all, ms])
            p.op("vector", lambda v, arev=arev: v.tensor_tensor(d0[:, 1:T], arev, d1[:, 1:T], op=OP.subtract),
                 outs=[d0], ins=[a_all, d1])
            p.op("vector", lambda v: v.tensor_copy(d1[:, 0:T], ms[:, 0:T]), outs=[d1], ins=[ms])
            p.op("vector", lambda v: v.tensor_tensor_scan(Zr, d0, d1, 0.0, op0=OP.mult, op1=OP.add),
                 outs=[Zr], ins=[d0, d1])
            for i in range(NT):
                znat = ctx.sbuf("l3_znat", [128, 128], F32)
                zstop = None if i == NT - 1 else T - 1 - i * 128 - 128
                p.op("vector", lambda v, i=i, zstop=zstop: v.tensor_copy(
                    znat, Zr[:, T - 1 - i * 128: zstop:-1]), outs=[znat], ins=[Zr])
                p.op("tensor", lambda t, i=i: t.transpose(pB[:, 0:128], znat, ctx.idn),
                     outs=[pB], ins=[znat, ctx.idn])
                zt_t = ctx.sbuf("l3_ztt", [128, 128], F32)
                p.op("scalar", lambda s, i=i: s.copy(zt_t, pB[:, 0:128]), outs=[zt_t], ins=[pB])
                kzt = ctx.sbuf("l3_kzt", [128, 128], BF16)
                p.op("vector", lambda v, i=i: v.tensor_tensor(kzt, zt_t, k_tm2[:, i, :], op=OP.mult),
                     outs=[kzt], ins=[zt_t, k_tm2])
                p.dma("scalar", lambda e, i=i, g=g, kzt=kzt: e.dma_start(
                    out=scr["kzd"][i * 128:(i + 1) * 128, g, :], in_=kzt), outs=[scr["kzd"]], ins=[kzt],
                    sem_key="D:l3kz")
            vsf = [ctx.view(f"l3_vsf{hh}", [128, L + 2048], BF16, 102400 + hh * 4128) for hh in range(2)]
            for hh in range(2):
                p.op("gpsimd", lambda g_, hh=hh: g_.memset(vsf[hh][:, 0:L], 0.0), outs=[vsf[hh]])
                for tc in range(T // 512):
                    p.op("tensor", lambda t, g=g, hh=hh, tc=tc: t.matmul(
                        pA[:, 0:512], Vc_sb[:, 2 * g + hh, :], v_lat[:, tc * 512:(tc + 1) * 512],
                        start=True, stop=True), outs=[pA], ins=[Vc_sb, v_lat])
                    p.op("vector", lambda v, hh=hh, tc=tc: v.tensor_tensor(
                        vsf[hh][:, L + tc * 512:L + (tc + 1) * 512], pA[:, 0:512],
                        rknbc[:, tc * 512:(tc + 1) * 512], op=OP.mult),
                        outs=[vsf[hh]], ins=[pA, rknbc])
            Rbuf = [ctx.view(f"l3_R{x}", [128, 512], F32, 127072 + x * 2048) for x in range(2)]
            prod1 = ctx.sbuf("l3_prod1", [128, 512], BF16)
            prod = ctx.sbuf("l3_prod", [128, 512], BF16)
            s2 = ctx.sbuf("l3_s2", [2, 512], BF16)
            q2tc = ctx.sbuf("l3_q2tc", [128, 512], BF16)
            oacc = [ctx.view(f"l3_oacc{hh}", [128, 512], F32, 131168 + hh * 2048) for hh in range(2)]
            pS, pP = ctx.bk(2), ctx.bk(3)
            for tc in range(T // 512):
                t0 = tc * 512
                for d in range(L):
                    if d == 0:
                        for hh in range(2):
                            p.op("tensor", lambda t, g=g, hh=hh, t0=t0: t.matmul(
                                pA[hh * 64:(hh + 1) * 64, 0:512], Qc_sb[:, 2 * g + hh, :],
                                q_lat[:, t0:t0 + 512], start=True, stop=True), outs=[pA], ins=[Qc_sb, q_lat])
                        p.op("vector", lambda v, t0=t0: v.tensor_copy(q2tc, pA[:, 0:512]),
                             outs=[q2tc], ins=[pA])
                        p.op("vector", lambda v, t0=t0: v.tensor_tensor(
                            prod, q2tc, kk[:, L + t0:L + t0 + 512], op=OP.mult), outs=[prod], ins=[q2tc, kk])
                    else:
                        Rc, Rp = Rbuf[d % 2], Rbuf[(d + 1) % 2]
                        if d == 1:
                            p.op("vector", lambda v, t0=t0, Rc=Rc: v.tensor_copy(
                                Rc, a0z[:, L + t0:L + t0 + 512]), outs=[Rc], ins=[a0z])
                        else:
                            p.op("vector", lambda v, d=d, t0=t0, Rc=Rc, Rp=Rp: v.tensor_tensor(
                                Rc, Rp, a0z[:, L + t0 - d + 1:L + t0 - d + 1 + 512], op=OP.mult),
                                outs=[Rc], ins=[Rp, a0z])
                        p.op("gpsimd", lambda g_, d=d, t0=t0: g_.tensor_tensor(
                            prod1, q2tc, kk[:, L + t0 - d:L + t0 - d + 512], op=OP.mult),
                            outs=[prod1], ins=[q2tc, kk])
                        p.op("vector", lambda v, d=d, Rc=Rc: v.tensor_tensor(prod, prod1, Rc, op=OP.mult),
                             outs=[prod], ins=[prod1, Rc])
                    p.op("tensor", lambda t, d=d: t.matmul(pS[0:2, 0:512], bo2, prod, start=True, stop=True),
                         outs=[pS], ins=[bo2, prod])
                    p.op("scalar", lambda s, d=d: s.copy(s2, pS[0:2, 0:512]), outs=[s2], ins=[pS])
                    for hh in range(2):
                        p.op("tensor", lambda t, hh=hh, d=d: t.matmul(pP[:, 0:512], sel[hh], s2,
                                                                      start=True, stop=True),
                             outs=[pP], ins=[sel[hh], s2])
                        if d == 0:
                            p.op("vector", lambda v, hh=hh, t0=t0: v.tensor_tensor(
                                oacc[hh], vsf[hh][:, L + t0:L + t0 + 512], pP[:, 0:512], op=OP.mult),
                                outs=[oacc[hh]], ins=[vsf[hh], pP])
                        else:
                            tmp = ctx.sbuf("l3_otmp", [128, 512], F32)
                            p.op("vector", lambda v, hh=hh, d=d, t0=t0, tmp=tmp: v.tensor_tensor(
                                tmp, vsf[hh][:, L + t0 - d:L + t0 - d + 512], pP[:, 0:512], op=OP.mult),
                                outs=[tmp], ins=[vsf[hh], pP])
                            p.op("gpsimd", lambda g_, hh=hh, tmp=tmp: g_.tensor_tensor(
                                oacc[hh], oacc[hh], tmp, op=OP.add), outs=[oacc[hh]], ins=[oacc[hh], tmp])
                for hh in range(2):
                    ob = ctx.sbuf("l3_prod", [128, 512], BF16)
                    p.op("vector", lambda v, hh=hh: v.tensor_copy(ob, oacc[hh]), outs=[ob], ins=[oacc[hh]])
                    p.dma("scalar", lambda e, g=g, hh=hh, tc=tc, ob=ob: e.dma_start(
                        out=scr["oin"][2 * g + hh, :, tc * 512:(tc + 1) * 512], in_=ob),
                        outs=[scr["oin"]], ins=[ob], sem_key="D:l3oi")

        if PH < 3:
            continue
        S_g = [ctx.view(f"l3_S{g}", [128, DV], F32, 126976 + g * 512) for g in range(NG)]
        for g in range(NG):
            p.op("gpsimd", lambda g_, g2=g: g_.memset(S_g[g2], 0.0), outs=[S_g[g]])
        KZT = ctx.view("l3_KZT", [8, 16, 512], BF16, 77824)
        VST = ctx.view("l3_VST", [8, 16, 1024], BF16, 94208)
        QCPs = ctx.sbuf("l3_QCPs", [128, NG, 128], F32)
        QCPb = ctx.sbuf("l3_QCPb", [128, NG * 128], BF16)
        pDS = ctx.bk(4)
        poA, poB = ctx.bk(5), ctx.bk(6)
        for sp in range(T // 128):
            p.dma("sync", lambda e, sp=sp: e.dma_start(
                out=KZT, in_=scr["kzd"].rearrange("(n l) g c -> l n (g c)", l=L)[:, sp * 16:(sp + 1) * 16, :]),
                outs=[KZT], ins=[scr["kzd"]], sem_key="D:l3kzt")
            p.dma("sync", lambda e, sp=sp: e.dma_start(
                out=VST, in_=scr["vsd"].rearrange("(n l) h v -> l n (h v)", l=L)[:, sp * 16:(sp + 1) * 16, :]),
                outs=[VST], ins=[scr["vsd"]], sem_key="D:l3vst")
            p.dma("sync", lambda e, sp=sp: e.dma_start(
                out=QCPs, in_=scr["qcpd"].rearrange("g p t -> p g t")[:, :, sp * 128:(sp + 1) * 128]),
                outs=[QCPs], ins=[scr["qcpd"]], sem_key="D:l3qcps")
            p.op("vector", lambda v: v.tensor_copy(QCPb.rearrange("p (g t) -> p g t", g=NG), QCPs), outs=[QCPb], ins=[QCPs])
            L3I = int(os.environ.get("L3I", "99"))
            for ii in range(16):
                i = sp * 16 + ii
                if L3I < 2:
                    continue
                for g in range(NG):
                    Sb = ctx.sbuf(f"l3_Sb{g}", [128, DV], BF16)
                    p.op("scalar", lambda sc, g=g, Sb=Sb: sc.copy(Sb, S_g[g]), outs=[Sb], ins=[S_g[g]])
                    for hh in range(2):
                        if L3I < 4:
                            continue
                        h = 2 * g + hh
                        po = poA if hh == 0 else poB
                        p.op("tensor", lambda t, g=g, hh=hh, i=i, ii=ii, h=h, po=po, Sb=Sb: t.matmul(
                            po[:, g * 128 + ii * 8:g * 128 + ii * 8 + 8],
                            Sb[hh * 64:(hh + 1) * 64, :],
                            QCPb[hh * 64:(hh + 1) * 64, g * 128 + ii * L:g * 128 + (ii + 1) * L],
                            start=True, stop=True), outs=[po], ins=[Sb, QCPb])
                    for hh in range(2):
                        if L3I < 3:
                            continue
                        h = 2 * g + hh
                        p.op("tensor", lambda t, g=g, hh=hh, ii=ii, h=h: t.matmul(
                            pDS[hh * 64:(hh + 1) * 64, g * 128:(g + 1) * 128],
                            KZT[:, ii, g * 128 + hh * 64:g * 128 + (hh + 1) * 64],
                            VST[:, ii, h * 128:(h + 1) * 128], start=True, stop=True),
                            outs=[pDS], ins=[KZT, VST])
                for g in (range(NG) if L3I >= 3 else []):
                    p.op("vector", lambda v, g=g, i=i: v.scalar_tensor_tensor(
                        S_g[g], S_g[g], D_cols[:, g, i:i + 1], pDS[:, g * 128:(g + 1) * 128],
                        op0=OP.mult, op1=OP.add), outs=[S_g[g]], ins=[S_g[g], D_cols, pDS])
            for h in (range(NHEAD) if L3I >= 5 else []):
                g, hh = h // 2, h % 2
                po = poA if hh == 0 else poB
                ob2 = ctx.sbuf("l3_ob2", [128, 128], BF16)
                p.op("vector", lambda v, h=h, po=po, g=g: v.tensor_copy(
                    ob2, po[:, g * 128:g * 128 + 128]), outs=[ob2], ins=[po])
                p.dma("scalar", lambda e, h=h, sp=sp, ob2=ob2: e.dma_start(
                    out=scr["oit"][h, :, sp * 128:(sp + 1) * 128], in_=ob2), outs=[scr["oit"]], ins=[ob2],
                    sem_key="D:l3ot")

        if PH < 4:
            continue
        WoT3 = ctx.load_transposed("l3_WoT3", W["Wo"], D, NHEAD * DV, stage_off=94208, out_off=32768)
        os_sl = ctx.view("l3_osl", [128, NHEAD, 512], BF16, 77824)
        os_s2 = ctx.view("l3_os2", [128, NHEAD, 512], BF16, 86016)
        s_t = ctx.sbuf("ln_sq", [128, D], F32)
        for tc in range(T // 512):
            for h in range(NHEAD):
                p.dma("sync", lambda e, h=h, tc=tc: e.dma_start(
                    out=os_sl[:, h, :], in_=scr["oin"][h, :, tc * 512:(tc + 1) * 512]),
                    outs=[os_sl], ins=[scr["oin"]], sem_key="D:l3or1")
                p.dma("sync", lambda e, h=h, tc=tc: e.dma_start(
                    out=os_s2[:, h, :], in_=scr["oit"][h, :, tc * 512:(tc + 1) * 512]),
                    outs=[os_s2], ins=[scr["oit"]], sem_key="D:l3or2")
            for sub in range(4):
                i = tc * 4 + sub
                xnt = ctx.sbuf("pn_x0", [128, D], F32)
                p.dma("sync", lambda e, b=b, i=i, xnt=xnt: e.dma_start(
                    out=xnt, in_=scr["xnf"][b, i * 128:(i + 1) * 128, :]), outs=[xnt], ins=[scr["xnf"]],
                    sem_key="D:l3xn")
                for half in range(2):
                    for h in range(NHEAD):
                        p.op("tensor", lambda t, h=h, sub=sub, half=half: t.matmul(
                            pA[:, 0:512], os_sl[:, h, sub * 128:(sub + 1) * 128],
                            WoT3[:, h, half * 512:(half + 1) * 512], start=(h == 0), stop=False),
                            outs=[pA], ins=[os_sl, WoT3])
                    for h in range(NHEAD):
                        p.op("tensor", lambda t, h=h, sub=sub, half=half: t.matmul(
                            pA[:, 0:512], os_s2[:, h, sub * 128:(sub + 1) * 128],
                            WoT3[:, h, half * 512:(half + 1) * 512], start=False, stop=(h == NHEAD - 1)),
                            outs=[pA], ins=[os_s2, WoT3])
                    p.op("vector", lambda v, half=half: v.tensor_tensor(
                        s_t[:, half * 512:(half + 1) * 512], pA[:, 0:512],
                        xnt[:, half * 512:(half + 1) * 512], op=OP.add), outs=[s_t], ins=[pA, xnt])
                p.dma("scalar", lambda e, i=i, b=b: e.dma_start(
                    out=scr["s"][b, i * 128:(i + 1) * 128, :], in_=s_t), outs=[scr["s"]], ins=[s_t],
                    sem_key="D:l3s")

        if PH < 5:
            continue
        WrT3 = ctx.view("l3_WrT3", [128, DC, D], BF16, 49152)
        WkT3 = ctx.view("l3_WkT3", [128, DC, D], BF16, 65536)
        WvT3 = ctx.view("l3_WvT3", [128, DC, D], BF16, 81920)
        for dst, wsrc in ((WrT3, "W_r"), (WkT3, "W_k"), (WvT3, "W_v")):
            stg = ctx.view("l3_fstg", [128, DC, D], F32, 98304)
            p.dma("gpsimd", lambda e, stg=stg, wsrc=wsrc: e.dma_start(
                out=stg, in_=W[wsrc].rearrange("(a p) m -> p a m", p=128)), outs=[stg], ins=[W[wsrc]],
                sem_key="D:l3fw")
            for r in range(DC):
                for c in range(DC):
                    p.op("tensor", lambda tt, r=r, c=c, stg=stg: tt.transpose(
                        pB[:, 0:128], stg[:, r, c * 128:(c + 1) * 128], ctx.idn), outs=[pB], ins=[stg, ctx.idn])
                    p.op("scalar", lambda s, r=r, c=c, dst=dst: s.copy(
                        dst[:, c, r * 128:(r + 1) * 128], pB[:, 0:128]), outs=[dst], ins=[pB])
        xk = ctx.view("l3_xk", [128, DC, T], BF16, 98304)
        for mcol, dst in ((muk, xk),):
            for c in range(DC):
                omm = ctx.sbuf("l3_omm", [128, 1], F32)
                p.op("vector", lambda v, c=c, mcol=mcol, omm=omm: v.tensor_scalar(
                    omm, mcol[:, c:c + 1], 1.0, -1.0, op0=OP.subtract, op1=OP.mult),
                    outs=[omm], ins=[mcol])
                for tc in range(T // 512):
                    t1 = ctx.sbuf("l3_otmp", [128, 512], F32)
                    lo = tc * 512
                    p.op("vector", lambda v, c=c, lo=lo, mcol=mcol, t1=t1: v.tensor_scalar(
                        t1, xn_fm[:, c, lo:lo + 512], mcol[:, c:c + 1], None, op0=OP.mult, op1=OP.bypass),
                        outs=[t1], ins=[xn_fm, mcol])
                    if tc == 0:
                        p.op("vector", lambda v, c=c, dst=dst, t1=t1, omm=omm: v.scalar_tensor_tensor(
                            dst[:, c, 1:512], xn_fm[:, c, 0:511], omm, t1[:, 1:512], op0=OP.mult, op1=OP.add),
                            outs=[dst], ins=[xn_fm, omm, t1])
                        p.op("vector", lambda v, c=c, dst=dst, t1=t1: v.tensor_copy(dst[:, c, 0:1], t1[:, 0:1]),
                             outs=[dst], ins=[t1])
                    else:
                        p.op("vector", lambda v, c=c, lo=lo, dst=dst, t1=t1, omm=omm: v.scalar_tensor_tensor(
                            dst[:, c, lo:lo + 512], xn_fm[:, c, lo - 1:lo + 511], omm, t1,
                            op0=OP.mult, op1=OP.add), outs=[dst], ins=[xn_fm, omm, t1])
        for i in range(NT):
            for half in range(2):
                for c in range(DC):
                    p.op("tensor", lambda t, i=i, half=half, c=c: t.matmul(
                        pA[:, 0:512], xk[:, c, i * 128:(i + 1) * 128], WkT3[:, c, half * 512:(half + 1) * 512],
                        start=(c == 0), stop=(c == DC - 1)), outs=[pA], ins=[xk, WkT3])
                kr = ctx.sbuf("l3_otmp", [128, 512], F32)
                p.op("vector", lambda v, kr=kr: v.tensor_scalar(kr, pA[:, 0:512], 0.0, None,
                                                                op0=OP.max, op1=OP.bypass), outs=[kr], ins=[pA])
                ka = ctx.sbuf("l3_ka", [128, 512], BF16)
                p.op("scalar", lambda s, ka=ka, kr=kr: s.activation(ka, kr, AF.Square), outs=[ka], ins=[kr])
                p.dma("scalar", lambda e, i=i, half=half, ka=ka: e.dma_start(
                    out=scr["kad"][i * 128:(i + 1) * 128, half * 512:(half + 1) * 512], in_=ka),
                    outs=[scr["kad"]], ins=[ka], sem_key="D:l3ka")
        xr = ctx.view("l3_xr", [128, DC, T], BF16, 98304)
        for mcol, dst in ((mur, xr),):
            for c in range(DC):
                omm = ctx.sbuf("l3_omm", [128, 1], F32)
                p.op("vector", lambda v, c=c, mcol=mcol, omm=omm: v.tensor_scalar(
                    omm, mcol[:, c:c + 1], 1.0, -1.0, op0=OP.subtract, op1=OP.mult),
                    outs=[omm], ins=[mcol])
                for tc in range(T // 512):
                    t1 = ctx.sbuf("l3_otmp", [128, 512], F32)
                    lo = tc * 512
                    p.op("vector", lambda v, c=c, lo=lo, mcol=mcol, t1=t1: v.tensor_scalar(
                        t1, xn_fm[:, c, lo:lo + 512], mcol[:, c:c + 1], None, op0=OP.mult, op1=OP.bypass),
                        outs=[t1], ins=[xn_fm, mcol])
                    if tc == 0:
                        p.op("vector", lambda v, c=c, dst=dst, t1=t1, omm=omm: v.scalar_tensor_tensor(
                            dst[:, c, 1:512], xn_fm[:, c, 0:511], omm, t1[:, 1:512], op0=OP.mult, op1=OP.add),
                            outs=[dst], ins=[xn_fm, omm, t1])
                        p.op("vector", lambda v, c=c, dst=dst, t1=t1: v.tensor_copy(dst[:, c, 0:1], t1[:, 0:1]),
                             outs=[dst], ins=[t1])
                    else:
                        p.op("vector", lambda v, c=c, lo=lo, dst=dst, t1=t1, omm=omm: v.scalar_tensor_tensor(
                            dst[:, c, lo:lo + 512], xn_fm[:, c, lo - 1:lo + 511], omm, t1,
                            op0=OP.mult, op1=OP.add), outs=[dst], ins=[xn_fm, omm, t1])
        kaf = ctx.view("l3_kaf", [128, DC, T], BF16, 0)
        for c in range(DC):
            p.dma("sync", lambda e, c=c: e.dma_start_transpose(
                kaf[:, c, :], scr["kad"][:, c * 128:(c + 1) * 128]), outs=[kaf], ins=[scr["kad"]],
                sem_key="D:l3kaf")
        for i in range(NT):
            rt_ = ctx.sbuf("pn_x0", [128, D], F32)
            o2 = ctx.view("l3_ep_o2", [128, D], F32, 32768)
            for half in range(2):
                for c in range(DC):
                    p.op("tensor", lambda t, i=i, half=half, c=c: t.matmul(
                        pA[:, 0:512], xr[:, c, i * 128:(i + 1) * 128], WrT3[:, c, half * 512:(half + 1) * 512],
                        start=(c == 0), stop=(c == DC - 1)), outs=[pA], ins=[xr, WrT3])
                p.op("scalar", lambda s, half=half, rt_=rt_: s.activation(
                    rt_[:, half * 512:(half + 1) * 512], pA[:, 0:512], AF.Sigmoid), outs=[rt_], ins=[pA])
                for c in range(DC):
                    p.op("tensor", lambda t, i=i, half=half, c=c: t.matmul(
                        pA[:, 0:512], kaf[:, c, i * 128:(i + 1) * 128], WvT3[:, c, half * 512:(half + 1) * 512],
                        start=(c == 0), stop=(c == DC - 1)), outs=[pA], ins=[kaf, WvT3])
                p.op("vector", lambda v, half=half, o2=o2: v.tensor_copy(
                    o2[:, half * 512:(half + 1) * 512], pA[:, 0:512]), outs=[o2], ins=[pA])
            p.op("vector", lambda v, rt_=rt_, o2=o2: v.tensor_tensor(o2, o2, rt_, op=OP.mult),
                 outs=[o2], ins=[o2, rt_])
            xt = ctx.sbuf(f"pn_x{i % ctx.pnb}", [128, D], F32)
            st_ = ctx.sbuf(f"pn_xn{i % ctx.pnb}", [128, D], F32)
            gs = ctx.sbuf("ln_sq", [128, D], F32)
            p.dma("sync", lambda e, b=b, i=i, xt=xt: e.dma_start(
                out=xt, in_=x_in[b, i * 128:(i + 1) * 128, :]), outs=[xt], ins=[x_in])
            p.dma("sync", lambda e, b=b, i=i, st_=st_: e.dma_start(
                out=st_, in_=scr["s"][b, i * 128:(i + 1) * 128, :]), outs=[st_], ins=[scr["s"]],
                sem_key="D:l3s_rd")
            p.op("scalar", lambda s, o2=o2, gs=gs: s.activation(gs, o2, AF.Silu), outs=[gs], ins=[o2])
            p.op("vector", lambda v, gs=gs, st_=st_: v.tensor_tensor(gs, gs, st_, op=OP.mult),
                 outs=[gs], ins=[gs, st_])
            p.op("vector", lambda v, gs=gs, xt=xt: v.tensor_tensor(gs, gs, xt, op=OP.add),
                 outs=[gs], ins=[gs, xt])
            p.dma("scalar", lambda e, b=b, i=i, gs=gs: e.dma_start(
                out=y_out[b, i * 128:(i + 1) * 128, :], in_=gs), outs=[y_out], ins=[gs], sem_key="D:y")

from concourse.bass_utils import run_bass_kernel_spmd

_CACHE = {}

B, T, D = 2, 2048, 1024


def _mk_fused():
    if "fused" in _CACHE:
        return _CACHE["fused"]
    nc = bass.Bass("TRN2", target_bir_lowering=False)
    NH, DH = 16, 64
    H3, C3, DV3 = 8, 64, 128
    dx = nc.dram_tensor("x", [B, T, D], F32, kind="ExternalInput").ap()
    x1 = nc.dram_tensor("x1", [B, T, D], F32, kind="Internal").ap()
    x2 = nc.dram_tensor("x2", [B, T, D], F32, kind="Internal").ap()
    x3 = nc.dram_tensor("x3", [B, T, D], F32, kind="Internal").ap()
    dy = nc.dram_tensor("y", [B, T, D], F32, kind="ExternalOutput").ap()
    WD = {}
    for k, shp in [
            ("L0_Wq", [2, 128, D]), ("L0_Wk", [128, D]), ("L0_Wv", [128, D]), ("L0_Wo", [D, 256]),
            ("L0_sg", [D, 2816]), ("L0_su", [D, 2816]), ("L0_sd", [2816, D]),
            ("L0_rg", [7, D, 2816]), ("L0_ru", [7, D, 2816]), ("L0_rd", [7, 2816, D]),
            ("L0_Wr", [D, 7]), ("L0_rb", [7]),
            ("L0_cosk", [T, 64]), ("L0_sink", [T, 64]), ("L0_cosq", [T, 64]), ("L0_sinq", [T, 64]),
            ("L1_w_qkv", [D, 3072]), ("L1_w_out", [1024, D]), ("L1_routing_token", [D]),
            ("L1_gamma_light", [D]), ("L1_w1_light", [D, 512]), ("L1_b1_light", [512]),
            ("L1_w2_light", [512, D]), ("L1_b2_light", [D]), ("L1_gamma_heavy", [D]),
            ("L1_w1_heavy", [D, 4096]), ("L1_b1_heavy", [4096]), ("L1_w2_heavy", [4096, D]),
            ("L1_b2_heavy", [D]),
            ("L2_W_Q", [NH, D, DH]), ("L2_b_Q", [NH, DH]), ("L2_W_K", [NH, D, DH]),
            ("L2_b_K", [NH, DH]), ("L2_W_V", [NH, D, DH]), ("L2_b_V", [NH, DH]),
            ("L2_W_O", [NH, DH, D]), ("L2_b_O", [D]), ("L2_sim_matrix", [D, NH]),
            ("L2_gates", [NH]), ("L2_q_proj", [NH, D, DH]), ("L2_k_proj", [NH, D, DH]),
            ("L2_v_proj", [NH, D, DH]), ("L2_o_proj", [NH, DH, D]),
            ("L3_Wq", [C3, D]), ("L3_Wk", [C3, D]), ("L3_Wv", [C3, D]), ("L3_Qc", [H3, C3, C3]),
            ("L3_Vc", [H3, C3, DV3]), ("L3_Wa_w", [512, D]), ("L3_Wa_b", [512]),
            ("L3_Wo", [D, H3 * DV3]), ("L3_gamma", [D]), ("L3_beta", [D]),
            ("L3_W_r", [D, D]), ("L3_W_k", [D, D]), ("L3_W_v", [D, D]),
            ("L3_mu_r", [D]), ("L3_mu_k", [D])]:
        WD[k] = nc.dram_tensor(k, shp, F32, kind="ExternalInput").ap()
    scr = {
        "xnb": nc.dram_tensor("scr_xnb", [B, T, D], BF16, kind="Internal").ap(),
        "xnb_lo": nc.dram_tensor("scr_xnb_lo", [B, T, D], BF16, kind="Internal").ap(),
        "s": nc.dram_tensor("scr_s", [B, T, D], F32, kind="Internal").ap(),
        "h": nc.dram_tensor("scr_h", [128, 22, T // 2], BF16, kind="Internal").ap(),
        "xnf": nc.dram_tensor("scr_xnf", [B, T, D], F32, kind="Internal").ap(),
        "vsd": nc.dram_tensor("scr_vsd", [T, H3, DV3], BF16, kind="Internal").ap(),
        "kzd": nc.dram_tensor("scr_kzd", [T, 4, 128], BF16, kind="Internal").ap(),
        "oin": nc.dram_tensor("scr_oin", [H3, DV3, T], BF16, kind="Internal").ap(),
        "oit": nc.dram_tensor("scr_oit", [H3, DV3, T], BF16, kind="Internal").ap(),
        "kad": nc.dram_tensor("scr_kad", [T, D], BF16, kind="Internal").ap(),
        "qcpd": nc.dram_tensor("scr_qcpd", [4, 128, T], F32, kind="Internal").ap(),
    }
    p = Prog(nc)
    ctx = Ctx(nc, p, B=B, T=T, D=D)
    W0 = {k[3:]: WD[k] for k in WD if k.startswith("L0_") and "cos" not in k and "sin" not in k}
    C0 = {k: WD["L0_" + k] for k in ("cosk", "sink", "cosq", "sinq")}
    build_l0(ctx, dx, x1, W0, C0, scr)
    ctx.fresh_layer()
    build_l1(ctx, x1, x2, {k[3:]: WD[k] for k in WD if k.startswith("L1_")}, scr)
    ctx.fresh_layer()
    build_l2(ctx, x2, x3, {k[3:]: WD[k] for k in WD if k.startswith("L2_")}, scr)
    ctx.fresh_layer()
    build_l3(ctx, x3, dy, {k[3:]: WD[k] for k in WD if k.startswith("L3_")}, scr)
    p.build()
    _CACHE["fused"] = nc
    return nc


_WMAP = {
    "L0_Wq": "L0_attn_Wq", "L0_Wk": "L0_attn_Wk", "L0_Wv": "L0_attn_Wv", "L0_Wo": "L0_attn_Wo",
    "L0_sg": "L0_ffn_shared_gate", "L0_su": "L0_ffn_shared_up", "L0_sd": "L0_ffn_shared_down",
    "L0_rg": "L0_ffn_routed_gate", "L0_ru": "L0_ffn_routed_up", "L0_rd": "L0_ffn_routed_down",
    "L0_Wr": "L0_ffn_Wr", "L0_rb": "L0_ffn_routing_bias",
    "L1_w_qkv": "L1_attn_w_qkv", "L1_w_out": "L1_attn_w_out",
    "L1_routing_token": "L1_ffn_routing_token", "L1_gamma_light": "L1_ffn_gamma_light",
    "L1_w1_light": "L1_ffn_w1_light", "L1_b1_light": "L1_ffn_b1_light",
    "L1_w2_light": "L1_ffn_w2_light", "L1_b2_light": "L1_ffn_b2_light",
    "L1_gamma_heavy": "L1_ffn_gamma_heavy", "L1_w1_heavy": "L1_ffn_w1_heavy",
    "L1_b1_heavy": "L1_ffn_b1_heavy", "L1_w2_heavy": "L1_ffn_w2_heavy",
    "L1_b2_heavy": "L1_ffn_b2_heavy",
    "L2_W_Q": "L2_attn_W_Q", "L2_b_Q": "L2_attn_b_Q", "L2_W_K": "L2_attn_W_K",
    "L2_b_K": "L2_attn_b_K", "L2_W_V": "L2_attn_W_V", "L2_b_V": "L2_attn_b_V",
    "L2_W_O": "L2_attn_W_O", "L2_b_O": "L2_attn_b_O", "L2_sim_matrix": "L2_ffn_sim_matrix",
    "L2_gates": "L2_ffn_gates", "L2_q_proj": "L2_ffn_q_proj", "L2_k_proj": "L2_ffn_k_proj",
    "L2_v_proj": "L2_ffn_v_proj", "L2_o_proj": "L2_ffn_o_proj",
    "L3_Wq": "L3_attn_Wq", "L3_Wk": "L3_attn_Wk", "L3_Wv": "L3_attn_Wv", "L3_Qc": "L3_attn_Qc",
    "L3_Vc": "L3_attn_Vc", "L3_Wa_w": "L3_attn_Wa_w", "L3_Wa_b": "L3_attn_Wa_b",
    "L3_Wo": "L3_attn_Wo", "L3_gamma": "L3_attn_gamma", "L3_beta": "L3_attn_beta",
    "L3_W_r": "L3_ffn_W_r", "L3_W_k": "L3_ffn_W_k", "L3_W_v": "L3_ffn_W_v",
    "L3_mu_r": "L3_ffn_mu_r", "L3_mu_k": "L3_ffn_mu_k",
}


def kernel(**inputs):
    _ = inputs["L2_attn_pos_embed"]
    n = 8
    x = np.ascontiguousarray(inputs["L0_attn_x"], dtype=np.float32)
    consts = l0_consts(T)
    nc = _mk_fused()
    wvals = {k: np.ascontiguousarray(inputs[v], dtype=np.float32) for k, v in _WMAP.items()}
    for k in ("cosk", "sink", "cosq", "sinq"):
        wvals["L0_" + k] = consts[k]
    in_maps = []
    for c in range(n):
        m = {"x": np.ascontiguousarray(x[2 * c:2 * c + 2])}
        m.update(wvals)
        in_maps.append(m)
    res = run_bass_kernel_spmd(nc, in_maps, core_ids=list(range(n)))
    return np.concatenate([np.asarray(res.results[c]["y"], dtype=np.float32) for c in range(n)], axis=0)
